# version 05
# baseline (speedup 1.0000x reference)
import numpy as np
import concourse.bass as bass
import concourse.mybir as mybir
from concourse.bass_utils import run_bass_kernel_spmd

F32 = mybir.dt.float32
BF16 = mybir.dt.bfloat16
I32 = mybir.dt.int32
ALU = mybir.AluOpType
AF = mybir.ActivationFunctionType
AX = mybir.AxisListType.X

NCORES = 8
NSEQ = 8
T = 512
D = 1024
H = 16
HD = 64
HID = 4096
NB = T // 128
NT = NSEQ * NB
DCH = D // 128
SCALE = float(HD) ** -0.5
EPS = 1e-5
NEG_BIG = -1.0e30

ENGINES = ("sync", "tensor", "vector", "scalar", "gpsimd")


class Emitter:

    def __init__(self, nc, engine, handles):
        self.nc = nc
        self.engine = engine
        self.h = handles
        self.c = {k: 0 for k in handles}

    def eng(self, engine):
        return getattr(self.nc, engine)

    def emit(self, engine, fn, inc=None, n=1):
        if self.engine == engine:
            ins = fn(self.eng(engine))
            if inc is not None:
                ins.then_inc(self.h[inc], n)
        if inc is not None:
            self.c[inc] += n

    def wait(self, engine, sem, val=None):
        if self.engine == engine:
            self.eng(engine).wait_ge(self.h[sem], self.c[sem] if val is None else val)

    def fence(self, engine, sem):
        self.wait(engine, sem)

    def val(self, sem):
        return self.c[sem]


def build_program(debug=(), limit_b=NB, phases=("A", "B", "C", "D")):
    nc = bass.Bass("TRN2", target_bir_lowering=False, detect_race_conditions=False)

    x_in = nc.dram_tensor("x", [NSEQ, T, D], F32, kind="ExternalInput").ap()
    wqkvT = nc.dram_tensor("wqkvT", [D, 3 * D], F32, kind="ExternalInput").ap()
    bqkv = nc.dram_tensor("bqkv", [1, 3 * D], F32, kind="ExternalInput").ap()
    wprojT = nc.dram_tensor("wprojT", [D, D], F32, kind="ExternalInput").ap()
    bproj = nc.dram_tensor("bproj", [1, D], F32, kind="ExternalInput").ap()
    ln_g = nc.dram_tensor("ln_g", [1, D], F32, kind="ExternalInput").ap()
    ln_b = nc.dram_tensor("ln_b", [1, D], F32, kind="ExternalInput").ap()
    w1T = nc.dram_tensor("w1T", [D, HID], F32, kind="ExternalInput").ap()
    w2T = nc.dram_tensor("w2T", [HID, D], F32, kind="ExternalInput").ap()
    out = nc.dram_tensor("out", [NSEQ, T, D], F32, kind="ExternalOutput").ap()

    x2_kind = "ExternalOutput" if "x2" in debug else "Internal"
    x2d = nc.dram_tensor("dbg_x2" if "x2" in debug else "x2d",
                         [NT, 128, D], F32, kind=x2_kind).ap()
    zd = nc.dram_tensor("zd", [NT, 128, HID], F32, kind="Internal").ap()
    cc_in = nc.dram_tensor("cc_in", [1, 1], F32, kind="Internal").ap()
    cc_out = nc.dram_tensor("cc_out", [1, 1], F32, kind="Internal").ap()
    h1_dbg = (nc.dram_tensor("dbg_h1", [NT, 128, D], F32, kind="ExternalOutput").ap()
              if "h1" in debug else None)
    q2_dbg = (nc.dram_tensor("dbg_q2", [6, 128, 32], F32, kind="ExternalOutput").ap()
              if "q2" in debug else None)
    c8_dbg = (nc.dram_tensor("dbg_c8", [2, 128, HID], mybir.dt.int8, kind="ExternalOutput").ap()
              if "q2" in debug else None)
    st_dbg = (nc.dram_tensor("dbg_st", [128, 32], F32, kind="ExternalOutput").ap()
              if "st" in debug else None)

    sem_names = ["lxa", "lxb", "lwa", "lwb", "tp", "vp", "sp",
                 "gpa", "gpb", "gi", "so", "cc", "lb"]

    import contextlib
    with contextlib.ExitStack() as _stk:
        def sb(name, shape, dtype=F32):
            return _stk.enter_context(nc.sbuf_tensor(name, shape, dtype))

        def psb(name, shape, dtype=F32):
            return _stk.enter_context(nc.psum_tensor(name, shape, dtype))

        ident = sb("ident", [128, 128])
        identb = sb("identb", [128, 128], BF16)
        ones1 = sb("ones1", [1, 128])
        bqkv_t = sb("bqkv_t", [1, 3 * D])
        bproj_t = sb("bproj_t", [1, D])
        xt = sb("xt", [128, 2, D])
        ln1 = sb("ln1", [128, 2, D])
        smat = sb("smat", [128, 8, NSEQ, NSEQ])
        emat = sb("emat", [128, 8, NSEQ, NSEQ])
        mx = sb("mx", [128, 64])
        sm = sb("sm", [128, 64])
        prods = sb("prods", [128, 8, HD])
        junk = sb("junk", [128, D])
        junkS = sb("junkS", [128, D])
        stats = sb("stats", [128, 8, 8])
        nstat = sb("nstat", [128, 16])
        x2t = sb("x2t", [128, 2, D])
        wstage = sb("wstage", [128, 2, 512])
        bst = sb("bst", [128, 32])
        cnts = sb("cnts", [128, 64])
        ccrow = sb("ccrow", [1, 8])
        ARENA_F32 = 36864
        arena = sb("arena", [128, ARENA_F32])
        arena_addr = nc.lookup_mloc(arena).addr

        def overlay(name, shape, dtype, byte_off):
            h = nc.virtual_sbuf_tensor(name, shape, dtype)
            nc._tensor(h.name, list(h.shape), h.dtype, type="SB")
            mloc = nc.lookup_mloc(h)
            mloc.allocated = True
            mloc.addr = arena_addr + byte_off
            return h

        xfm = overlay("xfm", [128, NSEQ, DCH, 128], F32, 0)
        wbuf = overlay("wbuf", [128, 2, DCH, 512], F32, 32768)
        ybuf = overlay("ybuf", [128, NSEQ, 3, 512], F32, 65536)
        aout = overlay("aout", [128, NSEQ, D], F32, 114688)
        h1 = overlay("h1", [128, NT, D], F32, 0)
        u1tok = overlay("u1tok", [128, NT, D], BF16, 0)
        t1 = overlay("t1", [128, DCH, HID], BF16, 65536)
        c8 = overlay("c8", [128, NT, HID], mybir.dt.int8, 0)
        u1f = overlay("u1f", [128, DCH, 128], BF16, 131072)
        zstage = overlay("zstage", [128, 2, 512], F32, 133120)
        zrt2 = overlay("zrt2", [128, 2, 1024], F32, 131072)
        t2 = overlay("t2", [128, 32, D], BF16, 0)
        u2fm = overlay("u2fm", [128, 32, 128], BF16, 65536)
        u2tok = overlay("u2tok", [128, HID], BF16, 73728)
        zrt = overlay("zrt", [128, 2, HID], F32, 81920)
        o2b = overlay("o2b", [128, D], F32, 114688)
        ptr0 = psb("ptr0", [128, 128])
        ptr1 = psb("ptr1", [128, 128])
        ptr2 = psb("ptr2", [128, 128])
        ptr3 = psb("ptr3", [128, 128])
        ptrs = [ptr0, ptr1, ptr2, ptr3]
        pmm0 = psb("pmm0", [128, 512])
        pmm1 = psb("pmm1", [128, 512])

        handles = {nm: _stk.enter_context(nc.semaphore(nm)) for nm in sem_names}

        def program(E: Emitter):
            FS, FS2, FM, FT, FR = 0, 1, 2, 3, 4

            E.emit("gpsimd", lambda g: g.iota(junk[:, 0:128], [[1, 128]], channel_multiplier=0,
                                              allow_small_or_imprecise_dtypes=True), inc="gi")
            E.emit("gpsimd", lambda g: g.iota(junkS[:, 0:128], [[0, 128]], channel_multiplier=1,
                                              allow_small_or_imprecise_dtypes=True), inc="gi")
            E.wait("vector", "gi")
            E.emit("vector", lambda v: v.tensor_tensor(ident[:], junk[:, 0:128], junkS[:, 0:128],
                                                       ALU.is_equal), inc="vp")
            E.emit("vector", lambda v: v.memset(ones1[:], 1.0), inc="vp")
            E.fence("vector", "vp")
            E.emit("vector", lambda v: v.tensor_copy(identb[:], ident[:]), inc="vp")
            E.emit("sync", lambda s: s.dma_start(out=bqkv_t[:], in_=bqkv[:]), inc="lb", n=16)
            E.emit("sync", lambda s: s.dma_start(out=bproj_t[:], in_=bproj[:]), inc="lb", n=16)

            E.wait("tensor", "vp")
            E.wait("tensor", "lb")

            xslot_free = {0: (0, 0), 1: (0, 0)}
            lnslot_free = {0: 0, 1: 0}
            wseq = [0]
            wslot_free = {0: 0, 1: 0}
            pslot = {0: 0, 1: 1}
            pwsem = {0: "lwa", 1: "lwb"}
            x2_gp = []
            pmm_free = {0: 0, 1: 0}
            pmmS_free = {0: 0, 1: 0}
            ptr_free = {0: 0, 1: 0, 2: 0, 3: 0}

            def load_x(n, b):
                sl = n % 2
                lxs = "lxa" if sl == 0 else "lxb"
                vv, ss = xslot_free[sl]
                E.wait("sync", "vp", vv)
                E.wait("sync", "sp", ss)
                E.emit("sync", lambda s, n=n, b=b, sl=sl: s.dma_start(
                    out=xt[:, sl, :], in_=x_in[n, b * 128:(b + 1) * 128, :]),
                    inc=lxs, n=16)
                return sl, lxs

            def rsqrt_chain(tcol, rcol, jcol, thcol, w):
                tA = nstat[:, tcol:tcol + w]
                rA = nstat[:, rcol:rcol + w]
                jA = nstat[:, jcol:jcol + w]
                thA = nstat[:, thcol:thcol + w]
                E.wait("scalar", "vp")
                E.emit("scalar", lambda s: s.activation(jA, tA, AF.Ln), inc="sp")
                E.fence("scalar", "sp")
                E.emit("scalar", lambda s: s.activation(rA, jA, AF.Exp, scale=-0.5), inc="sp")
                E.wait("vector", "sp")
                E.emit("vector", lambda v: v.tensor_scalar(thA, tA, 0.5, None, ALU.mult), inc="vp")
                for _ in range(2):
                    E.fence("vector", "vp")
                    E.emit("vector", lambda v: v.tensor_mul(jA, rA, rA), inc="vp")
                    E.fence("vector", "vp")
                    E.emit("vector", lambda v: v.tensor_mul(jA, jA, thA), inc="vp")
                    E.fence("vector", "vp")
                    E.emit("vector", lambda v: v.tensor_scalar(jA, jA, -1.0, 1.5,
                                                               ALU.mult, op1=ALU.add), inc="vp")
                    E.fence("vector", "vp")
                    E.emit("vector", lambda v: v.tensor_mul(rA, rA, jA), inc="vp")

            allowed = [(n, m) for n in range(NSEQ) for m in range(NSEQ)
                       if not (n < 4 and m >= 4)]

            def stats_chain(fs, fs2, fm, ft, fr):
                E.wait("vector", "sp")
                E.fence("vector", "vp")
                sA = stats[:, :, fs]
                s2A = stats[:, :, fs2]
                mA = stats[:, :, fm]
                tA = stats[:, :, ft]
                rA = stats[:, :, fr]
                E.emit("vector", lambda v: v.tensor_scalar(mA, sA, 1.0 / D, None, ALU.mult), inc="vp")
                E.emit("vector", lambda v: v.tensor_scalar(nstat[:, 0:8], s2A, 1.0 / D,
                                                           None, ALU.mult), inc="vp")
                E.fence("vector", "vp")
                E.emit("vector", lambda v: v.tensor_mul(nstat[:, 8:16], mA, mA), inc="vp")
                E.fence("vector", "vp")
                E.emit("vector", lambda v: v.tensor_sub(tA, nstat[:, 0:8], nstat[:, 8:16]), inc="vp")
                E.fence("vector", "vp")
                E.emit("vector", lambda v: v.tensor_scalar(tA, tA, 1.0, EPS,
                                                           ALU.mult, op1=ALU.add), inc="vp")
                E.wait("scalar", "vp")
                E.emit("scalar", lambda s: s.activation(nstat[:, 0:8], tA, AF.Ln), inc="sp")
                E.fence("scalar", "sp")
                E.emit("scalar", lambda s: s.activation(rA, nstat[:, 0:8], AF.Exp,
                                                        scale=-0.5), inc="sp")
                E.wait("vector", "sp")
                E.emit("vector", lambda v: v.tensor_scalar(nstat[:, 8:16], tA, 0.5,
                                                           None, ALU.mult), inc="vp")
                for _ in range(2):
                    E.fence("vector", "vp")
                    E.emit("vector", lambda v: v.tensor_mul(nstat[:, 0:8], rA, rA), inc="vp")
                    E.fence("vector", "vp")
                    E.emit("vector", lambda v: v.tensor_mul(nstat[:, 0:8], nstat[:, 0:8],
                                                            nstat[:, 8:16]), inc="vp")
                    E.fence("vector", "vp")
                    E.emit("vector", lambda v: v.tensor_scalar(nstat[:, 0:8], nstat[:, 0:8],
                                                               -1.0, 1.5, ALU.mult,
                                                               op1=ALU.add), inc="vp")
                    E.fence("vector", "vp")
                    E.emit("vector", lambda v: v.tensor_mul(rA, rA, nstat[:, 0:8]), inc="vp")
                E.fence("vector", "vp")

            for b in range(limit_b if "A" in phases else 0):
                for n in range(NSEQ):
                    sl, lxs = load_x(n, b)
                    E.wait("vector", lxs)
                    E.wait("scalar", lxs)
                    E.emit("scalar", lambda s, n=n, sl=sl: s.activation(
                        junkS[:], xt[:, sl, :], AF.Copy,
                        accum_out=stats[:, n, FS:FS + 1]), inc="sp")
                    E.emit("vector", lambda v, n=n, sl=sl: v.scalar_tensor_tensor(
                        junk[:], xt[:, sl, :], 1.0, xt[:, sl, :], ALU.mult, ALU.mult,
                        accum_out=stats[:, n, FS2:FS2 + 1]), inc="vp")
                    xslot_free[sl] = (E.val("vp"), E.val("sp"))
                E.wait("vector", "sp")
                E.fence("vector", "vp")
                sA = stats[:, :, FS]
                s2A = stats[:, :, FS2]
                mA = stats[:, :, FM]
                tA = stats[:, :, FT]
                rA = stats[:, :, FR]
                E.emit("vector", lambda v: v.tensor_scalar(mA, sA, 1.0 / D, None, ALU.mult), inc="vp")
                E.emit("vector", lambda v: v.tensor_scalar(nstat[:, 0:8], s2A, 1.0 / D,
                                                           None, ALU.mult), inc="vp")
                E.fence("vector", "vp")
                E.emit("vector", lambda v: v.tensor_mul(nstat[:, 8:16], mA, mA), inc="vp")
                E.fence("vector", "vp")
                E.emit("vector", lambda v: v.tensor_sub(tA, nstat[:, 0:8], nstat[:, 8:16]), inc="vp")
                E.fence("vector", "vp")
                E.emit("vector", lambda v: v.tensor_scalar(tA, tA, 1.0, EPS,
                                                           ALU.mult, op1=ALU.add), inc="vp")
                E.wait("scalar", "vp")
                E.emit("scalar", lambda s: s.activation(nstat[:, 0:8], tA, AF.Ln), inc="sp")
                E.fence("scalar", "sp")
                E.emit("scalar", lambda s: s.activation(rA, nstat[:, 0:8], AF.Exp,
                                                        scale=-0.5), inc="sp")
                E.wait("vector", "sp")
                E.emit("vector", lambda v: v.tensor_scalar(nstat[:, 8:16], tA, 0.5,
                                                           None, ALU.mult), inc="vp")
                for _ in range(2):
                    E.fence("vector", "vp")
                    E.emit("vector", lambda v: v.tensor_mul(nstat[:, 0:8], rA, rA), inc="vp")
                    E.fence("vector", "vp")
                    E.emit("vector", lambda v: v.tensor_mul(nstat[:, 0:8], nstat[:, 0:8],
                                                            nstat[:, 8:16]), inc="vp")
                    E.fence("vector", "vp")
                    E.emit("vector", lambda v: v.tensor_scalar(nstat[:, 0:8], nstat[:, 0:8],
                                                               -1.0, 1.5, ALU.mult,
                                                               op1=ALU.add), inc="vp")
                    E.fence("vector", "vp")
                    E.emit("vector", lambda v: v.tensor_mul(rA, rA, nstat[:, 0:8]), inc="vp")
                E.fence("vector", "vp")
                for n in range(NSEQ):
                    sl, lxs = load_x(n, b)
                    E.wait("vector", lxs)
                    E.wait("vector", "sp", lnslot_free[sl])
                    E.emit("vector", lambda v, n=n, sl=sl: v.tensor_scalar(
                        ln1[:, sl, :], xt[:, sl, :], stats[:, n, FM:FM + 1],
                        stats[:, n, FR:FR + 1], ALU.subtract, op1=ALU.mult), inc="vp")
                    xslot_free[sl] = (E.val("vp"), E.val("sp"))
                    E.wait("tensor", "vp")
                    for k in range(DCH):
                        pt = ptrs[k % 4]
                        E.wait("tensor", "sp", ptr_free[k % 4])
                        E.emit("tensor", lambda t, k=k, pt=pt, sl=sl: t.transpose(
                            pt[:], ln1[:, sl, k * 128:(k + 1) * 128], ident[:]), inc="tp")
                        E.wait("scalar", "tp")
                        E.emit("scalar", lambda s, n=n, k=k, pt=pt: s.copy(
                            xfm[:, n, k, :], pt[:]), inc="sp")
                        ptr_free[k % 4] = E.val("sp")
                    lnslot_free[sl] = E.val("sp")
                for hc in range(2):
                    for cg in range(3):
                        ws = wseq[0] % 2
                        lws = "lwa" if ws == 0 else "lwb"
                        E.wait("sync", "tp", wslot_free[ws])
                        for k in range(DCH):
                            E.emit("sync", lambda s, k=k, cg=cg, hc=hc, ws=ws: s.dma_start(
                                out=wbuf[:, ws, k, :],
                                in_=wqkvT[k * 128:(k + 1) * 128,
                                          cg * D + hc * 512: cg * D + hc * 512 + 512]),
                                inc=lws, n=16)
                        wseq[0] += 1
                        E.wait("tensor", lws)
                        if cg == 0:
                            E.wait("tensor", "sp")
                        for n in range(NSEQ):
                            pm = pmm0 if n % 2 == 0 else pmm1
                            E.wait("tensor", "vp", pmm_free[n % 2])
                            E.emit("tensor", lambda t, cg=cg, hc=hc, pm=pm: t.matmul(
                                pm[:], ones1[:],
                                bqkv_t[0:1, cg * D + hc * 512: cg * D + hc * 512 + 512],
                                start=True, stop=False))
                            for k in range(DCH):
                                E.emit("tensor", lambda t, n=n, k=k, ws=ws, pm=pm: t.matmul(
                                    pm[:], xfm[:, n, k, :], wbuf[:, ws, k, :],
                                    start=False, stop=(k == DCH - 1)),
                                    inc=("tp" if k == DCH - 1 else None))
                            E.wait("vector", "tp")
                            E.emit("vector", lambda v, n=n, cg=cg, pm=pm: v.tensor_copy(
                                ybuf[:, n, cg, :], pm[:]), inc="vp")
                            pmm_free[n % 2] = E.val("vp")
                        wslot_free[ws] = E.val("tp")
                    E.wait("vector", "tp")
                    E.fence("vector", "vp")
                    E.emit("vector", lambda v: v.memset(smat[:, :, 0:4, 4:8], NEG_BIG), inc="vp")
                    for hl in range(8):
                        for (n, m) in allowed:
                            E.emit("vector", lambda v, hl=hl, n=n, m=m: v.scalar_tensor_tensor(
                                junk[:, 0:HD],
                                ybuf[:, n, 0, hl * HD:(hl + 1) * HD], SCALE,
                                ybuf[:, m, 1, hl * HD:(hl + 1) * HD],
                                ALU.mult, ALU.mult,
                                accum_out=smat[:, hl, n, m:m + 1]), inc="vp")
                    E.fence("vector", "vp")
                    E.emit("vector", lambda v: v.reduce_max(
                        mx[:].unsqueeze(2),
                        smat[:].rearrange("p h n m -> p (h n) m"), axis=AX), inc="vp")
                    E.fence("vector", "vp")
                    E.emit("vector", lambda v: v.tensor_tensor(
                        emat[:].rearrange("p h n m -> p (h n) m"),
                        smat[:].rearrange("p h n m -> p (h n) m"),
                        mx[:].unsqueeze(2).to_broadcast([128, 64, 8]), ALU.subtract), inc="vp")
                    E.wait("scalar", "vp")
                    E.emit("scalar", lambda s: s.activation(emat[:], emat[:], AF.Exp), inc="sp")
                    E.wait("vector", "sp")
                    E.emit("vector", lambda v: v.reduce_sum(
                        sm[:].unsqueeze(2),
                        emat[:].rearrange("p h n m -> p (h n) m"), axis=AX), inc="vp")
                    E.fence("vector", "vp")
                    E.emit("vector", lambda v: v.reciprocal(sm[:], sm[:]), inc="vp")
                    E.fence("vector", "vp")
                    E.emit("vector", lambda v: v.tensor_tensor(
                        emat[:].rearrange("p h n m -> p (h n) m"),
                        emat[:].rearrange("p h n m -> p (h n) m"),
                        sm[:].unsqueeze(2).to_broadcast([128, 64, 8]), ALU.mult), inc="vp")
                    E.fence("vector", "vp")
                    for hl in range(8):
                        for n in range(NSEQ):
                            nm = 4 if n < 4 else 8
                            for m in range(nm):
                                E.emit("vector", lambda v, hl=hl, n=n, m=m: v.tensor_scalar(
                                    prods[:, m, :],
                                    ybuf[:, m, 2, hl * HD:(hl + 1) * HD],
                                    emat[:, hl, n, m:m + 1], None, ALU.mult), inc="vp")
                            E.emit("vector", lambda v, hl=hl, n=n, hc=hc, nm=nm: v.reduce_sum(
                                aout[:, n, hc * 512 + hl * HD: hc * 512 + (hl + 1) * HD]
                                .unsqueeze(2),
                                prods[:, 0:nm, :].transpose([0, 2, 1]), axis=AX), inc="vp")
                for cg in range(2):
                    ws = wseq[0] % 2
                    lws = "lwa" if ws == 0 else "lwb"
                    E.wait("sync", "tp", wslot_free[ws])
                    for k in range(DCH):
                        E.emit("sync", lambda s, k=k, cg=cg, ws=ws: s.dma_start(
                            out=wbuf[:, ws, k, :],
                            in_=wprojT[k * 128:(k + 1) * 128, cg * 512:(cg + 1) * 512]),
                            inc=lws, n=16)
                    pslot[cg] = ws
                    pwsem[cg] = lws
                    wseq[0] += 1
                attn_done_vp = E.val("vp")
                for n in range(NSEQ):
                    E.wait("tensor", "vp", attn_done_vp)
                    for k in range(DCH):
                        pt = ptrs[k % 4]
                        E.wait("tensor", "sp", ptr_free[k % 4])
                        E.emit("tensor", lambda t, n=n, k=k, pt=pt: t.transpose(
                            pt[:], aout[:, n, k * 128:(k + 1) * 128], ident[:]), inc="tp")
                        E.wait("scalar", "tp")
                        E.emit("scalar", lambda s, n=n, k=k, pt=pt: s.copy(
                            xfm[:, n, k, :], pt[:]), inc="sp")
                        ptr_free[k % 4] = E.val("sp")
                    E.wait("tensor", pwsem[0])
                    E.wait("tensor", pwsem[1])
                    E.wait("tensor", "sp")
                    E.wait("tensor", "vp", max(pmm_free[0], pmm_free[1]))
                    for cg in range(2):
                        pm = pmm0 if cg == 0 else pmm1
                        E.emit("tensor", lambda t, cg=cg, pm=pm: t.matmul(
                            pm[:], ones1[:], bproj_t[0:1, cg * 512:(cg + 1) * 512],
                            start=True, stop=False))
                        for k in range(DCH):
                            E.emit("tensor", lambda t, n=n, k=k, cg=cg, pm=pm: t.matmul(
                                pm[:], xfm[:, n, k, :], wbuf[:, pslot[cg], k, :],
                                start=False, stop=(k == DCH - 1)),
                                inc=("tp" if k == DCH - 1 else None))
                    E.wait("vector", "tp")
                    E.emit("vector", lambda v: v.tensor_copy(junk[:, 0:512], pmm0[:]), inc="vp")
                    E.emit("vector", lambda v: v.tensor_copy(junk[:, 512:1024], pmm1[:]), inc="vp")
                    pmm_free[0] = E.val("vp")
                    pmm_free[1] = E.val("vp")
                    E.emit("vector", lambda v: v.reduce_sum(nstat[:, 0:1], junk[:], axis=AX), inc="vp")
                    E.emit("vector", lambda v: v.scalar_tensor_tensor(
                        junkS[:], junk[:], 1.0, junk[:], ALU.mult, ALU.mult,
                        accum_out=nstat[:, 4:5]), inc="vp")
                    E.fence("vector", "vp")
                    E.emit("vector", lambda v: v.tensor_scalar(nstat[:, 8:9], nstat[:, 0:1],
                                                               1.0 / D, None, ALU.mult), inc="vp")
                    E.emit("vector", lambda v: v.tensor_scalar(nstat[:, 9:10], nstat[:, 4:5],
                                                               1.0 / D, None, ALU.mult), inc="vp")
                    E.fence("vector", "vp")
                    E.emit("vector", lambda v: v.tensor_mul(nstat[:, 10:11], nstat[:, 8:9],
                                                            nstat[:, 8:9]), inc="vp")
                    E.fence("vector", "vp")
                    E.emit("vector", lambda v: v.tensor_sub(nstat[:, 11:12], nstat[:, 9:10],
                                                            nstat[:, 10:11]), inc="vp")
                    E.fence("vector", "vp")
                    E.emit("vector", lambda v: v.tensor_scalar(nstat[:, 11:12], nstat[:, 11:12],
                                                               1.0, EPS, ALU.mult,
                                                               op1=ALU.add), inc="vp")
                    rsqrt_chain(11, 12, 13, 14, 1)
                    sl, lxs = load_x(n, b)
                    xp = len(x2_gp) % 2
                    gps = "gpa" if xp == 0 else "gpb"
                    if len(x2_gp) >= 2:
                        E.wait("vector", gps, x2_gp[-2])
                    E.fence("vector", "vp")
                    E.emit("vector", lambda v, xp=xp: v.tensor_scalar(
                        x2t[:, xp, :], junk[:], nstat[:, 8:9], nstat[:, 12:13],
                        ALU.subtract, op1=ALU.mult), inc="vp")
                    E.wait("vector", lxs)
                    E.emit("vector", lambda v, sl=sl, xp=xp: v.tensor_tensor(
                        x2t[:, xp, :], x2t[:, xp, :], xt[:, sl, :], ALU.add), inc="vp")
                    xslot_free[sl] = (E.val("vp"), E.val("sp"))
                    rt = 4 * n + b
                    E.wait("gpsimd", "vp")
                    E.emit("gpsimd", lambda g, rt=rt, xp=xp: g.dma_start(
                        out=x2d[rt], in_=x2t[:, xp, :]), inc=gps, n=16)
                    x2_gp.append(E.val(gps))
                wslot_free[pslot[0]] = E.val("tp")
                wslot_free[pslot[1]] = E.val("tp")

            def barrier():
                for eng in ENGINES:
                    for s in ("tp", "vp", "sp", "lxa", "lxb", "lwa", "lwb",
                              "gpa", "gpb", "so", "cc"):
                        E.wait(eng, s)

            def greduce(op, incol, outcol, rowslot):
                E.wait("tensor", "vp")
                E.emit("tensor", lambda t: t.transpose(ptr0[0:1, :], incol, ident[:]), inc="tp")
                E.wait("vector", "tp")
                E.emit("vector", lambda v: v.tensor_copy(junk[0:1, 0:128], ptr0[0:1, 0:128]), inc="vp")
                E.fence("vector", "vp")
                E.emit("vector", lambda v: v.tensor_reduce(
                    ccrow[0:1, rowslot:rowslot + 1], junk[0:1, 0:128], AX, op), inc="vp")
                E.wait("gpsimd", "vp")
                E.emit("gpsimd", lambda g: g.dma_start(
                    out=cc_in[0:1, 0:1], in_=ccrow[0:1, rowslot:rowslot + 1]), inc="gpa", n=16)
                E.wait("gpsimd", "gpa")
                E.emit("gpsimd", lambda g: g.collective_compute(
                    "AllReduce", op, ins=[cc_in[:]], outs=[cc_out[:]],
                    replica_groups=[[0, 1, 2, 3, 4, 5, 6, 7]]), inc="cc")
                E.wait("gpsimd", "cc")
                E.emit("gpsimd", lambda g: g.dma_start(
                    out=ccrow[0:1, rowslot + 1:rowslot + 2], in_=cc_out[0:1, 0:1]), inc="gpb", n=16)
                E.wait("tensor", "gpb")
                E.wait("tensor", "vp")
                E.emit("tensor", lambda t: t.matmul(
                    ptr1[:, 0:1], ones1[:], ccrow[0:1, rowslot + 1:rowslot + 2],
                    start=True, stop=True), inc="tp")
                E.wait("vector", "tp")
                E.emit("vector", lambda v: v.tensor_copy(outcol, ptr1[:, 0:1]), inc="vp")
                E.fence("vector", "vp")

            def lreduce(op, incol, outcol, rowslot):
                E.wait("tensor", "vp")
                E.emit("tensor", lambda t: t.transpose(ptr0[0:1, :], incol, ident[:]), inc="tp")
                E.wait("vector", "tp")
                E.emit("vector", lambda v: v.tensor_copy(junk[0:1, 0:128], ptr0[0:1, 0:128]), inc="vp")
                E.fence("vector", "vp")
                E.emit("vector", lambda v: v.tensor_reduce(
                    ccrow[0:1, rowslot:rowslot + 1], junk[0:1, 0:128], AX, op), inc="vp")
                E.wait("tensor", "vp")
                E.emit("tensor", lambda t: t.matmul(
                    ptr1[:, 0:1], ones1[:], ccrow[0:1, rowslot:rowslot + 1],
                    start=True, stop=True), inc="tp")
                E.wait("vector", "tp")
                E.emit("vector", lambda v: v.tensor_copy(outcol, ptr1[:, 0:1]), inc="vp")
                E.fence("vector", "vp")

            def bcol(i):
                return bst[:, i:i + 1]

            def bis_select():
                E.fence("vector", "vp")
                E.emit("vector", lambda v: v.tensor_tensor(bcol(5), bcol(4), bcol(6),
                                                           ALU.is_ge), inc="vp")
                E.fence("vector", "vp")
                E.emit("vector", lambda v: v.tensor_sub(bcol(20), bcol(2), bcol(0)), inc="vp")
                E.emit("vector", lambda v: v.tensor_sub(bcol(21), bcol(2), bcol(1)), inc="vp")
                E.fence("vector", "vp")
                E.emit("vector", lambda v: v.tensor_mul(bcol(20), bcol(20), bcol(5)), inc="vp")
                E.fence("vector", "vp")
                E.emit("vector", lambda v: v.tensor_add(bcol(0), bcol(0), bcol(20)), inc="vp")
                E.emit("vector", lambda v: v.tensor_sub(bcol(21), bcol(1), bcol(2)), inc="vp")
                E.fence("vector", "vp")
                E.emit("vector", lambda v: v.tensor_mul(bcol(21), bcol(21), bcol(5)), inc="vp")
                E.fence("vector", "vp")
                E.emit("vector", lambda v: v.tensor_add(bcol(1), bcol(2), bcol(21)), inc="vp")
                E.fence("vector", "vp")

            if "B" in phases:
                barrier()
                E.emit("vector", lambda v: v.memset(bst[:], 0.0), inc="vp")
                E.emit("vector", lambda v: v.memset(cnts[:], 0.0), inc="vp")
                E.emit("vector", lambda v: v.memset(ccrow[:], 0.0), inc="vp")
                E.fence("vector", "vp")
                FS, FS2, FM, FT, FR = 0, 1, 2, 3, 4
                FU, FU2, FM3 = 5, 6, 7
                gbc = overlay("gbc", [128, D], F32, 131072)
                bbc = overlay("bbc", [128, D], F32, 135168)
                jI8 = overlay("jI8", [128, 8192], mybir.dt.int8, 139264)
                for half in range(2):
                    E.emit("sync", lambda s, half=half: s.dma_start(
                        out=ln1[0:1, 0, 0:512],
                        in_=ln_g[0:1, half * 512:(half + 1) * 512]), inc="lb", n=16)
                    E.emit("sync", lambda s, half=half: s.dma_start(
                        out=ln1[0:1, 1, 0:512],
                        in_=ln_b[0:1, half * 512:(half + 1) * 512]), inc="lb", n=16)
                    E.wait("tensor", "lb")
                    E.wait("tensor", "vp")
                    E.emit("tensor", lambda t: t.matmul(
                        pmm0[:], ones1[:], ln1[0:1, 0, 0:512],
                        start=True, stop=True), inc="tp")
                    E.emit("tensor", lambda t: t.matmul(
                        pmm1[:], ones1[:], ln1[0:1, 1, 0:512],
                        start=True, stop=True), inc="tp")
                    E.wait("vector", "tp")
                    E.emit("vector", lambda v, half=half: v.tensor_copy(
                        gbc[:, half * 512:(half + 1) * 512], pmm0[:]), inc="vp")
                    E.emit("vector", lambda v, half=half: v.tensor_copy(
                        bbc[:, half * 512:(half + 1) * 512], pmm1[:]), inc="vp")
                E.fence("vector", "vp")
                NW1CH = (D // 128) * (HID // 512)
                for ch in range(NW1CH):
                    kd, hcol = ch // 8, ch % 8
                    ws = ch % 2
                    lws = "lwa" if ws == 0 else "lwb"
                    E.wait("sync", "sp", wslot_free.get(("g1", ws), 0))
                    E.emit("sync", lambda s, kd=kd, hcol=hcol, ws=ws: s.dma_start(
                        out=wstage[:, ws, :],
                        in_=w1T[kd * 128:(kd + 1) * 128, hcol * 512:(hcol + 1) * 512]),
                        inc=lws, n=16)
                    E.wait("scalar", lws)
                    E.emit("scalar", lambda s, ch=ch, ws=ws: s.activation(
                        junkS[:, 0:512], wstage[:, ws, :], AF.Abs,
                        accum_out=cnts[:, ch % 64:ch % 64 + 1]), inc="sp")
                    wslot_free[("g1", ws)] = E.val("sp")
                E.wait("vector", "sp")
                E.emit("vector", lambda v: v.reduce_sum(bcol(6), cnts[:, 0:64], axis=AX), inc="vp")
                lreduce(ALU.add, bcol(6), bcol(19), 0)
                E.emit("vector", lambda v: v.tensor_scalar(
                    bcol(19), bcol(19), 1.0 / (D * HID), EPS, ALU.mult, op1=ALU.add), inc="vp")
                ln2slot = {0: (0, 0), 1: (0, 0)}
                for g8 in range(NT // 8):
                    for j in range(8):
                        rt = g8 * 8 + j
                        sl = rt % 2
                        lxs = "lxa" if sl == 0 else "lxb"
                        vv, ss = xslot_free[sl]
                        E.wait("sync", "vp", vv)
                        E.wait("sync", "sp", ss)
                        E.emit("sync", lambda s, rt=rt, sl=sl: s.dma_start(
                            out=xt[:, sl, :], in_=x2d[rt]), inc=lxs, n=16)
                        E.wait("vector", lxs)
                        E.wait("scalar", lxs)
                        E.emit("scalar", lambda s, j=j, sl=sl: s.activation(
                            junkS[:], xt[:, sl, :], AF.Copy,
                            accum_out=stats[:, j, FS:FS + 1]), inc="sp")
                        E.emit("vector", lambda v, j=j, sl=sl: v.scalar_tensor_tensor(
                            junk[:], xt[:, sl, :], 1.0, xt[:, sl, :], ALU.mult, ALU.mult,
                            accum_out=stats[:, j, FS2:FS2 + 1]), inc="vp")
                        xslot_free[sl] = (E.val("vp"), E.val("sp"))
                    stats_chain(FS, FS2, FM, FT, FR)
                    for j in range(8):
                        rt = g8 * 8 + j
                        sl = rt % 2
                        lxs = "lxa" if sl == 0 else "lxb"
                        vv, ss = xslot_free[sl]
                        E.wait("sync", "vp", vv)
                        E.wait("sync", "sp", ss)
                        E.emit("sync", lambda s, rt=rt, sl=sl: s.dma_start(
                            out=xt[:, sl, :], in_=x2d[rt]), inc=lxs, n=16)
                        E.wait("vector", lxs)
                        lv, ls = ln2slot[sl]
                        E.wait("vector", "vp", lv)
                        E.wait("vector", "sp", ls)
                        E.emit("vector", lambda v, j=j, sl=sl: v.tensor_scalar(
                            ln1[:, sl, :], xt[:, sl, :], stats[:, j, FM:FM + 1],
                            stats[:, j, FR:FR + 1], ALU.subtract, op1=ALU.mult), inc="vp")
                        E.wait("scalar", "vp")
                        E.emit("scalar", lambda s, j=j, sl=sl: s.activation(
                            junkS[:], ln1[:, sl, :], AF.Copy,
                            accum_out=stats[:, j, FU:FU + 1]), inc="sp")
                        E.emit("vector", lambda v, j=j, sl=sl: v.scalar_tensor_tensor(
                            junk[:], ln1[:, sl, :], 1.0, ln1[:, sl, :], ALU.mult, ALU.mult,
                            accum_out=stats[:, j, FU2:FU2 + 1]), inc="vp")
                        xslot_free[sl] = (E.val("vp"), E.val("sp"))
                        ln2slot[sl] = (E.val("vp"), E.val("sp"))
                    stats_chain(FU, FU2, FM3, FT, FU2)
                    for j in range(8):
                        rt = g8 * 8 + j
                        sl = rt % 2
                        lxs = "lxa" if sl == 0 else "lxb"
                        vv, ss = xslot_free[sl]
                        E.wait("sync", "vp", vv)
                        E.wait("sync", "sp", ss)
                        E.emit("sync", lambda s, rt=rt, sl=sl: s.dma_start(
                            out=xt[:, sl, :], in_=x2d[rt]), inc=lxs, n=16)
                        E.wait("vector", lxs)
                        lv, ls = ln2slot[sl]
                        E.wait("vector", "sp", ls)
                        E.emit("vector", lambda v, j=j, sl=sl: v.tensor_scalar(
                            ln1[:, sl, :], xt[:, sl, :], stats[:, j, FM:FM + 1],
                            stats[:, j, FR:FR + 1], ALU.subtract, op1=ALU.mult), inc="vp")
                        E.emit("vector", lambda v, j=j, sl=sl: v.tensor_scalar(
                            ln1[:, sl, :], ln1[:, sl, :], stats[:, j, FM3:FM3 + 1],
                            stats[:, j, FU2:FU2 + 1], ALU.subtract, op1=ALU.mult), inc="vp")
                        E.emit("vector", lambda v, sl=sl: v.tensor_mul(
                            ln1[:, sl, :], ln1[:, sl, :], gbc[:]), inc="vp")
                        E.emit("vector", lambda v, rt=rt, sl=sl: v.tensor_add(
                            h1[:, rt, :], ln1[:, sl, :], bbc[:]), inc="vp")
                        xslot_free[sl] = (E.val("vp"), E.val("sp"))
                        E.emit("vector", lambda v, rt=rt: v.reduce_max(
                            bcol(6), h1[:, rt, :], axis=AX,
                            apply_absolute_value=True), inc="vp")
                        E.fence("vector", "vp")
                        if rt == 0:
                            E.emit("vector", lambda v: v.tensor_copy(bcol(7), bcol(6)), inc="vp")
                        else:
                            E.emit("vector", lambda v: v.tensor_max(bcol(7), bcol(7), bcol(6)), inc="vp")
                greduce(ALU.max, bcol(7), bcol(17), 0)
                E.emit("vector", lambda v: v.memset(bcol(10), float(NSEQ * 4096 * D)), inc="vp")
                E.fence("vector", "vp")
                E.emit("vector", lambda v: v.tensor_scalar(bcol(10), bcol(10), -1.0,
                                                           None, ALU.add), inc="vp")
                E.fence("vector", "vp")
                E.emit("vector", lambda v: v.tensor_scalar(bcol(10), bcol(10),
                                                           float(np.float32(0.995)),
                                                           None, ALU.mult), inc="vp")
                E.fence("vector", "vp")
                E.emit("vector", lambda v: v.tensor_copy(x2t[0:128, 0, 0:1].bitcast(I32), bcol(10)), inc="vp")
                E.fence("vector", "vp")
                E.emit("vector", lambda v: v.tensor_copy(bcol(6), x2t[0:128, 0, 0:1].bitcast(I32)), inc="vp")
                E.fence("vector", "vp")
                E.emit("vector", lambda v: v.tensor_tensor(bcol(20), bcol(6), bcol(10), ALU.is_gt), inc="vp")
                E.emit("vector", lambda v: v.tensor_tensor(bcol(21), bcol(6), bcol(10), ALU.is_lt), inc="vp")
                E.fence("vector", "vp")
                E.emit("vector", lambda v: v.tensor_sub(bcol(20), bcol(6), bcol(20)), inc="vp")
                E.emit("vector", lambda v: v.tensor_add(bcol(21), bcol(6), bcol(21)), inc="vp")
                E.fence("vector", "vp")
                E.emit("vector", lambda v: v.tensor_sub(bcol(11), bcol(10), bcol(20)), inc="vp")
                E.fence("vector", "vp")
                E.emit("vector", lambda v: v.tensor_scalar(bcol(12), bcol(11), -1.0, 1.0,
                                                           ALU.mult, op1=ALU.add), inc="vp")
                E.emit("vector", lambda v: v.memset(bcol(6), float(NSEQ * 4096 * D)), inc="vp")
                E.fence("vector", "vp")
                E.emit("vector", lambda v: v.tensor_scalar(bcol(6), bcol(6), -1.0,
                                                           None, ALU.add), inc="vp")
                E.fence("vector", "vp")
                E.emit("vector", lambda v: v.tensor_sub(bcol(8), bcol(6), bcol(21)), inc="vp")
                E.emit("vector", lambda v: v.tensor_sub(bcol(9), bcol(6), bcol(20)), inc="vp")
                E.fence("vector", "vp")
                E.emit("vector", lambda v: v.tensor_scalar(bcol(8), bcol(8), 1.0, 1.0,
                                                           ALU.mult, op1=ALU.add), inc="vp")
                E.emit("vector", lambda v: v.tensor_scalar(bcol(9), bcol(9), 1.0, 1.0,
                                                           ALU.mult, op1=ALU.add), inc="vp")
                E.emit("vector", lambda v: v.memset(bcol(0), -1.0), inc="vp")
                E.fence("vector", "vp")
                E.emit("vector", lambda v: v.tensor_copy(bcol(1), bcol(17)), inc="vp")
                h1flat = h1[:].rearrange("p a b -> p (a b)")
                def count_pass_h1(taucol, cmp1, cmp2, dstcol):
                    E.fence("vector", "vp")
                    E.emit("vector", lambda v: v.tensor_scalar(bcol(22), taucol, -1.0,
                                                               None, ALU.mult), inc="vp")
                    E.fence("vector", "vp")
                    for q in range(4):
                        E.emit("vector", lambda v, q=q: v.tensor_scalar(
                            jI8[:], h1flat[:, q * 8192:(q + 1) * 8192], taucol, None,
                            cmp1, op1=ALU.add, accum_out=cnts[:, q:q + 1]), inc="vp")
                        E.emit("vector", lambda v, q=q: v.tensor_scalar(
                            jI8[:], h1flat[:, q * 8192:(q + 1) * 8192], bcol(22), None,
                            cmp2, op1=ALU.add, accum_out=cnts[:, 4 + q:5 + q]), inc="vp")
                    E.fence("vector", "vp")
                    E.emit("vector", lambda v: v.reduce_sum(bcol(25), cnts[:, 0:8], axis=AX), inc="vp")
                    greduce(ALU.add, bcol(25), dstcol, 0)
                R1 = 30
                for rnd in range(R1):
                    E.fence("vector", "vp")
                    E.emit("vector", lambda v: v.tensor_add(bcol(2), bcol(0), bcol(1)), inc="vp")
                    E.fence("vector", "vp")
                    E.emit("vector", lambda v: v.tensor_scalar(bcol(2), bcol(2), 0.5,
                                                               None, ALU.mult), inc="vp")
                    count_pass_h1(bcol(2), ALU.is_gt, ALU.is_lt, bcol(4))
                    E.fence("vector", "vp")
                    E.emit("vector", lambda v: v.tensor_copy(bcol(6), bcol(8)), inc="vp")
                    bis_select()
                E.emit("vector", lambda v: v.tensor_copy(bcol(13), bcol(1)), inc="vp")
                count_pass_h1(bcol(13), ALU.is_ge, ALU.is_le, bcol(4))
                for rt in range(NT):
                    E.wait("scalar", "vp")
                    E.emit("scalar", lambda s, rt=rt: s.activation(
                        junkS[:], h1flat[:, rt * D:(rt + 1) * D], AF.Abs), inc="sp")
                    E.wait("vector", "sp")
                    E.emit("vector", lambda v: v.scalar_tensor_tensor(
                        junk[:], junkS[:], bcol(13), junkS[:], ALU.is_lt, ALU.mult), inc="vp")
                    E.fence("vector", "vp")
                    E.emit("vector", lambda v: v.reduce_max(bcol(6), junk[:, 0:D], axis=AX), inc="vp")
                    E.fence("vector", "vp")
                    if rt == 0:
                        E.emit("vector", lambda v: v.tensor_copy(bcol(7), bcol(6)), inc="vp")
                    else:
                        E.emit("vector", lambda v: v.tensor_max(bcol(7), bcol(7), bcol(6)), inc="vp")
                greduce(ALU.max, bcol(7), bcol(14), 0)
                E.fence("vector", "vp")
                E.emit("vector", lambda v: v.tensor_tensor(bcol(5), bcol(4), bcol(9), ALU.is_ge), inc="vp")
                E.fence("vector", "vp")
                E.emit("vector", lambda v: v.tensor_sub(bcol(20), bcol(13), bcol(14)), inc="vp")
                E.fence("vector", "vp")
                E.emit("vector", lambda v: v.tensor_mul(bcol(20), bcol(20), bcol(5)), inc="vp")
                E.fence("vector", "vp")
                E.emit("vector", lambda v: v.tensor_add(bcol(14), bcol(14), bcol(20)), inc="vp")
                E.fence("vector", "vp")
                E.emit("vector", lambda v: v.tensor_mul(bcol(20), bcol(14), bcol(12)), inc="vp")
                E.emit("vector", lambda v: v.tensor_mul(bcol(21), bcol(13), bcol(11)), inc="vp")
                E.fence("vector", "vp")
                E.emit("vector", lambda v: v.tensor_add(bcol(15), bcol(20), bcol(21)), inc="vp")
                E.fence("vector", "vp")
                E.emit("vector", lambda v: v.tensor_scalar(bcol(15), bcol(15), EPS,
                                                           None, ALU.max), inc="vp")
                E.fence("vector", "vp")
                E.emit("vector", lambda v: v.reciprocal(bcol(16), bcol(15)), inc="vp")
                E.fence("vector", "vp")
                E.emit("vector", lambda v: v.tensor_scalar(bcol(16), bcol(16), 127.0,
                                                           None, ALU.mult), inc="vp")
                E.fence("vector", "vp")
                if st_dbg is not None:
                    E.wait("gpsimd", "vp")
                    E.emit("gpsimd", lambda g: g.dma_start(out=st_dbg, in_=bst[:]), inc="gpa", n=16)
                if h1_dbg is not None:
                    for rt in range(NT):
                        E.wait("gpsimd", "vp")
                        E.emit("gpsimd", lambda g, rt=rt: g.dma_start(
                            out=h1_dbg[rt], in_=h1[:, rt, :]), inc="gpa", n=16)

            def quantile_targets(nval, hwcol, lwcol, t1col, t2col):
                E.emit("vector", lambda v: v.memset(bcol(10), float(nval)), inc="vp")
                E.fence("vector", "vp")
                E.emit("vector", lambda v: v.tensor_scalar(bcol(10), bcol(10), -1.0,
                                                           None, ALU.add), inc="vp")
                E.fence("vector", "vp")
                E.emit("vector", lambda v: v.tensor_scalar(bcol(10), bcol(10),
                                                           float(np.float32(0.995)),
                                                           None, ALU.mult), inc="vp")
                E.fence("vector", "vp")
                E.emit("vector", lambda v: v.tensor_copy(x2t[0:128, 0, 0:1].bitcast(I32), bcol(10)), inc="vp")
                E.fence("vector", "vp")
                E.emit("vector", lambda v: v.tensor_copy(bcol(6), x2t[0:128, 0, 0:1].bitcast(I32)), inc="vp")
                E.fence("vector", "vp")
                E.emit("vector", lambda v: v.tensor_tensor(bcol(20), bcol(6), bcol(10), ALU.is_gt), inc="vp")
                E.emit("vector", lambda v: v.tensor_tensor(bcol(21), bcol(6), bcol(10), ALU.is_lt), inc="vp")
                E.fence("vector", "vp")
                E.emit("vector", lambda v: v.tensor_sub(bcol(20), bcol(6), bcol(20)), inc="vp")
                E.emit("vector", lambda v: v.tensor_add(bcol(21), bcol(6), bcol(21)), inc="vp")
                E.fence("vector", "vp")
                E.emit("vector", lambda v: v.tensor_sub(hwcol, bcol(10), bcol(20)), inc="vp")
                E.fence("vector", "vp")
                E.emit("vector", lambda v: v.tensor_scalar(lwcol, hwcol, -1.0, 1.0,
                                                           ALU.mult, op1=ALU.add), inc="vp")
                E.emit("vector", lambda v: v.memset(bcol(6), float(nval)), inc="vp")
                E.fence("vector", "vp")
                E.emit("vector", lambda v: v.tensor_scalar(bcol(6), bcol(6), -1.0,
                                                           None, ALU.add), inc="vp")
                E.fence("vector", "vp")
                E.emit("vector", lambda v: v.tensor_sub(t1col, bcol(6), bcol(21)), inc="vp")
                E.emit("vector", lambda v: v.tensor_sub(t2col, bcol(6), bcol(20)), inc="vp")
                E.fence("vector", "vp")
                E.emit("vector", lambda v: v.tensor_scalar(t1col, t1col, 1.0, 1.0,
                                                           ALU.mult, op1=ALU.add), inc="vp")
                E.emit("vector", lambda v: v.tensor_scalar(t2col, t2col, 1.0, 1.0,
                                                           ALU.mult, op1=ALU.add), inc="vp")
                E.fence("vector", "vp")

            if "C" in phases:
                barrier()
                E.emit("vector", lambda v: v.reciprocal(bcol(26), bcol(19)), inc="vp")
                E.fence("vector", "vp")
                for rt in range(NT):
                    sl = rt % 2
                    E.emit("vector", lambda v, rt=rt, sl=sl: v.tensor_scalar(
                        ln1[:, sl, :], h1[:, rt, :], bcol(16), None, ALU.mult), inc="vp")
                    E.emit("vector", lambda v, sl=sl: v.tensor_copy(
                        x2t[:, sl, :].bitcast(I32), ln1[:, sl, :]), inc="vp")
                    E.emit("vector", lambda v, rt=rt, sl=sl: v.tensor_scalar(
                        u1tok[:, rt, :], x2t[:, sl, :].bitcast(I32), -128.0, 127.0,
                        ALU.max, op1=ALU.min), inc="vp")
                wslot2 = {0: (0, 0), 1: (0, 0)}
                for ch in range(64):
                    kd, hcol = ch // 8, ch % 8
                    ws = ch % 2
                    lws = "lwa" if ws == 0 else "lwb"
                    vv, _ = wslot2[ws]
                    E.wait("sync", "vp", vv)
                    E.emit("sync", lambda s, kd=kd, hcol=hcol, ws=ws: s.dma_start(
                        out=wstage[:, ws, :],
                        in_=w1T[kd * 128:(kd + 1) * 128, hcol * 512:(hcol + 1) * 512]),
                        inc=lws, n=16)
                    E.wait("vector", lws)
                    E.emit("vector", lambda v, ws=ws: v.tensor_scalar(
                        junk[:, 0:512], wstage[:, ws, :], bcol(26), None, ALU.mult), inc="vp")
                    wslot2[ws] = (E.val("vp"), 0)
                    E.emit("vector", lambda v: v.tensor_copy(
                        x2t[:, 0, 0:512].bitcast(I32), junk[:, 0:512]), inc="vp")
                    E.emit("vector", lambda v, kd=kd, hcol=hcol: v.tensor_scalar(
                        t1[:, kd, hcol * 512:(hcol + 1) * 512],
                        x2t[:, 0, 0:512].bitcast(I32), -1.0, 1.0,
                        ALU.max, op1=ALU.min), inc="vp")
                zslot = {0: (0, 0), 1: (0, 0)}
                zseq = [0]
                E.emit("vector", lambda v: v.memset(bcol(27), -3.0e38), inc="vp")
                E.fence("vector", "vp")
                conv_done_vp = E.val("vp")
                for rt in range(NT):
                    E.wait("tensor", "vp", conv_done_vp)
                    for k in range(DCH):
                        pt = ptrs[k % 4]
                        E.wait("tensor", "sp", ptr_free[k % 4])
                        E.emit("tensor", lambda t, rt=rt, k=k, pt=pt: t.transpose(
                            pt[:].bitcast(BF16)[:, 0:128],
                            u1tok[:, rt, k * 128:(k + 1) * 128], identb[:]), inc="tp")
                        E.wait("scalar", "tp")
                        E.emit("scalar", lambda s, k=k, pt=pt: s.copy(
                            u1f[:, k, :], pt[:].bitcast(BF16)[:, 0:128]), inc="sp")
                        ptr_free[k % 4] = E.val("sp")
                    E.wait("tensor", "sp")
                    for hcol in range(DCH):
                        pm = pmm0 if hcol % 2 == 0 else pmm1
                        E.wait("tensor", "sp", pmmS_free[hcol % 2])
                        for kd in range(DCH):
                            E.emit("tensor", lambda t, kd=kd, hcol=hcol, pm=pm: t.matmul(
                                pm[:], u1f[:, kd, :], t1[:, kd, hcol * 512:(hcol + 1) * 512],
                                start=(kd == 0), stop=(kd == DCH - 1)),
                                inc=("tp" if kd == DCH - 1 else None))
                        zs = zseq[0] % 2
                        gps = "gpa" if zs == 0 else "gpb"
                        gv, vv = zslot[zs]
                        E.wait("scalar", "tp")
                        E.wait("scalar", gps, gv)
                        E.wait("scalar", "vp", vv)
                        E.emit("scalar", lambda s, zs=zs, pm=pm: s.copy(
                            zstage[:, zs, :], pm[:]), inc="sp")
                        pmmS_free[hcol % 2] = E.val("sp")
                        E.wait("vector", "sp")
                        E.emit("vector", lambda v, zs=zs: v.reduce_max(
                            bcol(6), zstage[:, zs, :], axis=AX), inc="vp")
                        E.fence("vector", "vp")
                        E.emit("vector", lambda v: v.tensor_max(bcol(27), bcol(27), bcol(6)), inc="vp")
                        E.wait("gpsimd", "sp")
                        E.emit("gpsimd", lambda g, rt=rt, hcol=hcol, zs=zs: g.dma_start(
                            out=zd[rt, :, hcol * 512:(hcol + 1) * 512],
                            in_=zstage[:, zs, :]), inc=gps, n=16)
                        zslot[zs] = (E.val(gps), E.val("vp"))
                        zseq[0] += 1
                greduce(ALU.max, bcol(27), bcol(28), 0)

            if "C" in phases:
                barrier()
                c8flat = c8[:].rearrange("p a b -> p (a b)")
                E.emit("vector", lambda v: v.reciprocal(bcol(29), bcol(28)), inc="vp")
                E.fence("vector", "vp")
                E.emit("vector", lambda v: v.tensor_scalar(bcol(29), bcol(29), 127.0,
                                                           None, ALU.mult), inc="vp")
                E.fence("vector", "vp")
                zqslot = {0: 0, 1: 0}
                for rt in range(NT):
                    for qq in range(4):
                        zs = (rt * 4 + qq) % 2
                        lws = "lwa" if zs == 0 else "lwb"
                        qeng = "sync" if zs == 0 else "gpsimd"
                        E.wait(qeng, "vp", zqslot[zs])
                        E.emit(qeng, lambda s, rt=rt, qq=qq, zs=zs: s.dma_start(
                            out=zrt2[:, zs, :],
                            in_=zd[rt, :, qq * D:(qq + 1) * D]), inc=lws, n=16)
                        E.wait("vector", lws)
                        E.emit("vector", lambda v, zs=zs: v.tensor_scalar(
                            x2t[:, 0, :], zrt2[:, zs, :],
                            bcol(29), -127.0, ALU.mult, op1=ALU.max), inc="vp")
                        zqslot[zs] = E.val("vp")
                        E.emit("vector", lambda v, rt=rt, qq=qq: v.tensor_scalar(
                            c8flat[:, rt * HID + qq * D: rt * HID + (qq + 1) * D],
                            x2t[:, 0, :], 127.0, None,
                            ALU.min), inc="vp")
                def count_c8(taucol, dstcol, cmp=ALU.is_gt):
                    E.fence("vector", "vp")
                    off = -0.5 if cmp in (ALU.is_gt,) else 0.5
                    E.emit("vector", lambda v: v.tensor_scalar(
                        nstat[:, 13:14], taucol, -1.0, off, ALU.mult, op1=ALU.add), inc="vp")
                    E.fence("vector", "vp")
                    for q in range(8):
                        E.emit("vector", lambda v, q=q: v.tensor_scalar(
                            jI8[:], c8flat[:, q * 8192:(q + 1) * 8192], taucol, None,
                            cmp, op1=ALU.add, accum_out=cnts[:, q:q + 1]), inc="vp")
                    E.wait("scalar", "vp")
                    for q in range(8, 16):
                        E.emit("scalar", lambda s, q=q: s.activation(
                            jI8[:], c8flat[:, q * 8192:(q + 1) * 8192], AF.Sign,
                            bias=nstat[:, 13:14], accum_out=cnts[:, q:q + 1]), inc="sp")
                    E.wait("vector", "sp")
                    E.fence("vector", "vp")
                    E.emit("vector", lambda v: v.reduce_sum(bcol(25), cnts[:, 0:8], axis=AX), inc="vp")
                    E.emit("vector", lambda v: v.reduce_sum(bcol(22), cnts[:, 8:16], axis=AX), inc="vp")
                    E.fence("vector", "vp")
                    E.emit("vector", lambda v: v.tensor_scalar(bcol(22), bcol(22), 0.5,
                                                               32768.0, ALU.mult, op1=ALU.add), inc="vp")
                    E.fence("vector", "vp")
                    E.emit("vector", lambda v: v.tensor_add(bcol(25), bcol(25), bcol(22)), inc="vp")
                    greduce(ALU.add, bcol(25), dstcol, 0)

                def int_bisect(tgtcol, rounds=9):
                    E.emit("vector", lambda v: v.memset(bcol(0), -128.0), inc="vp")
                    E.emit("vector", lambda v: v.memset(bcol(1), 127.0), inc="vp")
                    for rnd in range(rounds):
                        E.fence("vector", "vp")
                        E.emit("vector", lambda v: v.tensor_add(bcol(2), bcol(0), bcol(1)), inc="vp")
                        E.fence("vector", "vp")
                        E.emit("vector", lambda v: v.tensor_scalar(bcol(2), bcol(2), 0.5,
                                                                   None, ALU.mult), inc="vp")
                        E.fence("vector", "vp")
                        E.emit("vector", lambda v: v.tensor_copy(x2t[0:128, 0, 0:1].bitcast(I32), bcol(2)), inc="vp")
                        E.fence("vector", "vp")
                        E.emit("vector", lambda v: v.tensor_copy(bcol(2), x2t[0:128, 0, 0:1].bitcast(I32)), inc="vp")
                        count_c8(bcol(2), bcol(4))
                        E.fence("vector", "vp")
                        E.emit("vector", lambda v: v.tensor_copy(bcol(6), tgtcol), inc="vp")
                        bis_select()

                quantile_targets(NSEQ * 4096 * HID, bcol(11), bcol(12), bcol(8), bcol(9))
                if q2_dbg is not None:
                    E.wait("gpsimd", "vp")
                    E.emit("gpsimd", lambda g: g.dma_start(out=q2_dbg[0], in_=bst[:]), inc="gpa", n=16)
                if q2_dbg is not None:
                    E.wait("gpsimd", "vp")
                    E.emit("gpsimd", lambda g: g.dma_start(out=c8_dbg[0], in_=c8[:, 0, :]), inc="gpa", n=16)
                    E.emit("gpsimd", lambda g: g.dma_start(out=c8_dbg[1], in_=c8[:, 1, :]), inc="gpa", n=16)
                int_bisect(bcol(8), 9)
                if q2_dbg is not None:
                    E.wait("gpsimd", "vp")
                    E.emit("gpsimd", lambda g: g.dma_start(out=q2_dbg[1], in_=bst[:]), inc="gpa", n=16)
                E.fence("vector", "vp")
                E.emit("vector", lambda v: v.tensor_mul(bcol(20), bcol(1), bcol(28)), inc="vp")
                E.fence("vector", "vp")
                E.emit("vector", lambda v: v.tensor_scalar(bcol(20), bcol(20), 1.0 / 127.0,
                                                           None, ALU.mult), inc="vp")
                E.fence("vector", "vp")
                E.emit("vector", lambda v: v.tensor_copy(x2t[0:128, 0, 0:1].bitcast(I32), bcol(20)), inc="vp")
                E.fence("vector", "vp")
                E.emit("vector", lambda v: v.tensor_copy(nstat[:, 15:16], x2t[0:128, 0, 0:1].bitcast(I32)), inc="vp")
                E.fence("vector", "vp")
                if q2_dbg is not None:
                    E.wait("gpsimd", "vp")
                    E.emit("gpsimd", lambda g: g.dma_start(out=q2_dbg[2], in_=bst[:]), inc="gpa", n=16)
                for rt in range(NT):
                    for qq in range(4):
                        zs = (rt * 4 + qq) % 2
                        lws = "lwa" if zs == 0 else "lwb"
                        qeng = "sync" if zs == 0 else "gpsimd"
                        E.wait(qeng, "vp", zqslot[zs])
                        E.emit(qeng, lambda s, rt=rt, qq=qq, zs=zs: s.dma_start(
                            out=zrt2[:, zs, :],
                            in_=zd[rt, :, qq * D:(qq + 1) * D]), inc=lws, n=16)
                        E.wait("vector", lws)
                        E.emit("vector", lambda v, zs=zs: v.tensor_scalar(
                            x2t[:, 0, :], zrt2[:, zs, :],
                            nstat[:, 15:16], -127.0, ALU.subtract, op1=ALU.max), inc="vp")
                        zqslot[zs] = E.val("vp")
                        E.emit("vector", lambda v, rt=rt, qq=qq: v.tensor_scalar(
                            c8flat[:, rt * HID + qq * D: rt * HID + (qq + 1) * D],
                            x2t[:, 0, :], 127.0, None,
                            ALU.min), inc="vp")
                int_bisect(bcol(8), 9)
                if q2_dbg is not None:
                    E.wait("gpsimd", "vp")
                    E.emit("gpsimd", lambda g: g.dma_start(out=q2_dbg[3], in_=bst[:]), inc="gpa", n=16)
                E.fence("vector", "vp")
                E.emit("vector", lambda v: v.tensor_add(bcol(13), nstat[:, 15:16], bcol(1)), inc="vp")
                E.fence("vector", "vp")
                E.emit("vector", lambda v: v.tensor_scalar(bcol(22), bcol(1), -1.0, None,
                                                           ALU.add), inc="vp")
                count_c8(bcol(22), bcol(4))
                E.fence("vector", "vp")
                E.emit("vector", lambda v: v.tensor_scalar(bcol(23), bcol(1), 1.0, 200.0,
                                                           ALU.mult, op1=ALU.add), inc="vp")
                E.fence("vector", "vp")
                def mm_bufs(q):
                    if q % 2 == 0:
                        return junk[:], junkS[:]
                    return x2t[:, 0, :], x2t[:, 1, :]
                for half in range(2):
                    for qi in range(65):
                        q = half * 64 + qi
                        if qi < 64:
                            ja, jb = mm_bufs(q)
                            E.emit("vector", lambda v, q=q, ja=ja: v.tensor_scalar(
                                ja, c8flat[:, q * 1024:(q + 1) * 1024], 200.0, None,
                                ALU.add), inc="vp")
                            E.emit("vector", lambda v, ja=ja, jb=jb: v.scalar_tensor_tensor(
                                jb, ja, bcol(23), ja, ALU.is_lt, ALU.mult), inc="vp")
                        if qi > 0:
                            _, jbp = mm_bufs(q - 1)
                            E.emit("vector", lambda v, q=q, jbp=jbp: v.reduce_max(
                                cnts[:, (q - 1) % 64:(q - 1) % 64 + 1], jbp, axis=AX), inc="vp")
                    E.fence("vector", "vp")
                    if half == 0:
                        E.emit("vector", lambda v: v.reduce_max(bcol(7), cnts[:, 0:64], axis=AX), inc="vp")
                    else:
                        E.emit("vector", lambda v: v.reduce_max(bcol(6), cnts[:, 0:64], axis=AX), inc="vp")
                        E.fence("vector", "vp")
                        E.emit("vector", lambda v: v.tensor_max(bcol(7), bcol(7), bcol(6)), inc="vp")
                greduce(ALU.max, bcol(7), bcol(14), 0)
                E.fence("vector", "vp")
                E.emit("vector", lambda v: v.tensor_scalar(bcol(14), bcol(14), 1.0, -200.0,
                                                           ALU.mult, op1=ALU.add), inc="vp")
                E.fence("vector", "vp")
                E.emit("vector", lambda v: v.tensor_add(bcol(14), bcol(14), nstat[:, 15:16]), inc="vp")
                E.fence("vector", "vp")
                E.emit("vector", lambda v: v.tensor_tensor(bcol(5), bcol(4), bcol(9), ALU.is_ge), inc="vp")
                E.fence("vector", "vp")
                E.emit("vector", lambda v: v.tensor_sub(bcol(21), bcol(13), bcol(14)), inc="vp")
                E.fence("vector", "vp")
                E.emit("vector", lambda v: v.tensor_mul(bcol(21), bcol(21), bcol(5)), inc="vp")
                E.fence("vector", "vp")
                E.emit("vector", lambda v: v.tensor_add(bcol(14), bcol(14), bcol(21)), inc="vp")
                E.fence("vector", "vp")
                E.emit("vector", lambda v: v.tensor_mul(bcol(18), bcol(19), bcol(15)), inc="vp")
                E.fence("vector", "vp")
                E.emit("vector", lambda v: v.tensor_scalar(bcol(18), bcol(18), 1.0 / 127.0,
                                                           None, ALU.mult), inc="vp")
                E.fence("vector", "vp")
                for (zc_col, out_col) in ((13, 13), (14, 14)):
                    E.emit("vector", lambda v, zc_col=zc_col: v.tensor_mul(
                        bcol(22), bcol(zc_col), bcol(18)), inc="vp")
                    E.wait("scalar", "vp")
                    E.emit("scalar", lambda s, out_col=out_col: s.activation(
                        bcol(out_col), bcol(22), AF.Gelu), inc="sp")
                    E.wait("vector", "sp")
                E.emit("vector", lambda v: v.tensor_mul(bcol(20), bcol(14), bcol(12)), inc="vp")
                E.emit("vector", lambda v: v.tensor_mul(bcol(21), bcol(13), bcol(11)), inc="vp")
                E.fence("vector", "vp")
                E.emit("vector", lambda v: v.tensor_add(bcol(30), bcol(20), bcol(21)), inc="vp")
                E.fence("vector", "vp")
                E.emit("vector", lambda v: v.tensor_scalar(bcol(30), bcol(30), EPS,
                                                           None, ALU.max), inc="vp")
                E.fence("vector", "vp")
                E.emit("vector", lambda v: v.reciprocal(bcol(31), bcol(30)), inc="vp")
                E.fence("vector", "vp")
                E.emit("vector", lambda v: v.tensor_scalar(bcol(31), bcol(31), 127.0,
                                                           None, ALU.mult), inc="vp")
                E.fence("vector", "vp")
                if st_dbg is not None:
                    E.wait("gpsimd", "vp")
                    E.emit("gpsimd", lambda g: g.dma_start(out=st_dbg, in_=bst[:]), inc="gpa", n=16)

            if "D" in phases:
                barrier()
                for ch in range(64):
                    kh, dcol = ch // 2, ch % 2
                    ws = ch % 2
                    lws = "lwa" if ws == 0 else "lwb"
                    E.wait("sync", "sp", wslot_free.get(("g2", ws), 0))
                    E.emit("sync", lambda s, kh=kh, dcol=dcol, ws=ws: s.dma_start(
                        out=wstage[:, ws, :],
                        in_=w2T[kh * 128:(kh + 1) * 128, dcol * 512:(dcol + 1) * 512]),
                        inc=lws, n=16)
                    E.wait("scalar", lws)
                    E.emit("scalar", lambda s, ch=ch, ws=ws: s.activation(
                        junkS[:, 0:512], wstage[:, ws, :], AF.Abs,
                        accum_out=cnts[:, ch % 64:ch % 64 + 1]), inc="sp")
                    wslot_free[("g2", ws)] = E.val("sp")
                E.wait("vector", "sp")
                E.emit("vector", lambda v: v.reduce_sum(bcol(6), cnts[:, 0:64], axis=AX), inc="vp")
                lreduce(ALU.add, bcol(6), bcol(19), 0)
                E.emit("vector", lambda v: v.tensor_scalar(
                    bcol(19), bcol(19), 1.0 / (D * HID), EPS, ALU.mult, op1=ALU.add), inc="vp")
                E.fence("vector", "vp")
                E.emit("vector", lambda v: v.reciprocal(bcol(26), bcol(19)), inc="vp")
                E.fence("vector", "vp")
                E.emit("vector", lambda v: v.tensor_mul(bcol(25), bcol(19), bcol(30)), inc="vp")
                E.fence("vector", "vp")
                E.emit("vector", lambda v: v.tensor_scalar(bcol(25), bcol(25), 1.0 / 127.0,
                                                           None, ALU.mult), inc="vp")
                E.emit("vector", lambda v: v.tensor_scalar(bcol(24), bcol(31), 0.5,
                                                           None, ALU.mult), inc="vp")
                E.fence("vector", "vp")
                wslot3 = {0: (0, 0), 1: (0, 0)}
                for ch in range(64):
                    kh, dcol = ch // 2, ch % 2
                    ws = ch % 2
                    lws = "lwa" if ws == 0 else "lwb"
                    vv, _ = wslot3[ws]
                    E.wait("sync", "vp", vv)
                    E.emit("sync", lambda s, kh=kh, dcol=dcol, ws=ws: s.dma_start(
                        out=wstage[:, ws, :],
                        in_=w2T[kh * 128:(kh + 1) * 128, dcol * 512:(dcol + 1) * 512]),
                        inc=lws, n=16)
                    E.wait("vector", lws)
                    E.emit("vector", lambda v, ws=ws: v.tensor_scalar(
                        junk[:, 0:512], wstage[:, ws, :], bcol(26), None, ALU.mult), inc="vp")
                    wslot3[ws] = (E.val("vp"), 0)
                    E.emit("vector", lambda v: v.tensor_copy(
                        x2t[:, 0, 0:512].bitcast(I32), junk[:, 0:512]), inc="vp")
                    E.emit("vector", lambda v, kh=kh, dcol=dcol: v.tensor_scalar(
                        t2[:, kh, dcol * 512:(dcol + 1) * 512],
                        x2t[:, 0, 0:512].bitcast(I32), -1.0, 1.0,
                        ALU.max, op1=ALU.min), inc="vp")
                zdslot = {0: (0, 0), 1: (0, 0)}
                oslot = {0: 0, 1: 0}
                for rt in range(NT):
                    n, b = rt // NB, rt % NB
                    zs = rt % 2
                    lws = "lwa" if zs == 0 else "lwb"
                    vv, _ = zdslot[zs]
                    E.wait("sync", "vp", vv)
                    E.emit("sync", lambda s, rt=rt, zs=zs: s.dma_start(
                        out=zrt[:, zs, :], in_=zd[rt]), inc=lws, n=16)
                    E.wait("vector", lws)
                    for qq in range(4):
                        E.emit("vector", lambda v, zs=zs, qq=qq: v.tensor_scalar(
                            x2t[:, 0, :], zrt[:, zs, qq * D:(qq + 1) * D],
                            bcol(18), None, ALU.mult), inc="vp")
                        E.wait("scalar", "vp")
                        E.emit("scalar", lambda s: s.activation(
                            junkS[:], x2t[:, 0, :], AF.Gelu), inc="sp")
                        E.wait("vector", "sp")
                        E.emit("vector", lambda v: v.tensor_scalar(
                            x2t[:, 1, :], junkS[:], bcol(31), None, ALU.mult), inc="vp")
                        E.emit("vector", lambda v: v.tensor_copy(
                            ln1[:, 0, :].bitcast(I32), x2t[:, 1, :]), inc="vp")
                        E.emit("vector", lambda v, qq=qq: v.tensor_scalar(
                            u2tok[:, qq * D:(qq + 1) * D], ln1[:, 0, :].bitcast(I32),
                            -128.0, 127.0, ALU.max, op1=ALU.min), inc="vp")
                    zdslot[zs] = (E.val("vp"), 0)
                    E.wait("tensor", "vp")
                    for kh in range(32):
                        pt = ptrs[kh % 4]
                        E.wait("tensor", "sp", ptr_free[kh % 4])
                        E.emit("tensor", lambda t, kh=kh, pt=pt: t.transpose(
                            pt[:].bitcast(BF16)[:, 0:128],
                            u2tok[:, kh * 128:(kh + 1) * 128], identb[:]), inc="tp")
                        E.wait("scalar", "tp")
                        E.emit("scalar", lambda s, kh=kh, pt=pt: s.copy(
                            u2fm[:, kh, :], pt[:].bitcast(BF16)[:, 0:128]), inc="sp")
                        ptr_free[kh % 4] = E.val("sp")
                    E.wait("tensor", "sp")
                    E.wait("tensor", "vp", max(pmm_free[0], pmm_free[1]))
                    for dcol in range(2):
                        pm = pmm0 if dcol == 0 else pmm1
                        for kh in range(32):
                            E.emit("tensor", lambda t, kh=kh, dcol=dcol, pm=pm: t.matmul(
                                pm[:], u2fm[:, kh, :], t2[:, kh, dcol * 512:(dcol + 1) * 512],
                                start=(kh == 0), stop=(kh == 31)),
                                inc=("tp" if kh == 31 else None))
                    E.wait("vector", "tp")
                    E.emit("vector", lambda v: v.tensor_scalar(
                        o2b[:, 0:512], pmm0[:], bcol(25), None, ALU.mult), inc="vp")
                    E.emit("vector", lambda v: v.tensor_scalar(
                        o2b[:, 512:1024], pmm1[:], bcol(25), None, ALU.mult), inc="vp")
                    pmm_free[0] = E.val("vp")
                    pmm_free[1] = E.val("vp")
                    E.fence("vector", "vp")
                    E.emit("vector", lambda v: v.reduce_sum(nstat[:, 0:1], o2b[:], axis=AX), inc="vp")
                    E.emit("vector", lambda v: v.scalar_tensor_tensor(
                        junk[:], o2b[:], 1.0, o2b[:], ALU.mult, ALU.mult,
                        accum_out=nstat[:, 4:5]), inc="vp")
                    E.fence("vector", "vp")
                    E.emit("vector", lambda v: v.tensor_scalar(nstat[:, 8:9], nstat[:, 0:1],
                                                               1.0 / D, None, ALU.mult), inc="vp")
                    E.emit("vector", lambda v: v.tensor_scalar(nstat[:, 9:10], nstat[:, 4:5],
                                                               1.0 / D, None, ALU.mult), inc="vp")
                    E.fence("vector", "vp")
                    E.emit("vector", lambda v: v.tensor_mul(nstat[:, 10:11], nstat[:, 8:9],
                                                            nstat[:, 8:9]), inc="vp")
                    E.fence("vector", "vp")
                    E.emit("vector", lambda v: v.tensor_sub(nstat[:, 11:12], nstat[:, 9:10],
                                                            nstat[:, 10:11]), inc="vp")
                    E.fence("vector", "vp")
                    E.emit("vector", lambda v: v.tensor_scalar(nstat[:, 11:12], nstat[:, 11:12],
                                                               1.0, EPS, ALU.mult,
                                                               op1=ALU.add), inc="vp")
                    rsqrt_chain(11, 12, 13, 14, 1)
                    sl = rt % 2
                    lxs = "lxa" if sl == 0 else "lxb"
                    vv2, ss2 = xslot_free[sl]
                    E.wait("sync", "vp", vv2)
                    E.wait("sync", "sp", ss2)
                    E.emit("sync", lambda s, rt=rt, sl=sl: s.dma_start(
                        out=xt[:, sl, :], in_=x2d[rt]), inc=lxs, n=16)
                    xp = rt % 2
                    E.wait("vector", "so", oslot[xp])
                    E.fence("vector", "vp")
                    E.emit("vector", lambda v, xp=xp: v.tensor_scalar(
                        x2t[:, xp, :], o2b[:], nstat[:, 8:9], nstat[:, 12:13],
                        ALU.subtract, op1=ALU.mult), inc="vp")
                    E.wait("vector", lxs)
                    E.emit("vector", lambda v, sl=sl, xp=xp: v.tensor_tensor(
                        x2t[:, xp, :], x2t[:, xp, :], xt[:, sl, :], ALU.add), inc="vp")
                    xslot_free[sl] = (E.val("vp"), E.val("sp"))
                    E.wait("sync", "vp")
                    E.emit("sync", lambda s, n=n, b=b, xp=xp: s.dma_start(
                        out=out[n, b * 128:(b + 1) * 128, :], in_=x2t[:, xp, :]), inc="so", n=16)
                    oslot[xp] = E.val("so")
                E.wait("sync", "so")
                E.wait("vector", "so")

            if "D" not in phases:
                barrier()
                for rt in range(NT):
                    n, b = rt // NB, rt % NB
                    E.emit("sync", lambda s, rt=rt: s.dma_start(
                        out=junk[:], in_=x2d[rt]), inc="lxa", n=16)
                    E.wait("sync", "lxa")
                    E.emit("sync", lambda s, n=n, b=b: s.dma_start(
                        out=out[n, b * 128:(b + 1) * 128, :], in_=junk[:]), inc="so", n=16)
                    E.wait("sync", "so")
            E.wait("sync", "so")
            E.wait("vector", "so")

        with nc.Block() as block:
            for eng in ENGINES:
                def make(eng):
                    def f(_q):
                        program(Emitter(nc, eng, handles))
                    return f
                getattr(block, eng)(make(eng))

    return nc


def _prep_inputs(L0_attn_x, L0_attn_Wqkv, L0_attn_bqkv, L0_attn_Wproj, L0_attn_bproj,
                 L0_ffn_ln_g, L0_ffn_ln_b, L0_ffn_w1, L0_ffn_w2):
    x = np.ascontiguousarray(L0_attn_x, dtype=np.float32)
    shared = {
        "wqkvT": np.ascontiguousarray(np.asarray(L0_attn_Wqkv, dtype=np.float32).T),
        "bqkv": np.ascontiguousarray(np.asarray(L0_attn_bqkv, dtype=np.float32).reshape(1, -1)),
        "wprojT": np.ascontiguousarray(np.asarray(L0_attn_Wproj, dtype=np.float32).T),
        "bproj": np.ascontiguousarray(np.asarray(L0_attn_bproj, dtype=np.float32).reshape(1, -1)),
        "ln_g": np.ascontiguousarray(np.asarray(L0_ffn_ln_g, dtype=np.float32).reshape(1, -1)),
        "ln_b": np.ascontiguousarray(np.asarray(L0_ffn_ln_b, dtype=np.float32).reshape(1, -1)),
        "w1T": np.ascontiguousarray(np.asarray(L0_ffn_w1, dtype=np.float32).T),
        "w2T": np.ascontiguousarray(np.asarray(L0_ffn_w2, dtype=np.float32).T),
    }
    in_maps = []
    for c in range(NCORES):
        m = dict(shared)
        m["x"] = np.ascontiguousarray(x[:, c * T:(c + 1) * T, :])
        in_maps.append(m)
    return in_maps


def kernel(**inputs):
    in_maps = _prep_inputs(**inputs)
    nc = build_program()
    res = run_bass_kernel_spmd(nc, in_maps, core_ids=list(range(NCORES)))
    shards = [np.asarray(res.results[c]["out"]) for c in range(NCORES)]
    return np.concatenate(shards, axis=1)


# version 07
# speedup vs baseline: 1.1500x; 1.1500x over previous
import numpy as np
import concourse.bass as bass
import concourse.mybir as mybir
from concourse.bass_utils import run_bass_kernel_spmd

F32 = mybir.dt.float32
BF16 = mybir.dt.bfloat16
I32 = mybir.dt.int32
ALU = mybir.AluOpType
AF = mybir.ActivationFunctionType
AX = mybir.AxisListType.X

NCORES = 8
NSEQ = 8
T = 512
D = 1024
H = 16
HD = 64
HID = 4096
NB = T // 128
NT = NSEQ * NB
DCH = D // 128
SCALE = float(HD) ** -0.5
EPS = 1e-5
NEG_BIG = -1.0e30

ENGINES = ("sync", "tensor", "vector", "scalar", "gpsimd")


class Emitter:

    def __init__(self, nc, engine, handles):
        self.nc = nc
        self.engine = engine
        self.h = handles
        self.c = {k: 0 for k in handles}

    def eng(self, engine):
        return getattr(self.nc, engine)

    def emit(self, engine, fn, inc=None, n=1):
        if self.engine == engine:
            ins = fn(self.eng(engine))
            if inc is not None:
                ins.then_inc(self.h[inc], n)
        if inc is not None:
            self.c[inc] += n

    def wait(self, engine, sem, val=None):
        if self.engine == engine:
            self.eng(engine).wait_ge(self.h[sem], self.c[sem] if val is None else val)

    def fence(self, engine, sem):
        self.wait(engine, sem)

    def val(self, sem):
        return self.c[sem]


def build_program(debug=(), limit_b=NB, phases=("A", "B", "C", "D")):
    nc = bass.Bass("TRN2", target_bir_lowering=False, detect_race_conditions=False)

    x_in = nc.dram_tensor("x", [NSEQ, T, D], F32, kind="ExternalInput").ap()
    wqkvT = nc.dram_tensor("wqkvT", [D, 3 * D], F32, kind="ExternalInput").ap()
    bqkv = nc.dram_tensor("bqkv", [1, 3 * D], F32, kind="ExternalInput").ap()
    wprojT = nc.dram_tensor("wprojT", [D, D], F32, kind="ExternalInput").ap()
    bproj = nc.dram_tensor("bproj", [1, D], F32, kind="ExternalInput").ap()
    ln_g = nc.dram_tensor("ln_g", [1, D], F32, kind="ExternalInput").ap()
    ln_b = nc.dram_tensor("ln_b", [1, D], F32, kind="ExternalInput").ap()
    w1T = nc.dram_tensor("w1T", [D, HID], F32, kind="ExternalInput").ap()
    w2T = nc.dram_tensor("w2T", [HID, D], F32, kind="ExternalInput").ap()
    out = nc.dram_tensor("out", [NSEQ, T, D], F32, kind="ExternalOutput").ap()

    x2_kind = "ExternalOutput" if "x2" in debug else "Internal"
    x2d = nc.dram_tensor("dbg_x2" if "x2" in debug else "x2d",
                         [NT, 128, D], F32, kind=x2_kind).ap()
    zd = nc.dram_tensor("zd", [NT, 128, HID], F32, kind="Internal").ap()
    cc_in = nc.dram_tensor("cc_in", [1, 1], F32, kind="Internal").ap()
    cc_out = nc.dram_tensor("cc_out", [1, 1], F32, kind="Internal").ap()
    h1_dbg = (nc.dram_tensor("dbg_h1", [NT, 128, D], F32, kind="ExternalOutput").ap()
              if "h1" in debug else None)
    q2_dbg = (nc.dram_tensor("dbg_q2", [6, 128, 32], F32, kind="ExternalOutput").ap()
              if "q2" in debug else None)
    c8_dbg = (nc.dram_tensor("dbg_c8", [2, 128, HID], mybir.dt.int8, kind="ExternalOutput").ap()
              if "q2" in debug else None)
    st_dbg = (nc.dram_tensor("dbg_st", [128, 32], F32, kind="ExternalOutput").ap()
              if "st" in debug else None)

    sem_names = ["lxa", "lxb", "lwa", "lwb", "tp", "vp", "sp",
                 "gpa", "gpb", "gi", "so", "cc", "lb"]

    import contextlib
    with contextlib.ExitStack() as _stk:
        def sb(name, shape, dtype=F32):
            return _stk.enter_context(nc.sbuf_tensor(name, shape, dtype))

        def psb(name, shape, dtype=F32):
            return _stk.enter_context(nc.psum_tensor(name, shape, dtype))

        ident = sb("ident", [128, 128])
        identb = sb("identb", [128, 128], BF16)
        ones1 = sb("ones1", [1, 128])
        bqkv_t = sb("bqkv_t", [1, 3 * D])
        bproj_t = sb("bproj_t", [1, D])
        xt = sb("xt", [128, 2, D])
        ln1 = sb("ln1", [128, 2, D])
        smat = sb("smat", [128, 8, NSEQ, NSEQ])
        emat = sb("emat", [128, 8, NSEQ, NSEQ])
        mx = sb("mx", [128, 64])
        sm = sb("sm", [128, 64])
        prods = sb("prods", [128, 8, HD])
        junk = sb("junk", [128, D])
        junkS = sb("junkS", [128, D])
        stats = sb("stats", [128, 8, 8])
        nstat = sb("nstat", [128, 16])
        x2t = sb("x2t", [128, 2, D])
        wstage = sb("wstage", [128, 2, 512])
        bst = sb("bst", [128, 32])
        cnts = sb("cnts", [128, 64])
        ccrow = sb("ccrow", [1, 8])
        ARENA_F32 = 36864
        arena = sb("arena", [128, ARENA_F32])
        arena_addr = nc.lookup_mloc(arena).addr

        def overlay(name, shape, dtype, byte_off):
            h = nc.virtual_sbuf_tensor(name, shape, dtype)
            nc._tensor(h.name, list(h.shape), h.dtype, type="SB")
            mloc = nc.lookup_mloc(h)
            mloc.allocated = True
            mloc.addr = arena_addr + byte_off
            return h

        xfm = overlay("xfm", [128, NSEQ, DCH, 128], F32, 0)
        wbuf = overlay("wbuf", [128, 2, DCH, 512], F32, 32768)
        ybuf = overlay("ybuf", [128, NSEQ, 3, 512], F32, 65536)
        aout = overlay("aout", [128, NSEQ, D], F32, 114688)
        h1 = overlay("h1", [128, NT, D], F32, 0)
        u1tok = overlay("u1tok", [128, NT, D], BF16, 0)
        t1 = overlay("t1", [128, DCH, HID], BF16, 65536)
        c8 = overlay("c8", [128, NT, HID], mybir.dt.int8, 0)
        u1f = overlay("u1f", [128, DCH, 128], BF16, 131072)
        zstage = overlay("zstage", [128, 2, 512], F32, 133120)
        zrt2 = overlay("zrt2", [128, 2, 1024], F32, 131072)
        t2 = overlay("t2", [128, 32, D], BF16, 0)
        u2fm = overlay("u2fm", [128, 32, 128], BF16, 65536)
        u2tok = overlay("u2tok", [128, HID], BF16, 73728)
        zrt = overlay("zrt", [128, 2, HID], F32, 81920)
        o2b = overlay("o2b", [128, D], F32, 114688)
        u2st = overlay("u2st", [128, 2, D], F32, 118784)
        ptr0 = psb("ptr0", [128, 128])
        ptr1 = psb("ptr1", [128, 128])
        ptr2 = psb("ptr2", [128, 128])
        ptr3 = psb("ptr3", [128, 128])
        ptrs = [ptr0, ptr1, ptr2, ptr3]
        pmm0 = psb("pmm0", [128, 512])
        pmm1 = psb("pmm1", [128, 512])

        handles = {nm: _stk.enter_context(nc.semaphore(nm)) for nm in sem_names}

        def program(E: Emitter):
            FS, FS2, FM, FT, FR = 0, 1, 2, 3, 4

            E.emit("gpsimd", lambda g: g.iota(junk[:, 0:128], [[1, 128]], channel_multiplier=0,
                                              allow_small_or_imprecise_dtypes=True), inc="gi")
            E.emit("gpsimd", lambda g: g.iota(junkS[:, 0:128], [[0, 128]], channel_multiplier=1,
                                              allow_small_or_imprecise_dtypes=True), inc="gi")
            E.wait("vector", "gi")
            E.emit("vector", lambda v: v.tensor_tensor(ident[:], junk[:, 0:128], junkS[:, 0:128],
                                                       ALU.is_equal), inc="vp")
            E.emit("vector", lambda v: v.memset(ones1[:], 1.0), inc="vp")
            E.fence("vector", "vp")
            E.emit("vector", lambda v: v.tensor_copy(identb[:], ident[:]), inc="vp")
            E.emit("sync", lambda s: s.dma_start(out=bqkv_t[:], in_=bqkv[:]), inc="lb", n=16)
            E.emit("sync", lambda s: s.dma_start(out=bproj_t[:], in_=bproj[:]), inc="lb", n=16)

            E.wait("tensor", "vp")
            E.wait("tensor", "lb")

            xslot_free = {0: (0, 0), 1: (0, 0)}
            lnslot_free = {0: 0, 1: 0}
            wseq = [0]
            wslot_free = {0: 0, 1: 0}
            pslot = {0: 0, 1: 1}
            pwsem = {0: "lwa", 1: "lwb"}
            x2_gp = []
            pmm_free = {0: 0, 1: 0}
            pmmS_free = {0: 0, 1: 0}
            ptr_free = {0: 0, 1: 0, 2: 0, 3: 0}

            def load_x(n, b):
                sl = n % 2
                lxs = "lxa" if sl == 0 else "lxb"
                vv, ss = xslot_free[sl]
                E.wait("sync", "vp", vv)
                E.wait("sync", "sp", ss)
                E.emit("sync", lambda s, n=n, b=b, sl=sl: s.dma_start(
                    out=xt[:, sl, :], in_=x_in[n, b * 128:(b + 1) * 128, :]),
                    inc=lxs, n=16)
                return sl, lxs

            def rsqrt_chain(tcol, rcol, jcol, thcol, w):
                tA = nstat[:, tcol:tcol + w]
                rA = nstat[:, rcol:rcol + w]
                jA = nstat[:, jcol:jcol + w]
                thA = nstat[:, thcol:thcol + w]
                E.wait("scalar", "vp")
                E.emit("scalar", lambda s: s.activation(jA, tA, AF.Ln), inc="sp")
                E.fence("scalar", "sp")
                E.emit("scalar", lambda s: s.activation(rA, jA, AF.Exp, scale=-0.5), inc="sp")
                E.wait("vector", "sp")
                E.emit("vector", lambda v: v.tensor_scalar(thA, tA, 0.5, None, ALU.mult), inc="vp")
                for _ in range(2):
                    E.fence("vector", "vp")
                    E.emit("vector", lambda v: v.tensor_mul(jA, rA, rA), inc="vp")
                    E.fence("vector", "vp")
                    E.emit("vector", lambda v: v.tensor_mul(jA, jA, thA), inc="vp")
                    E.fence("vector", "vp")
                    E.emit("vector", lambda v: v.tensor_scalar(jA, jA, -1.0, 1.5,
                                                               ALU.mult, op1=ALU.add), inc="vp")
                    E.fence("vector", "vp")
                    E.emit("vector", lambda v: v.tensor_mul(rA, rA, jA), inc="vp")

            allowed = [(n, m) for n in range(NSEQ) for m in range(NSEQ)
                       if not (n < 4 and m >= 4)]

            def stats_chain(fs, fs2, fm, ft, fr):
                E.wait("vector", "sp")
                E.fence("vector", "vp")
                sA = stats[:, :, fs]
                s2A = stats[:, :, fs2]
                mA = stats[:, :, fm]
                tA = stats[:, :, ft]
                rA = stats[:, :, fr]
                E.emit("vector", lambda v: v.tensor_scalar(mA, sA, 1.0 / D, None, ALU.mult), inc="vp")
                E.emit("vector", lambda v: v.tensor_scalar(nstat[:, 0:8], s2A, 1.0 / D,
                                                           None, ALU.mult), inc="vp")
                E.fence("vector", "vp")
                E.emit("vector", lambda v: v.tensor_mul(nstat[:, 8:16], mA, mA), inc="vp")
                E.fence("vector", "vp")
                E.emit("vector", lambda v: v.tensor_sub(tA, nstat[:, 0:8], nstat[:, 8:16]), inc="vp")
                E.fence("vector", "vp")
                E.emit("vector", lambda v: v.tensor_scalar(tA, tA, 1.0, EPS,
                                                           ALU.mult, op1=ALU.add), inc="vp")
                E.wait("scalar", "vp")
                E.emit("scalar", lambda s: s.activation(nstat[:, 0:8], tA, AF.Ln), inc="sp")
                E.fence("scalar", "sp")
                E.emit("scalar", lambda s: s.activation(rA, nstat[:, 0:8], AF.Exp,
                                                        scale=-0.5), inc="sp")
                E.wait("vector", "sp")
                E.emit("vector", lambda v: v.tensor_scalar(nstat[:, 8:16], tA, 0.5,
                                                           None, ALU.mult), inc="vp")
                for _ in range(2):
                    E.fence("vector", "vp")
                    E.emit("vector", lambda v: v.tensor_mul(nstat[:, 0:8], rA, rA), inc="vp")
                    E.fence("vector", "vp")
                    E.emit("vector", lambda v: v.tensor_mul(nstat[:, 0:8], nstat[:, 0:8],
                                                            nstat[:, 8:16]), inc="vp")
                    E.fence("vector", "vp")
                    E.emit("vector", lambda v: v.tensor_scalar(nstat[:, 0:8], nstat[:, 0:8],
                                                               -1.0, 1.5, ALU.mult,
                                                               op1=ALU.add), inc="vp")
                    E.fence("vector", "vp")
                    E.emit("vector", lambda v: v.tensor_mul(rA, rA, nstat[:, 0:8]), inc="vp")
                E.fence("vector", "vp")

            for b in range(limit_b if "A" in phases else 0):
                for n in range(NSEQ):
                    sl, lxs = load_x(n, b)
                    E.wait("vector", lxs)
                    E.wait("scalar", lxs)
                    E.emit("scalar", lambda s, n=n, sl=sl: s.activation(
                        junkS[:], xt[:, sl, :], AF.Copy,
                        accum_out=stats[:, n, FS:FS + 1]), inc="sp")
                    E.emit("vector", lambda v, n=n, sl=sl: v.scalar_tensor_tensor(
                        junk[:], xt[:, sl, :], 1.0, xt[:, sl, :], ALU.mult, ALU.mult,
                        accum_out=stats[:, n, FS2:FS2 + 1]), inc="vp")
                    xslot_free[sl] = (E.val("vp"), E.val("sp"))
                E.wait("vector", "sp")
                E.fence("vector", "vp")
                sA = stats[:, :, FS]
                s2A = stats[:, :, FS2]
                mA = stats[:, :, FM]
                tA = stats[:, :, FT]
                rA = stats[:, :, FR]
                E.emit("vector", lambda v: v.tensor_scalar(mA, sA, 1.0 / D, None, ALU.mult), inc="vp")
                E.emit("vector", lambda v: v.tensor_scalar(nstat[:, 0:8], s2A, 1.0 / D,
                                                           None, ALU.mult), inc="vp")
                E.fence("vector", "vp")
                E.emit("vector", lambda v: v.tensor_mul(nstat[:, 8:16], mA, mA), inc="vp")
                E.fence("vector", "vp")
                E.emit("vector", lambda v: v.tensor_sub(tA, nstat[:, 0:8], nstat[:, 8:16]), inc="vp")
                E.fence("vector", "vp")
                E.emit("vector", lambda v: v.tensor_scalar(tA, tA, 1.0, EPS,
                                                           ALU.mult, op1=ALU.add), inc="vp")
                E.wait("scalar", "vp")
                E.emit("scalar", lambda s: s.activation(nstat[:, 0:8], tA, AF.Ln), inc="sp")
                E.fence("scalar", "sp")
                E.emit("scalar", lambda s: s.activation(rA, nstat[:, 0:8], AF.Exp,
                                                        scale=-0.5), inc="sp")
                E.wait("vector", "sp")
                E.emit("vector", lambda v: v.tensor_scalar(nstat[:, 8:16], tA, 0.5,
                                                           None, ALU.mult), inc="vp")
                for _ in range(2):
                    E.fence("vector", "vp")
                    E.emit("vector", lambda v: v.tensor_mul(nstat[:, 0:8], rA, rA), inc="vp")
                    E.fence("vector", "vp")
                    E.emit("vector", lambda v: v.tensor_mul(nstat[:, 0:8], nstat[:, 0:8],
                                                            nstat[:, 8:16]), inc="vp")
                    E.fence("vector", "vp")
                    E.emit("vector", lambda v: v.tensor_scalar(nstat[:, 0:8], nstat[:, 0:8],
                                                               -1.0, 1.5, ALU.mult,
                                                               op1=ALU.add), inc="vp")
                    E.fence("vector", "vp")
                    E.emit("vector", lambda v: v.tensor_mul(rA, rA, nstat[:, 0:8]), inc="vp")
                E.fence("vector", "vp")
                for n in range(NSEQ):
                    sl, lxs = load_x(n, b)
                    E.wait("vector", lxs)
                    E.wait("vector", "sp", lnslot_free[sl])
                    E.emit("vector", lambda v, n=n, sl=sl: v.tensor_scalar(
                        ln1[:, sl, :], xt[:, sl, :], stats[:, n, FM:FM + 1],
                        stats[:, n, FR:FR + 1], ALU.subtract, op1=ALU.mult), inc="vp")
                    xslot_free[sl] = (E.val("vp"), E.val("sp"))
                    E.wait("tensor", "vp")
                    for k in range(DCH):
                        pt = ptrs[k % 4]
                        E.wait("tensor", "sp", ptr_free[k % 4])
                        E.emit("tensor", lambda t, k=k, pt=pt, sl=sl: t.transpose(
                            pt[:], ln1[:, sl, k * 128:(k + 1) * 128], ident[:]), inc="tp")
                        E.wait("scalar", "tp")
                        E.emit("scalar", lambda s, n=n, k=k, pt=pt: s.copy(
                            xfm[:, n, k, :], pt[:]), inc="sp")
                        ptr_free[k % 4] = E.val("sp")
                    lnslot_free[sl] = E.val("sp")
                for hc in range(2):
                    for cg in range(3):
                        ws = wseq[0] % 2
                        lws = "lwa" if ws == 0 else "lwb"
                        E.wait("sync", "tp", wslot_free[ws])
                        for k in range(DCH):
                            E.emit("sync", lambda s, k=k, cg=cg, hc=hc, ws=ws: s.dma_start(
                                out=wbuf[:, ws, k, :],
                                in_=wqkvT[k * 128:(k + 1) * 128,
                                          cg * D + hc * 512: cg * D + hc * 512 + 512]),
                                inc=lws, n=16)
                        wseq[0] += 1
                        E.wait("tensor", lws)
                        if cg == 0:
                            E.wait("tensor", "sp")
                        for n in range(NSEQ):
                            pm = pmm0 if n % 2 == 0 else pmm1
                            E.wait("tensor", "vp", pmm_free[n % 2])
                            E.emit("tensor", lambda t, cg=cg, hc=hc, pm=pm: t.matmul(
                                pm[:], ones1[:],
                                bqkv_t[0:1, cg * D + hc * 512: cg * D + hc * 512 + 512],
                                start=True, stop=False))
                            for k in range(DCH):
                                E.emit("tensor", lambda t, n=n, k=k, ws=ws, pm=pm: t.matmul(
                                    pm[:], xfm[:, n, k, :], wbuf[:, ws, k, :],
                                    start=False, stop=(k == DCH - 1)),
                                    inc=("tp" if k == DCH - 1 else None))
                            E.wait("vector", "tp")
                            E.emit("vector", lambda v, n=n, cg=cg, pm=pm: v.tensor_copy(
                                ybuf[:, n, cg, :], pm[:]), inc="vp")
                            pmm_free[n % 2] = E.val("vp")
                        wslot_free[ws] = E.val("tp")
                    E.wait("vector", "tp")
                    E.fence("vector", "vp")
                    E.emit("vector", lambda v: v.memset(smat[:, :, 0:4, 4:8], NEG_BIG), inc="vp")
                    for hl in range(8):
                        for (n, m) in allowed:
                            E.emit("vector", lambda v, hl=hl, n=n, m=m: v.scalar_tensor_tensor(
                                junk[:, 0:HD],
                                ybuf[:, n, 0, hl * HD:(hl + 1) * HD], SCALE,
                                ybuf[:, m, 1, hl * HD:(hl + 1) * HD],
                                ALU.mult, ALU.mult,
                                accum_out=smat[:, hl, n, m:m + 1]), inc="vp")
                    E.fence("vector", "vp")
                    E.emit("vector", lambda v: v.reduce_max(
                        mx[:].unsqueeze(2),
                        smat[:].rearrange("p h n m -> p (h n) m"), axis=AX), inc="vp")
                    E.fence("vector", "vp")
                    E.emit("vector", lambda v: v.tensor_tensor(
                        emat[:].rearrange("p h n m -> p (h n) m"),
                        smat[:].rearrange("p h n m -> p (h n) m"),
                        mx[:].unsqueeze(2).to_broadcast([128, 64, 8]), ALU.subtract), inc="vp")
                    E.wait("scalar", "vp")
                    E.emit("scalar", lambda s: s.activation(emat[:], emat[:], AF.Exp), inc="sp")
                    E.wait("vector", "sp")
                    E.emit("vector", lambda v: v.reduce_sum(
                        sm[:].unsqueeze(2),
                        emat[:].rearrange("p h n m -> p (h n) m"), axis=AX), inc="vp")
                    E.fence("vector", "vp")
                    E.emit("vector", lambda v: v.reciprocal(sm[:], sm[:]), inc="vp")
                    E.fence("vector", "vp")
                    E.emit("vector", lambda v: v.tensor_tensor(
                        emat[:].rearrange("p h n m -> p (h n) m"),
                        emat[:].rearrange("p h n m -> p (h n) m"),
                        sm[:].unsqueeze(2).to_broadcast([128, 64, 8]), ALU.mult), inc="vp")
                    E.fence("vector", "vp")
                    for hl in range(8):
                        for n in range(NSEQ):
                            nm = 4 if n < 4 else 8
                            for m in range(nm):
                                E.emit("vector", lambda v, hl=hl, n=n, m=m: v.tensor_scalar(
                                    prods[:, m, :],
                                    ybuf[:, m, 2, hl * HD:(hl + 1) * HD],
                                    emat[:, hl, n, m:m + 1], None, ALU.mult), inc="vp")
                            E.emit("vector", lambda v, hl=hl, n=n, hc=hc, nm=nm: v.reduce_sum(
                                aout[:, n, hc * 512 + hl * HD: hc * 512 + (hl + 1) * HD]
                                .unsqueeze(2),
                                prods[:, 0:nm, :].transpose([0, 2, 1]), axis=AX), inc="vp")
                for cg in range(2):
                    ws = wseq[0] % 2
                    lws = "lwa" if ws == 0 else "lwb"
                    E.wait("sync", "tp", wslot_free[ws])
                    for k in range(DCH):
                        E.emit("sync", lambda s, k=k, cg=cg, ws=ws: s.dma_start(
                            out=wbuf[:, ws, k, :],
                            in_=wprojT[k * 128:(k + 1) * 128, cg * 512:(cg + 1) * 512]),
                            inc=lws, n=16)
                    pslot[cg] = ws
                    pwsem[cg] = lws
                    wseq[0] += 1
                attn_done_vp = E.val("vp")
                for n in range(NSEQ):
                    E.wait("tensor", "vp", attn_done_vp)
                    for k in range(DCH):
                        pt = ptrs[k % 4]
                        E.wait("tensor", "sp", ptr_free[k % 4])
                        E.emit("tensor", lambda t, n=n, k=k, pt=pt: t.transpose(
                            pt[:], aout[:, n, k * 128:(k + 1) * 128], ident[:]), inc="tp")
                        E.wait("scalar", "tp")
                        E.emit("scalar", lambda s, n=n, k=k, pt=pt: s.copy(
                            xfm[:, n, k, :], pt[:]), inc="sp")
                        ptr_free[k % 4] = E.val("sp")
                    E.wait("tensor", pwsem[0])
                    E.wait("tensor", pwsem[1])
                    E.wait("tensor", "sp")
                    E.wait("tensor", "vp", max(pmm_free[0], pmm_free[1]))
                    for cg in range(2):
                        pm = pmm0 if cg == 0 else pmm1
                        E.emit("tensor", lambda t, cg=cg, pm=pm: t.matmul(
                            pm[:], ones1[:], bproj_t[0:1, cg * 512:(cg + 1) * 512],
                            start=True, stop=False))
                        for k in range(DCH):
                            E.emit("tensor", lambda t, n=n, k=k, cg=cg, pm=pm: t.matmul(
                                pm[:], xfm[:, n, k, :], wbuf[:, pslot[cg], k, :],
                                start=False, stop=(k == DCH - 1)),
                                inc=("tp" if k == DCH - 1 else None))
                    E.wait("vector", "tp")
                    E.emit("vector", lambda v: v.tensor_copy(junk[:, 0:512], pmm0[:]), inc="vp")
                    E.emit("vector", lambda v: v.tensor_copy(junk[:, 512:1024], pmm1[:]), inc="vp")
                    pmm_free[0] = E.val("vp")
                    pmm_free[1] = E.val("vp")
                    E.emit("vector", lambda v: v.reduce_sum(nstat[:, 0:1], junk[:], axis=AX), inc="vp")
                    E.emit("vector", lambda v: v.scalar_tensor_tensor(
                        junkS[:], junk[:], 1.0, junk[:], ALU.mult, ALU.mult,
                        accum_out=nstat[:, 4:5]), inc="vp")
                    E.fence("vector", "vp")
                    E.emit("vector", lambda v: v.tensor_scalar(nstat[:, 8:9], nstat[:, 0:1],
                                                               1.0 / D, None, ALU.mult), inc="vp")
                    E.emit("vector", lambda v: v.tensor_scalar(nstat[:, 9:10], nstat[:, 4:5],
                                                               1.0 / D, None, ALU.mult), inc="vp")
                    E.fence("vector", "vp")
                    E.emit("vector", lambda v: v.tensor_mul(nstat[:, 10:11], nstat[:, 8:9],
                                                            nstat[:, 8:9]), inc="vp")
                    E.fence("vector", "vp")
                    E.emit("vector", lambda v: v.tensor_sub(nstat[:, 11:12], nstat[:, 9:10],
                                                            nstat[:, 10:11]), inc="vp")
                    E.fence("vector", "vp")
                    E.emit("vector", lambda v: v.tensor_scalar(nstat[:, 11:12], nstat[:, 11:12],
                                                               1.0, EPS, ALU.mult,
                                                               op1=ALU.add), inc="vp")
                    rsqrt_chain(11, 12, 13, 14, 1)
                    sl, lxs = load_x(n, b)
                    xp = len(x2_gp) % 2
                    gps = "gpa" if xp == 0 else "gpb"
                    if len(x2_gp) >= 2:
                        E.wait("vector", gps, x2_gp[-2])
                    E.fence("vector", "vp")
                    E.emit("vector", lambda v, xp=xp: v.tensor_scalar(
                        x2t[:, xp, :], junk[:], nstat[:, 8:9], nstat[:, 12:13],
                        ALU.subtract, op1=ALU.mult), inc="vp")
                    E.wait("vector", lxs)
                    E.emit("vector", lambda v, sl=sl, xp=xp: v.tensor_tensor(
                        x2t[:, xp, :], x2t[:, xp, :], xt[:, sl, :], ALU.add), inc="vp")
                    xslot_free[sl] = (E.val("vp"), E.val("sp"))
                    rt = 4 * n + b
                    E.wait("gpsimd", "vp")
                    E.emit("gpsimd", lambda g, rt=rt, xp=xp: g.dma_start(
                        out=x2d[rt], in_=x2t[:, xp, :]), inc=gps, n=16)
                    x2_gp.append(E.val(gps))
                wslot_free[pslot[0]] = E.val("tp")
                wslot_free[pslot[1]] = E.val("tp")

            def barrier():
                for eng in ENGINES:
                    for s in ("tp", "vp", "sp", "lxa", "lxb", "lwa", "lwb",
                              "gpa", "gpb", "so", "cc"):
                        E.wait(eng, s)

            def greduce(op, incol, outcol, rowslot):
                E.wait("tensor", "vp")
                E.emit("tensor", lambda t: t.transpose(ptr0[0:1, :], incol, ident[:]), inc="tp")
                E.wait("vector", "tp")
                E.emit("vector", lambda v: v.tensor_copy(junk[0:1, 0:128], ptr0[0:1, 0:128]), inc="vp")
                E.fence("vector", "vp")
                E.emit("vector", lambda v: v.tensor_reduce(
                    ccrow[0:1, rowslot:rowslot + 1], junk[0:1, 0:128], AX, op), inc="vp")
                E.wait("gpsimd", "vp")
                E.emit("gpsimd", lambda g: g.dma_start(
                    out=cc_in[0:1, 0:1], in_=ccrow[0:1, rowslot:rowslot + 1]), inc="gpa", n=16)
                E.wait("gpsimd", "gpa")
                E.emit("gpsimd", lambda g: g.collective_compute(
                    "AllReduce", op, ins=[cc_in[:]], outs=[cc_out[:]],
                    replica_groups=[[0, 1, 2, 3, 4, 5, 6, 7]]), inc="cc")
                E.wait("gpsimd", "cc")
                E.emit("gpsimd", lambda g: g.dma_start(
                    out=ccrow[0:1, rowslot + 1:rowslot + 2], in_=cc_out[0:1, 0:1]), inc="gpb", n=16)
                E.wait("tensor", "gpb")
                E.wait("tensor", "vp")
                E.emit("tensor", lambda t: t.matmul(
                    ptr1[:, 0:1], ones1[:], ccrow[0:1, rowslot + 1:rowslot + 2],
                    start=True, stop=True), inc="tp")
                E.wait("vector", "tp")
                E.emit("vector", lambda v: v.tensor_copy(outcol, ptr1[:, 0:1]), inc="vp")
                E.fence("vector", "vp")

            def lreduce(op, incol, outcol, rowslot):
                E.wait("tensor", "vp")
                E.emit("tensor", lambda t: t.transpose(ptr0[0:1, :], incol, ident[:]), inc="tp")
                E.wait("vector", "tp")
                E.emit("vector", lambda v: v.tensor_copy(junk[0:1, 0:128], ptr0[0:1, 0:128]), inc="vp")
                E.fence("vector", "vp")
                E.emit("vector", lambda v: v.tensor_reduce(
                    ccrow[0:1, rowslot:rowslot + 1], junk[0:1, 0:128], AX, op), inc="vp")
                E.wait("tensor", "vp")
                E.emit("tensor", lambda t: t.matmul(
                    ptr1[:, 0:1], ones1[:], ccrow[0:1, rowslot:rowslot + 1],
                    start=True, stop=True), inc="tp")
                E.wait("vector", "tp")
                E.emit("vector", lambda v: v.tensor_copy(outcol, ptr1[:, 0:1]), inc="vp")
                E.fence("vector", "vp")

            def bcol(i):
                return bst[:, i:i + 1]

            def bis_select():
                E.fence("vector", "vp")
                E.emit("vector", lambda v: v.tensor_tensor(bcol(5), bcol(4), bcol(6),
                                                           ALU.is_ge), inc="vp")
                E.fence("vector", "vp")
                E.emit("vector", lambda v: v.tensor_sub(bcol(20), bcol(2), bcol(0)), inc="vp")
                E.emit("vector", lambda v: v.tensor_sub(bcol(21), bcol(2), bcol(1)), inc="vp")
                E.fence("vector", "vp")
                E.emit("vector", lambda v: v.tensor_mul(bcol(20), bcol(20), bcol(5)), inc="vp")
                E.fence("vector", "vp")
                E.emit("vector", lambda v: v.tensor_add(bcol(0), bcol(0), bcol(20)), inc="vp")
                E.emit("vector", lambda v: v.tensor_sub(bcol(21), bcol(1), bcol(2)), inc="vp")
                E.fence("vector", "vp")
                E.emit("vector", lambda v: v.tensor_mul(bcol(21), bcol(21), bcol(5)), inc="vp")
                E.fence("vector", "vp")
                E.emit("vector", lambda v: v.tensor_add(bcol(1), bcol(2), bcol(21)), inc="vp")
                E.fence("vector", "vp")

            if "B" in phases:
                barrier()
                E.emit("vector", lambda v: v.memset(bst[:], 0.0), inc="vp")
                E.emit("vector", lambda v: v.memset(cnts[:], 0.0), inc="vp")
                E.emit("vector", lambda v: v.memset(ccrow[:], 0.0), inc="vp")
                E.fence("vector", "vp")
                FS, FS2, FM, FT, FR = 0, 1, 2, 3, 4
                FU, FU2, FM3 = 5, 6, 7
                gbc = overlay("gbc", [128, D], F32, 131072)
                bbc = overlay("bbc", [128, D], F32, 135168)
                jI8 = overlay("jI8", [128, 8192], mybir.dt.int8, 139264)
                for half in range(2):
                    E.emit("sync", lambda s, half=half: s.dma_start(
                        out=ln1[0:1, 0, 0:512],
                        in_=ln_g[0:1, half * 512:(half + 1) * 512]), inc="lb", n=16)
                    E.emit("sync", lambda s, half=half: s.dma_start(
                        out=ln1[0:1, 1, 0:512],
                        in_=ln_b[0:1, half * 512:(half + 1) * 512]), inc="lb", n=16)
                    E.wait("tensor", "lb")
                    E.wait("tensor", "vp")
                    E.emit("tensor", lambda t: t.matmul(
                        pmm0[:], ones1[:], ln1[0:1, 0, 0:512],
                        start=True, stop=True), inc="tp")
                    E.emit("tensor", lambda t: t.matmul(
                        pmm1[:], ones1[:], ln1[0:1, 1, 0:512],
                        start=True, stop=True), inc="tp")
                    E.wait("vector", "tp")
                    E.emit("vector", lambda v, half=half: v.tensor_copy(
                        gbc[:, half * 512:(half + 1) * 512], pmm0[:]), inc="vp")
                    E.emit("vector", lambda v, half=half: v.tensor_copy(
                        bbc[:, half * 512:(half + 1) * 512], pmm1[:]), inc="vp")
                E.fence("vector", "vp")
                NW1CH = (D // 128) * (HID // 512)
                for ch in range(NW1CH):
                    kd, hcol = ch // 8, ch % 8
                    ws = ch % 2
                    lws = "lwa" if ws == 0 else "lwb"
                    E.wait("sync", "sp", wslot_free.get(("g1", ws), 0))
                    E.emit("sync", lambda s, kd=kd, hcol=hcol, ws=ws: s.dma_start(
                        out=wstage[:, ws, :],
                        in_=w1T[kd * 128:(kd + 1) * 128, hcol * 512:(hcol + 1) * 512]),
                        inc=lws, n=16)
                    E.wait("scalar", lws)
                    E.emit("scalar", lambda s, ch=ch, ws=ws: s.activation(
                        junkS[:, 0:512], wstage[:, ws, :], AF.Abs,
                        accum_out=cnts[:, ch % 64:ch % 64 + 1]), inc="sp")
                    wslot_free[("g1", ws)] = E.val("sp")
                E.wait("vector", "sp")
                E.emit("vector", lambda v: v.reduce_sum(bcol(6), cnts[:, 0:64], axis=AX), inc="vp")
                lreduce(ALU.add, bcol(6), bcol(19), 0)
                E.emit("vector", lambda v: v.tensor_scalar(
                    bcol(19), bcol(19), 1.0 / (D * HID), EPS, ALU.mult, op1=ALU.add), inc="vp")
                ln2slot = {0: (0, 0), 1: (0, 0)}
                for g8 in range(NT // 8):
                    for j in range(8):
                        rt = g8 * 8 + j
                        sl = rt % 2
                        lxs = "lxa" if sl == 0 else "lxb"
                        vv, ss = xslot_free[sl]
                        E.wait("sync", "vp", vv)
                        E.wait("sync", "sp", ss)
                        E.emit("sync", lambda s, rt=rt, sl=sl: s.dma_start(
                            out=xt[:, sl, :], in_=x2d[rt]), inc=lxs, n=16)
                        E.wait("vector", lxs)
                        E.wait("scalar", lxs)
                        E.emit("scalar", lambda s, j=j, sl=sl: s.activation(
                            junkS[:], xt[:, sl, :], AF.Copy,
                            accum_out=stats[:, j, FS:FS + 1]), inc="sp")
                        E.emit("vector", lambda v, j=j, sl=sl: v.scalar_tensor_tensor(
                            junk[:], xt[:, sl, :], 1.0, xt[:, sl, :], ALU.mult, ALU.mult,
                            accum_out=stats[:, j, FS2:FS2 + 1]), inc="vp")
                        xslot_free[sl] = (E.val("vp"), E.val("sp"))
                    stats_chain(FS, FS2, FM, FT, FR)
                    for j in range(8):
                        rt = g8 * 8 + j
                        sl = rt % 2
                        lxs = "lxa" if sl == 0 else "lxb"
                        vv, ss = xslot_free[sl]
                        E.wait("sync", "vp", vv)
                        E.wait("sync", "sp", ss)
                        E.emit("sync", lambda s, rt=rt, sl=sl: s.dma_start(
                            out=xt[:, sl, :], in_=x2d[rt]), inc=lxs, n=16)
                        E.wait("vector", lxs)
                        lv, ls = ln2slot[sl]
                        E.wait("vector", "vp", lv)
                        E.wait("vector", "sp", ls)
                        E.emit("vector", lambda v, j=j, sl=sl: v.tensor_scalar(
                            ln1[:, sl, :], xt[:, sl, :], stats[:, j, FM:FM + 1],
                            stats[:, j, FR:FR + 1], ALU.subtract, op1=ALU.mult), inc="vp")
                        E.wait("scalar", "vp")
                        E.emit("scalar", lambda s, j=j, sl=sl: s.activation(
                            junkS[:], ln1[:, sl, :], AF.Copy,
                            accum_out=stats[:, j, FU:FU + 1]), inc="sp")
                        E.emit("vector", lambda v, j=j, sl=sl: v.scalar_tensor_tensor(
                            junk[:], ln1[:, sl, :], 1.0, ln1[:, sl, :], ALU.mult, ALU.mult,
                            accum_out=stats[:, j, FU2:FU2 + 1]), inc="vp")
                        xslot_free[sl] = (E.val("vp"), E.val("sp"))
                        ln2slot[sl] = (E.val("vp"), E.val("sp"))
                    stats_chain(FU, FU2, FM3, FT, FU2)
                    for j in range(8):
                        rt = g8 * 8 + j
                        sl = rt % 2
                        lxs = "lxa" if sl == 0 else "lxb"
                        vv, ss = xslot_free[sl]
                        E.wait("sync", "vp", vv)
                        E.wait("sync", "sp", ss)
                        E.emit("sync", lambda s, rt=rt, sl=sl: s.dma_start(
                            out=xt[:, sl, :], in_=x2d[rt]), inc=lxs, n=16)
                        E.wait("vector", lxs)
                        lv, ls = ln2slot[sl]
                        E.wait("vector", "sp", ls)
                        E.emit("vector", lambda v, j=j, sl=sl: v.tensor_scalar(
                            ln1[:, sl, :], xt[:, sl, :], stats[:, j, FM:FM + 1],
                            stats[:, j, FR:FR + 1], ALU.subtract, op1=ALU.mult), inc="vp")
                        E.emit("vector", lambda v, j=j, sl=sl: v.tensor_scalar(
                            ln1[:, sl, :], ln1[:, sl, :], stats[:, j, FM3:FM3 + 1],
                            stats[:, j, FU2:FU2 + 1], ALU.subtract, op1=ALU.mult), inc="vp")
                        E.emit("vector", lambda v, sl=sl: v.tensor_mul(
                            ln1[:, sl, :], ln1[:, sl, :], gbc[:]), inc="vp")
                        E.emit("vector", lambda v, rt=rt, sl=sl: v.tensor_add(
                            h1[:, rt, :], ln1[:, sl, :], bbc[:]), inc="vp")
                        xslot_free[sl] = (E.val("vp"), E.val("sp"))
                        E.emit("vector", lambda v, rt=rt: v.reduce_max(
                            bcol(6), h1[:, rt, :], axis=AX,
                            apply_absolute_value=True), inc="vp")
                        E.fence("vector", "vp")
                        if rt == 0:
                            E.emit("vector", lambda v: v.tensor_copy(bcol(7), bcol(6)), inc="vp")
                        else:
                            E.emit("vector", lambda v: v.tensor_max(bcol(7), bcol(7), bcol(6)), inc="vp")
                greduce(ALU.max, bcol(7), bcol(17), 0)
                E.emit("vector", lambda v: v.memset(bcol(10), float(NSEQ * 4096 * D)), inc="vp")
                E.fence("vector", "vp")
                E.emit("vector", lambda v: v.tensor_scalar(bcol(10), bcol(10), -1.0,
                                                           None, ALU.add), inc="vp")
                E.fence("vector", "vp")
                E.emit("vector", lambda v: v.tensor_scalar(bcol(10), bcol(10),
                                                           float(np.float32(0.995)),
                                                           None, ALU.mult), inc="vp")
                E.fence("vector", "vp")
                E.emit("vector", lambda v: v.tensor_copy(x2t[0:128, 0, 0:1].bitcast(I32), bcol(10)), inc="vp")
                E.fence("vector", "vp")
                E.emit("vector", lambda v: v.tensor_copy(bcol(6), x2t[0:128, 0, 0:1].bitcast(I32)), inc="vp")
                E.fence("vector", "vp")
                E.emit("vector", lambda v: v.tensor_tensor(bcol(20), bcol(6), bcol(10), ALU.is_gt), inc="vp")
                E.emit("vector", lambda v: v.tensor_tensor(bcol(21), bcol(6), bcol(10), ALU.is_lt), inc="vp")
                E.fence("vector", "vp")
                E.emit("vector", lambda v: v.tensor_sub(bcol(20), bcol(6), bcol(20)), inc="vp")
                E.emit("vector", lambda v: v.tensor_add(bcol(21), bcol(6), bcol(21)), inc="vp")
                E.fence("vector", "vp")
                E.emit("vector", lambda v: v.tensor_sub(bcol(11), bcol(10), bcol(20)), inc="vp")
                E.fence("vector", "vp")
                E.emit("vector", lambda v: v.tensor_scalar(bcol(12), bcol(11), -1.0, 1.0,
                                                           ALU.mult, op1=ALU.add), inc="vp")
                E.emit("vector", lambda v: v.memset(bcol(6), float(NSEQ * 4096 * D)), inc="vp")
                E.fence("vector", "vp")
                E.emit("vector", lambda v: v.tensor_scalar(bcol(6), bcol(6), -1.0,
                                                           None, ALU.add), inc="vp")
                E.fence("vector", "vp")
                E.emit("vector", lambda v: v.tensor_sub(bcol(8), bcol(6), bcol(21)), inc="vp")
                E.emit("vector", lambda v: v.tensor_sub(bcol(9), bcol(6), bcol(20)), inc="vp")
                E.fence("vector", "vp")
                E.emit("vector", lambda v: v.tensor_scalar(bcol(8), bcol(8), 1.0, 1.0,
                                                           ALU.mult, op1=ALU.add), inc="vp")
                E.emit("vector", lambda v: v.tensor_scalar(bcol(9), bcol(9), 1.0, 1.0,
                                                           ALU.mult, op1=ALU.add), inc="vp")
                E.emit("vector", lambda v: v.memset(bcol(0), -1.0), inc="vp")
                E.fence("vector", "vp")
                E.emit("vector", lambda v: v.tensor_copy(bcol(1), bcol(17)), inc="vp")
                h1flat = h1[:].rearrange("p a b -> p (a b)")
                def count_pass_h1(taucol, cmp1, cmp2, dstcol):
                    E.fence("vector", "vp")
                    E.emit("vector", lambda v: v.tensor_scalar(bcol(22), taucol, -1.0,
                                                               None, ALU.mult), inc="vp")
                    E.fence("vector", "vp")
                    for q in range(4):
                        E.emit("vector", lambda v, q=q: v.tensor_scalar(
                            jI8[:], h1flat[:, q * 8192:(q + 1) * 8192], taucol, None,
                            cmp1, op1=ALU.add, accum_out=cnts[:, q:q + 1]), inc="vp")
                        E.emit("vector", lambda v, q=q: v.tensor_scalar(
                            jI8[:], h1flat[:, q * 8192:(q + 1) * 8192], bcol(22), None,
                            cmp2, op1=ALU.add, accum_out=cnts[:, 4 + q:5 + q]), inc="vp")
                    E.fence("vector", "vp")
                    E.emit("vector", lambda v: v.reduce_sum(bcol(25), cnts[:, 0:8], axis=AX), inc="vp")
                    greduce(ALU.add, bcol(25), dstcol, 0)
                R1 = 30
                for rnd in range(R1):
                    E.fence("vector", "vp")
                    E.emit("vector", lambda v: v.tensor_add(bcol(2), bcol(0), bcol(1)), inc="vp")
                    E.fence("vector", "vp")
                    E.emit("vector", lambda v: v.tensor_scalar(bcol(2), bcol(2), 0.5,
                                                               None, ALU.mult), inc="vp")
                    count_pass_h1(bcol(2), ALU.is_gt, ALU.is_lt, bcol(4))
                    E.fence("vector", "vp")
                    E.emit("vector", lambda v: v.tensor_copy(bcol(6), bcol(8)), inc="vp")
                    bis_select()
                E.emit("vector", lambda v: v.tensor_copy(bcol(13), bcol(1)), inc="vp")
                count_pass_h1(bcol(13), ALU.is_ge, ALU.is_le, bcol(4))
                for rt in range(NT):
                    E.wait("scalar", "vp")
                    E.emit("scalar", lambda s, rt=rt: s.activation(
                        junkS[:], h1flat[:, rt * D:(rt + 1) * D], AF.Abs), inc="sp")
                    E.wait("vector", "sp")
                    E.emit("vector", lambda v: v.scalar_tensor_tensor(
                        junk[:], junkS[:], bcol(13), junkS[:], ALU.is_lt, ALU.mult), inc="vp")
                    E.fence("vector", "vp")
                    E.emit("vector", lambda v: v.reduce_max(bcol(6), junk[:, 0:D], axis=AX), inc="vp")
                    E.fence("vector", "vp")
                    if rt == 0:
                        E.emit("vector", lambda v: v.tensor_copy(bcol(7), bcol(6)), inc="vp")
                    else:
                        E.emit("vector", lambda v: v.tensor_max(bcol(7), bcol(7), bcol(6)), inc="vp")
                greduce(ALU.max, bcol(7), bcol(14), 0)
                E.fence("vector", "vp")
                E.emit("vector", lambda v: v.tensor_tensor(bcol(5), bcol(4), bcol(9), ALU.is_ge), inc="vp")
                E.fence("vector", "vp")
                E.emit("vector", lambda v: v.tensor_sub(bcol(20), bcol(13), bcol(14)), inc="vp")
                E.fence("vector", "vp")
                E.emit("vector", lambda v: v.tensor_mul(bcol(20), bcol(20), bcol(5)), inc="vp")
                E.fence("vector", "vp")
                E.emit("vector", lambda v: v.tensor_add(bcol(14), bcol(14), bcol(20)), inc="vp")
                E.fence("vector", "vp")
                E.emit("vector", lambda v: v.tensor_mul(bcol(20), bcol(14), bcol(12)), inc="vp")
                E.emit("vector", lambda v: v.tensor_mul(bcol(21), bcol(13), bcol(11)), inc="vp")
                E.fence("vector", "vp")
                E.emit("vector", lambda v: v.tensor_add(bcol(15), bcol(20), bcol(21)), inc="vp")
                E.fence("vector", "vp")
                E.emit("vector", lambda v: v.tensor_scalar(bcol(15), bcol(15), EPS,
                                                           None, ALU.max), inc="vp")
                E.fence("vector", "vp")
                E.emit("vector", lambda v: v.reciprocal(bcol(16), bcol(15)), inc="vp")
                E.fence("vector", "vp")
                E.emit("vector", lambda v: v.tensor_scalar(bcol(16), bcol(16), 127.0,
                                                           None, ALU.mult), inc="vp")
                E.fence("vector", "vp")
                if st_dbg is not None:
                    E.wait("gpsimd", "vp")
                    E.emit("gpsimd", lambda g: g.dma_start(out=st_dbg, in_=bst[:]), inc="gpa", n=16)
                if h1_dbg is not None:
                    for rt in range(NT):
                        E.wait("gpsimd", "vp")
                        E.emit("gpsimd", lambda g, rt=rt: g.dma_start(
                            out=h1_dbg[rt], in_=h1[:, rt, :]), inc="gpa", n=16)

            def quantile_targets(nval, hwcol, lwcol, t1col, t2col):
                E.emit("vector", lambda v: v.memset(bcol(10), float(nval)), inc="vp")
                E.fence("vector", "vp")
                E.emit("vector", lambda v: v.tensor_scalar(bcol(10), bcol(10), -1.0,
                                                           None, ALU.add), inc="vp")
                E.fence("vector", "vp")
                E.emit("vector", lambda v: v.tensor_scalar(bcol(10), bcol(10),
                                                           float(np.float32(0.995)),
                                                           None, ALU.mult), inc="vp")
                E.fence("vector", "vp")
                E.emit("vector", lambda v: v.tensor_copy(x2t[0:128, 0, 0:1].bitcast(I32), bcol(10)), inc="vp")
                E.fence("vector", "vp")
                E.emit("vector", lambda v: v.tensor_copy(bcol(6), x2t[0:128, 0, 0:1].bitcast(I32)), inc="vp")
                E.fence("vector", "vp")
                E.emit("vector", lambda v: v.tensor_tensor(bcol(20), bcol(6), bcol(10), ALU.is_gt), inc="vp")
                E.emit("vector", lambda v: v.tensor_tensor(bcol(21), bcol(6), bcol(10), ALU.is_lt), inc="vp")
                E.fence("vector", "vp")
                E.emit("vector", lambda v: v.tensor_sub(bcol(20), bcol(6), bcol(20)), inc="vp")
                E.emit("vector", lambda v: v.tensor_add(bcol(21), bcol(6), bcol(21)), inc="vp")
                E.fence("vector", "vp")
                E.emit("vector", lambda v: v.tensor_sub(hwcol, bcol(10), bcol(20)), inc="vp")
                E.fence("vector", "vp")
                E.emit("vector", lambda v: v.tensor_scalar(lwcol, hwcol, -1.0, 1.0,
                                                           ALU.mult, op1=ALU.add), inc="vp")
                E.emit("vector", lambda v: v.memset(bcol(6), float(nval)), inc="vp")
                E.fence("vector", "vp")
                E.emit("vector", lambda v: v.tensor_scalar(bcol(6), bcol(6), -1.0,
                                                           None, ALU.add), inc="vp")
                E.fence("vector", "vp")
                E.emit("vector", lambda v: v.tensor_sub(t1col, bcol(6), bcol(21)), inc="vp")
                E.emit("vector", lambda v: v.tensor_sub(t2col, bcol(6), bcol(20)), inc="vp")
                E.fence("vector", "vp")
                E.emit("vector", lambda v: v.tensor_scalar(t1col, t1col, 1.0, 1.0,
                                                           ALU.mult, op1=ALU.add), inc="vp")
                E.emit("vector", lambda v: v.tensor_scalar(t2col, t2col, 1.0, 1.0,
                                                           ALU.mult, op1=ALU.add), inc="vp")
                E.fence("vector", "vp")

            if "C" in phases:
                barrier()
                E.emit("vector", lambda v: v.reciprocal(bcol(26), bcol(19)), inc="vp")
                E.fence("vector", "vp")
                for rt in range(NT):
                    sl = rt % 2
                    E.emit("vector", lambda v, rt=rt, sl=sl: v.tensor_scalar(
                        ln1[:, sl, :], h1[:, rt, :], bcol(16), None, ALU.mult), inc="vp")
                    E.emit("vector", lambda v, sl=sl: v.tensor_copy(
                        x2t[:, sl, :].bitcast(I32), ln1[:, sl, :]), inc="vp")
                    E.emit("vector", lambda v, rt=rt, sl=sl: v.tensor_scalar(
                        u1tok[:, rt, :], x2t[:, sl, :].bitcast(I32), -128.0, 127.0,
                        ALU.max, op1=ALU.min), inc="vp")
                wslot2 = {0: (0, 0), 1: (0, 0)}
                for ch in range(64):
                    kd, hcol = ch // 8, ch % 8
                    ws = ch % 2
                    lws = "lwa" if ws == 0 else "lwb"
                    vv, _ = wslot2[ws]
                    E.wait("sync", "vp", vv)
                    E.emit("sync", lambda s, kd=kd, hcol=hcol, ws=ws: s.dma_start(
                        out=wstage[:, ws, :],
                        in_=w1T[kd * 128:(kd + 1) * 128, hcol * 512:(hcol + 1) * 512]),
                        inc=lws, n=16)
                    E.wait("vector", lws)
                    E.emit("vector", lambda v, ws=ws: v.tensor_scalar(
                        junk[:, 0:512], wstage[:, ws, :], bcol(26), None, ALU.mult), inc="vp")
                    wslot2[ws] = (E.val("vp"), 0)
                    E.emit("vector", lambda v: v.tensor_copy(
                        x2t[:, 0, 0:512].bitcast(I32), junk[:, 0:512]), inc="vp")
                    E.emit("vector", lambda v, kd=kd, hcol=hcol: v.tensor_scalar(
                        t1[:, kd, hcol * 512:(hcol + 1) * 512],
                        x2t[:, 0, 0:512].bitcast(I32), -1.0, 1.0,
                        ALU.max, op1=ALU.min), inc="vp")
                zslot = {0: (0, 0), 1: (0, 0)}
                zseq = [0]
                E.emit("vector", lambda v: v.memset(bcol(27), -3.0e38), inc="vp")
                E.fence("vector", "vp")
                conv_done_vp = E.val("vp")
                for rt in range(NT):
                    E.wait("tensor", "vp", conv_done_vp)
                    for k in range(DCH):
                        pt = ptrs[k % 4]
                        E.wait("tensor", "sp", ptr_free[k % 4])
                        E.emit("tensor", lambda t, rt=rt, k=k, pt=pt: t.transpose(
                            pt[:].bitcast(BF16)[:, 0:128],
                            u1tok[:, rt, k * 128:(k + 1) * 128], identb[:]), inc="tp")
                        E.wait("scalar", "tp")
                        E.emit("scalar", lambda s, k=k, pt=pt: s.copy(
                            u1f[:, k, :], pt[:].bitcast(BF16)[:, 0:128]), inc="sp")
                        ptr_free[k % 4] = E.val("sp")
                    E.wait("tensor", "sp")
                    for hcol in range(DCH):
                        pm = pmm0 if hcol % 2 == 0 else pmm1
                        E.wait("tensor", "sp", pmmS_free[hcol % 2])
                        for kd in range(DCH):
                            E.emit("tensor", lambda t, kd=kd, hcol=hcol, pm=pm: t.matmul(
                                pm[:], u1f[:, kd, :], t1[:, kd, hcol * 512:(hcol + 1) * 512],
                                start=(kd == 0), stop=(kd == DCH - 1)),
                                inc=("tp" if kd == DCH - 1 else None))
                        zs = zseq[0] % 2
                        gps = "gpa" if zs == 0 else "gpb"
                        gv, vv = zslot[zs]
                        E.wait("scalar", "tp")
                        E.wait("scalar", gps, gv)
                        E.wait("scalar", "vp", vv)
                        E.emit("scalar", lambda s, zs=zs, pm=pm: s.copy(
                            zstage[:, zs, :], pm[:]), inc="sp")
                        pmmS_free[hcol % 2] = E.val("sp")
                        E.wait("vector", "sp")
                        E.emit("vector", lambda v, zs=zs: v.reduce_max(
                            bcol(6), zstage[:, zs, :], axis=AX), inc="vp")
                        E.fence("vector", "vp")
                        E.emit("vector", lambda v: v.tensor_max(bcol(27), bcol(27), bcol(6)), inc="vp")
                        E.wait("gpsimd", "sp")
                        E.emit("gpsimd", lambda g, rt=rt, hcol=hcol, zs=zs: g.dma_start(
                            out=zd[rt, :, hcol * 512:(hcol + 1) * 512],
                            in_=zstage[:, zs, :]), inc=gps, n=16)
                        zslot[zs] = (E.val(gps), E.val("vp"))
                        zseq[0] += 1
                greduce(ALU.max, bcol(27), bcol(28), 0)

            if "C" in phases:
                barrier()
                c8flat = c8[:].rearrange("p a b -> p (a b)")
                E.emit("vector", lambda v: v.reciprocal(bcol(29), bcol(28)), inc="vp")
                E.fence("vector", "vp")
                E.emit("vector", lambda v: v.tensor_scalar(bcol(29), bcol(29), 127.0,
                                                           None, ALU.mult), inc="vp")
                E.fence("vector", "vp")
                zqslot = {0: 0, 1: 0}
                for rt in range(NT):
                    for qq in range(4):
                        zs = (rt * 4 + qq) % 2
                        lws = "lwa" if zs == 0 else "lwb"
                        qeng = "sync" if zs == 0 else "gpsimd"
                        E.wait(qeng, "vp", zqslot[zs])
                        E.emit(qeng, lambda s, rt=rt, qq=qq, zs=zs: s.dma_start(
                            out=zrt2[:, zs, :],
                            in_=zd[rt, :, qq * D:(qq + 1) * D]), inc=lws, n=16)
                        E.wait("vector", lws)
                        E.emit("vector", lambda v, zs=zs: v.tensor_scalar(
                            x2t[:, 0, :], zrt2[:, zs, :],
                            bcol(29), -127.0, ALU.mult, op1=ALU.max), inc="vp")
                        zqslot[zs] = E.val("vp")
                        E.emit("vector", lambda v, rt=rt, qq=qq: v.tensor_scalar(
                            c8flat[:, rt * HID + qq * D: rt * HID + (qq + 1) * D],
                            x2t[:, 0, :], 127.0, None,
                            ALU.min), inc="vp")
                def count_c8(taucol, dstcol, cmp=ALU.is_gt):
                    E.fence("vector", "vp")
                    off = -0.5 if cmp in (ALU.is_gt,) else 0.5
                    E.emit("vector", lambda v: v.tensor_scalar(
                        nstat[:, 13:14], taucol, -1.0, off, ALU.mult, op1=ALU.add), inc="vp")
                    E.fence("vector", "vp")
                    for q in range(8):
                        E.emit("vector", lambda v, q=q: v.tensor_scalar(
                            jI8[:], c8flat[:, q * 8192:(q + 1) * 8192], taucol, None,
                            cmp, op1=ALU.add, accum_out=cnts[:, q:q + 1]), inc="vp")
                    E.wait("scalar", "vp")
                    for q in range(8, 16):
                        E.emit("scalar", lambda s, q=q: s.activation(
                            jI8[:], c8flat[:, q * 8192:(q + 1) * 8192], AF.Sign,
                            bias=nstat[:, 13:14], accum_out=cnts[:, q:q + 1]), inc="sp")
                    E.wait("vector", "sp")
                    E.fence("vector", "vp")
                    E.emit("vector", lambda v: v.reduce_sum(bcol(25), cnts[:, 0:8], axis=AX), inc="vp")
                    E.emit("vector", lambda v: v.reduce_sum(bcol(22), cnts[:, 8:16], axis=AX), inc="vp")
                    E.fence("vector", "vp")
                    E.emit("vector", lambda v: v.tensor_scalar(bcol(22), bcol(22), 0.5,
                                                               32768.0, ALU.mult, op1=ALU.add), inc="vp")
                    E.fence("vector", "vp")
                    E.emit("vector", lambda v: v.tensor_add(bcol(25), bcol(25), bcol(22)), inc="vp")
                    greduce(ALU.add, bcol(25), dstcol, 0)

                def int_bisect(tgtcol, rounds=9):
                    E.emit("vector", lambda v: v.memset(bcol(0), -128.0), inc="vp")
                    E.emit("vector", lambda v: v.memset(bcol(1), 127.0), inc="vp")
                    for rnd in range(rounds):
                        E.fence("vector", "vp")
                        E.emit("vector", lambda v: v.tensor_add(bcol(2), bcol(0), bcol(1)), inc="vp")
                        E.fence("vector", "vp")
                        E.emit("vector", lambda v: v.tensor_scalar(bcol(2), bcol(2), 0.5,
                                                                   None, ALU.mult), inc="vp")
                        E.fence("vector", "vp")
                        E.emit("vector", lambda v: v.tensor_copy(x2t[0:128, 0, 0:1].bitcast(I32), bcol(2)), inc="vp")
                        E.fence("vector", "vp")
                        E.emit("vector", lambda v: v.tensor_copy(bcol(2), x2t[0:128, 0, 0:1].bitcast(I32)), inc="vp")
                        count_c8(bcol(2), bcol(4))
                        E.fence("vector", "vp")
                        E.emit("vector", lambda v: v.tensor_copy(bcol(6), tgtcol), inc="vp")
                        bis_select()

                quantile_targets(NSEQ * 4096 * HID, bcol(11), bcol(12), bcol(8), bcol(9))
                if q2_dbg is not None:
                    E.wait("gpsimd", "vp")
                    E.emit("gpsimd", lambda g: g.dma_start(out=q2_dbg[0], in_=bst[:]), inc="gpa", n=16)
                if q2_dbg is not None:
                    E.wait("gpsimd", "vp")
                    E.emit("gpsimd", lambda g: g.dma_start(out=c8_dbg[0], in_=c8[:, 0, :]), inc="gpa", n=16)
                    E.emit("gpsimd", lambda g: g.dma_start(out=c8_dbg[1], in_=c8[:, 1, :]), inc="gpa", n=16)
                int_bisect(bcol(8), 9)
                if q2_dbg is not None:
                    E.wait("gpsimd", "vp")
                    E.emit("gpsimd", lambda g: g.dma_start(out=q2_dbg[1], in_=bst[:]), inc="gpa", n=16)
                E.fence("vector", "vp")
                E.emit("vector", lambda v: v.tensor_mul(bcol(20), bcol(1), bcol(28)), inc="vp")
                E.fence("vector", "vp")
                E.emit("vector", lambda v: v.tensor_scalar(bcol(20), bcol(20), 1.0 / 127.0,
                                                           None, ALU.mult), inc="vp")
                E.fence("vector", "vp")
                E.emit("vector", lambda v: v.tensor_copy(x2t[0:128, 0, 0:1].bitcast(I32), bcol(20)), inc="vp")
                E.fence("vector", "vp")
                E.emit("vector", lambda v: v.tensor_copy(nstat[:, 15:16], x2t[0:128, 0, 0:1].bitcast(I32)), inc="vp")
                E.fence("vector", "vp")
                if q2_dbg is not None:
                    E.wait("gpsimd", "vp")
                    E.emit("gpsimd", lambda g: g.dma_start(out=q2_dbg[2], in_=bst[:]), inc="gpa", n=16)
                for rt in range(NT):
                    for qq in range(4):
                        zs = (rt * 4 + qq) % 2
                        lws = "lwa" if zs == 0 else "lwb"
                        qeng = "sync" if zs == 0 else "gpsimd"
                        E.wait(qeng, "vp", zqslot[zs])
                        E.emit(qeng, lambda s, rt=rt, qq=qq, zs=zs: s.dma_start(
                            out=zrt2[:, zs, :],
                            in_=zd[rt, :, qq * D:(qq + 1) * D]), inc=lws, n=16)
                        E.wait("vector", lws)
                        E.emit("vector", lambda v, zs=zs: v.tensor_scalar(
                            x2t[:, 0, :], zrt2[:, zs, :],
                            nstat[:, 15:16], -127.0, ALU.subtract, op1=ALU.max), inc="vp")
                        zqslot[zs] = E.val("vp")
                        E.emit("vector", lambda v, rt=rt, qq=qq: v.tensor_scalar(
                            c8flat[:, rt * HID + qq * D: rt * HID + (qq + 1) * D],
                            x2t[:, 0, :], 127.0, None,
                            ALU.min), inc="vp")
                int_bisect(bcol(8), 9)
                if q2_dbg is not None:
                    E.wait("gpsimd", "vp")
                    E.emit("gpsimd", lambda g: g.dma_start(out=q2_dbg[3], in_=bst[:]), inc="gpa", n=16)
                E.fence("vector", "vp")
                E.emit("vector", lambda v: v.tensor_add(bcol(13), nstat[:, 15:16], bcol(1)), inc="vp")
                E.fence("vector", "vp")
                E.emit("vector", lambda v: v.tensor_scalar(bcol(22), bcol(1), -1.0, None,
                                                           ALU.add), inc="vp")
                count_c8(bcol(22), bcol(4))
                E.fence("vector", "vp")
                E.emit("vector", lambda v: v.tensor_scalar(bcol(23), bcol(1), 1.0, 200.0,
                                                           ALU.mult, op1=ALU.add), inc="vp")
                E.fence("vector", "vp")
                def mm_bufs(q):
                    if q % 2 == 0:
                        return junk[:], junkS[:]
                    return x2t[:, 0, :], x2t[:, 1, :]
                for half in range(2):
                    for qi in range(65):
                        q = half * 64 + qi
                        if qi < 64:
                            ja, jb = mm_bufs(q)
                            E.emit("vector", lambda v, q=q, ja=ja: v.tensor_scalar(
                                ja, c8flat[:, q * 1024:(q + 1) * 1024], 200.0, None,
                                ALU.add), inc="vp")
                            E.emit("vector", lambda v, ja=ja, jb=jb: v.scalar_tensor_tensor(
                                jb, ja, bcol(23), ja, ALU.is_lt, ALU.mult), inc="vp")
                        if qi > 0:
                            _, jbp = mm_bufs(q - 1)
                            E.emit("vector", lambda v, q=q, jbp=jbp: v.reduce_max(
                                cnts[:, (q - 1) % 64:(q - 1) % 64 + 1], jbp, axis=AX), inc="vp")
                    E.fence("vector", "vp")
                    if half == 0:
                        E.emit("vector", lambda v: v.reduce_max(bcol(7), cnts[:, 0:64], axis=AX), inc="vp")
                    else:
                        E.emit("vector", lambda v: v.reduce_max(bcol(6), cnts[:, 0:64], axis=AX), inc="vp")
                        E.fence("vector", "vp")
                        E.emit("vector", lambda v: v.tensor_max(bcol(7), bcol(7), bcol(6)), inc="vp")
                greduce(ALU.max, bcol(7), bcol(14), 0)
                E.fence("vector", "vp")
                E.emit("vector", lambda v: v.tensor_scalar(bcol(14), bcol(14), 1.0, -200.0,
                                                           ALU.mult, op1=ALU.add), inc="vp")
                E.fence("vector", "vp")
                E.emit("vector", lambda v: v.tensor_add(bcol(14), bcol(14), nstat[:, 15:16]), inc="vp")
                E.fence("vector", "vp")
                E.emit("vector", lambda v: v.tensor_tensor(bcol(5), bcol(4), bcol(9), ALU.is_ge), inc="vp")
                E.fence("vector", "vp")
                E.emit("vector", lambda v: v.tensor_sub(bcol(21), bcol(13), bcol(14)), inc="vp")
                E.fence("vector", "vp")
                E.emit("vector", lambda v: v.tensor_mul(bcol(21), bcol(21), bcol(5)), inc="vp")
                E.fence("vector", "vp")
                E.emit("vector", lambda v: v.tensor_add(bcol(14), bcol(14), bcol(21)), inc="vp")
                E.fence("vector", "vp")
                E.emit("vector", lambda v: v.tensor_mul(bcol(18), bcol(19), bcol(15)), inc="vp")
                E.fence("vector", "vp")
                E.emit("vector", lambda v: v.tensor_scalar(bcol(18), bcol(18), 1.0 / 127.0,
                                                           None, ALU.mult), inc="vp")
                E.fence("vector", "vp")
                for (zc_col, out_col) in ((13, 13), (14, 14)):
                    E.emit("vector", lambda v, zc_col=zc_col: v.tensor_mul(
                        bcol(22), bcol(zc_col), bcol(18)), inc="vp")
                    E.wait("scalar", "vp")
                    E.emit("scalar", lambda s, out_col=out_col: s.activation(
                        bcol(out_col), bcol(22), AF.Gelu), inc="sp")
                    E.wait("vector", "sp")
                E.emit("vector", lambda v: v.tensor_mul(bcol(20), bcol(14), bcol(12)), inc="vp")
                E.emit("vector", lambda v: v.tensor_mul(bcol(21), bcol(13), bcol(11)), inc="vp")
                E.fence("vector", "vp")
                E.emit("vector", lambda v: v.tensor_add(bcol(30), bcol(20), bcol(21)), inc="vp")
                E.fence("vector", "vp")
                E.emit("vector", lambda v: v.tensor_scalar(bcol(30), bcol(30), EPS,
                                                           None, ALU.max), inc="vp")
                E.fence("vector", "vp")
                E.emit("vector", lambda v: v.reciprocal(bcol(31), bcol(30)), inc="vp")
                E.fence("vector", "vp")
                E.emit("vector", lambda v: v.tensor_scalar(bcol(31), bcol(31), 127.0,
                                                           None, ALU.mult), inc="vp")
                E.fence("vector", "vp")
                if st_dbg is not None:
                    E.wait("gpsimd", "vp")
                    E.emit("gpsimd", lambda g: g.dma_start(out=st_dbg, in_=bst[:]), inc="gpa", n=16)

            if "D" in phases:
                barrier()
                for ch in range(64):
                    kh, dcol = ch // 2, ch % 2
                    ws = ch % 2
                    lws = "lwa" if ws == 0 else "lwb"
                    E.wait("sync", "sp", wslot_free.get(("g2", ws), 0))
                    E.emit("sync", lambda s, kh=kh, dcol=dcol, ws=ws: s.dma_start(
                        out=wstage[:, ws, :],
                        in_=w2T[kh * 128:(kh + 1) * 128, dcol * 512:(dcol + 1) * 512]),
                        inc=lws, n=16)
                    E.wait("scalar", lws)
                    E.emit("scalar", lambda s, ch=ch, ws=ws: s.activation(
                        junkS[:, 0:512], wstage[:, ws, :], AF.Abs,
                        accum_out=cnts[:, ch % 64:ch % 64 + 1]), inc="sp")
                    wslot_free[("g2", ws)] = E.val("sp")
                E.wait("vector", "sp")
                E.emit("vector", lambda v: v.reduce_sum(bcol(6), cnts[:, 0:64], axis=AX), inc="vp")
                lreduce(ALU.add, bcol(6), bcol(19), 0)
                E.emit("vector", lambda v: v.tensor_scalar(
                    bcol(19), bcol(19), 1.0 / (D * HID), EPS, ALU.mult, op1=ALU.add), inc="vp")
                E.fence("vector", "vp")
                E.emit("vector", lambda v: v.reciprocal(bcol(26), bcol(19)), inc="vp")
                E.fence("vector", "vp")
                E.emit("vector", lambda v: v.tensor_mul(bcol(25), bcol(19), bcol(30)), inc="vp")
                E.fence("vector", "vp")
                E.emit("vector", lambda v: v.tensor_scalar(bcol(25), bcol(25), 1.0 / 127.0,
                                                           None, ALU.mult), inc="vp")
                E.emit("vector", lambda v: v.tensor_scalar(bcol(24), bcol(31), 0.5,
                                                           None, ALU.mult), inc="vp")
                E.fence("vector", "vp")
                wslot3 = {0: (0, 0), 1: (0, 0)}
                for ch in range(64):
                    kh, dcol = ch // 2, ch % 2
                    ws = ch % 2
                    lws = "lwa" if ws == 0 else "lwb"
                    vv, _ = wslot3[ws]
                    E.wait("sync", "vp", vv)
                    E.emit("sync", lambda s, kh=kh, dcol=dcol, ws=ws: s.dma_start(
                        out=wstage[:, ws, :],
                        in_=w2T[kh * 128:(kh + 1) * 128, dcol * 512:(dcol + 1) * 512]),
                        inc=lws, n=16)
                    E.wait("vector", lws)
                    E.emit("vector", lambda v, ws=ws: v.tensor_scalar(
                        junk[:, 0:512], wstage[:, ws, :], bcol(26), None, ALU.mult), inc="vp")
                    wslot3[ws] = (E.val("vp"), 0)
                    E.emit("vector", lambda v: v.tensor_copy(
                        x2t[:, 0, 0:512].bitcast(I32), junk[:, 0:512]), inc="vp")
                    E.emit("vector", lambda v, kh=kh, dcol=dcol: v.tensor_scalar(
                        t2[:, kh, dcol * 512:(dcol + 1) * 512],
                        x2t[:, 0, 0:512].bitcast(I32), -1.0, 1.0,
                        ALU.max, op1=ALU.min), inc="vp")
                zdslot = {0: (0, 0), 1: (0, 0)}
                oslot = {0: 0, 1: 0}
                pv2, ps2 = xslot_free[0]
                E.wait("sync", "vp", pv2)
                E.wait("sync", "sp", ps2)
                E.emit("sync", lambda s: s.dma_start(
                    out=xt[:, 0, :], in_=x2d[0]), inc="lxa", n=16)
                for rt in range(NT):
                    n, b = rt // NB, rt % NB
                    zs = rt % 2
                    lws = "lwa" if zs == 0 else "lwb"
                    vv, _ = zdslot[zs]
                    E.wait("sync", "vp", vv)
                    E.emit("sync", lambda s, rt=rt, zs=zs: s.dma_start(
                        out=zrt[:, zs, :], in_=zd[rt]), inc=lws, n=16)
                    if rt + 1 < NT:
                        nsl = (rt + 1) % 2
                        nlxs = "lxa" if nsl == 0 else "lxb"
                        vv2, ss2 = xslot_free[nsl]
                        E.wait("sync", "vp", vv2)
                        E.wait("sync", "sp", ss2)
                        E.emit("sync", lambda s, rt=rt, nsl=nsl: s.dma_start(
                            out=xt[:, nsl, :], in_=x2d[rt + 1]), inc=nlxs, n=16)
                    E.wait("vector", lws)
                    for qq in range(4):
                        E.emit("vector", lambda v, zs=zs, qq=qq: v.tensor_scalar(
                            u2st[:, 0, :], zrt[:, zs, qq * D:(qq + 1) * D],
                            bcol(18), None, ALU.mult), inc="vp")
                        E.wait("scalar", "vp")
                        E.emit("scalar", lambda s: s.activation(
                            junkS[:], u2st[:, 0, :], AF.Gelu), inc="sp")
                        E.wait("vector", "sp")
                        E.emit("vector", lambda v: v.tensor_scalar(
                            u2st[:, 1, :], junkS[:], bcol(31), None, ALU.mult), inc="vp")
                        E.emit("vector", lambda v: v.tensor_copy(
                            ln1[:, 0, :].bitcast(I32), u2st[:, 1, :]), inc="vp")
                        E.emit("vector", lambda v, qq=qq: v.tensor_scalar(
                            u2tok[:, qq * D:(qq + 1) * D], ln1[:, 0, :].bitcast(I32),
                            -128.0, 127.0, ALU.max, op1=ALU.min), inc="vp")
                    zdslot[zs] = (E.val("vp"), 0)
                    E.wait("tensor", "vp")
                    for kh in range(32):
                        pt = ptrs[kh % 4]
                        E.wait("tensor", "sp", ptr_free[kh % 4])
                        E.emit("tensor", lambda t, kh=kh, pt=pt: t.transpose(
                            pt[:].bitcast(BF16)[:, 0:128],
                            u2tok[:, kh * 128:(kh + 1) * 128], identb[:]), inc="tp")
                        E.wait("scalar", "tp")
                        E.emit("scalar", lambda s, kh=kh, pt=pt: s.copy(
                            u2fm[:, kh, :], pt[:].bitcast(BF16)[:, 0:128]), inc="sp")
                        ptr_free[kh % 4] = E.val("sp")
                    E.wait("tensor", "sp")
                    E.wait("tensor", "vp", max(pmm_free[0], pmm_free[1]))
                    for dcol in range(2):
                        pm = pmm0 if dcol == 0 else pmm1
                        for kh in range(32):
                            E.emit("tensor", lambda t, kh=kh, dcol=dcol, pm=pm: t.matmul(
                                pm[:], u2fm[:, kh, :], t2[:, kh, dcol * 512:(dcol + 1) * 512],
                                start=(kh == 0), stop=(kh == 31)),
                                inc=("tp" if kh == 31 else None))
                    E.wait("vector", "tp")
                    E.emit("vector", lambda v: v.tensor_scalar(
                        o2b[:, 0:512], pmm0[:], bcol(25), None, ALU.mult), inc="vp")
                    E.emit("vector", lambda v: v.tensor_scalar(
                        o2b[:, 512:1024], pmm1[:], bcol(25), None, ALU.mult), inc="vp")
                    pmm_free[0] = E.val("vp")
                    pmm_free[1] = E.val("vp")
                    E.fence("vector", "vp")
                    E.emit("vector", lambda v: v.reduce_sum(nstat[:, 0:1], o2b[:], axis=AX), inc="vp")
                    E.emit("vector", lambda v: v.scalar_tensor_tensor(
                        junk[:], o2b[:], 1.0, o2b[:], ALU.mult, ALU.mult,
                        accum_out=nstat[:, 4:5]), inc="vp")
                    E.fence("vector", "vp")
                    E.emit("vector", lambda v: v.tensor_scalar(nstat[:, 8:9], nstat[:, 0:1],
                                                               1.0 / D, None, ALU.mult), inc="vp")
                    E.emit("vector", lambda v: v.tensor_scalar(nstat[:, 9:10], nstat[:, 4:5],
                                                               1.0 / D, None, ALU.mult), inc="vp")
                    E.fence("vector", "vp")
                    E.emit("vector", lambda v: v.tensor_mul(nstat[:, 10:11], nstat[:, 8:9],
                                                            nstat[:, 8:9]), inc="vp")
                    E.fence("vector", "vp")
                    E.emit("vector", lambda v: v.tensor_sub(nstat[:, 11:12], nstat[:, 9:10],
                                                            nstat[:, 10:11]), inc="vp")
                    E.fence("vector", "vp")
                    E.emit("vector", lambda v: v.tensor_scalar(nstat[:, 11:12], nstat[:, 11:12],
                                                               1.0, EPS, ALU.mult,
                                                               op1=ALU.add), inc="vp")
                    rsqrt_chain(11, 12, 13, 14, 1)
                    sl = rt % 2
                    lxs = "lxa" if sl == 0 else "lxb"
                    xp = rt % 2
                    E.wait("vector", "so", oslot[xp])
                    E.fence("vector", "vp")
                    E.emit("vector", lambda v, xp=xp: v.tensor_scalar(
                        x2t[:, xp, :], o2b[:], nstat[:, 8:9], nstat[:, 12:13],
                        ALU.subtract, op1=ALU.mult), inc="vp")
                    E.wait("vector", lxs)
                    E.emit("vector", lambda v, sl=sl, xp=xp: v.tensor_tensor(
                        x2t[:, xp, :], x2t[:, xp, :], xt[:, sl, :], ALU.add), inc="vp")
                    xslot_free[sl] = (E.val("vp"), E.val("sp"))
                    E.wait("sync", "vp")
                    E.emit("sync", lambda s, n=n, b=b, xp=xp: s.dma_start(
                        out=out[n, b * 128:(b + 1) * 128, :], in_=x2t[:, xp, :]), inc="so", n=16)
                    oslot[xp] = E.val("so")
                E.wait("sync", "so")
                E.wait("vector", "so")

            if "D" not in phases:
                barrier()
                for rt in range(NT):
                    n, b = rt // NB, rt % NB
                    E.emit("sync", lambda s, rt=rt: s.dma_start(
                        out=junk[:], in_=x2d[rt]), inc="lxa", n=16)
                    E.wait("sync", "lxa")
                    E.emit("sync", lambda s, n=n, b=b: s.dma_start(
                        out=out[n, b * 128:(b + 1) * 128, :], in_=junk[:]), inc="so", n=16)
                    E.wait("sync", "so")
            E.wait("sync", "so")
            E.wait("vector", "so")

        with nc.Block() as block:
            for eng in ENGINES:
                def make(eng):
                    def f(_q):
                        program(Emitter(nc, eng, handles))
                    return f
                getattr(block, eng)(make(eng))

    return nc


def _prep_inputs(L0_attn_x, L0_attn_Wqkv, L0_attn_bqkv, L0_attn_Wproj, L0_attn_bproj,
                 L0_ffn_ln_g, L0_ffn_ln_b, L0_ffn_w1, L0_ffn_w2):
    x = np.ascontiguousarray(L0_attn_x, dtype=np.float32)
    shared = {
        "wqkvT": np.ascontiguousarray(np.asarray(L0_attn_Wqkv, dtype=np.float32).T),
        "bqkv": np.ascontiguousarray(np.asarray(L0_attn_bqkv, dtype=np.float32).reshape(1, -1)),
        "wprojT": np.ascontiguousarray(np.asarray(L0_attn_Wproj, dtype=np.float32).T),
        "bproj": np.ascontiguousarray(np.asarray(L0_attn_bproj, dtype=np.float32).reshape(1, -1)),
        "ln_g": np.ascontiguousarray(np.asarray(L0_ffn_ln_g, dtype=np.float32).reshape(1, -1)),
        "ln_b": np.ascontiguousarray(np.asarray(L0_ffn_ln_b, dtype=np.float32).reshape(1, -1)),
        "w1T": np.ascontiguousarray(np.asarray(L0_ffn_w1, dtype=np.float32).T),
        "w2T": np.ascontiguousarray(np.asarray(L0_ffn_w2, dtype=np.float32).T),
    }
    in_maps = []
    for c in range(NCORES):
        m = dict(shared)
        m["x"] = np.ascontiguousarray(x[:, c * T:(c + 1) * T, :])
        in_maps.append(m)
    return in_maps


def kernel(**inputs):
    in_maps = _prep_inputs(**inputs)
    nc = build_program()
    res = run_bass_kernel_spmd(nc, in_maps, core_ids=list(range(NCORES)))
    shards = [np.asarray(res.results[c]["out"]) for c in range(NCORES)]
    return np.concatenate(shards, axis=1)


# version 08
# speedup vs baseline: 1.1768x; 1.0233x over previous
import numpy as np
import concourse.bass as bass
import concourse.mybir as mybir
from concourse.bass_utils import run_bass_kernel_spmd

F32 = mybir.dt.float32
BF16 = mybir.dt.bfloat16
I32 = mybir.dt.int32
ALU = mybir.AluOpType
AF = mybir.ActivationFunctionType
AX = mybir.AxisListType.X

NCORES = 8
NSEQ = 8
T = 512
D = 1024
H = 16
HD = 64
HID = 4096
NB = T // 128
NT = NSEQ * NB
DCH = D // 128
SCALE = float(HD) ** -0.5
EPS = 1e-5
NEG_BIG = -1.0e30

ENGINES = ("sync", "tensor", "vector", "scalar", "gpsimd")


class Emitter:

    def __init__(self, nc, engine, handles):
        self.nc = nc
        self.engine = engine
        self.h = handles
        self.c = {k: 0 for k in handles}

    def eng(self, engine):
        return getattr(self.nc, engine)

    def emit(self, engine, fn, inc=None, n=1):
        if self.engine == engine:
            ins = fn(self.eng(engine))
            if inc is not None:
                ins.then_inc(self.h[inc], n)
        if inc is not None:
            self.c[inc] += n

    def wait(self, engine, sem, val=None):
        if self.engine == engine:
            self.eng(engine).wait_ge(self.h[sem], self.c[sem] if val is None else val)

    def fence(self, engine, sem):
        self.wait(engine, sem)

    def val(self, sem):
        return self.c[sem]


def build_program(debug=(), limit_b=NB, phases=("A", "B", "C", "D")):
    nc = bass.Bass("TRN2", target_bir_lowering=False, detect_race_conditions=False)

    x_in = nc.dram_tensor("x", [NSEQ, T, D], F32, kind="ExternalInput").ap()
    wqkvT = nc.dram_tensor("wqkvT", [D, 3 * D], F32, kind="ExternalInput").ap()
    bqkv = nc.dram_tensor("bqkv", [1, 3 * D], F32, kind="ExternalInput").ap()
    wprojT = nc.dram_tensor("wprojT", [D, D], F32, kind="ExternalInput").ap()
    bproj = nc.dram_tensor("bproj", [1, D], F32, kind="ExternalInput").ap()
    ln_g = nc.dram_tensor("ln_g", [1, D], F32, kind="ExternalInput").ap()
    ln_b = nc.dram_tensor("ln_b", [1, D], F32, kind="ExternalInput").ap()
    w1T = nc.dram_tensor("w1T", [D, HID], F32, kind="ExternalInput").ap()
    w2T = nc.dram_tensor("w2T", [HID, D], F32, kind="ExternalInput").ap()
    out = nc.dram_tensor("out", [NSEQ, T, D], F32, kind="ExternalOutput").ap()

    x2_kind = "ExternalOutput" if "x2" in debug else "Internal"
    x2d = nc.dram_tensor("dbg_x2" if "x2" in debug else "x2d",
                         [NT, 128, D], F32, kind=x2_kind).ap()
    zd = nc.dram_tensor("zd", [NT, 128, HID], F32, kind="Internal").ap()
    cc_in = nc.dram_tensor("cc_in", [1, 1], F32, kind="Internal").ap()
    cc_out = nc.dram_tensor("cc_out", [1, 1], F32, kind="Internal").ap()
    h1_dbg = (nc.dram_tensor("dbg_h1", [NT, 128, D], F32, kind="ExternalOutput").ap()
              if "h1" in debug else None)
    q2_dbg = (nc.dram_tensor("dbg_q2", [6, 128, 32], F32, kind="ExternalOutput").ap()
              if "q2" in debug else None)
    c8_dbg = (nc.dram_tensor("dbg_c8", [2, 128, HID], mybir.dt.int8, kind="ExternalOutput").ap()
              if "q2" in debug else None)
    st_dbg = (nc.dram_tensor("dbg_st", [128, 32], F32, kind="ExternalOutput").ap()
              if "st" in debug else None)

    sem_names = ["lxa", "lxb", "lwa", "lwb", "tp", "vp", "sp",
                 "gpa", "gpb", "gi", "so", "cc", "lb"]

    import contextlib
    with contextlib.ExitStack() as _stk:
        def sb(name, shape, dtype=F32):
            return _stk.enter_context(nc.sbuf_tensor(name, shape, dtype))

        def psb(name, shape, dtype=F32):
            return _stk.enter_context(nc.psum_tensor(name, shape, dtype))

        ident = sb("ident", [128, 128])
        identb = sb("identb", [128, 128], BF16)
        ones1 = sb("ones1", [1, 128])
        bqkv_t = sb("bqkv_t", [1, 3 * D])
        bproj_t = sb("bproj_t", [1, D])
        xt = sb("xt", [128, 2, D])
        ln1 = sb("ln1", [128, 2, D])
        smat = sb("smat", [128, 8, NSEQ, NSEQ])
        emat = sb("emat", [128, 8, NSEQ, NSEQ])
        mx = sb("mx", [128, 64])
        sm = sb("sm", [128, 64])
        prods = sb("prods", [128, 8, HD])
        junk = sb("junk", [128, D])
        junkS = sb("junkS", [128, D])
        stats = sb("stats", [128, 8, 8])
        nstat = sb("nstat", [128, 16])
        x2t = sb("x2t", [128, 2, D])
        wstage = sb("wstage", [128, 2, 512])
        bst = sb("bst", [128, 32])
        cnts = sb("cnts", [128, 64])
        ccrow = sb("ccrow", [1, 8])
        ARENA_F32 = 36864
        arena = sb("arena", [128, ARENA_F32])
        arena_addr = nc.lookup_mloc(arena).addr

        def overlay(name, shape, dtype, byte_off):
            h = nc.virtual_sbuf_tensor(name, shape, dtype)
            nc._tensor(h.name, list(h.shape), h.dtype, type="SB")
            mloc = nc.lookup_mloc(h)
            mloc.allocated = True
            mloc.addr = arena_addr + byte_off
            return h

        xfm = overlay("xfm", [128, NSEQ, DCH, 128], F32, 0)
        wbuf = overlay("wbuf", [128, 2, DCH, 512], F32, 32768)
        ybuf = overlay("ybuf", [128, NSEQ, 3, 512], F32, 65536)
        aout = overlay("aout", [128, NSEQ, D], F32, 114688)
        h1 = overlay("h1", [128, NT, D], F32, 0)
        u1tok = overlay("u1tok", [128, NT, D], BF16, 0)
        t1 = overlay("t1", [128, DCH, HID], BF16, 65536)
        c8 = overlay("c8", [128, NT, HID], mybir.dt.int8, 0)
        u1f = overlay("u1f", [128, DCH, 128], BF16, 131072)
        zstage = overlay("zstage", [128, 2, 512], F32, 133120)
        zrt2 = overlay("zrt2", [128, 2, 1024], F32, 131072)
        t2 = overlay("t2", [128, 32, D], BF16, 0)
        u2fm = overlay("u2fm", [128, 32, 128], BF16, 65536)
        u2tok = overlay("u2tok", [128, HID], BF16, 73728)
        zrt = overlay("zrt", [128, 2, HID], F32, 81920)
        o2b = overlay("o2b", [128, D], F32, 114688)
        u2st = overlay("u2st", [128, 2, D], F32, 118784)
        ptr0 = psb("ptr0", [128, 128])
        ptr1 = psb("ptr1", [128, 128])
        ptr2 = psb("ptr2", [128, 128])
        ptr3 = psb("ptr3", [128, 128])
        ptrs = [ptr0, ptr1, ptr2, ptr3]
        pmm0 = psb("pmm0", [128, 512])
        pmm1 = psb("pmm1", [128, 512])

        handles = {nm: _stk.enter_context(nc.semaphore(nm)) for nm in sem_names}

        def program(E: Emitter):
            FS, FS2, FM, FT, FR = 0, 1, 2, 3, 4

            E.emit("gpsimd", lambda g: g.iota(junk[:, 0:128], [[1, 128]], channel_multiplier=0,
                                              allow_small_or_imprecise_dtypes=True), inc="gi")
            E.emit("gpsimd", lambda g: g.iota(junkS[:, 0:128], [[0, 128]], channel_multiplier=1,
                                              allow_small_or_imprecise_dtypes=True), inc="gi")
            E.wait("vector", "gi")
            E.emit("vector", lambda v: v.tensor_tensor(ident[:], junk[:, 0:128], junkS[:, 0:128],
                                                       ALU.is_equal), inc="vp")
            E.emit("vector", lambda v: v.memset(ones1[:], 1.0), inc="vp")
            E.fence("vector", "vp")
            E.emit("vector", lambda v: v.tensor_copy(identb[:], ident[:]), inc="vp")
            E.emit("sync", lambda s: s.dma_start(out=bqkv_t[:], in_=bqkv[:]), inc="lb", n=16)
            E.emit("sync", lambda s: s.dma_start(out=bproj_t[:], in_=bproj[:]), inc="lb", n=16)

            E.wait("tensor", "vp")
            E.wait("tensor", "lb")

            xslot_free = {0: (0, 0), 1: (0, 0)}
            lnslot_free = {0: 0, 1: 0}
            wseq = [0]
            wslot_free = {0: 0, 1: 0}
            pslot = {0: 0, 1: 1}
            pwsem = {0: "lwa", 1: "lwb"}
            x2_gp = []
            pmm_free = {0: 0, 1: 0}
            pmmS_free = {0: 0, 1: 0}
            ptr_free = {0: 0, 1: 0, 2: 0, 3: 0}

            def load_x(n, b):
                sl = n % 2
                lxs = "lxa" if sl == 0 else "lxb"
                vv, ss = xslot_free[sl]
                E.wait("sync", "vp", vv)
                E.wait("sync", "sp", ss)
                E.emit("sync", lambda s, n=n, b=b, sl=sl: s.dma_start(
                    out=xt[:, sl, :], in_=x_in[n, b * 128:(b + 1) * 128, :]),
                    inc=lxs, n=16)
                return sl, lxs

            def rsqrt_chain(tcol, rcol, jcol, thcol, w):
                tA = nstat[:, tcol:tcol + w]
                rA = nstat[:, rcol:rcol + w]
                jA = nstat[:, jcol:jcol + w]
                thA = nstat[:, thcol:thcol + w]
                E.wait("scalar", "vp")
                E.emit("scalar", lambda s: s.activation(jA, tA, AF.Ln), inc="sp")
                E.fence("scalar", "sp")
                E.emit("scalar", lambda s: s.activation(rA, jA, AF.Exp, scale=-0.5), inc="sp")
                E.wait("vector", "sp")
                E.emit("vector", lambda v: v.tensor_scalar(thA, tA, 0.5, None, ALU.mult), inc="vp")
                for _ in range(2):
                    E.fence("vector", "vp")
                    E.emit("vector", lambda v: v.tensor_mul(jA, rA, rA), inc="vp")
                    E.fence("vector", "vp")
                    E.emit("vector", lambda v: v.tensor_mul(jA, jA, thA), inc="vp")
                    E.fence("vector", "vp")
                    E.emit("vector", lambda v: v.tensor_scalar(jA, jA, -1.0, 1.5,
                                                               ALU.mult, op1=ALU.add), inc="vp")
                    E.fence("vector", "vp")
                    E.emit("vector", lambda v: v.tensor_mul(rA, rA, jA), inc="vp")

            allowed = [(n, m) for n in range(NSEQ) for m in range(NSEQ)
                       if not (n < 4 and m >= 4)]

            def stats_chain(fs, fs2, fm, ft, fr):
                E.wait("vector", "sp")
                E.fence("vector", "vp")
                sA = stats[:, :, fs]
                s2A = stats[:, :, fs2]
                mA = stats[:, :, fm]
                tA = stats[:, :, ft]
                rA = stats[:, :, fr]
                E.emit("vector", lambda v: v.tensor_scalar(mA, sA, 1.0 / D, None, ALU.mult), inc="vp")
                E.emit("vector", lambda v: v.tensor_scalar(nstat[:, 0:8], s2A, 1.0 / D,
                                                           None, ALU.mult), inc="vp")
                E.fence("vector", "vp")
                E.emit("vector", lambda v: v.tensor_mul(nstat[:, 8:16], mA, mA), inc="vp")
                E.fence("vector", "vp")
                E.emit("vector", lambda v: v.tensor_sub(tA, nstat[:, 0:8], nstat[:, 8:16]), inc="vp")
                E.fence("vector", "vp")
                E.emit("vector", lambda v: v.tensor_scalar(tA, tA, 1.0, EPS,
                                                           ALU.mult, op1=ALU.add), inc="vp")
                E.wait("scalar", "vp")
                E.emit("scalar", lambda s: s.activation(nstat[:, 0:8], tA, AF.Ln), inc="sp")
                E.fence("scalar", "sp")
                E.emit("scalar", lambda s: s.activation(rA, nstat[:, 0:8], AF.Exp,
                                                        scale=-0.5), inc="sp")
                E.wait("vector", "sp")
                E.emit("vector", lambda v: v.tensor_scalar(nstat[:, 8:16], tA, 0.5,
                                                           None, ALU.mult), inc="vp")
                for _ in range(2):
                    E.fence("vector", "vp")
                    E.emit("vector", lambda v: v.tensor_mul(nstat[:, 0:8], rA, rA), inc="vp")
                    E.fence("vector", "vp")
                    E.emit("vector", lambda v: v.tensor_mul(nstat[:, 0:8], nstat[:, 0:8],
                                                            nstat[:, 8:16]), inc="vp")
                    E.fence("vector", "vp")
                    E.emit("vector", lambda v: v.tensor_scalar(nstat[:, 0:8], nstat[:, 0:8],
                                                               -1.0, 1.5, ALU.mult,
                                                               op1=ALU.add), inc="vp")
                    E.fence("vector", "vp")
                    E.emit("vector", lambda v: v.tensor_mul(rA, rA, nstat[:, 0:8]), inc="vp")
                E.fence("vector", "vp")

            for b in range(limit_b if "A" in phases else 0):
                for n in range(NSEQ):
                    sl, lxs = load_x(n, b)
                    E.wait("vector", lxs)
                    E.wait("scalar", lxs)
                    E.emit("scalar", lambda s, n=n, sl=sl: s.activation(
                        junkS[:], xt[:, sl, :], AF.Copy,
                        accum_out=stats[:, n, FS:FS + 1]), inc="sp")
                    E.emit("vector", lambda v, n=n, sl=sl: v.scalar_tensor_tensor(
                        junk[:], xt[:, sl, :], 1.0, xt[:, sl, :], ALU.mult, ALU.mult,
                        accum_out=stats[:, n, FS2:FS2 + 1]), inc="vp")
                    xslot_free[sl] = (E.val("vp"), E.val("sp"))
                E.wait("vector", "sp")
                E.fence("vector", "vp")
                sA = stats[:, :, FS]
                s2A = stats[:, :, FS2]
                mA = stats[:, :, FM]
                tA = stats[:, :, FT]
                rA = stats[:, :, FR]
                E.emit("vector", lambda v: v.tensor_scalar(mA, sA, 1.0 / D, None, ALU.mult), inc="vp")
                E.emit("vector", lambda v: v.tensor_scalar(nstat[:, 0:8], s2A, 1.0 / D,
                                                           None, ALU.mult), inc="vp")
                E.fence("vector", "vp")
                E.emit("vector", lambda v: v.tensor_mul(nstat[:, 8:16], mA, mA), inc="vp")
                E.fence("vector", "vp")
                E.emit("vector", lambda v: v.tensor_sub(tA, nstat[:, 0:8], nstat[:, 8:16]), inc="vp")
                E.fence("vector", "vp")
                E.emit("vector", lambda v: v.tensor_scalar(tA, tA, 1.0, EPS,
                                                           ALU.mult, op1=ALU.add), inc="vp")
                E.wait("scalar", "vp")
                E.emit("scalar", lambda s: s.activation(nstat[:, 0:8], tA, AF.Ln), inc="sp")
                E.fence("scalar", "sp")
                E.emit("scalar", lambda s: s.activation(rA, nstat[:, 0:8], AF.Exp,
                                                        scale=-0.5), inc="sp")
                E.wait("vector", "sp")
                E.emit("vector", lambda v: v.tensor_scalar(nstat[:, 8:16], tA, 0.5,
                                                           None, ALU.mult), inc="vp")
                for _ in range(2):
                    E.fence("vector", "vp")
                    E.emit("vector", lambda v: v.tensor_mul(nstat[:, 0:8], rA, rA), inc="vp")
                    E.fence("vector", "vp")
                    E.emit("vector", lambda v: v.tensor_mul(nstat[:, 0:8], nstat[:, 0:8],
                                                            nstat[:, 8:16]), inc="vp")
                    E.fence("vector", "vp")
                    E.emit("vector", lambda v: v.tensor_scalar(nstat[:, 0:8], nstat[:, 0:8],
                                                               -1.0, 1.5, ALU.mult,
                                                               op1=ALU.add), inc="vp")
                    E.fence("vector", "vp")
                    E.emit("vector", lambda v: v.tensor_mul(rA, rA, nstat[:, 0:8]), inc="vp")
                E.fence("vector", "vp")
                for n in range(NSEQ):
                    sl, lxs = load_x(n, b)
                    E.wait("vector", lxs)
                    E.wait("vector", "sp", lnslot_free[sl])
                    E.emit("vector", lambda v, n=n, sl=sl: v.tensor_scalar(
                        ln1[:, sl, :], xt[:, sl, :], stats[:, n, FM:FM + 1],
                        stats[:, n, FR:FR + 1], ALU.subtract, op1=ALU.mult), inc="vp")
                    xslot_free[sl] = (E.val("vp"), E.val("sp"))
                    E.wait("tensor", "vp")
                    for k in range(DCH):
                        pt = ptrs[k % 4]
                        E.wait("tensor", "sp", ptr_free[k % 4])
                        E.emit("tensor", lambda t, k=k, pt=pt, sl=sl: t.transpose(
                            pt[:], ln1[:, sl, k * 128:(k + 1) * 128], ident[:]), inc="tp")
                        E.wait("scalar", "tp")
                        E.emit("scalar", lambda s, n=n, k=k, pt=pt: s.copy(
                            xfm[:, n, k, :], pt[:]), inc="sp")
                        ptr_free[k % 4] = E.val("sp")
                    lnslot_free[sl] = E.val("sp")
                for hc in range(2):
                    for cg in range(3):
                        ws = wseq[0] % 2
                        lws = "lwa" if ws == 0 else "lwb"
                        E.wait("sync", "tp", wslot_free[ws])
                        for k in range(DCH):
                            E.emit("sync", lambda s, k=k, cg=cg, hc=hc, ws=ws: s.dma_start(
                                out=wbuf[:, ws, k, :],
                                in_=wqkvT[k * 128:(k + 1) * 128,
                                          cg * D + hc * 512: cg * D + hc * 512 + 512]),
                                inc=lws, n=16)
                        wseq[0] += 1
                        E.wait("tensor", lws)
                        if cg == 0:
                            E.wait("tensor", "sp")
                        for n in range(NSEQ):
                            pm = pmm0 if n % 2 == 0 else pmm1
                            E.wait("tensor", "vp", pmm_free[n % 2])
                            E.emit("tensor", lambda t, cg=cg, hc=hc, pm=pm: t.matmul(
                                pm[:], ones1[:],
                                bqkv_t[0:1, cg * D + hc * 512: cg * D + hc * 512 + 512],
                                start=True, stop=False))
                            for k in range(DCH):
                                E.emit("tensor", lambda t, n=n, k=k, ws=ws, pm=pm: t.matmul(
                                    pm[:], xfm[:, n, k, :], wbuf[:, ws, k, :],
                                    start=False, stop=(k == DCH - 1)),
                                    inc=("tp" if k == DCH - 1 else None))
                            E.wait("vector", "tp")
                            E.emit("vector", lambda v, n=n, cg=cg, pm=pm: v.tensor_copy(
                                ybuf[:, n, cg, :], pm[:]), inc="vp")
                            pmm_free[n % 2] = E.val("vp")
                        wslot_free[ws] = E.val("tp")
                    E.wait("vector", "tp")
                    E.fence("vector", "vp")
                    E.emit("vector", lambda v: v.memset(smat[:, :, 0:4, 4:8], NEG_BIG), inc="vp")
                    for hl in range(8):
                        for (n, m) in allowed:
                            E.emit("vector", lambda v, hl=hl, n=n, m=m: v.scalar_tensor_tensor(
                                junk[:, 0:HD],
                                ybuf[:, n, 0, hl * HD:(hl + 1) * HD], SCALE,
                                ybuf[:, m, 1, hl * HD:(hl + 1) * HD],
                                ALU.mult, ALU.mult,
                                accum_out=smat[:, hl, n, m:m + 1]), inc="vp")
                    E.fence("vector", "vp")
                    E.emit("vector", lambda v: v.reduce_max(
                        mx[:].unsqueeze(2),
                        smat[:].rearrange("p h n m -> p (h n) m"), axis=AX), inc="vp")
                    E.fence("vector", "vp")
                    E.emit("vector", lambda v: v.tensor_tensor(
                        emat[:].rearrange("p h n m -> p (h n) m"),
                        smat[:].rearrange("p h n m -> p (h n) m"),
                        mx[:].unsqueeze(2).to_broadcast([128, 64, 8]), ALU.subtract), inc="vp")
                    E.wait("scalar", "vp")
                    E.emit("scalar", lambda s: s.activation(emat[:], emat[:], AF.Exp), inc="sp")
                    E.wait("vector", "sp")
                    E.emit("vector", lambda v: v.reduce_sum(
                        sm[:].unsqueeze(2),
                        emat[:].rearrange("p h n m -> p (h n) m"), axis=AX), inc="vp")
                    E.fence("vector", "vp")
                    E.emit("vector", lambda v: v.reciprocal(sm[:], sm[:]), inc="vp")
                    E.fence("vector", "vp")
                    E.emit("vector", lambda v: v.tensor_tensor(
                        emat[:].rearrange("p h n m -> p (h n) m"),
                        emat[:].rearrange("p h n m -> p (h n) m"),
                        sm[:].unsqueeze(2).to_broadcast([128, 64, 8]), ALU.mult), inc="vp")
                    E.fence("vector", "vp")
                    for hl in range(8):
                        for n in range(NSEQ):
                            nm = 4 if n < 4 else 8
                            for m in range(nm):
                                E.emit("vector", lambda v, hl=hl, n=n, m=m: v.tensor_scalar(
                                    prods[:, m, :],
                                    ybuf[:, m, 2, hl * HD:(hl + 1) * HD],
                                    emat[:, hl, n, m:m + 1], None, ALU.mult), inc="vp")
                            E.emit("vector", lambda v, hl=hl, n=n, hc=hc, nm=nm: v.reduce_sum(
                                aout[:, n, hc * 512 + hl * HD: hc * 512 + (hl + 1) * HD]
                                .unsqueeze(2),
                                prods[:, 0:nm, :].transpose([0, 2, 1]), axis=AX), inc="vp")
                for cg in range(2):
                    ws = wseq[0] % 2
                    lws = "lwa" if ws == 0 else "lwb"
                    E.wait("sync", "tp", wslot_free[ws])
                    for k in range(DCH):
                        E.emit("sync", lambda s, k=k, cg=cg, ws=ws: s.dma_start(
                            out=wbuf[:, ws, k, :],
                            in_=wprojT[k * 128:(k + 1) * 128, cg * 512:(cg + 1) * 512]),
                            inc=lws, n=16)
                    pslot[cg] = ws
                    pwsem[cg] = lws
                    wseq[0] += 1
                attn_done_vp = E.val("vp")
                for n in range(NSEQ):
                    E.wait("tensor", "vp", attn_done_vp)
                    for k in range(DCH):
                        pt = ptrs[k % 4]
                        E.wait("tensor", "sp", ptr_free[k % 4])
                        E.emit("tensor", lambda t, n=n, k=k, pt=pt: t.transpose(
                            pt[:], aout[:, n, k * 128:(k + 1) * 128], ident[:]), inc="tp")
                        E.wait("scalar", "tp")
                        E.emit("scalar", lambda s, n=n, k=k, pt=pt: s.copy(
                            xfm[:, n, k, :], pt[:]), inc="sp")
                        ptr_free[k % 4] = E.val("sp")
                    E.wait("tensor", pwsem[0])
                    E.wait("tensor", pwsem[1])
                    E.wait("tensor", "sp")
                    E.wait("tensor", "vp", max(pmm_free[0], pmm_free[1]))
                    for cg in range(2):
                        pm = pmm0 if cg == 0 else pmm1
                        E.emit("tensor", lambda t, cg=cg, pm=pm: t.matmul(
                            pm[:], ones1[:], bproj_t[0:1, cg * 512:(cg + 1) * 512],
                            start=True, stop=False))
                        for k in range(DCH):
                            E.emit("tensor", lambda t, n=n, k=k, cg=cg, pm=pm: t.matmul(
                                pm[:], xfm[:, n, k, :], wbuf[:, pslot[cg], k, :],
                                start=False, stop=(k == DCH - 1)),
                                inc=("tp" if k == DCH - 1 else None))
                    E.wait("vector", "tp")
                    E.emit("vector", lambda v: v.tensor_copy(junk[:, 0:512], pmm0[:]), inc="vp")
                    E.emit("vector", lambda v: v.tensor_copy(junk[:, 512:1024], pmm1[:]), inc="vp")
                    pmm_free[0] = E.val("vp")
                    pmm_free[1] = E.val("vp")
                    E.emit("vector", lambda v: v.reduce_sum(nstat[:, 0:1], junk[:], axis=AX), inc="vp")
                    E.emit("vector", lambda v: v.scalar_tensor_tensor(
                        junkS[:], junk[:], 1.0, junk[:], ALU.mult, ALU.mult,
                        accum_out=nstat[:, 4:5]), inc="vp")
                    E.fence("vector", "vp")
                    E.emit("vector", lambda v: v.tensor_scalar(nstat[:, 8:9], nstat[:, 0:1],
                                                               1.0 / D, None, ALU.mult), inc="vp")
                    E.emit("vector", lambda v: v.tensor_scalar(nstat[:, 9:10], nstat[:, 4:5],
                                                               1.0 / D, None, ALU.mult), inc="vp")
                    E.fence("vector", "vp")
                    E.emit("vector", lambda v: v.tensor_mul(nstat[:, 10:11], nstat[:, 8:9],
                                                            nstat[:, 8:9]), inc="vp")
                    E.fence("vector", "vp")
                    E.emit("vector", lambda v: v.tensor_sub(nstat[:, 11:12], nstat[:, 9:10],
                                                            nstat[:, 10:11]), inc="vp")
                    E.fence("vector", "vp")
                    E.emit("vector", lambda v: v.tensor_scalar(nstat[:, 11:12], nstat[:, 11:12],
                                                               1.0, EPS, ALU.mult,
                                                               op1=ALU.add), inc="vp")
                    rsqrt_chain(11, 12, 13, 14, 1)
                    sl, lxs = load_x(n, b)
                    xp = len(x2_gp) % 2
                    gps = "gpa" if xp == 0 else "gpb"
                    if len(x2_gp) >= 2:
                        E.wait("vector", gps, x2_gp[-2])
                    E.fence("vector", "vp")
                    E.emit("vector", lambda v, xp=xp: v.tensor_scalar(
                        x2t[:, xp, :], junk[:], nstat[:, 8:9], nstat[:, 12:13],
                        ALU.subtract, op1=ALU.mult), inc="vp")
                    E.wait("vector", lxs)
                    E.emit("vector", lambda v, sl=sl, xp=xp: v.tensor_tensor(
                        x2t[:, xp, :], x2t[:, xp, :], xt[:, sl, :], ALU.add), inc="vp")
                    xslot_free[sl] = (E.val("vp"), E.val("sp"))
                    rt = 4 * n + b
                    E.wait("gpsimd", "vp")
                    E.emit("gpsimd", lambda g, rt=rt, xp=xp: g.dma_start(
                        out=x2d[rt], in_=x2t[:, xp, :]), inc=gps, n=16)
                    x2_gp.append(E.val(gps))
                wslot_free[pslot[0]] = E.val("tp")
                wslot_free[pslot[1]] = E.val("tp")

            def barrier():
                for eng in ENGINES:
                    for s in ("tp", "vp", "sp", "lxa", "lxb", "lwa", "lwb",
                              "gpa", "gpb", "so", "cc"):
                        E.wait(eng, s)

            def greduce(op, incol, outcol, rowslot):
                E.wait("tensor", "vp")
                E.emit("tensor", lambda t: t.transpose(ptr0[0:1, :], incol, ident[:]), inc="tp")
                E.wait("vector", "tp")
                E.emit("vector", lambda v: v.tensor_copy(junk[0:1, 0:128], ptr0[0:1, 0:128]), inc="vp")
                E.fence("vector", "vp")
                E.emit("vector", lambda v: v.tensor_reduce(
                    ccrow[0:1, rowslot:rowslot + 1], junk[0:1, 0:128], AX, op), inc="vp")
                E.wait("gpsimd", "vp")
                E.emit("gpsimd", lambda g: g.dma_start(
                    out=cc_in[0:1, 0:1], in_=ccrow[0:1, rowslot:rowslot + 1]), inc="gpa", n=16)
                E.wait("gpsimd", "gpa")
                E.emit("gpsimd", lambda g: g.collective_compute(
                    "AllReduce", op, ins=[cc_in[:]], outs=[cc_out[:]],
                    replica_groups=[[0, 1, 2, 3, 4, 5, 6, 7]]), inc="cc")
                E.wait("gpsimd", "cc")
                E.emit("gpsimd", lambda g: g.dma_start(
                    out=ccrow[0:1, rowslot + 1:rowslot + 2], in_=cc_out[0:1, 0:1]), inc="gpb", n=16)
                E.wait("tensor", "gpb")
                E.wait("tensor", "vp")
                E.emit("tensor", lambda t: t.matmul(
                    ptr1[:, 0:1], ones1[:], ccrow[0:1, rowslot + 1:rowslot + 2],
                    start=True, stop=True), inc="tp")
                E.wait("vector", "tp")
                E.emit("vector", lambda v: v.tensor_copy(outcol, ptr1[:, 0:1]), inc="vp")
                E.fence("vector", "vp")

            def lreduce(op, incol, outcol, rowslot):
                E.wait("tensor", "vp")
                E.emit("tensor", lambda t: t.transpose(ptr0[0:1, :], incol, ident[:]), inc="tp")
                E.wait("vector", "tp")
                E.emit("vector", lambda v: v.tensor_copy(junk[0:1, 0:128], ptr0[0:1, 0:128]), inc="vp")
                E.fence("vector", "vp")
                E.emit("vector", lambda v: v.tensor_reduce(
                    ccrow[0:1, rowslot:rowslot + 1], junk[0:1, 0:128], AX, op), inc="vp")
                E.wait("tensor", "vp")
                E.emit("tensor", lambda t: t.matmul(
                    ptr1[:, 0:1], ones1[:], ccrow[0:1, rowslot:rowslot + 1],
                    start=True, stop=True), inc="tp")
                E.wait("vector", "tp")
                E.emit("vector", lambda v: v.tensor_copy(outcol, ptr1[:, 0:1]), inc="vp")
                E.fence("vector", "vp")

            def bcol(i):
                return bst[:, i:i + 1]

            def bis_select():
                E.fence("vector", "vp")
                E.emit("vector", lambda v: v.tensor_tensor(bcol(5), bcol(4), bcol(6),
                                                           ALU.is_ge), inc="vp")
                E.fence("vector", "vp")
                E.emit("vector", lambda v: v.tensor_sub(bcol(20), bcol(2), bcol(0)), inc="vp")
                E.emit("vector", lambda v: v.tensor_sub(bcol(21), bcol(2), bcol(1)), inc="vp")
                E.fence("vector", "vp")
                E.emit("vector", lambda v: v.tensor_mul(bcol(20), bcol(20), bcol(5)), inc="vp")
                E.fence("vector", "vp")
                E.emit("vector", lambda v: v.tensor_add(bcol(0), bcol(0), bcol(20)), inc="vp")
                E.emit("vector", lambda v: v.tensor_sub(bcol(21), bcol(1), bcol(2)), inc="vp")
                E.fence("vector", "vp")
                E.emit("vector", lambda v: v.tensor_mul(bcol(21), bcol(21), bcol(5)), inc="vp")
                E.fence("vector", "vp")
                E.emit("vector", lambda v: v.tensor_add(bcol(1), bcol(2), bcol(21)), inc="vp")
                E.fence("vector", "vp")

            if "B" in phases:
                barrier()
                E.emit("vector", lambda v: v.memset(bst[:], 0.0), inc="vp")
                E.emit("vector", lambda v: v.memset(cnts[:], 0.0), inc="vp")
                E.emit("vector", lambda v: v.memset(ccrow[:], 0.0), inc="vp")
                E.fence("vector", "vp")
                FS, FS2, FM, FT, FR = 0, 1, 2, 3, 4
                FU, FU2, FM3 = 5, 6, 7
                gbc = overlay("gbc", [128, D], F32, 131072)
                bbc = overlay("bbc", [128, D], F32, 135168)
                jI8 = overlay("jI8", [128, 8192], mybir.dt.int8, 139264)
                for half in range(2):
                    E.emit("sync", lambda s, half=half: s.dma_start(
                        out=ln1[0:1, 0, 0:512],
                        in_=ln_g[0:1, half * 512:(half + 1) * 512]), inc="lb", n=16)
                    E.emit("sync", lambda s, half=half: s.dma_start(
                        out=ln1[0:1, 1, 0:512],
                        in_=ln_b[0:1, half * 512:(half + 1) * 512]), inc="lb", n=16)
                    E.wait("tensor", "lb")
                    E.wait("tensor", "vp")
                    E.emit("tensor", lambda t: t.matmul(
                        pmm0[:], ones1[:], ln1[0:1, 0, 0:512],
                        start=True, stop=True), inc="tp")
                    E.emit("tensor", lambda t: t.matmul(
                        pmm1[:], ones1[:], ln1[0:1, 1, 0:512],
                        start=True, stop=True), inc="tp")
                    E.wait("vector", "tp")
                    E.emit("vector", lambda v, half=half: v.tensor_copy(
                        gbc[:, half * 512:(half + 1) * 512], pmm0[:]), inc="vp")
                    E.emit("vector", lambda v, half=half: v.tensor_copy(
                        bbc[:, half * 512:(half + 1) * 512], pmm1[:]), inc="vp")
                E.fence("vector", "vp")
                NW1CH = (D // 128) * (HID // 512)
                for ch in range(NW1CH):
                    kd, hcol = ch // 8, ch % 8
                    ws = ch % 2
                    lws = "lwa" if ws == 0 else "lwb"
                    E.wait("sync", "sp", wslot_free.get(("g1", ws), 0))
                    E.emit("sync", lambda s, kd=kd, hcol=hcol, ws=ws: s.dma_start(
                        out=wstage[:, ws, :],
                        in_=w1T[kd * 128:(kd + 1) * 128, hcol * 512:(hcol + 1) * 512]),
                        inc=lws, n=16)
                    E.wait("scalar", lws)
                    E.emit("scalar", lambda s, ch=ch, ws=ws: s.activation(
                        junkS[:, 0:512], wstage[:, ws, :], AF.Abs,
                        accum_out=cnts[:, ch % 64:ch % 64 + 1]), inc="sp")
                    wslot_free[("g1", ws)] = E.val("sp")
                E.wait("vector", "sp")
                E.emit("vector", lambda v: v.reduce_sum(bcol(6), cnts[:, 0:64], axis=AX), inc="vp")
                lreduce(ALU.add, bcol(6), bcol(19), 0)
                E.emit("vector", lambda v: v.tensor_scalar(
                    bcol(19), bcol(19), 1.0 / (D * HID), EPS, ALU.mult, op1=ALU.add), inc="vp")
                ln2slot = {0: (0, 0), 1: (0, 0)}
                for g8 in range(NT // 8):
                    for j in range(8):
                        rt = g8 * 8 + j
                        sl = rt % 2
                        lxs = "lxa" if sl == 0 else "lxb"
                        vv, ss = xslot_free[sl]
                        E.wait("sync", "vp", vv)
                        E.wait("sync", "sp", ss)
                        E.emit("sync", lambda s, rt=rt, sl=sl: s.dma_start(
                            out=xt[:, sl, :], in_=x2d[rt]), inc=lxs, n=16)
                        E.wait("vector", lxs)
                        E.wait("scalar", lxs)
                        E.emit("scalar", lambda s, j=j, sl=sl: s.activation(
                            junkS[:], xt[:, sl, :], AF.Copy,
                            accum_out=stats[:, j, FS:FS + 1]), inc="sp")
                        E.emit("vector", lambda v, j=j, sl=sl: v.scalar_tensor_tensor(
                            junk[:], xt[:, sl, :], 1.0, xt[:, sl, :], ALU.mult, ALU.mult,
                            accum_out=stats[:, j, FS2:FS2 + 1]), inc="vp")
                        xslot_free[sl] = (E.val("vp"), E.val("sp"))
                    stats_chain(FS, FS2, FM, FT, FR)
                    for j in range(8):
                        rt = g8 * 8 + j
                        sl = rt % 2
                        lxs = "lxa" if sl == 0 else "lxb"
                        vv, ss = xslot_free[sl]
                        E.wait("sync", "vp", vv)
                        E.wait("sync", "sp", ss)
                        E.emit("sync", lambda s, rt=rt, sl=sl: s.dma_start(
                            out=xt[:, sl, :], in_=x2d[rt]), inc=lxs, n=16)
                        E.wait("vector", lxs)
                        lv, ls = ln2slot[sl]
                        E.wait("vector", "vp", lv)
                        E.wait("vector", "sp", ls)
                        E.emit("vector", lambda v, j=j, sl=sl: v.tensor_scalar(
                            ln1[:, sl, :], xt[:, sl, :], stats[:, j, FM:FM + 1],
                            stats[:, j, FR:FR + 1], ALU.subtract, op1=ALU.mult), inc="vp")
                        E.wait("scalar", "vp")
                        E.emit("scalar", lambda s, j=j, sl=sl: s.activation(
                            junkS[:], ln1[:, sl, :], AF.Copy,
                            accum_out=stats[:, j, FU:FU + 1]), inc="sp")
                        E.emit("vector", lambda v, j=j, sl=sl: v.scalar_tensor_tensor(
                            junk[:], ln1[:, sl, :], 1.0, ln1[:, sl, :], ALU.mult, ALU.mult,
                            accum_out=stats[:, j, FU2:FU2 + 1]), inc="vp")
                        xslot_free[sl] = (E.val("vp"), E.val("sp"))
                        ln2slot[sl] = (E.val("vp"), E.val("sp"))
                    stats_chain(FU, FU2, FM3, FT, FU2)
                    for j in range(8):
                        rt = g8 * 8 + j
                        sl = rt % 2
                        lxs = "lxa" if sl == 0 else "lxb"
                        vv, ss = xslot_free[sl]
                        E.wait("sync", "vp", vv)
                        E.wait("sync", "sp", ss)
                        E.emit("sync", lambda s, rt=rt, sl=sl: s.dma_start(
                            out=xt[:, sl, :], in_=x2d[rt]), inc=lxs, n=16)
                        E.wait("vector", lxs)
                        lv, ls = ln2slot[sl]
                        E.wait("vector", "sp", ls)
                        E.emit("vector", lambda v, j=j, sl=sl: v.tensor_scalar(
                            ln1[:, sl, :], xt[:, sl, :], stats[:, j, FM:FM + 1],
                            stats[:, j, FR:FR + 1], ALU.subtract, op1=ALU.mult), inc="vp")
                        E.emit("vector", lambda v, j=j, sl=sl: v.tensor_scalar(
                            ln1[:, sl, :], ln1[:, sl, :], stats[:, j, FM3:FM3 + 1],
                            stats[:, j, FU2:FU2 + 1], ALU.subtract, op1=ALU.mult), inc="vp")
                        E.emit("vector", lambda v, sl=sl: v.tensor_mul(
                            ln1[:, sl, :], ln1[:, sl, :], gbc[:]), inc="vp")
                        E.emit("vector", lambda v, rt=rt, sl=sl: v.tensor_add(
                            h1[:, rt, :], ln1[:, sl, :], bbc[:]), inc="vp")
                        xslot_free[sl] = (E.val("vp"), E.val("sp"))
                        E.emit("vector", lambda v, rt=rt: v.reduce_max(
                            bcol(6), h1[:, rt, :], axis=AX,
                            apply_absolute_value=True), inc="vp")
                        E.fence("vector", "vp")
                        if rt == 0:
                            E.emit("vector", lambda v: v.tensor_copy(bcol(7), bcol(6)), inc="vp")
                        else:
                            E.emit("vector", lambda v: v.tensor_max(bcol(7), bcol(7), bcol(6)), inc="vp")
                greduce(ALU.max, bcol(7), bcol(17), 0)
                E.emit("vector", lambda v: v.memset(bcol(10), float(NSEQ * 4096 * D)), inc="vp")
                E.fence("vector", "vp")
                E.emit("vector", lambda v: v.tensor_scalar(bcol(10), bcol(10), -1.0,
                                                           None, ALU.add), inc="vp")
                E.fence("vector", "vp")
                E.emit("vector", lambda v: v.tensor_scalar(bcol(10), bcol(10),
                                                           float(np.float32(0.995)),
                                                           None, ALU.mult), inc="vp")
                E.fence("vector", "vp")
                E.emit("vector", lambda v: v.tensor_copy(x2t[0:128, 0, 0:1].bitcast(I32), bcol(10)), inc="vp")
                E.fence("vector", "vp")
                E.emit("vector", lambda v: v.tensor_copy(bcol(6), x2t[0:128, 0, 0:1].bitcast(I32)), inc="vp")
                E.fence("vector", "vp")
                E.emit("vector", lambda v: v.tensor_tensor(bcol(20), bcol(6), bcol(10), ALU.is_gt), inc="vp")
                E.emit("vector", lambda v: v.tensor_tensor(bcol(21), bcol(6), bcol(10), ALU.is_lt), inc="vp")
                E.fence("vector", "vp")
                E.emit("vector", lambda v: v.tensor_sub(bcol(20), bcol(6), bcol(20)), inc="vp")
                E.emit("vector", lambda v: v.tensor_add(bcol(21), bcol(6), bcol(21)), inc="vp")
                E.fence("vector", "vp")
                E.emit("vector", lambda v: v.tensor_sub(bcol(11), bcol(10), bcol(20)), inc="vp")
                E.fence("vector", "vp")
                E.emit("vector", lambda v: v.tensor_scalar(bcol(12), bcol(11), -1.0, 1.0,
                                                           ALU.mult, op1=ALU.add), inc="vp")
                E.emit("vector", lambda v: v.memset(bcol(6), float(NSEQ * 4096 * D)), inc="vp")
                E.fence("vector", "vp")
                E.emit("vector", lambda v: v.tensor_scalar(bcol(6), bcol(6), -1.0,
                                                           None, ALU.add), inc="vp")
                E.fence("vector", "vp")
                E.emit("vector", lambda v: v.tensor_sub(bcol(8), bcol(6), bcol(21)), inc="vp")
                E.emit("vector", lambda v: v.tensor_sub(bcol(9), bcol(6), bcol(20)), inc="vp")
                E.fence("vector", "vp")
                E.emit("vector", lambda v: v.tensor_scalar(bcol(8), bcol(8), 1.0, 1.0,
                                                           ALU.mult, op1=ALU.add), inc="vp")
                E.emit("vector", lambda v: v.tensor_scalar(bcol(9), bcol(9), 1.0, 1.0,
                                                           ALU.mult, op1=ALU.add), inc="vp")
                E.emit("vector", lambda v: v.memset(bcol(0), -1.0), inc="vp")
                E.fence("vector", "vp")
                E.emit("vector", lambda v: v.tensor_copy(bcol(1), bcol(17)), inc="vp")
                h1flat = h1[:].rearrange("p a b -> p (a b)")
                def count_pass_h1(taucol, cmp1, cmp2, dstcol):
                    E.fence("vector", "vp")
                    E.emit("vector", lambda v: v.tensor_scalar(bcol(22), taucol, -1.0,
                                                               None, ALU.mult), inc="vp")
                    E.fence("vector", "vp")
                    for q in range(4):
                        E.emit("vector", lambda v, q=q: v.tensor_scalar(
                            jI8[:], h1flat[:, q * 8192:(q + 1) * 8192], taucol, None,
                            cmp1, op1=ALU.add, accum_out=cnts[:, q:q + 1]), inc="vp")
                        E.emit("vector", lambda v, q=q: v.tensor_scalar(
                            jI8[:], h1flat[:, q * 8192:(q + 1) * 8192], bcol(22), None,
                            cmp2, op1=ALU.add, accum_out=cnts[:, 4 + q:5 + q]), inc="vp")
                    E.fence("vector", "vp")
                    E.emit("vector", lambda v: v.reduce_sum(bcol(25), cnts[:, 0:8], axis=AX), inc="vp")
                    greduce(ALU.add, bcol(25), dstcol, 0)
                R1 = 30
                for rnd in range(R1):
                    E.fence("vector", "vp")
                    E.emit("vector", lambda v: v.tensor_add(bcol(2), bcol(0), bcol(1)), inc="vp")
                    E.fence("vector", "vp")
                    E.emit("vector", lambda v: v.tensor_scalar(bcol(2), bcol(2), 0.5,
                                                               None, ALU.mult), inc="vp")
                    count_pass_h1(bcol(2), ALU.is_gt, ALU.is_lt, bcol(4))
                    E.fence("vector", "vp")
                    E.emit("vector", lambda v: v.tensor_copy(bcol(6), bcol(8)), inc="vp")
                    bis_select()
                E.emit("vector", lambda v: v.tensor_copy(bcol(13), bcol(1)), inc="vp")
                count_pass_h1(bcol(13), ALU.is_ge, ALU.is_le, bcol(4))
                for rt in range(NT):
                    E.wait("scalar", "vp")
                    E.emit("scalar", lambda s, rt=rt: s.activation(
                        junkS[:], h1flat[:, rt * D:(rt + 1) * D], AF.Abs), inc="sp")
                    E.wait("vector", "sp")
                    E.emit("vector", lambda v: v.scalar_tensor_tensor(
                        junk[:], junkS[:], bcol(13), junkS[:], ALU.is_lt, ALU.mult), inc="vp")
                    E.fence("vector", "vp")
                    E.emit("vector", lambda v: v.reduce_max(bcol(6), junk[:, 0:D], axis=AX), inc="vp")
                    E.fence("vector", "vp")
                    if rt == 0:
                        E.emit("vector", lambda v: v.tensor_copy(bcol(7), bcol(6)), inc="vp")
                    else:
                        E.emit("vector", lambda v: v.tensor_max(bcol(7), bcol(7), bcol(6)), inc="vp")
                greduce(ALU.max, bcol(7), bcol(14), 0)
                E.fence("vector", "vp")
                E.emit("vector", lambda v: v.tensor_tensor(bcol(5), bcol(4), bcol(9), ALU.is_ge), inc="vp")
                E.fence("vector", "vp")
                E.emit("vector", lambda v: v.tensor_sub(bcol(20), bcol(13), bcol(14)), inc="vp")
                E.fence("vector", "vp")
                E.emit("vector", lambda v: v.tensor_mul(bcol(20), bcol(20), bcol(5)), inc="vp")
                E.fence("vector", "vp")
                E.emit("vector", lambda v: v.tensor_add(bcol(14), bcol(14), bcol(20)), inc="vp")
                E.fence("vector", "vp")
                E.emit("vector", lambda v: v.tensor_mul(bcol(20), bcol(14), bcol(12)), inc="vp")
                E.emit("vector", lambda v: v.tensor_mul(bcol(21), bcol(13), bcol(11)), inc="vp")
                E.fence("vector", "vp")
                E.emit("vector", lambda v: v.tensor_add(bcol(15), bcol(20), bcol(21)), inc="vp")
                E.fence("vector", "vp")
                E.emit("vector", lambda v: v.tensor_scalar(bcol(15), bcol(15), EPS,
                                                           None, ALU.max), inc="vp")
                E.fence("vector", "vp")
                E.emit("vector", lambda v: v.reciprocal(bcol(16), bcol(15)), inc="vp")
                E.fence("vector", "vp")
                E.emit("vector", lambda v: v.tensor_scalar(bcol(16), bcol(16), 127.0,
                                                           None, ALU.mult), inc="vp")
                E.fence("vector", "vp")
                if st_dbg is not None:
                    E.wait("gpsimd", "vp")
                    E.emit("gpsimd", lambda g: g.dma_start(out=st_dbg, in_=bst[:]), inc="gpa", n=16)
                if h1_dbg is not None:
                    for rt in range(NT):
                        E.wait("gpsimd", "vp")
                        E.emit("gpsimd", lambda g, rt=rt: g.dma_start(
                            out=h1_dbg[rt], in_=h1[:, rt, :]), inc="gpa", n=16)

            def quantile_targets(nval, hwcol, lwcol, t1col, t2col):
                E.emit("vector", lambda v: v.memset(bcol(10), float(nval)), inc="vp")
                E.fence("vector", "vp")
                E.emit("vector", lambda v: v.tensor_scalar(bcol(10), bcol(10), -1.0,
                                                           None, ALU.add), inc="vp")
                E.fence("vector", "vp")
                E.emit("vector", lambda v: v.tensor_scalar(bcol(10), bcol(10),
                                                           float(np.float32(0.995)),
                                                           None, ALU.mult), inc="vp")
                E.fence("vector", "vp")
                E.emit("vector", lambda v: v.tensor_copy(x2t[0:128, 0, 0:1].bitcast(I32), bcol(10)), inc="vp")
                E.fence("vector", "vp")
                E.emit("vector", lambda v: v.tensor_copy(bcol(6), x2t[0:128, 0, 0:1].bitcast(I32)), inc="vp")
                E.fence("vector", "vp")
                E.emit("vector", lambda v: v.tensor_tensor(bcol(20), bcol(6), bcol(10), ALU.is_gt), inc="vp")
                E.emit("vector", lambda v: v.tensor_tensor(bcol(21), bcol(6), bcol(10), ALU.is_lt), inc="vp")
                E.fence("vector", "vp")
                E.emit("vector", lambda v: v.tensor_sub(bcol(20), bcol(6), bcol(20)), inc="vp")
                E.emit("vector", lambda v: v.tensor_add(bcol(21), bcol(6), bcol(21)), inc="vp")
                E.fence("vector", "vp")
                E.emit("vector", lambda v: v.tensor_sub(hwcol, bcol(10), bcol(20)), inc="vp")
                E.fence("vector", "vp")
                E.emit("vector", lambda v: v.tensor_scalar(lwcol, hwcol, -1.0, 1.0,
                                                           ALU.mult, op1=ALU.add), inc="vp")
                E.emit("vector", lambda v: v.memset(bcol(6), float(nval)), inc="vp")
                E.fence("vector", "vp")
                E.emit("vector", lambda v: v.tensor_scalar(bcol(6), bcol(6), -1.0,
                                                           None, ALU.add), inc="vp")
                E.fence("vector", "vp")
                E.emit("vector", lambda v: v.tensor_sub(t1col, bcol(6), bcol(21)), inc="vp")
                E.emit("vector", lambda v: v.tensor_sub(t2col, bcol(6), bcol(20)), inc="vp")
                E.fence("vector", "vp")
                E.emit("vector", lambda v: v.tensor_scalar(t1col, t1col, 1.0, 1.0,
                                                           ALU.mult, op1=ALU.add), inc="vp")
                E.emit("vector", lambda v: v.tensor_scalar(t2col, t2col, 1.0, 1.0,
                                                           ALU.mult, op1=ALU.add), inc="vp")
                E.fence("vector", "vp")

            if "C" in phases:
                barrier()
                E.emit("vector", lambda v: v.reciprocal(bcol(26), bcol(19)), inc="vp")
                E.fence("vector", "vp")
                for rt in range(NT):
                    sl = rt % 2
                    E.emit("vector", lambda v, rt=rt, sl=sl: v.tensor_scalar(
                        ln1[:, sl, :], h1[:, rt, :], bcol(16), None, ALU.mult), inc="vp")
                    E.emit("vector", lambda v, sl=sl: v.tensor_copy(
                        x2t[:, sl, :].bitcast(I32), ln1[:, sl, :]), inc="vp")
                    E.emit("vector", lambda v, rt=rt, sl=sl: v.tensor_scalar(
                        u1tok[:, rt, :], x2t[:, sl, :].bitcast(I32), -128.0, 127.0,
                        ALU.max, op1=ALU.min), inc="vp")
                wslot2 = {0: (0, 0), 1: (0, 0)}
                for ch in range(64):
                    kd, hcol = ch // 8, ch % 8
                    ws = ch % 2
                    lws = "lwa" if ws == 0 else "lwb"
                    vv, _ = wslot2[ws]
                    E.wait("sync", "vp", vv)
                    E.emit("sync", lambda s, kd=kd, hcol=hcol, ws=ws: s.dma_start(
                        out=wstage[:, ws, :],
                        in_=w1T[kd * 128:(kd + 1) * 128, hcol * 512:(hcol + 1) * 512]),
                        inc=lws, n=16)
                    E.wait("vector", lws)
                    E.emit("vector", lambda v, ws=ws: v.tensor_scalar(
                        junk[:, 0:512], wstage[:, ws, :], bcol(26), None, ALU.mult), inc="vp")
                    wslot2[ws] = (E.val("vp"), 0)
                    E.emit("vector", lambda v: v.tensor_copy(
                        x2t[:, 0, 0:512].bitcast(I32), junk[:, 0:512]), inc="vp")
                    E.emit("vector", lambda v, kd=kd, hcol=hcol: v.tensor_scalar(
                        t1[:, kd, hcol * 512:(hcol + 1) * 512],
                        x2t[:, 0, 0:512].bitcast(I32), -1.0, 1.0,
                        ALU.max, op1=ALU.min), inc="vp")
                zslot = {0: (0, 0), 1: (0, 0)}
                zseq = [0]
                E.emit("vector", lambda v: v.memset(bcol(27), -3.0e38), inc="vp")
                E.fence("vector", "vp")
                conv_done_vp = E.val("vp")
                for rt in range(NT):
                    E.wait("tensor", "vp", conv_done_vp)
                    for k in range(DCH):
                        pt = ptrs[k % 4]
                        E.wait("tensor", "sp", ptr_free[k % 4])
                        E.emit("tensor", lambda t, rt=rt, k=k, pt=pt: t.transpose(
                            pt[:].bitcast(BF16)[:, 0:128],
                            u1tok[:, rt, k * 128:(k + 1) * 128], identb[:]), inc="tp")
                        E.wait("scalar", "tp")
                        E.emit("scalar", lambda s, k=k, pt=pt: s.copy(
                            u1f[:, k, :], pt[:].bitcast(BF16)[:, 0:128]), inc="sp")
                        ptr_free[k % 4] = E.val("sp")
                    E.wait("tensor", "sp")
                    for hcol in range(DCH):
                        pm = pmm0 if hcol % 2 == 0 else pmm1
                        E.wait("tensor", "sp", pmmS_free[hcol % 2])
                        for kd in range(DCH):
                            E.emit("tensor", lambda t, kd=kd, hcol=hcol, pm=pm: t.matmul(
                                pm[:], u1f[:, kd, :], t1[:, kd, hcol * 512:(hcol + 1) * 512],
                                start=(kd == 0), stop=(kd == DCH - 1)),
                                inc=("tp" if kd == DCH - 1 else None))
                        zs = zseq[0] % 2
                        gps = "gpa" if zs == 0 else "gpb"
                        gv, vv = zslot[zs]
                        E.wait("scalar", "tp")
                        E.wait("scalar", gps, gv)
                        E.wait("scalar", "vp", vv)
                        E.emit("scalar", lambda s, zs=zs, pm=pm: s.copy(
                            zstage[:, zs, :], pm[:]), inc="sp")
                        pmmS_free[hcol % 2] = E.val("sp")
                        E.wait("vector", "sp")
                        E.emit("vector", lambda v, zs=zs: v.reduce_max(
                            bcol(6), zstage[:, zs, :], axis=AX), inc="vp")
                        E.fence("vector", "vp")
                        E.emit("vector", lambda v: v.tensor_max(bcol(27), bcol(27), bcol(6)), inc="vp")
                        E.wait("gpsimd", "sp")
                        E.emit("gpsimd", lambda g, rt=rt, hcol=hcol, zs=zs: g.dma_start(
                            out=zd[rt, :, hcol * 512:(hcol + 1) * 512],
                            in_=zstage[:, zs, :]), inc=gps, n=16)
                        zslot[zs] = (E.val(gps), E.val("vp"))
                        zseq[0] += 1
                greduce(ALU.max, bcol(27), bcol(28), 0)

            if "C" in phases:
                barrier()
                c8flat = c8[:].rearrange("p a b -> p (a b)")
                E.emit("vector", lambda v: v.reciprocal(bcol(29), bcol(28)), inc="vp")
                E.fence("vector", "vp")
                E.emit("vector", lambda v: v.tensor_scalar(bcol(29), bcol(29), 127.0,
                                                           None, ALU.mult), inc="vp")
                E.fence("vector", "vp")
                zqslot = {0: 0, 1: 0}
                for rt in range(NT):
                    for qq in range(4):
                        zs = (rt * 4 + qq) % 2
                        lws = "lwa" if zs == 0 else "lwb"
                        qeng = "sync" if zs == 0 else "gpsimd"
                        E.wait(qeng, "vp", zqslot[zs])
                        E.emit(qeng, lambda s, rt=rt, qq=qq, zs=zs: s.dma_start(
                            out=zrt2[:, zs, :],
                            in_=zd[rt, :, qq * D:(qq + 1) * D]), inc=lws, n=16)
                        E.wait("vector", lws)
                        E.emit("vector", lambda v, zs=zs: v.tensor_scalar(
                            x2t[:, 0, :], zrt2[:, zs, :],
                            bcol(29), -127.0, ALU.mult, op1=ALU.max), inc="vp")
                        zqslot[zs] = E.val("vp")
                        E.emit("vector", lambda v, rt=rt, qq=qq: v.tensor_scalar(
                            c8flat[:, rt * HID + qq * D: rt * HID + (qq + 1) * D],
                            x2t[:, 0, :], 127.0, None,
                            ALU.min), inc="vp")
                def count_c8(taucol, dstcol, cmp=ALU.is_gt):
                    E.fence("vector", "vp")
                    off = -0.5 if cmp in (ALU.is_gt,) else 0.5
                    E.emit("vector", lambda v: v.tensor_scalar(
                        nstat[:, 13:14], taucol, -1.0, off, ALU.mult, op1=ALU.add), inc="vp")
                    E.fence("vector", "vp")
                    for q in range(8):
                        E.emit("vector", lambda v, q=q: v.tensor_scalar(
                            jI8[:], c8flat[:, q * 8192:(q + 1) * 8192], taucol, None,
                            cmp, op1=ALU.add, accum_out=cnts[:, q:q + 1]), inc="vp")
                    E.wait("scalar", "vp")
                    for q in range(8, 16):
                        E.emit("scalar", lambda s, q=q: s.activation(
                            jI8[:], c8flat[:, q * 8192:(q + 1) * 8192], AF.Sign,
                            bias=nstat[:, 13:14], accum_out=cnts[:, q:q + 1]), inc="sp")
                    E.wait("vector", "sp")
                    E.fence("vector", "vp")
                    E.emit("vector", lambda v: v.reduce_sum(bcol(25), cnts[:, 0:8], axis=AX), inc="vp")
                    E.emit("vector", lambda v: v.reduce_sum(bcol(22), cnts[:, 8:16], axis=AX), inc="vp")
                    E.fence("vector", "vp")
                    E.emit("vector", lambda v: v.tensor_scalar(bcol(22), bcol(22), 0.5,
                                                               32768.0, ALU.mult, op1=ALU.add), inc="vp")
                    E.fence("vector", "vp")
                    E.emit("vector", lambda v: v.tensor_add(bcol(25), bcol(25), bcol(22)), inc="vp")
                    greduce(ALU.add, bcol(25), dstcol, 0)

                def int_bisect(tgtcol, rounds=9):
                    E.emit("vector", lambda v: v.memset(bcol(0), -128.0), inc="vp")
                    E.emit("vector", lambda v: v.memset(bcol(1), 127.0), inc="vp")
                    for rnd in range(rounds):
                        E.fence("vector", "vp")
                        E.emit("vector", lambda v: v.tensor_add(bcol(2), bcol(0), bcol(1)), inc="vp")
                        E.fence("vector", "vp")
                        E.emit("vector", lambda v: v.tensor_scalar(bcol(2), bcol(2), 0.5,
                                                                   None, ALU.mult), inc="vp")
                        E.fence("vector", "vp")
                        E.emit("vector", lambda v: v.tensor_copy(x2t[0:128, 0, 0:1].bitcast(I32), bcol(2)), inc="vp")
                        E.fence("vector", "vp")
                        E.emit("vector", lambda v: v.tensor_copy(bcol(2), x2t[0:128, 0, 0:1].bitcast(I32)), inc="vp")
                        count_c8(bcol(2), bcol(4))
                        E.fence("vector", "vp")
                        E.emit("vector", lambda v: v.tensor_copy(bcol(6), tgtcol), inc="vp")
                        bis_select()

                quantile_targets(NSEQ * 4096 * HID, bcol(11), bcol(12), bcol(8), bcol(9))
                if q2_dbg is not None:
                    E.wait("gpsimd", "vp")
                    E.emit("gpsimd", lambda g: g.dma_start(out=q2_dbg[0], in_=bst[:]), inc="gpa", n=16)
                if q2_dbg is not None:
                    E.wait("gpsimd", "vp")
                    E.emit("gpsimd", lambda g: g.dma_start(out=c8_dbg[0], in_=c8[:, 0, :]), inc="gpa", n=16)
                    E.emit("gpsimd", lambda g: g.dma_start(out=c8_dbg[1], in_=c8[:, 1, :]), inc="gpa", n=16)
                int_bisect(bcol(8), 9)
                if q2_dbg is not None:
                    E.wait("gpsimd", "vp")
                    E.emit("gpsimd", lambda g: g.dma_start(out=q2_dbg[1], in_=bst[:]), inc="gpa", n=16)
                E.fence("vector", "vp")
                E.emit("vector", lambda v: v.tensor_mul(bcol(20), bcol(1), bcol(28)), inc="vp")
                E.fence("vector", "vp")
                E.emit("vector", lambda v: v.tensor_scalar(bcol(20), bcol(20), 1.0 / 127.0,
                                                           None, ALU.mult), inc="vp")
                E.fence("vector", "vp")
                E.emit("vector", lambda v: v.tensor_copy(x2t[0:128, 0, 0:1].bitcast(I32), bcol(20)), inc="vp")
                E.fence("vector", "vp")
                E.emit("vector", lambda v: v.tensor_copy(nstat[:, 15:16], x2t[0:128, 0, 0:1].bitcast(I32)), inc="vp")
                E.fence("vector", "vp")
                if q2_dbg is not None:
                    E.wait("gpsimd", "vp")
                    E.emit("gpsimd", lambda g: g.dma_start(out=q2_dbg[2], in_=bst[:]), inc="gpa", n=16)
                for rt in range(NT):
                    for qq in range(4):
                        zs = (rt * 4 + qq) % 2
                        lws = "lwa" if zs == 0 else "lwb"
                        qeng = "sync" if zs == 0 else "gpsimd"
                        E.wait(qeng, "vp", zqslot[zs])
                        E.emit(qeng, lambda s, rt=rt, qq=qq, zs=zs: s.dma_start(
                            out=zrt2[:, zs, :],
                            in_=zd[rt, :, qq * D:(qq + 1) * D]), inc=lws, n=16)
                        E.wait("vector", lws)
                        E.emit("vector", lambda v, zs=zs: v.tensor_scalar(
                            x2t[:, 0, :], zrt2[:, zs, :],
                            nstat[:, 15:16], -127.0, ALU.subtract, op1=ALU.max), inc="vp")
                        zqslot[zs] = E.val("vp")
                        E.emit("vector", lambda v, rt=rt, qq=qq: v.tensor_scalar(
                            c8flat[:, rt * HID + qq * D: rt * HID + (qq + 1) * D],
                            x2t[:, 0, :], 127.0, None,
                            ALU.min), inc="vp")
                int_bisect(bcol(8), 9)
                if q2_dbg is not None:
                    E.wait("gpsimd", "vp")
                    E.emit("gpsimd", lambda g: g.dma_start(out=q2_dbg[3], in_=bst[:]), inc="gpa", n=16)
                E.fence("vector", "vp")
                E.emit("vector", lambda v: v.tensor_add(bcol(13), nstat[:, 15:16], bcol(1)), inc="vp")
                E.fence("vector", "vp")
                E.emit("vector", lambda v: v.tensor_scalar(bcol(22), bcol(1), -1.0, None,
                                                           ALU.add), inc="vp")
                count_c8(bcol(22), bcol(4))
                E.fence("vector", "vp")
                E.emit("vector", lambda v: v.tensor_scalar(bcol(23), bcol(1), 1.0, 200.0,
                                                           ALU.mult, op1=ALU.add), inc="vp")
                E.fence("vector", "vp")
                def mm_bufs(q):
                    if q % 2 == 0:
                        return junk[:], junkS[:]
                    return x2t[:, 0, :], x2t[:, 1, :]
                for half in range(2):
                    for qi in range(65):
                        q = half * 64 + qi
                        if qi < 64:
                            ja, jb = mm_bufs(q)
                            E.emit("vector", lambda v, q=q, ja=ja: v.tensor_scalar(
                                ja, c8flat[:, q * 1024:(q + 1) * 1024], 200.0, None,
                                ALU.add), inc="vp")
                            E.emit("vector", lambda v, ja=ja, jb=jb: v.scalar_tensor_tensor(
                                jb, ja, bcol(23), ja, ALU.is_lt, ALU.mult), inc="vp")
                        if qi > 0:
                            _, jbp = mm_bufs(q - 1)
                            E.emit("vector", lambda v, q=q, jbp=jbp: v.reduce_max(
                                cnts[:, (q - 1) % 64:(q - 1) % 64 + 1], jbp, axis=AX), inc="vp")
                    E.fence("vector", "vp")
                    if half == 0:
                        E.emit("vector", lambda v: v.reduce_max(bcol(7), cnts[:, 0:64], axis=AX), inc="vp")
                    else:
                        E.emit("vector", lambda v: v.reduce_max(bcol(6), cnts[:, 0:64], axis=AX), inc="vp")
                        E.fence("vector", "vp")
                        E.emit("vector", lambda v: v.tensor_max(bcol(7), bcol(7), bcol(6)), inc="vp")
                greduce(ALU.max, bcol(7), bcol(14), 0)
                E.fence("vector", "vp")
                E.emit("vector", lambda v: v.tensor_scalar(bcol(14), bcol(14), 1.0, -200.0,
                                                           ALU.mult, op1=ALU.add), inc="vp")
                E.fence("vector", "vp")
                E.emit("vector", lambda v: v.tensor_add(bcol(14), bcol(14), nstat[:, 15:16]), inc="vp")
                E.fence("vector", "vp")
                E.emit("vector", lambda v: v.tensor_tensor(bcol(5), bcol(4), bcol(9), ALU.is_ge), inc="vp")
                E.fence("vector", "vp")
                E.emit("vector", lambda v: v.tensor_sub(bcol(21), bcol(13), bcol(14)), inc="vp")
                E.fence("vector", "vp")
                E.emit("vector", lambda v: v.tensor_mul(bcol(21), bcol(21), bcol(5)), inc="vp")
                E.fence("vector", "vp")
                E.emit("vector", lambda v: v.tensor_add(bcol(14), bcol(14), bcol(21)), inc="vp")
                E.fence("vector", "vp")
                E.emit("vector", lambda v: v.tensor_mul(bcol(18), bcol(19), bcol(15)), inc="vp")
                E.fence("vector", "vp")
                E.emit("vector", lambda v: v.tensor_scalar(bcol(18), bcol(18), 1.0 / 127.0,
                                                           None, ALU.mult), inc="vp")
                E.fence("vector", "vp")
                for (zc_col, out_col) in ((13, 13), (14, 14)):
                    E.emit("vector", lambda v, zc_col=zc_col: v.tensor_mul(
                        bcol(22), bcol(zc_col), bcol(18)), inc="vp")
                    E.wait("scalar", "vp")
                    E.emit("scalar", lambda s, out_col=out_col: s.activation(
                        bcol(out_col), bcol(22), AF.Gelu), inc="sp")
                    E.wait("vector", "sp")
                E.emit("vector", lambda v: v.tensor_mul(bcol(20), bcol(14), bcol(12)), inc="vp")
                E.emit("vector", lambda v: v.tensor_mul(bcol(21), bcol(13), bcol(11)), inc="vp")
                E.fence("vector", "vp")
                E.emit("vector", lambda v: v.tensor_add(bcol(30), bcol(20), bcol(21)), inc="vp")
                E.fence("vector", "vp")
                E.emit("vector", lambda v: v.tensor_scalar(bcol(30), bcol(30), EPS,
                                                           None, ALU.max), inc="vp")
                E.fence("vector", "vp")
                E.emit("vector", lambda v: v.reciprocal(bcol(31), bcol(30)), inc="vp")
                E.fence("vector", "vp")
                E.emit("vector", lambda v: v.tensor_scalar(bcol(31), bcol(31), 127.0,
                                                           None, ALU.mult), inc="vp")
                E.fence("vector", "vp")
                if st_dbg is not None:
                    E.wait("gpsimd", "vp")
                    E.emit("gpsimd", lambda g: g.dma_start(out=st_dbg, in_=bst[:]), inc="gpa", n=16)

            if "D" in phases:
                barrier()
                for ch in range(64):
                    kh, dcol = ch // 2, ch % 2
                    ws = ch % 2
                    lws = "lwa" if ws == 0 else "lwb"
                    E.wait("sync", "sp", wslot_free.get(("g2", ws), 0))
                    E.emit("sync", lambda s, kh=kh, dcol=dcol, ws=ws: s.dma_start(
                        out=wstage[:, ws, :],
                        in_=w2T[kh * 128:(kh + 1) * 128, dcol * 512:(dcol + 1) * 512]),
                        inc=lws, n=16)
                    E.wait("scalar", lws)
                    E.emit("scalar", lambda s, ch=ch, ws=ws: s.activation(
                        junkS[:, 0:512], wstage[:, ws, :], AF.Abs,
                        accum_out=cnts[:, ch % 64:ch % 64 + 1]), inc="sp")
                    wslot_free[("g2", ws)] = E.val("sp")
                E.wait("vector", "sp")
                E.emit("vector", lambda v: v.reduce_sum(bcol(6), cnts[:, 0:64], axis=AX), inc="vp")
                lreduce(ALU.add, bcol(6), bcol(19), 0)
                E.emit("vector", lambda v: v.tensor_scalar(
                    bcol(19), bcol(19), 1.0 / (D * HID), EPS, ALU.mult, op1=ALU.add), inc="vp")
                E.fence("vector", "vp")
                E.emit("vector", lambda v: v.reciprocal(bcol(26), bcol(19)), inc="vp")
                E.fence("vector", "vp")
                E.emit("vector", lambda v: v.tensor_mul(bcol(25), bcol(19), bcol(30)), inc="vp")
                E.fence("vector", "vp")
                E.emit("vector", lambda v: v.tensor_scalar(bcol(25), bcol(25), 1.0 / 127.0,
                                                           None, ALU.mult), inc="vp")
                E.emit("vector", lambda v: v.tensor_scalar(bcol(24), bcol(31), 0.5,
                                                           None, ALU.mult), inc="vp")
                E.fence("vector", "vp")
                wslot3 = {0: (0, 0), 1: (0, 0)}
                for ch in range(64):
                    kh, dcol = ch // 2, ch % 2
                    ws = ch % 2
                    lws = "lwa" if ws == 0 else "lwb"
                    vv, _ = wslot3[ws]
                    E.wait("sync", "vp", vv)
                    E.emit("sync", lambda s, kh=kh, dcol=dcol, ws=ws: s.dma_start(
                        out=wstage[:, ws, :],
                        in_=w2T[kh * 128:(kh + 1) * 128, dcol * 512:(dcol + 1) * 512]),
                        inc=lws, n=16)
                    E.wait("vector", lws)
                    E.emit("vector", lambda v, ws=ws: v.tensor_scalar(
                        junk[:, 0:512], wstage[:, ws, :], bcol(26), None, ALU.mult), inc="vp")
                    wslot3[ws] = (E.val("vp"), 0)
                    E.emit("vector", lambda v: v.tensor_copy(
                        x2t[:, 0, 0:512].bitcast(I32), junk[:, 0:512]), inc="vp")
                    E.emit("vector", lambda v, kh=kh, dcol=dcol: v.tensor_scalar(
                        t2[:, kh, dcol * 512:(dcol + 1) * 512],
                        x2t[:, 0, 0:512].bitcast(I32), -1.0, 1.0,
                        ALU.max, op1=ALU.min), inc="vp")
                zdslot = {0: (0, 0), 1: (0, 0)}
                oslot = {0: 0, 1: 0}
                pv2, ps2 = xslot_free[0]
                E.wait("sync", "vp", pv2)
                E.wait("sync", "sp", ps2)
                E.emit("sync", lambda s: s.dma_start(
                    out=xt[:, 0, :], in_=x2d[0]), inc="lxa", n=16)
                E.emit("sync", lambda s: s.dma_start(
                    out=zrt[:, 0, :], in_=zd[0]), inc="lwa", n=16)
                for rt in range(NT):
                    n, b = rt // NB, rt % NB
                    zs = rt % 2
                    lws = "lwa" if zs == 0 else "lwb"
                    if rt + 1 < NT:
                        nsl = (rt + 1) % 2
                        nlxs = "lxa" if nsl == 0 else "lxb"
                        vv2, ss2 = xslot_free[nsl]
                        E.wait("sync", "vp", vv2)
                        E.wait("sync", "sp", ss2)
                        E.emit("sync", lambda s, rt=rt, nsl=nsl: s.dma_start(
                            out=xt[:, nsl, :], in_=x2d[rt + 1]), inc=nlxs, n=16)
                        nzs = (rt + 1) % 2
                        nlws = "lwa" if nzs == 0 else "lwb"
                        vv, _ = zdslot[nzs]
                        E.wait("sync", "vp", vv)
                        E.emit("sync", lambda s, rt=rt, nzs=nzs: s.dma_start(
                            out=zrt[:, nzs, :], in_=zd[rt + 1]), inc=nlws, n=16)
                    E.wait("vector", lws)
                    mult_done = {}
                    gelu_done = {}

                    def pre_ap(p):
                        return u2st[:, 0, :] if p == 0 else ln1[:, 1, :]

                    def gout_ap(p):
                        return junkS[:] if p == 0 else junk[:]

                    def emit_mult(qq):
                        p = qq % 2
                        if qq >= 2:
                            E.wait("vector", "sp", gelu_done[qq - 2])
                        E.emit("vector", lambda v, zs=zs, qq=qq, p=p: v.tensor_scalar(
                            pre_ap(p), zrt[:, zs, qq * D:(qq + 1) * D],
                            bcol(18), None, ALU.mult), inc="vp")
                        mult_done[qq] = E.val("vp")
                        E.wait("scalar", "vp", mult_done[qq])
                        E.emit("scalar", lambda s, p=p: s.activation(
                            gout_ap(p), pre_ap(p), AF.Gelu), inc="sp")
                        gelu_done[qq] = E.val("sp")

                    def emit_post(qq):
                        p = qq % 2
                        E.wait("vector", "sp", gelu_done[qq])
                        E.emit("vector", lambda v, p=p: v.tensor_scalar(
                            u2st[:, 1, :], gout_ap(p), bcol(31), None, ALU.mult), inc="vp")
                        E.emit("vector", lambda v: v.tensor_copy(
                            ln1[:, 0, :].bitcast(I32), u2st[:, 1, :]), inc="vp")
                        E.emit("vector", lambda v, qq=qq: v.tensor_scalar(
                            u2tok[:, qq * D:(qq + 1) * D], ln1[:, 0, :].bitcast(I32),
                            -128.0, 127.0, ALU.max, op1=ALU.min), inc="vp")

                    emit_mult(0)
                    emit_mult(1)
                    emit_post(0)
                    emit_mult(2)
                    emit_post(1)
                    emit_mult(3)
                    emit_post(2)
                    emit_post(3)
                    zdslot[zs] = (mult_done[3], 0)
                    E.wait("tensor", "vp")
                    for kh in range(32):
                        pt = ptrs[kh % 4]
                        E.wait("tensor", "sp", ptr_free[kh % 4])
                        E.emit("tensor", lambda t, kh=kh, pt=pt: t.transpose(
                            pt[:].bitcast(BF16)[:, 0:128],
                            u2tok[:, kh * 128:(kh + 1) * 128], identb[:]), inc="tp")
                        E.wait("scalar", "tp")
                        E.emit("scalar", lambda s, kh=kh, pt=pt: s.copy(
                            u2fm[:, kh, :], pt[:].bitcast(BF16)[:, 0:128]), inc="sp")
                        ptr_free[kh % 4] = E.val("sp")
                    E.wait("tensor", "sp")
                    E.wait("tensor", "vp", max(pmm_free[0], pmm_free[1]))
                    for dcol in range(2):
                        pm = pmm0 if dcol == 0 else pmm1
                        for kh in range(32):
                            E.emit("tensor", lambda t, kh=kh, dcol=dcol, pm=pm: t.matmul(
                                pm[:], u2fm[:, kh, :], t2[:, kh, dcol * 512:(dcol + 1) * 512],
                                start=(kh == 0), stop=(kh == 31)),
                                inc=("tp" if kh == 31 else None))
                    E.wait("vector", "tp")
                    E.emit("vector", lambda v: v.tensor_scalar(
                        o2b[:, 0:512], pmm0[:], bcol(25), None, ALU.mult), inc="vp")
                    E.emit("vector", lambda v: v.tensor_scalar(
                        o2b[:, 512:1024], pmm1[:], bcol(25), None, ALU.mult), inc="vp")
                    pmm_free[0] = E.val("vp")
                    pmm_free[1] = E.val("vp")
                    E.fence("vector", "vp")
                    E.emit("vector", lambda v: v.reduce_sum(nstat[:, 0:1], o2b[:], axis=AX), inc="vp")
                    E.emit("vector", lambda v: v.scalar_tensor_tensor(
                        junk[:], o2b[:], 1.0, o2b[:], ALU.mult, ALU.mult,
                        accum_out=nstat[:, 4:5]), inc="vp")
                    E.fence("vector", "vp")
                    E.emit("vector", lambda v: v.tensor_scalar(nstat[:, 8:9], nstat[:, 0:1],
                                                               1.0 / D, None, ALU.mult), inc="vp")
                    E.emit("vector", lambda v: v.tensor_scalar(nstat[:, 9:10], nstat[:, 4:5],
                                                               1.0 / D, None, ALU.mult), inc="vp")
                    E.fence("vector", "vp")
                    E.emit("vector", lambda v: v.tensor_mul(nstat[:, 10:11], nstat[:, 8:9],
                                                            nstat[:, 8:9]), inc="vp")
                    E.fence("vector", "vp")
                    E.emit("vector", lambda v: v.tensor_sub(nstat[:, 11:12], nstat[:, 9:10],
                                                            nstat[:, 10:11]), inc="vp")
                    E.fence("vector", "vp")
                    E.emit("vector", lambda v: v.tensor_scalar(nstat[:, 11:12], nstat[:, 11:12],
                                                               1.0, EPS, ALU.mult,
                                                               op1=ALU.add), inc="vp")
                    rsqrt_chain(11, 12, 13, 14, 1)
                    sl = rt % 2
                    lxs = "lxa" if sl == 0 else "lxb"
                    xp = rt % 2
                    E.wait("vector", "so", oslot[xp])
                    E.fence("vector", "vp")
                    E.emit("vector", lambda v, xp=xp: v.tensor_scalar(
                        x2t[:, xp, :], o2b[:], nstat[:, 8:9], nstat[:, 12:13],
                        ALU.subtract, op1=ALU.mult), inc="vp")
                    E.wait("vector", lxs)
                    E.emit("vector", lambda v, sl=sl, xp=xp: v.tensor_tensor(
                        x2t[:, xp, :], x2t[:, xp, :], xt[:, sl, :], ALU.add), inc="vp")
                    xslot_free[sl] = (E.val("vp"), E.val("sp"))
                    E.wait("sync", "vp")
                    E.emit("sync", lambda s, n=n, b=b, xp=xp: s.dma_start(
                        out=out[n, b * 128:(b + 1) * 128, :], in_=x2t[:, xp, :]), inc="so", n=16)
                    oslot[xp] = E.val("so")
                E.wait("sync", "so")
                E.wait("vector", "so")

            if "D" not in phases:
                barrier()
                for rt in range(NT):
                    n, b = rt // NB, rt % NB
                    E.emit("sync", lambda s, rt=rt: s.dma_start(
                        out=junk[:], in_=x2d[rt]), inc="lxa", n=16)
                    E.wait("sync", "lxa")
                    E.emit("sync", lambda s, n=n, b=b: s.dma_start(
                        out=out[n, b * 128:(b + 1) * 128, :], in_=junk[:]), inc="so", n=16)
                    E.wait("sync", "so")
            E.wait("sync", "so")
            E.wait("vector", "so")

        with nc.Block() as block:
            for eng in ENGINES:
                def make(eng):
                    def f(_q):
                        program(Emitter(nc, eng, handles))
                    return f
                getattr(block, eng)(make(eng))

    return nc


def _prep_inputs(L0_attn_x, L0_attn_Wqkv, L0_attn_bqkv, L0_attn_Wproj, L0_attn_bproj,
                 L0_ffn_ln_g, L0_ffn_ln_b, L0_ffn_w1, L0_ffn_w2):
    x = np.ascontiguousarray(L0_attn_x, dtype=np.float32)
    shared = {
        "wqkvT": np.ascontiguousarray(np.asarray(L0_attn_Wqkv, dtype=np.float32).T),
        "bqkv": np.ascontiguousarray(np.asarray(L0_attn_bqkv, dtype=np.float32).reshape(1, -1)),
        "wprojT": np.ascontiguousarray(np.asarray(L0_attn_Wproj, dtype=np.float32).T),
        "bproj": np.ascontiguousarray(np.asarray(L0_attn_bproj, dtype=np.float32).reshape(1, -1)),
        "ln_g": np.ascontiguousarray(np.asarray(L0_ffn_ln_g, dtype=np.float32).reshape(1, -1)),
        "ln_b": np.ascontiguousarray(np.asarray(L0_ffn_ln_b, dtype=np.float32).reshape(1, -1)),
        "w1T": np.ascontiguousarray(np.asarray(L0_ffn_w1, dtype=np.float32).T),
        "w2T": np.ascontiguousarray(np.asarray(L0_ffn_w2, dtype=np.float32).T),
    }
    in_maps = []
    for c in range(NCORES):
        m = dict(shared)
        m["x"] = np.ascontiguousarray(x[:, c * T:(c + 1) * T, :])
        in_maps.append(m)
    return in_maps


def kernel(**inputs):
    in_maps = _prep_inputs(**inputs)
    nc = build_program()
    res = run_bass_kernel_spmd(nc, in_maps, core_ids=list(range(NCORES)))
    shards = [np.asarray(res.results[c]["out"]) for c in range(NCORES)]
    return np.concatenate(shards, axis=1)


# version 09
# speedup vs baseline: 1.2487x; 1.0611x over previous
import numpy as np
import concourse.bass as bass
import concourse.mybir as mybir
from concourse.bass_utils import run_bass_kernel_spmd

F32 = mybir.dt.float32
BF16 = mybir.dt.bfloat16
I32 = mybir.dt.int32
ALU = mybir.AluOpType
AF = mybir.ActivationFunctionType
AX = mybir.AxisListType.X

NCORES = 8
NSEQ = 8
T = 512
D = 1024
H = 16
HD = 64
HID = 4096
NB = T // 128
NT = NSEQ * NB
DCH = D // 128
SCALE = float(HD) ** -0.5
EPS = 1e-5
NEG_BIG = -1.0e30

ENGINES = ("sync", "tensor", "vector", "scalar", "gpsimd")


class Emitter:

    def __init__(self, nc, engine, handles):
        self.nc = nc
        self.engine = engine
        self.h = handles
        self.c = {k: 0 for k in handles}

    def eng(self, engine):
        return getattr(self.nc, engine)

    def emit(self, engine, fn, inc=None, n=1):
        if self.engine == engine:
            ins = fn(self.eng(engine))
            if inc is not None:
                ins.then_inc(self.h[inc], n)
        if inc is not None:
            self.c[inc] += n

    def wait(self, engine, sem, val=None):
        if self.engine == engine:
            self.eng(engine).wait_ge(self.h[sem], self.c[sem] if val is None else val)

    def fence(self, engine, sem):
        self.wait(engine, sem)

    def val(self, sem):
        return self.c[sem]


def build_program(debug=(), limit_b=NB, phases=("A", "B", "C", "D")):
    nc = bass.Bass("TRN2", target_bir_lowering=False, detect_race_conditions=False)

    x_in = nc.dram_tensor("x", [NSEQ, T, D], F32, kind="ExternalInput").ap()
    wqkvT = nc.dram_tensor("wqkvT", [D, 3 * D], F32, kind="ExternalInput").ap()
    bqkv = nc.dram_tensor("bqkv", [1, 3 * D], F32, kind="ExternalInput").ap()
    wprojT = nc.dram_tensor("wprojT", [D, D], F32, kind="ExternalInput").ap()
    bproj = nc.dram_tensor("bproj", [1, D], F32, kind="ExternalInput").ap()
    ln_g = nc.dram_tensor("ln_g", [1, D], F32, kind="ExternalInput").ap()
    ln_b = nc.dram_tensor("ln_b", [1, D], F32, kind="ExternalInput").ap()
    w1T = nc.dram_tensor("w1T", [D, HID], F32, kind="ExternalInput").ap()
    w2T = nc.dram_tensor("w2T", [HID, D], F32, kind="ExternalInput").ap()
    out = nc.dram_tensor("out", [NSEQ, T, D], F32, kind="ExternalOutput").ap()

    x2_kind = "ExternalOutput" if "x2" in debug else "Internal"
    x2d = nc.dram_tensor("dbg_x2" if "x2" in debug else "x2d",
                         [NT, 128, D], F32, kind=x2_kind).ap()
    zd = nc.dram_tensor("zd", [NT, 128, HID], F32, kind="Internal").ap()
    cc_in = nc.dram_tensor("cc_in", [1, 1], F32, kind="Internal").ap()
    cc_out = nc.dram_tensor("cc_out", [1, 1], F32, kind="Internal").ap()
    h1_dbg = (nc.dram_tensor("dbg_h1", [NT, 128, D], F32, kind="ExternalOutput").ap()
              if "h1" in debug else None)
    q2_dbg = (nc.dram_tensor("dbg_q2", [6, 128, 32], F32, kind="ExternalOutput").ap()
              if "q2" in debug else None)
    c8_dbg = (nc.dram_tensor("dbg_c8", [2, 128, HID], mybir.dt.int8, kind="ExternalOutput").ap()
              if "q2" in debug else None)
    st_dbg = (nc.dram_tensor("dbg_st", [128, 32], F32, kind="ExternalOutput").ap()
              if "st" in debug else None)

    sem_names = ["lxa", "lxb", "lwa", "lwb", "tp", "vp", "sp",
                 "gpa", "gpb", "gi", "so", "cc", "lb"]

    import contextlib
    with contextlib.ExitStack() as _stk:
        def sb(name, shape, dtype=F32):
            return _stk.enter_context(nc.sbuf_tensor(name, shape, dtype))

        def psb(name, shape, dtype=F32):
            return _stk.enter_context(nc.psum_tensor(name, shape, dtype))

        ident = sb("ident", [128, 128])
        identb = sb("identb", [128, 128], BF16)
        ones1 = sb("ones1", [1, 128])
        bqkv_t = sb("bqkv_t", [1, 3 * D])
        bproj_t = sb("bproj_t", [1, D])
        xt = sb("xt", [128, 2, D])
        ln1 = sb("ln1", [128, 2, D])
        smat = sb("smat", [128, 8, NSEQ, NSEQ])
        emat = sb("emat", [128, 8, NSEQ, NSEQ])
        mx = sb("mx", [128, 64])
        sm = sb("sm", [128, 64])
        prods = sb("prods", [128, 8, HD])
        junk = sb("junk", [128, D])
        junkS = sb("junkS", [128, D])
        stats = sb("stats", [128, 8, 8])
        nstat = sb("nstat", [128, 16])
        x2t = sb("x2t", [128, 2, D])
        wstage = sb("wstage", [128, 2, 512])
        bst = sb("bst", [128, 32])
        cnts = sb("cnts", [128, 64])
        ccrow = sb("ccrow", [1, 8])
        ARENA_F32 = 36864
        arena = sb("arena", [128, ARENA_F32])
        arena_addr = nc.lookup_mloc(arena).addr

        def overlay(name, shape, dtype, byte_off):
            h = nc.virtual_sbuf_tensor(name, shape, dtype)
            nc._tensor(h.name, list(h.shape), h.dtype, type="SB")
            mloc = nc.lookup_mloc(h)
            mloc.allocated = True
            mloc.addr = arena_addr + byte_off
            return h

        xfm = overlay("xfm", [128, NSEQ, DCH, 128], F32, 0)
        wbuf = overlay("wbuf", [128, 2, DCH, 512], F32, 32768)
        ybuf = overlay("ybuf", [128, NSEQ, 3, 512], F32, 65536)
        aout = overlay("aout", [128, NSEQ, D], F32, 114688)
        h1 = overlay("h1", [128, NT, D], F32, 0)
        u1tok = overlay("u1tok", [128, NT, D], BF16, 0)
        t1 = overlay("t1", [128, DCH, HID], BF16, 65536)
        c8 = overlay("c8", [128, NT, HID], mybir.dt.int8, 0)
        u1f = overlay("u1f", [128, DCH, 128], BF16, 131072)
        zstage = overlay("zstage", [128, 2, 512], F32, 133120)
        zrt2 = overlay("zrt2", [128, 2, 1024], F32, 131072)
        t2 = overlay("t2", [128, 32, D], BF16, 0)
        u2fm = overlay("u2fm", [128, 32, 128], BF16, 65536)
        u2tok = overlay("u2tok", [128, 2, HID], BF16, 73728)
        zrt = overlay("zrt", [128, 2, HID], F32, 90112)
        o2b = overlay("o2b", [128, D], F32, 122880)
        u2st = overlay("u2st", [128, 2, D], F32, 126976)
        ptr0 = psb("ptr0", [128, 128])
        ptr1 = psb("ptr1", [128, 128])
        ptr2 = psb("ptr2", [128, 128])
        ptr3 = psb("ptr3", [128, 128])
        ptrs = [ptr0, ptr1, ptr2, ptr3]
        pmm0 = psb("pmm0", [128, 512])
        pmm1 = psb("pmm1", [128, 512])
        pmm2 = psb("pmm2", [128, 512])
        pmm3 = psb("pmm3", [128, 512])

        handles = {nm: _stk.enter_context(nc.semaphore(nm)) for nm in sem_names}

        def program(E: Emitter):
            FS, FS2, FM, FT, FR = 0, 1, 2, 3, 4

            E.emit("gpsimd", lambda g: g.iota(junk[:, 0:128], [[1, 128]], channel_multiplier=0,
                                              allow_small_or_imprecise_dtypes=True), inc="gi")
            E.emit("gpsimd", lambda g: g.iota(junkS[:, 0:128], [[0, 128]], channel_multiplier=1,
                                              allow_small_or_imprecise_dtypes=True), inc="gi")
            E.wait("vector", "gi")
            E.emit("vector", lambda v: v.tensor_tensor(ident[:], junk[:, 0:128], junkS[:, 0:128],
                                                       ALU.is_equal), inc="vp")
            E.emit("vector", lambda v: v.memset(ones1[:], 1.0), inc="vp")
            E.fence("vector", "vp")
            E.emit("vector", lambda v: v.tensor_copy(identb[:], ident[:]), inc="vp")
            E.emit("sync", lambda s: s.dma_start(out=bqkv_t[:], in_=bqkv[:]), inc="lb", n=16)
            E.emit("sync", lambda s: s.dma_start(out=bproj_t[:], in_=bproj[:]), inc="lb", n=16)

            E.wait("tensor", "vp")
            E.wait("tensor", "lb")

            xslot_free = {0: (0, 0), 1: (0, 0)}
            lnslot_free = {0: 0, 1: 0}
            wseq = [0]
            wslot_free = {0: 0, 1: 0}
            pslot = {0: 0, 1: 1}
            pwsem = {0: "lwa", 1: "lwb"}
            x2_gp = []
            pmm_free = {0: 0, 1: 0}
            pmmS_free = {0: 0, 1: 0}
            ptr_free = {0: 0, 1: 0, 2: 0, 3: 0}

            def load_x(n, b):
                sl = n % 2
                lxs = "lxa" if sl == 0 else "lxb"
                vv, ss = xslot_free[sl]
                E.wait("sync", "vp", vv)
                E.wait("sync", "sp", ss)
                E.emit("sync", lambda s, n=n, b=b, sl=sl: s.dma_start(
                    out=xt[:, sl, :], in_=x_in[n, b * 128:(b + 1) * 128, :]),
                    inc=lxs, n=16)
                return sl, lxs

            def rsqrt_chain(tcol, rcol, jcol, thcol, w):
                tA = nstat[:, tcol:tcol + w]
                rA = nstat[:, rcol:rcol + w]
                jA = nstat[:, jcol:jcol + w]
                thA = nstat[:, thcol:thcol + w]
                E.wait("scalar", "vp")
                E.emit("scalar", lambda s: s.activation(jA, tA, AF.Ln), inc="sp")
                E.fence("scalar", "sp")
                E.emit("scalar", lambda s: s.activation(rA, jA, AF.Exp, scale=-0.5), inc="sp")
                E.wait("vector", "sp")
                E.emit("vector", lambda v: v.tensor_scalar(thA, tA, 0.5, None, ALU.mult), inc="vp")
                for _ in range(2):
                    E.fence("vector", "vp")
                    E.emit("vector", lambda v: v.tensor_mul(jA, rA, rA), inc="vp")
                    E.fence("vector", "vp")
                    E.emit("vector", lambda v: v.tensor_mul(jA, jA, thA), inc="vp")
                    E.fence("vector", "vp")
                    E.emit("vector", lambda v: v.tensor_scalar(jA, jA, -1.0, 1.5,
                                                               ALU.mult, op1=ALU.add), inc="vp")
                    E.fence("vector", "vp")
                    E.emit("vector", lambda v: v.tensor_mul(rA, rA, jA), inc="vp")

            allowed = [(n, m) for n in range(NSEQ) for m in range(NSEQ)
                       if not (n < 4 and m >= 4)]

            def stats_chain(fs, fs2, fm, ft, fr):
                E.wait("vector", "sp")
                E.fence("vector", "vp")
                sA = stats[:, :, fs]
                s2A = stats[:, :, fs2]
                mA = stats[:, :, fm]
                tA = stats[:, :, ft]
                rA = stats[:, :, fr]
                E.emit("vector", lambda v: v.tensor_scalar(mA, sA, 1.0 / D, None, ALU.mult), inc="vp")
                E.emit("vector", lambda v: v.tensor_scalar(nstat[:, 0:8], s2A, 1.0 / D,
                                                           None, ALU.mult), inc="vp")
                E.fence("vector", "vp")
                E.emit("vector", lambda v: v.tensor_mul(nstat[:, 8:16], mA, mA), inc="vp")
                E.fence("vector", "vp")
                E.emit("vector", lambda v: v.tensor_sub(tA, nstat[:, 0:8], nstat[:, 8:16]), inc="vp")
                E.fence("vector", "vp")
                E.emit("vector", lambda v: v.tensor_scalar(tA, tA, 1.0, EPS,
                                                           ALU.mult, op1=ALU.add), inc="vp")
                E.wait("scalar", "vp")
                E.emit("scalar", lambda s: s.activation(nstat[:, 0:8], tA, AF.Ln), inc="sp")
                E.fence("scalar", "sp")
                E.emit("scalar", lambda s: s.activation(rA, nstat[:, 0:8], AF.Exp,
                                                        scale=-0.5), inc="sp")
                E.wait("vector", "sp")
                E.emit("vector", lambda v: v.tensor_scalar(nstat[:, 8:16], tA, 0.5,
                                                           None, ALU.mult), inc="vp")
                for _ in range(2):
                    E.fence("vector", "vp")
                    E.emit("vector", lambda v: v.tensor_mul(nstat[:, 0:8], rA, rA), inc="vp")
                    E.fence("vector", "vp")
                    E.emit("vector", lambda v: v.tensor_mul(nstat[:, 0:8], nstat[:, 0:8],
                                                            nstat[:, 8:16]), inc="vp")
                    E.fence("vector", "vp")
                    E.emit("vector", lambda v: v.tensor_scalar(nstat[:, 0:8], nstat[:, 0:8],
                                                               -1.0, 1.5, ALU.mult,
                                                               op1=ALU.add), inc="vp")
                    E.fence("vector", "vp")
                    E.emit("vector", lambda v: v.tensor_mul(rA, rA, nstat[:, 0:8]), inc="vp")
                E.fence("vector", "vp")

            for b in range(limit_b if "A" in phases else 0):
                for n in range(NSEQ):
                    sl, lxs = load_x(n, b)
                    E.wait("vector", lxs)
                    E.wait("scalar", lxs)
                    E.emit("scalar", lambda s, n=n, sl=sl: s.activation(
                        junkS[:], xt[:, sl, :], AF.Copy,
                        accum_out=stats[:, n, FS:FS + 1]), inc="sp")
                    E.emit("vector", lambda v, n=n, sl=sl: v.scalar_tensor_tensor(
                        junk[:], xt[:, sl, :], 1.0, xt[:, sl, :], ALU.mult, ALU.mult,
                        accum_out=stats[:, n, FS2:FS2 + 1]), inc="vp")
                    xslot_free[sl] = (E.val("vp"), E.val("sp"))
                E.wait("vector", "sp")
                E.fence("vector", "vp")
                sA = stats[:, :, FS]
                s2A = stats[:, :, FS2]
                mA = stats[:, :, FM]
                tA = stats[:, :, FT]
                rA = stats[:, :, FR]
                E.emit("vector", lambda v: v.tensor_scalar(mA, sA, 1.0 / D, None, ALU.mult), inc="vp")
                E.emit("vector", lambda v: v.tensor_scalar(nstat[:, 0:8], s2A, 1.0 / D,
                                                           None, ALU.mult), inc="vp")
                E.fence("vector", "vp")
                E.emit("vector", lambda v: v.tensor_mul(nstat[:, 8:16], mA, mA), inc="vp")
                E.fence("vector", "vp")
                E.emit("vector", lambda v: v.tensor_sub(tA, nstat[:, 0:8], nstat[:, 8:16]), inc="vp")
                E.fence("vector", "vp")
                E.emit("vector", lambda v: v.tensor_scalar(tA, tA, 1.0, EPS,
                                                           ALU.mult, op1=ALU.add), inc="vp")
                E.wait("scalar", "vp")
                E.emit("scalar", lambda s: s.activation(nstat[:, 0:8], tA, AF.Ln), inc="sp")
                E.fence("scalar", "sp")
                E.emit("scalar", lambda s: s.activation(rA, nstat[:, 0:8], AF.Exp,
                                                        scale=-0.5), inc="sp")
                E.wait("vector", "sp")
                E.emit("vector", lambda v: v.tensor_scalar(nstat[:, 8:16], tA, 0.5,
                                                           None, ALU.mult), inc="vp")
                for _ in range(2):
                    E.fence("vector", "vp")
                    E.emit("vector", lambda v: v.tensor_mul(nstat[:, 0:8], rA, rA), inc="vp")
                    E.fence("vector", "vp")
                    E.emit("vector", lambda v: v.tensor_mul(nstat[:, 0:8], nstat[:, 0:8],
                                                            nstat[:, 8:16]), inc="vp")
                    E.fence("vector", "vp")
                    E.emit("vector", lambda v: v.tensor_scalar(nstat[:, 0:8], nstat[:, 0:8],
                                                               -1.0, 1.5, ALU.mult,
                                                               op1=ALU.add), inc="vp")
                    E.fence("vector", "vp")
                    E.emit("vector", lambda v: v.tensor_mul(rA, rA, nstat[:, 0:8]), inc="vp")
                E.fence("vector", "vp")
                for n in range(NSEQ):
                    sl, lxs = load_x(n, b)
                    E.wait("vector", lxs)
                    E.wait("vector", "sp", lnslot_free[sl])
                    E.emit("vector", lambda v, n=n, sl=sl: v.tensor_scalar(
                        ln1[:, sl, :], xt[:, sl, :], stats[:, n, FM:FM + 1],
                        stats[:, n, FR:FR + 1], ALU.subtract, op1=ALU.mult), inc="vp")
                    xslot_free[sl] = (E.val("vp"), E.val("sp"))
                    E.wait("tensor", "vp")
                    for k in range(DCH):
                        pt = ptrs[k % 4]
                        E.wait("tensor", "sp", ptr_free[k % 4])
                        E.emit("tensor", lambda t, k=k, pt=pt, sl=sl: t.transpose(
                            pt[:], ln1[:, sl, k * 128:(k + 1) * 128], ident[:]), inc="tp")
                        E.wait("scalar", "tp")
                        E.emit("scalar", lambda s, n=n, k=k, pt=pt: s.copy(
                            xfm[:, n, k, :], pt[:]), inc="sp")
                        ptr_free[k % 4] = E.val("sp")
                    lnslot_free[sl] = E.val("sp")
                for hc in range(2):
                    for cg in range(3):
                        ws = wseq[0] % 2
                        lws = "lwa" if ws == 0 else "lwb"
                        E.wait("sync", "tp", wslot_free[ws])
                        for k in range(DCH):
                            E.emit("sync", lambda s, k=k, cg=cg, hc=hc, ws=ws: s.dma_start(
                                out=wbuf[:, ws, k, :],
                                in_=wqkvT[k * 128:(k + 1) * 128,
                                          cg * D + hc * 512: cg * D + hc * 512 + 512]),
                                inc=lws, n=16)
                        wseq[0] += 1
                        E.wait("tensor", lws)
                        if cg == 0:
                            E.wait("tensor", "sp")
                        for n in range(NSEQ):
                            pm = pmm0 if n % 2 == 0 else pmm1
                            E.wait("tensor", "vp", pmm_free[n % 2])
                            E.emit("tensor", lambda t, cg=cg, hc=hc, pm=pm: t.matmul(
                                pm[:], ones1[:],
                                bqkv_t[0:1, cg * D + hc * 512: cg * D + hc * 512 + 512],
                                start=True, stop=False))
                            for k in range(DCH):
                                E.emit("tensor", lambda t, n=n, k=k, ws=ws, pm=pm: t.matmul(
                                    pm[:], xfm[:, n, k, :], wbuf[:, ws, k, :],
                                    start=False, stop=(k == DCH - 1)),
                                    inc=("tp" if k == DCH - 1 else None))
                            E.wait("vector", "tp")
                            E.emit("vector", lambda v, n=n, cg=cg, pm=pm: v.tensor_copy(
                                ybuf[:, n, cg, :], pm[:]), inc="vp")
                            pmm_free[n % 2] = E.val("vp")
                        wslot_free[ws] = E.val("tp")
                    E.wait("vector", "tp")
                    E.fence("vector", "vp")
                    E.emit("vector", lambda v: v.memset(smat[:, :, 0:4, 4:8], NEG_BIG), inc="vp")
                    for hl in range(8):
                        for (n, m) in allowed:
                            E.emit("vector", lambda v, hl=hl, n=n, m=m: v.scalar_tensor_tensor(
                                junk[:, 0:HD],
                                ybuf[:, n, 0, hl * HD:(hl + 1) * HD], SCALE,
                                ybuf[:, m, 1, hl * HD:(hl + 1) * HD],
                                ALU.mult, ALU.mult,
                                accum_out=smat[:, hl, n, m:m + 1]), inc="vp")
                    E.fence("vector", "vp")
                    E.emit("vector", lambda v: v.reduce_max(
                        mx[:].unsqueeze(2),
                        smat[:].rearrange("p h n m -> p (h n) m"), axis=AX), inc="vp")
                    E.fence("vector", "vp")
                    E.emit("vector", lambda v: v.tensor_tensor(
                        emat[:].rearrange("p h n m -> p (h n) m"),
                        smat[:].rearrange("p h n m -> p (h n) m"),
                        mx[:].unsqueeze(2).to_broadcast([128, 64, 8]), ALU.subtract), inc="vp")
                    E.wait("scalar", "vp")
                    E.emit("scalar", lambda s: s.activation(emat[:], emat[:], AF.Exp), inc="sp")
                    E.wait("vector", "sp")
                    E.emit("vector", lambda v: v.reduce_sum(
                        sm[:].unsqueeze(2),
                        emat[:].rearrange("p h n m -> p (h n) m"), axis=AX), inc="vp")
                    E.fence("vector", "vp")
                    E.emit("vector", lambda v: v.reciprocal(sm[:], sm[:]), inc="vp")
                    E.fence("vector", "vp")
                    E.emit("vector", lambda v: v.tensor_tensor(
                        emat[:].rearrange("p h n m -> p (h n) m"),
                        emat[:].rearrange("p h n m -> p (h n) m"),
                        sm[:].unsqueeze(2).to_broadcast([128, 64, 8]), ALU.mult), inc="vp")
                    E.fence("vector", "vp")
                    for hl in range(8):
                        for n in range(NSEQ):
                            nm = 4 if n < 4 else 8
                            for m in range(nm):
                                E.emit("vector", lambda v, hl=hl, n=n, m=m: v.tensor_scalar(
                                    prods[:, m, :],
                                    ybuf[:, m, 2, hl * HD:(hl + 1) * HD],
                                    emat[:, hl, n, m:m + 1], None, ALU.mult), inc="vp")
                            E.emit("vector", lambda v, hl=hl, n=n, hc=hc, nm=nm: v.reduce_sum(
                                aout[:, n, hc * 512 + hl * HD: hc * 512 + (hl + 1) * HD]
                                .unsqueeze(2),
                                prods[:, 0:nm, :].transpose([0, 2, 1]), axis=AX), inc="vp")
                for cg in range(2):
                    ws = wseq[0] % 2
                    lws = "lwa" if ws == 0 else "lwb"
                    E.wait("sync", "tp", wslot_free[ws])
                    for k in range(DCH):
                        E.emit("sync", lambda s, k=k, cg=cg, ws=ws: s.dma_start(
                            out=wbuf[:, ws, k, :],
                            in_=wprojT[k * 128:(k + 1) * 128, cg * 512:(cg + 1) * 512]),
                            inc=lws, n=16)
                    pslot[cg] = ws
                    pwsem[cg] = lws
                    wseq[0] += 1
                attn_done_vp = E.val("vp")
                for n in range(NSEQ):
                    E.wait("tensor", "vp", attn_done_vp)
                    for k in range(DCH):
                        pt = ptrs[k % 4]
                        E.wait("tensor", "sp", ptr_free[k % 4])
                        E.emit("tensor", lambda t, n=n, k=k, pt=pt: t.transpose(
                            pt[:], aout[:, n, k * 128:(k + 1) * 128], ident[:]), inc="tp")
                        E.wait("scalar", "tp")
                        E.emit("scalar", lambda s, n=n, k=k, pt=pt: s.copy(
                            xfm[:, n, k, :], pt[:]), inc="sp")
                        ptr_free[k % 4] = E.val("sp")
                    E.wait("tensor", pwsem[0])
                    E.wait("tensor", pwsem[1])
                    E.wait("tensor", "sp")
                    E.wait("tensor", "vp", max(pmm_free[0], pmm_free[1]))
                    for cg in range(2):
                        pm = pmm0 if cg == 0 else pmm1
                        E.emit("tensor", lambda t, cg=cg, pm=pm: t.matmul(
                            pm[:], ones1[:], bproj_t[0:1, cg * 512:(cg + 1) * 512],
                            start=True, stop=False))
                        for k in range(DCH):
                            E.emit("tensor", lambda t, n=n, k=k, cg=cg, pm=pm: t.matmul(
                                pm[:], xfm[:, n, k, :], wbuf[:, pslot[cg], k, :],
                                start=False, stop=(k == DCH - 1)),
                                inc=("tp" if k == DCH - 1 else None))
                    E.wait("vector", "tp")
                    E.emit("vector", lambda v: v.tensor_copy(junk[:, 0:512], pmm0[:]), inc="vp")
                    E.emit("vector", lambda v: v.tensor_copy(junk[:, 512:1024], pmm1[:]), inc="vp")
                    pmm_free[0] = E.val("vp")
                    pmm_free[1] = E.val("vp")
                    E.emit("vector", lambda v: v.reduce_sum(nstat[:, 0:1], junk[:], axis=AX), inc="vp")
                    E.emit("vector", lambda v: v.scalar_tensor_tensor(
                        junkS[:], junk[:], 1.0, junk[:], ALU.mult, ALU.mult,
                        accum_out=nstat[:, 4:5]), inc="vp")
                    E.fence("vector", "vp")
                    E.emit("vector", lambda v: v.tensor_scalar(nstat[:, 8:9], nstat[:, 0:1],
                                                               1.0 / D, None, ALU.mult), inc="vp")
                    E.emit("vector", lambda v: v.tensor_scalar(nstat[:, 9:10], nstat[:, 4:5],
                                                               1.0 / D, None, ALU.mult), inc="vp")
                    E.fence("vector", "vp")
                    E.emit("vector", lambda v: v.tensor_mul(nstat[:, 10:11], nstat[:, 8:9],
                                                            nstat[:, 8:9]), inc="vp")
                    E.fence("vector", "vp")
                    E.emit("vector", lambda v: v.tensor_sub(nstat[:, 11:12], nstat[:, 9:10],
                                                            nstat[:, 10:11]), inc="vp")
                    E.fence("vector", "vp")
                    E.emit("vector", lambda v: v.tensor_scalar(nstat[:, 11:12], nstat[:, 11:12],
                                                               1.0, EPS, ALU.mult,
                                                               op1=ALU.add), inc="vp")
                    rsqrt_chain(11, 12, 13, 14, 1)
                    sl, lxs = load_x(n, b)
                    xp = len(x2_gp) % 2
                    gps = "gpa" if xp == 0 else "gpb"
                    if len(x2_gp) >= 2:
                        E.wait("vector", gps, x2_gp[-2])
                    E.fence("vector", "vp")
                    E.emit("vector", lambda v, xp=xp: v.tensor_scalar(
                        x2t[:, xp, :], junk[:], nstat[:, 8:9], nstat[:, 12:13],
                        ALU.subtract, op1=ALU.mult), inc="vp")
                    E.wait("vector", lxs)
                    E.emit("vector", lambda v, sl=sl, xp=xp: v.tensor_tensor(
                        x2t[:, xp, :], x2t[:, xp, :], xt[:, sl, :], ALU.add), inc="vp")
                    xslot_free[sl] = (E.val("vp"), E.val("sp"))
                    rt = 4 * n + b
                    E.wait("gpsimd", "vp")
                    E.emit("gpsimd", lambda g, rt=rt, xp=xp: g.dma_start(
                        out=x2d[rt], in_=x2t[:, xp, :]), inc=gps, n=16)
                    x2_gp.append(E.val(gps))
                wslot_free[pslot[0]] = E.val("tp")
                wslot_free[pslot[1]] = E.val("tp")

            def barrier():
                for eng in ENGINES:
                    for s in ("tp", "vp", "sp", "lxa", "lxb", "lwa", "lwb",
                              "gpa", "gpb", "so", "cc"):
                        E.wait(eng, s)

            def greduce(op, incol, outcol, rowslot):
                E.wait("tensor", "vp")
                E.emit("tensor", lambda t: t.transpose(ptr0[0:1, :], incol, ident[:]), inc="tp")
                E.wait("vector", "tp")
                E.emit("vector", lambda v: v.tensor_copy(junk[0:1, 0:128], ptr0[0:1, 0:128]), inc="vp")
                E.fence("vector", "vp")
                E.emit("vector", lambda v: v.tensor_reduce(
                    ccrow[0:1, rowslot:rowslot + 1], junk[0:1, 0:128], AX, op), inc="vp")
                E.wait("gpsimd", "vp")
                E.emit("gpsimd", lambda g: g.dma_start(
                    out=cc_in[0:1, 0:1], in_=ccrow[0:1, rowslot:rowslot + 1]), inc="gpa", n=16)
                E.wait("gpsimd", "gpa")
                E.emit("gpsimd", lambda g: g.collective_compute(
                    "AllReduce", op, ins=[cc_in[:]], outs=[cc_out[:]],
                    replica_groups=[[0, 1, 2, 3, 4, 5, 6, 7]]), inc="cc")
                E.wait("gpsimd", "cc")
                E.emit("gpsimd", lambda g: g.dma_start(
                    out=ccrow[0:1, rowslot + 1:rowslot + 2], in_=cc_out[0:1, 0:1]), inc="gpb", n=16)
                E.wait("tensor", "gpb")
                E.wait("tensor", "vp")
                E.emit("tensor", lambda t: t.matmul(
                    ptr1[:, 0:1], ones1[:], ccrow[0:1, rowslot + 1:rowslot + 2],
                    start=True, stop=True), inc="tp")
                E.wait("vector", "tp")
                E.emit("vector", lambda v: v.tensor_copy(outcol, ptr1[:, 0:1]), inc="vp")
                E.fence("vector", "vp")

            def lreduce(op, incol, outcol, rowslot):
                E.wait("tensor", "vp")
                E.emit("tensor", lambda t: t.transpose(ptr0[0:1, :], incol, ident[:]), inc="tp")
                E.wait("vector", "tp")
                E.emit("vector", lambda v: v.tensor_copy(junk[0:1, 0:128], ptr0[0:1, 0:128]), inc="vp")
                E.fence("vector", "vp")
                E.emit("vector", lambda v: v.tensor_reduce(
                    ccrow[0:1, rowslot:rowslot + 1], junk[0:1, 0:128], AX, op), inc="vp")
                E.wait("tensor", "vp")
                E.emit("tensor", lambda t: t.matmul(
                    ptr1[:, 0:1], ones1[:], ccrow[0:1, rowslot:rowslot + 1],
                    start=True, stop=True), inc="tp")
                E.wait("vector", "tp")
                E.emit("vector", lambda v: v.tensor_copy(outcol, ptr1[:, 0:1]), inc="vp")
                E.fence("vector", "vp")

            def bcol(i):
                return bst[:, i:i + 1]

            def bis_select():
                E.fence("vector", "vp")
                E.emit("vector", lambda v: v.tensor_tensor(bcol(5), bcol(4), bcol(6),
                                                           ALU.is_ge), inc="vp")
                E.fence("vector", "vp")
                E.emit("vector", lambda v: v.tensor_sub(bcol(20), bcol(2), bcol(0)), inc="vp")
                E.emit("vector", lambda v: v.tensor_sub(bcol(21), bcol(2), bcol(1)), inc="vp")
                E.fence("vector", "vp")
                E.emit("vector", lambda v: v.tensor_mul(bcol(20), bcol(20), bcol(5)), inc="vp")
                E.fence("vector", "vp")
                E.emit("vector", lambda v: v.tensor_add(bcol(0), bcol(0), bcol(20)), inc="vp")
                E.emit("vector", lambda v: v.tensor_sub(bcol(21), bcol(1), bcol(2)), inc="vp")
                E.fence("vector", "vp")
                E.emit("vector", lambda v: v.tensor_mul(bcol(21), bcol(21), bcol(5)), inc="vp")
                E.fence("vector", "vp")
                E.emit("vector", lambda v: v.tensor_add(bcol(1), bcol(2), bcol(21)), inc="vp")
                E.fence("vector", "vp")

            if "B" in phases:
                barrier()
                E.emit("vector", lambda v: v.memset(bst[:], 0.0), inc="vp")
                E.emit("vector", lambda v: v.memset(cnts[:], 0.0), inc="vp")
                E.emit("vector", lambda v: v.memset(ccrow[:], 0.0), inc="vp")
                E.fence("vector", "vp")
                FS, FS2, FM, FT, FR = 0, 1, 2, 3, 4
                FU, FU2, FM3 = 5, 6, 7
                gbc = overlay("gbc", [128, D], F32, 131072)
                bbc = overlay("bbc", [128, D], F32, 135168)
                jI8 = overlay("jI8", [128, 8192], mybir.dt.int8, 139264)
                for half in range(2):
                    E.emit("sync", lambda s, half=half: s.dma_start(
                        out=ln1[0:1, 0, 0:512],
                        in_=ln_g[0:1, half * 512:(half + 1) * 512]), inc="lb", n=16)
                    E.emit("sync", lambda s, half=half: s.dma_start(
                        out=ln1[0:1, 1, 0:512],
                        in_=ln_b[0:1, half * 512:(half + 1) * 512]), inc="lb", n=16)
                    E.wait("tensor", "lb")
                    E.wait("tensor", "vp")
                    E.emit("tensor", lambda t: t.matmul(
                        pmm0[:], ones1[:], ln1[0:1, 0, 0:512],
                        start=True, stop=True), inc="tp")
                    E.emit("tensor", lambda t: t.matmul(
                        pmm1[:], ones1[:], ln1[0:1, 1, 0:512],
                        start=True, stop=True), inc="tp")
                    E.wait("vector", "tp")
                    E.emit("vector", lambda v, half=half: v.tensor_copy(
                        gbc[:, half * 512:(half + 1) * 512], pmm0[:]), inc="vp")
                    E.emit("vector", lambda v, half=half: v.tensor_copy(
                        bbc[:, half * 512:(half + 1) * 512], pmm1[:]), inc="vp")
                E.fence("vector", "vp")
                NW1CH = (D // 128) * (HID // 512)
                for ch in range(NW1CH):
                    kd, hcol = ch // 8, ch % 8
                    ws = ch % 2
                    lws = "lwa" if ws == 0 else "lwb"
                    E.wait("sync", "sp", wslot_free.get(("g1", ws), 0))
                    E.emit("sync", lambda s, kd=kd, hcol=hcol, ws=ws: s.dma_start(
                        out=wstage[:, ws, :],
                        in_=w1T[kd * 128:(kd + 1) * 128, hcol * 512:(hcol + 1) * 512]),
                        inc=lws, n=16)
                    E.wait("scalar", lws)
                    E.emit("scalar", lambda s, ch=ch, ws=ws: s.activation(
                        junkS[:, 0:512], wstage[:, ws, :], AF.Abs,
                        accum_out=cnts[:, ch % 64:ch % 64 + 1]), inc="sp")
                    wslot_free[("g1", ws)] = E.val("sp")
                E.wait("vector", "sp")
                E.emit("vector", lambda v: v.reduce_sum(bcol(6), cnts[:, 0:64], axis=AX), inc="vp")
                lreduce(ALU.add, bcol(6), bcol(19), 0)
                E.emit("vector", lambda v: v.tensor_scalar(
                    bcol(19), bcol(19), 1.0 / (D * HID), EPS, ALU.mult, op1=ALU.add), inc="vp")
                ln2slot = {0: (0, 0), 1: (0, 0)}
                for g8 in range(NT // 8):
                    for j in range(8):
                        rt = g8 * 8 + j
                        sl = rt % 2
                        lxs = "lxa" if sl == 0 else "lxb"
                        vv, ss = xslot_free[sl]
                        E.wait("sync", "vp", vv)
                        E.wait("sync", "sp", ss)
                        E.emit("sync", lambda s, rt=rt, sl=sl: s.dma_start(
                            out=xt[:, sl, :], in_=x2d[rt]), inc=lxs, n=16)
                        E.wait("vector", lxs)
                        E.wait("scalar", lxs)
                        E.emit("scalar", lambda s, j=j, sl=sl: s.activation(
                            junkS[:], xt[:, sl, :], AF.Copy,
                            accum_out=stats[:, j, FS:FS + 1]), inc="sp")
                        E.emit("vector", lambda v, j=j, sl=sl: v.scalar_tensor_tensor(
                            junk[:], xt[:, sl, :], 1.0, xt[:, sl, :], ALU.mult, ALU.mult,
                            accum_out=stats[:, j, FS2:FS2 + 1]), inc="vp")
                        xslot_free[sl] = (E.val("vp"), E.val("sp"))
                    stats_chain(FS, FS2, FM, FT, FR)
                    for j in range(8):
                        rt = g8 * 8 + j
                        sl = rt % 2
                        lxs = "lxa" if sl == 0 else "lxb"
                        vv, ss = xslot_free[sl]
                        E.wait("sync", "vp", vv)
                        E.wait("sync", "sp", ss)
                        E.emit("sync", lambda s, rt=rt, sl=sl: s.dma_start(
                            out=xt[:, sl, :], in_=x2d[rt]), inc=lxs, n=16)
                        E.wait("vector", lxs)
                        lv, ls = ln2slot[sl]
                        E.wait("vector", "vp", lv)
                        E.wait("vector", "sp", ls)
                        E.emit("vector", lambda v, j=j, sl=sl: v.tensor_scalar(
                            ln1[:, sl, :], xt[:, sl, :], stats[:, j, FM:FM + 1],
                            stats[:, j, FR:FR + 1], ALU.subtract, op1=ALU.mult), inc="vp")
                        E.wait("scalar", "vp")
                        E.emit("scalar", lambda s, j=j, sl=sl: s.activation(
                            junkS[:], ln1[:, sl, :], AF.Copy,
                            accum_out=stats[:, j, FU:FU + 1]), inc="sp")
                        E.emit("vector", lambda v, j=j, sl=sl: v.scalar_tensor_tensor(
                            junk[:], ln1[:, sl, :], 1.0, ln1[:, sl, :], ALU.mult, ALU.mult,
                            accum_out=stats[:, j, FU2:FU2 + 1]), inc="vp")
                        xslot_free[sl] = (E.val("vp"), E.val("sp"))
                        ln2slot[sl] = (E.val("vp"), E.val("sp"))
                    stats_chain(FU, FU2, FM3, FT, FU2)
                    for j in range(8):
                        rt = g8 * 8 + j
                        sl = rt % 2
                        lxs = "lxa" if sl == 0 else "lxb"
                        vv, ss = xslot_free[sl]
                        E.wait("sync", "vp", vv)
                        E.wait("sync", "sp", ss)
                        E.emit("sync", lambda s, rt=rt, sl=sl: s.dma_start(
                            out=xt[:, sl, :], in_=x2d[rt]), inc=lxs, n=16)
                        E.wait("vector", lxs)
                        lv, ls = ln2slot[sl]
                        E.wait("vector", "sp", ls)
                        E.emit("vector", lambda v, j=j, sl=sl: v.tensor_scalar(
                            ln1[:, sl, :], xt[:, sl, :], stats[:, j, FM:FM + 1],
                            stats[:, j, FR:FR + 1], ALU.subtract, op1=ALU.mult), inc="vp")
                        E.emit("vector", lambda v, j=j, sl=sl: v.tensor_scalar(
                            ln1[:, sl, :], ln1[:, sl, :], stats[:, j, FM3:FM3 + 1],
                            stats[:, j, FU2:FU2 + 1], ALU.subtract, op1=ALU.mult), inc="vp")
                        E.emit("vector", lambda v, sl=sl: v.tensor_mul(
                            ln1[:, sl, :], ln1[:, sl, :], gbc[:]), inc="vp")
                        E.emit("vector", lambda v, rt=rt, sl=sl: v.tensor_add(
                            h1[:, rt, :], ln1[:, sl, :], bbc[:]), inc="vp")
                        xslot_free[sl] = (E.val("vp"), E.val("sp"))
                        E.emit("vector", lambda v, rt=rt: v.reduce_max(
                            bcol(6), h1[:, rt, :], axis=AX,
                            apply_absolute_value=True), inc="vp")
                        E.fence("vector", "vp")
                        if rt == 0:
                            E.emit("vector", lambda v: v.tensor_copy(bcol(7), bcol(6)), inc="vp")
                        else:
                            E.emit("vector", lambda v: v.tensor_max(bcol(7), bcol(7), bcol(6)), inc="vp")
                greduce(ALU.max, bcol(7), bcol(17), 0)
                E.emit("vector", lambda v: v.memset(bcol(10), float(NSEQ * 4096 * D)), inc="vp")
                E.fence("vector", "vp")
                E.emit("vector", lambda v: v.tensor_scalar(bcol(10), bcol(10), -1.0,
                                                           None, ALU.add), inc="vp")
                E.fence("vector", "vp")
                E.emit("vector", lambda v: v.tensor_scalar(bcol(10), bcol(10),
                                                           float(np.float32(0.995)),
                                                           None, ALU.mult), inc="vp")
                E.fence("vector", "vp")
                E.emit("vector", lambda v: v.tensor_copy(x2t[0:128, 0, 0:1].bitcast(I32), bcol(10)), inc="vp")
                E.fence("vector", "vp")
                E.emit("vector", lambda v: v.tensor_copy(bcol(6), x2t[0:128, 0, 0:1].bitcast(I32)), inc="vp")
                E.fence("vector", "vp")
                E.emit("vector", lambda v: v.tensor_tensor(bcol(20), bcol(6), bcol(10), ALU.is_gt), inc="vp")
                E.emit("vector", lambda v: v.tensor_tensor(bcol(21), bcol(6), bcol(10), ALU.is_lt), inc="vp")
                E.fence("vector", "vp")
                E.emit("vector", lambda v: v.tensor_sub(bcol(20), bcol(6), bcol(20)), inc="vp")
                E.emit("vector", lambda v: v.tensor_add(bcol(21), bcol(6), bcol(21)), inc="vp")
                E.fence("vector", "vp")
                E.emit("vector", lambda v: v.tensor_sub(bcol(11), bcol(10), bcol(20)), inc="vp")
                E.fence("vector", "vp")
                E.emit("vector", lambda v: v.tensor_scalar(bcol(12), bcol(11), -1.0, 1.0,
                                                           ALU.mult, op1=ALU.add), inc="vp")
                E.emit("vector", lambda v: v.memset(bcol(6), float(NSEQ * 4096 * D)), inc="vp")
                E.fence("vector", "vp")
                E.emit("vector", lambda v: v.tensor_scalar(bcol(6), bcol(6), -1.0,
                                                           None, ALU.add), inc="vp")
                E.fence("vector", "vp")
                E.emit("vector", lambda v: v.tensor_sub(bcol(8), bcol(6), bcol(21)), inc="vp")
                E.emit("vector", lambda v: v.tensor_sub(bcol(9), bcol(6), bcol(20)), inc="vp")
                E.fence("vector", "vp")
                E.emit("vector", lambda v: v.tensor_scalar(bcol(8), bcol(8), 1.0, 1.0,
                                                           ALU.mult, op1=ALU.add), inc="vp")
                E.emit("vector", lambda v: v.tensor_scalar(bcol(9), bcol(9), 1.0, 1.0,
                                                           ALU.mult, op1=ALU.add), inc="vp")
                E.emit("vector", lambda v: v.memset(bcol(0), -1.0), inc="vp")
                E.fence("vector", "vp")
                E.emit("vector", lambda v: v.tensor_copy(bcol(1), bcol(17)), inc="vp")
                h1flat = h1[:].rearrange("p a b -> p (a b)")
                def count_pass_h1(taucol, cmp1, cmp2, dstcol):
                    E.fence("vector", "vp")
                    E.emit("vector", lambda v: v.tensor_scalar(bcol(22), taucol, -1.0,
                                                               None, ALU.mult), inc="vp")
                    E.fence("vector", "vp")
                    for q in range(4):
                        E.emit("vector", lambda v, q=q: v.tensor_scalar(
                            jI8[:], h1flat[:, q * 8192:(q + 1) * 8192], taucol, None,
                            cmp1, op1=ALU.add, accum_out=cnts[:, q:q + 1]), inc="vp")
                        E.emit("vector", lambda v, q=q: v.tensor_scalar(
                            jI8[:], h1flat[:, q * 8192:(q + 1) * 8192], bcol(22), None,
                            cmp2, op1=ALU.add, accum_out=cnts[:, 4 + q:5 + q]), inc="vp")
                    E.fence("vector", "vp")
                    E.emit("vector", lambda v: v.reduce_sum(bcol(25), cnts[:, 0:8], axis=AX), inc="vp")
                    greduce(ALU.add, bcol(25), dstcol, 0)
                R1 = 30
                for rnd in range(R1):
                    E.fence("vector", "vp")
                    E.emit("vector", lambda v: v.tensor_add(bcol(2), bcol(0), bcol(1)), inc="vp")
                    E.fence("vector", "vp")
                    E.emit("vector", lambda v: v.tensor_scalar(bcol(2), bcol(2), 0.5,
                                                               None, ALU.mult), inc="vp")
                    count_pass_h1(bcol(2), ALU.is_gt, ALU.is_lt, bcol(4))
                    E.fence("vector", "vp")
                    E.emit("vector", lambda v: v.tensor_copy(bcol(6), bcol(8)), inc="vp")
                    bis_select()
                E.emit("vector", lambda v: v.tensor_copy(bcol(13), bcol(1)), inc="vp")
                count_pass_h1(bcol(13), ALU.is_ge, ALU.is_le, bcol(4))
                for rt in range(NT):
                    E.wait("scalar", "vp")
                    E.emit("scalar", lambda s, rt=rt: s.activation(
                        junkS[:], h1flat[:, rt * D:(rt + 1) * D], AF.Abs), inc="sp")
                    E.wait("vector", "sp")
                    E.emit("vector", lambda v: v.scalar_tensor_tensor(
                        junk[:], junkS[:], bcol(13), junkS[:], ALU.is_lt, ALU.mult), inc="vp")
                    E.fence("vector", "vp")
                    E.emit("vector", lambda v: v.reduce_max(bcol(6), junk[:, 0:D], axis=AX), inc="vp")
                    E.fence("vector", "vp")
                    if rt == 0:
                        E.emit("vector", lambda v: v.tensor_copy(bcol(7), bcol(6)), inc="vp")
                    else:
                        E.emit("vector", lambda v: v.tensor_max(bcol(7), bcol(7), bcol(6)), inc="vp")
                greduce(ALU.max, bcol(7), bcol(14), 0)
                E.fence("vector", "vp")
                E.emit("vector", lambda v: v.tensor_tensor(bcol(5), bcol(4), bcol(9), ALU.is_ge), inc="vp")
                E.fence("vector", "vp")
                E.emit("vector", lambda v: v.tensor_sub(bcol(20), bcol(13), bcol(14)), inc="vp")
                E.fence("vector", "vp")
                E.emit("vector", lambda v: v.tensor_mul(bcol(20), bcol(20), bcol(5)), inc="vp")
                E.fence("vector", "vp")
                E.emit("vector", lambda v: v.tensor_add(bcol(14), bcol(14), bcol(20)), inc="vp")
                E.fence("vector", "vp")
                E.emit("vector", lambda v: v.tensor_mul(bcol(20), bcol(14), bcol(12)), inc="vp")
                E.emit("vector", lambda v: v.tensor_mul(bcol(21), bcol(13), bcol(11)), inc="vp")
                E.fence("vector", "vp")
                E.emit("vector", lambda v: v.tensor_add(bcol(15), bcol(20), bcol(21)), inc="vp")
                E.fence("vector", "vp")
                E.emit("vector", lambda v: v.tensor_scalar(bcol(15), bcol(15), EPS,
                                                           None, ALU.max), inc="vp")
                E.fence("vector", "vp")
                E.emit("vector", lambda v: v.reciprocal(bcol(16), bcol(15)), inc="vp")
                E.fence("vector", "vp")
                E.emit("vector", lambda v: v.tensor_scalar(bcol(16), bcol(16), 127.0,
                                                           None, ALU.mult), inc="vp")
                E.fence("vector", "vp")
                if st_dbg is not None:
                    E.wait("gpsimd", "vp")
                    E.emit("gpsimd", lambda g: g.dma_start(out=st_dbg, in_=bst[:]), inc="gpa", n=16)
                if h1_dbg is not None:
                    for rt in range(NT):
                        E.wait("gpsimd", "vp")
                        E.emit("gpsimd", lambda g, rt=rt: g.dma_start(
                            out=h1_dbg[rt], in_=h1[:, rt, :]), inc="gpa", n=16)

            def quantile_targets(nval, hwcol, lwcol, t1col, t2col):
                E.emit("vector", lambda v: v.memset(bcol(10), float(nval)), inc="vp")
                E.fence("vector", "vp")
                E.emit("vector", lambda v: v.tensor_scalar(bcol(10), bcol(10), -1.0,
                                                           None, ALU.add), inc="vp")
                E.fence("vector", "vp")
                E.emit("vector", lambda v: v.tensor_scalar(bcol(10), bcol(10),
                                                           float(np.float32(0.995)),
                                                           None, ALU.mult), inc="vp")
                E.fence("vector", "vp")
                E.emit("vector", lambda v: v.tensor_copy(x2t[0:128, 0, 0:1].bitcast(I32), bcol(10)), inc="vp")
                E.fence("vector", "vp")
                E.emit("vector", lambda v: v.tensor_copy(bcol(6), x2t[0:128, 0, 0:1].bitcast(I32)), inc="vp")
                E.fence("vector", "vp")
                E.emit("vector", lambda v: v.tensor_tensor(bcol(20), bcol(6), bcol(10), ALU.is_gt), inc="vp")
                E.emit("vector", lambda v: v.tensor_tensor(bcol(21), bcol(6), bcol(10), ALU.is_lt), inc="vp")
                E.fence("vector", "vp")
                E.emit("vector", lambda v: v.tensor_sub(bcol(20), bcol(6), bcol(20)), inc="vp")
                E.emit("vector", lambda v: v.tensor_add(bcol(21), bcol(6), bcol(21)), inc="vp")
                E.fence("vector", "vp")
                E.emit("vector", lambda v: v.tensor_sub(hwcol, bcol(10), bcol(20)), inc="vp")
                E.fence("vector", "vp")
                E.emit("vector", lambda v: v.tensor_scalar(lwcol, hwcol, -1.0, 1.0,
                                                           ALU.mult, op1=ALU.add), inc="vp")
                E.emit("vector", lambda v: v.memset(bcol(6), float(nval)), inc="vp")
                E.fence("vector", "vp")
                E.emit("vector", lambda v: v.tensor_scalar(bcol(6), bcol(6), -1.0,
                                                           None, ALU.add), inc="vp")
                E.fence("vector", "vp")
                E.emit("vector", lambda v: v.tensor_sub(t1col, bcol(6), bcol(21)), inc="vp")
                E.emit("vector", lambda v: v.tensor_sub(t2col, bcol(6), bcol(20)), inc="vp")
                E.fence("vector", "vp")
                E.emit("vector", lambda v: v.tensor_scalar(t1col, t1col, 1.0, 1.0,
                                                           ALU.mult, op1=ALU.add), inc="vp")
                E.emit("vector", lambda v: v.tensor_scalar(t2col, t2col, 1.0, 1.0,
                                                           ALU.mult, op1=ALU.add), inc="vp")
                E.fence("vector", "vp")

            if "C" in phases:
                barrier()
                E.emit("vector", lambda v: v.reciprocal(bcol(26), bcol(19)), inc="vp")
                E.fence("vector", "vp")
                for rt in range(NT):
                    sl = rt % 2
                    E.emit("vector", lambda v, rt=rt, sl=sl: v.tensor_scalar(
                        ln1[:, sl, :], h1[:, rt, :], bcol(16), None, ALU.mult), inc="vp")
                    E.emit("vector", lambda v, sl=sl: v.tensor_copy(
                        x2t[:, sl, :].bitcast(I32), ln1[:, sl, :]), inc="vp")
                    E.emit("vector", lambda v, rt=rt, sl=sl: v.tensor_scalar(
                        u1tok[:, rt, :], x2t[:, sl, :].bitcast(I32), -128.0, 127.0,
                        ALU.max, op1=ALU.min), inc="vp")
                wslot2 = {0: (0, 0), 1: (0, 0)}
                for ch in range(64):
                    kd, hcol = ch // 8, ch % 8
                    ws = ch % 2
                    lws = "lwa" if ws == 0 else "lwb"
                    vv, _ = wslot2[ws]
                    E.wait("sync", "vp", vv)
                    E.emit("sync", lambda s, kd=kd, hcol=hcol, ws=ws: s.dma_start(
                        out=wstage[:, ws, :],
                        in_=w1T[kd * 128:(kd + 1) * 128, hcol * 512:(hcol + 1) * 512]),
                        inc=lws, n=16)
                    E.wait("vector", lws)
                    E.emit("vector", lambda v, ws=ws: v.tensor_scalar(
                        junk[:, 0:512], wstage[:, ws, :], bcol(26), None, ALU.mult), inc="vp")
                    wslot2[ws] = (E.val("vp"), 0)
                    E.emit("vector", lambda v: v.tensor_copy(
                        x2t[:, 0, 0:512].bitcast(I32), junk[:, 0:512]), inc="vp")
                    E.emit("vector", lambda v, kd=kd, hcol=hcol: v.tensor_scalar(
                        t1[:, kd, hcol * 512:(hcol + 1) * 512],
                        x2t[:, 0, 0:512].bitcast(I32), -1.0, 1.0,
                        ALU.max, op1=ALU.min), inc="vp")
                zslot = {0: (0, 0), 1: (0, 0)}
                zseq = [0]
                E.emit("vector", lambda v: v.memset(bcol(27), -3.0e38), inc="vp")
                E.fence("vector", "vp")
                conv_done_vp = E.val("vp")
                for rt in range(NT):
                    E.wait("tensor", "vp", conv_done_vp)
                    for k in range(DCH):
                        pt = ptrs[k % 4]
                        E.wait("tensor", "sp", ptr_free[k % 4])
                        E.emit("tensor", lambda t, rt=rt, k=k, pt=pt: t.transpose(
                            pt[:].bitcast(BF16)[:, 0:128],
                            u1tok[:, rt, k * 128:(k + 1) * 128], identb[:]), inc="tp")
                        E.wait("scalar", "tp")
                        E.emit("scalar", lambda s, k=k, pt=pt: s.copy(
                            u1f[:, k, :], pt[:].bitcast(BF16)[:, 0:128]), inc="sp")
                        ptr_free[k % 4] = E.val("sp")
                    E.wait("tensor", "sp")
                    for hcol in range(DCH):
                        pm = pmm0 if hcol % 2 == 0 else pmm1
                        E.wait("tensor", "sp", pmmS_free[hcol % 2])
                        for kd in range(DCH):
                            E.emit("tensor", lambda t, kd=kd, hcol=hcol, pm=pm: t.matmul(
                                pm[:], u1f[:, kd, :], t1[:, kd, hcol * 512:(hcol + 1) * 512],
                                start=(kd == 0), stop=(kd == DCH - 1)),
                                inc=("tp" if kd == DCH - 1 else None))
                        zs = zseq[0] % 2
                        gps = "gpa" if zs == 0 else "gpb"
                        gv, vv = zslot[zs]
                        E.wait("scalar", "tp")
                        E.wait("scalar", gps, gv)
                        E.wait("scalar", "vp", vv)
                        E.emit("scalar", lambda s, zs=zs, pm=pm: s.copy(
                            zstage[:, zs, :], pm[:]), inc="sp")
                        pmmS_free[hcol % 2] = E.val("sp")
                        E.wait("vector", "sp")
                        E.emit("vector", lambda v, zs=zs: v.reduce_max(
                            bcol(6), zstage[:, zs, :], axis=AX), inc="vp")
                        E.fence("vector", "vp")
                        E.emit("vector", lambda v: v.tensor_max(bcol(27), bcol(27), bcol(6)), inc="vp")
                        E.wait("gpsimd", "sp")
                        E.emit("gpsimd", lambda g, rt=rt, hcol=hcol, zs=zs: g.dma_start(
                            out=zd[rt, :, hcol * 512:(hcol + 1) * 512],
                            in_=zstage[:, zs, :]), inc=gps, n=16)
                        zslot[zs] = (E.val(gps), E.val("vp"))
                        zseq[0] += 1
                greduce(ALU.max, bcol(27), bcol(28), 0)

            if "C" in phases:
                barrier()
                c8flat = c8[:].rearrange("p a b -> p (a b)")
                E.emit("vector", lambda v: v.reciprocal(bcol(29), bcol(28)), inc="vp")
                E.fence("vector", "vp")
                E.emit("vector", lambda v: v.tensor_scalar(bcol(29), bcol(29), 127.0,
                                                           None, ALU.mult), inc="vp")
                E.fence("vector", "vp")
                zqslot = {0: 0, 1: 0}
                for rt in range(NT):
                    for qq in range(4):
                        zs = (rt * 4 + qq) % 2
                        lws = "lwa" if zs == 0 else "lwb"
                        qeng = "sync" if zs == 0 else "gpsimd"
                        E.wait(qeng, "vp", zqslot[zs])
                        E.emit(qeng, lambda s, rt=rt, qq=qq, zs=zs: s.dma_start(
                            out=zrt2[:, zs, :],
                            in_=zd[rt, :, qq * D:(qq + 1) * D]), inc=lws, n=16)
                        E.wait("vector", lws)
                        E.emit("vector", lambda v, zs=zs: v.tensor_scalar(
                            x2t[:, 0, :], zrt2[:, zs, :],
                            bcol(29), -127.0, ALU.mult, op1=ALU.max), inc="vp")
                        zqslot[zs] = E.val("vp")
                        E.emit("vector", lambda v, rt=rt, qq=qq: v.tensor_scalar(
                            c8flat[:, rt * HID + qq * D: rt * HID + (qq + 1) * D],
                            x2t[:, 0, :], 127.0, None,
                            ALU.min), inc="vp")
                def count_c8(taucol, dstcol, cmp=ALU.is_gt):
                    E.fence("vector", "vp")
                    off = -0.5 if cmp in (ALU.is_gt,) else 0.5
                    E.emit("vector", lambda v: v.tensor_scalar(
                        nstat[:, 13:14], taucol, -1.0, off, ALU.mult, op1=ALU.add), inc="vp")
                    E.fence("vector", "vp")
                    for q in range(8):
                        E.emit("vector", lambda v, q=q: v.tensor_scalar(
                            jI8[:], c8flat[:, q * 8192:(q + 1) * 8192], taucol, None,
                            cmp, op1=ALU.add, accum_out=cnts[:, q:q + 1]), inc="vp")
                    E.wait("scalar", "vp")
                    for q in range(8, 16):
                        E.emit("scalar", lambda s, q=q: s.activation(
                            jI8[:], c8flat[:, q * 8192:(q + 1) * 8192], AF.Sign,
                            bias=nstat[:, 13:14], accum_out=cnts[:, q:q + 1]), inc="sp")
                    E.wait("vector", "sp")
                    E.fence("vector", "vp")
                    E.emit("vector", lambda v: v.reduce_sum(bcol(25), cnts[:, 0:8], axis=AX), inc="vp")
                    E.emit("vector", lambda v: v.reduce_sum(bcol(22), cnts[:, 8:16], axis=AX), inc="vp")
                    E.fence("vector", "vp")
                    E.emit("vector", lambda v: v.tensor_scalar(bcol(22), bcol(22), 0.5,
                                                               32768.0, ALU.mult, op1=ALU.add), inc="vp")
                    E.fence("vector", "vp")
                    E.emit("vector", lambda v: v.tensor_add(bcol(25), bcol(25), bcol(22)), inc="vp")
                    greduce(ALU.add, bcol(25), dstcol, 0)

                def int_bisect(tgtcol, rounds=9):
                    E.emit("vector", lambda v: v.memset(bcol(0), -128.0), inc="vp")
                    E.emit("vector", lambda v: v.memset(bcol(1), 127.0), inc="vp")
                    for rnd in range(rounds):
                        E.fence("vector", "vp")
                        E.emit("vector", lambda v: v.tensor_add(bcol(2), bcol(0), bcol(1)), inc="vp")
                        E.fence("vector", "vp")
                        E.emit("vector", lambda v: v.tensor_scalar(bcol(2), bcol(2), 0.5,
                                                                   None, ALU.mult), inc="vp")
                        E.fence("vector", "vp")
                        E.emit("vector", lambda v: v.tensor_copy(x2t[0:128, 0, 0:1].bitcast(I32), bcol(2)), inc="vp")
                        E.fence("vector", "vp")
                        E.emit("vector", lambda v: v.tensor_copy(bcol(2), x2t[0:128, 0, 0:1].bitcast(I32)), inc="vp")
                        count_c8(bcol(2), bcol(4))
                        E.fence("vector", "vp")
                        E.emit("vector", lambda v: v.tensor_copy(bcol(6), tgtcol), inc="vp")
                        bis_select()

                quantile_targets(NSEQ * 4096 * HID, bcol(11), bcol(12), bcol(8), bcol(9))
                if q2_dbg is not None:
                    E.wait("gpsimd", "vp")
                    E.emit("gpsimd", lambda g: g.dma_start(out=q2_dbg[0], in_=bst[:]), inc="gpa", n=16)
                if q2_dbg is not None:
                    E.wait("gpsimd", "vp")
                    E.emit("gpsimd", lambda g: g.dma_start(out=c8_dbg[0], in_=c8[:, 0, :]), inc="gpa", n=16)
                    E.emit("gpsimd", lambda g: g.dma_start(out=c8_dbg[1], in_=c8[:, 1, :]), inc="gpa", n=16)
                int_bisect(bcol(8), 9)
                if q2_dbg is not None:
                    E.wait("gpsimd", "vp")
                    E.emit("gpsimd", lambda g: g.dma_start(out=q2_dbg[1], in_=bst[:]), inc="gpa", n=16)
                E.fence("vector", "vp")
                E.emit("vector", lambda v: v.tensor_mul(bcol(20), bcol(1), bcol(28)), inc="vp")
                E.fence("vector", "vp")
                E.emit("vector", lambda v: v.tensor_scalar(bcol(20), bcol(20), 1.0 / 127.0,
                                                           None, ALU.mult), inc="vp")
                E.fence("vector", "vp")
                E.emit("vector", lambda v: v.tensor_copy(x2t[0:128, 0, 0:1].bitcast(I32), bcol(20)), inc="vp")
                E.fence("vector", "vp")
                E.emit("vector", lambda v: v.tensor_copy(nstat[:, 15:16], x2t[0:128, 0, 0:1].bitcast(I32)), inc="vp")
                E.fence("vector", "vp")
                if q2_dbg is not None:
                    E.wait("gpsimd", "vp")
                    E.emit("gpsimd", lambda g: g.dma_start(out=q2_dbg[2], in_=bst[:]), inc="gpa", n=16)
                for rt in range(NT):
                    for qq in range(4):
                        zs = (rt * 4 + qq) % 2
                        lws = "lwa" if zs == 0 else "lwb"
                        qeng = "sync" if zs == 0 else "gpsimd"
                        E.wait(qeng, "vp", zqslot[zs])
                        E.emit(qeng, lambda s, rt=rt, qq=qq, zs=zs: s.dma_start(
                            out=zrt2[:, zs, :],
                            in_=zd[rt, :, qq * D:(qq + 1) * D]), inc=lws, n=16)
                        E.wait("vector", lws)
                        E.emit("vector", lambda v, zs=zs: v.tensor_scalar(
                            x2t[:, 0, :], zrt2[:, zs, :],
                            nstat[:, 15:16], -127.0, ALU.subtract, op1=ALU.max), inc="vp")
                        zqslot[zs] = E.val("vp")
                        E.emit("vector", lambda v, rt=rt, qq=qq: v.tensor_scalar(
                            c8flat[:, rt * HID + qq * D: rt * HID + (qq + 1) * D],
                            x2t[:, 0, :], 127.0, None,
                            ALU.min), inc="vp")
                int_bisect(bcol(8), 9)
                if q2_dbg is not None:
                    E.wait("gpsimd", "vp")
                    E.emit("gpsimd", lambda g: g.dma_start(out=q2_dbg[3], in_=bst[:]), inc="gpa", n=16)
                E.fence("vector", "vp")
                E.emit("vector", lambda v: v.tensor_add(bcol(13), nstat[:, 15:16], bcol(1)), inc="vp")
                E.fence("vector", "vp")
                E.emit("vector", lambda v: v.tensor_scalar(bcol(22), bcol(1), -1.0, None,
                                                           ALU.add), inc="vp")
                count_c8(bcol(22), bcol(4))
                E.fence("vector", "vp")
                E.emit("vector", lambda v: v.tensor_scalar(bcol(23), bcol(1), 1.0, 200.0,
                                                           ALU.mult, op1=ALU.add), inc="vp")
                E.fence("vector", "vp")
                def mm_bufs(q):
                    if q % 2 == 0:
                        return junk[:], junkS[:]
                    return x2t[:, 0, :], x2t[:, 1, :]
                for half in range(2):
                    for qi in range(65):
                        q = half * 64 + qi
                        if qi < 64:
                            ja, jb = mm_bufs(q)
                            E.emit("vector", lambda v, q=q, ja=ja: v.tensor_scalar(
                                ja, c8flat[:, q * 1024:(q + 1) * 1024], 200.0, None,
                                ALU.add), inc="vp")
                            E.emit("vector", lambda v, ja=ja, jb=jb: v.scalar_tensor_tensor(
                                jb, ja, bcol(23), ja, ALU.is_lt, ALU.mult), inc="vp")
                        if qi > 0:
                            _, jbp = mm_bufs(q - 1)
                            E.emit("vector", lambda v, q=q, jbp=jbp: v.reduce_max(
                                cnts[:, (q - 1) % 64:(q - 1) % 64 + 1], jbp, axis=AX), inc="vp")
                    E.fence("vector", "vp")
                    if half == 0:
                        E.emit("vector", lambda v: v.reduce_max(bcol(7), cnts[:, 0:64], axis=AX), inc="vp")
                    else:
                        E.emit("vector", lambda v: v.reduce_max(bcol(6), cnts[:, 0:64], axis=AX), inc="vp")
                        E.fence("vector", "vp")
                        E.emit("vector", lambda v: v.tensor_max(bcol(7), bcol(7), bcol(6)), inc="vp")
                greduce(ALU.max, bcol(7), bcol(14), 0)
                E.fence("vector", "vp")
                E.emit("vector", lambda v: v.tensor_scalar(bcol(14), bcol(14), 1.0, -200.0,
                                                           ALU.mult, op1=ALU.add), inc="vp")
                E.fence("vector", "vp")
                E.emit("vector", lambda v: v.tensor_add(bcol(14), bcol(14), nstat[:, 15:16]), inc="vp")
                E.fence("vector", "vp")
                E.emit("vector", lambda v: v.tensor_tensor(bcol(5), bcol(4), bcol(9), ALU.is_ge), inc="vp")
                E.fence("vector", "vp")
                E.emit("vector", lambda v: v.tensor_sub(bcol(21), bcol(13), bcol(14)), inc="vp")
                E.fence("vector", "vp")
                E.emit("vector", lambda v: v.tensor_mul(bcol(21), bcol(21), bcol(5)), inc="vp")
                E.fence("vector", "vp")
                E.emit("vector", lambda v: v.tensor_add(bcol(14), bcol(14), bcol(21)), inc="vp")
                E.fence("vector", "vp")
                E.emit("vector", lambda v: v.tensor_mul(bcol(18), bcol(19), bcol(15)), inc="vp")
                E.fence("vector", "vp")
                E.emit("vector", lambda v: v.tensor_scalar(bcol(18), bcol(18), 1.0 / 127.0,
                                                           None, ALU.mult), inc="vp")
                E.fence("vector", "vp")
                for (zc_col, out_col) in ((13, 13), (14, 14)):
                    E.emit("vector", lambda v, zc_col=zc_col: v.tensor_mul(
                        bcol(22), bcol(zc_col), bcol(18)), inc="vp")
                    E.wait("scalar", "vp")
                    E.emit("scalar", lambda s, out_col=out_col: s.activation(
                        bcol(out_col), bcol(22), AF.Gelu), inc="sp")
                    E.wait("vector", "sp")
                E.emit("vector", lambda v: v.tensor_mul(bcol(20), bcol(14), bcol(12)), inc="vp")
                E.emit("vector", lambda v: v.tensor_mul(bcol(21), bcol(13), bcol(11)), inc="vp")
                E.fence("vector", "vp")
                E.emit("vector", lambda v: v.tensor_add(bcol(30), bcol(20), bcol(21)), inc="vp")
                E.fence("vector", "vp")
                E.emit("vector", lambda v: v.tensor_scalar(bcol(30), bcol(30), EPS,
                                                           None, ALU.max), inc="vp")
                E.fence("vector", "vp")
                E.emit("vector", lambda v: v.reciprocal(bcol(31), bcol(30)), inc="vp")
                E.fence("vector", "vp")
                E.emit("vector", lambda v: v.tensor_scalar(bcol(31), bcol(31), 127.0,
                                                           None, ALU.mult), inc="vp")
                E.fence("vector", "vp")
                if st_dbg is not None:
                    E.wait("gpsimd", "vp")
                    E.emit("gpsimd", lambda g: g.dma_start(out=st_dbg, in_=bst[:]), inc="gpa", n=16)

            if "D" in phases:
                barrier()
                for ch in range(64):
                    kh, dcol = ch // 2, ch % 2
                    ws = ch % 2
                    lws = "lwa" if ws == 0 else "lwb"
                    E.wait("sync", "sp", wslot_free.get(("g2", ws), 0))
                    E.emit("sync", lambda s, kh=kh, dcol=dcol, ws=ws: s.dma_start(
                        out=wstage[:, ws, :],
                        in_=w2T[kh * 128:(kh + 1) * 128, dcol * 512:(dcol + 1) * 512]),
                        inc=lws, n=16)
                    E.wait("scalar", lws)
                    E.emit("scalar", lambda s, ch=ch, ws=ws: s.activation(
                        junkS[:, 0:512], wstage[:, ws, :], AF.Abs,
                        accum_out=cnts[:, ch % 64:ch % 64 + 1]), inc="sp")
                    wslot_free[("g2", ws)] = E.val("sp")
                E.wait("vector", "sp")
                E.emit("vector", lambda v: v.reduce_sum(bcol(6), cnts[:, 0:64], axis=AX), inc="vp")
                lreduce(ALU.add, bcol(6), bcol(19), 0)
                E.emit("vector", lambda v: v.tensor_scalar(
                    bcol(19), bcol(19), 1.0 / (D * HID), EPS, ALU.mult, op1=ALU.add), inc="vp")
                E.fence("vector", "vp")
                E.emit("vector", lambda v: v.reciprocal(bcol(26), bcol(19)), inc="vp")
                E.fence("vector", "vp")
                E.emit("vector", lambda v: v.tensor_mul(bcol(25), bcol(19), bcol(30)), inc="vp")
                E.fence("vector", "vp")
                E.emit("vector", lambda v: v.tensor_scalar(bcol(25), bcol(25), 1.0 / 127.0,
                                                           None, ALU.mult), inc="vp")
                E.emit("vector", lambda v: v.tensor_scalar(bcol(24), bcol(31), 0.5,
                                                           None, ALU.mult), inc="vp")
                E.fence("vector", "vp")
                wslot3 = {0: (0, 0), 1: (0, 0)}
                for ch in range(64):
                    kh, dcol = ch // 2, ch % 2
                    ws = ch % 2
                    lws = "lwa" if ws == 0 else "lwb"
                    vv, _ = wslot3[ws]
                    E.wait("sync", "vp", vv)
                    E.emit("sync", lambda s, kh=kh, dcol=dcol, ws=ws: s.dma_start(
                        out=wstage[:, ws, :],
                        in_=w2T[kh * 128:(kh + 1) * 128, dcol * 512:(dcol + 1) * 512]),
                        inc=lws, n=16)
                    E.wait("vector", lws)
                    E.emit("vector", lambda v, ws=ws: v.tensor_scalar(
                        junk[:, 0:512], wstage[:, ws, :], bcol(26), None, ALU.mult), inc="vp")
                    wslot3[ws] = (E.val("vp"), 0)
                    E.emit("vector", lambda v: v.tensor_copy(
                        x2t[:, 0, 0:512].bitcast(I32), junk[:, 0:512]), inc="vp")
                    E.emit("vector", lambda v, kh=kh, dcol=dcol: v.tensor_scalar(
                        t2[:, kh, dcol * 512:(dcol + 1) * 512],
                        x2t[:, 0, 0:512].bitcast(I32), -1.0, 1.0,
                        ALU.max, op1=ALU.min), inc="vp")
                zdslot = {0: (0, 0), 1: (0, 0)}
                oslot = {0: 0, 1: 0}
                trans_done = {0: 0, 1: 0}
                pair_free = {0: 0, 1: 0}
                chain_done = {}
                mm2_done = {}
                for sl0 in range(2):
                    pv2, ps2 = xslot_free[sl0]
                    E.wait("sync", "vp", pv2)
                    E.wait("sync", "sp", ps2)
                    E.emit("sync", lambda s, sl0=sl0: s.dma_start(
                        out=xt[:, sl0, :], in_=x2d[sl0]),
                        inc=("lxa" if sl0 == 0 else "lxb"), n=16)
                E.emit("sync", lambda s: s.dma_start(
                    out=zrt[:, 0, :], in_=zd[0]), inc="lwa", n=16)
                for it in range(NT + 1):
                    rt = it
                    if rt < NT:
                        zs = rt % 2
                        lws = "lwa" if zs == 0 else "lwb"
                        if rt + 1 < NT:
                            nzs = (rt + 1) % 2
                            nlws = "lwa" if nzs == 0 else "lwb"
                            vv, _ = zdslot[nzs]
                            E.wait("sync", "vp", vv)
                            E.emit("sync", lambda s, rt=rt, nzs=nzs: s.dma_start(
                                out=zrt[:, nzs, :], in_=zd[rt + 1]), inc=nlws, n=16)
                        E.wait("vector", lws)
                        E.wait("vector", "tp", trans_done[rt % 2])
                        mult_done = {}
                        gelu_done = {}

                        def pre_ap(p):
                            return u2st[:, 0, :] if p == 0 else ln1[:, 1, :]

                        def gout_ap(p):
                            return junkS[:] if p == 0 else junk[:]

                        def emit_mult(qq):
                            p = qq % 2
                            if qq >= 2:
                                E.wait("vector", "sp", gelu_done[qq - 2])
                            E.emit("vector", lambda v, zs=zs, qq=qq, p=p: v.tensor_scalar(
                                pre_ap(p), zrt[:, zs, qq * D:(qq + 1) * D],
                                bcol(18), None, ALU.mult), inc="vp")
                            mult_done[qq] = E.val("vp")
                            E.wait("scalar", "vp", mult_done[qq])
                            E.emit("scalar", lambda s, p=p: s.activation(
                                gout_ap(p), pre_ap(p), AF.Gelu), inc="sp")
                            gelu_done[qq] = E.val("sp")

                        def emit_post(qq):
                            p = qq % 2
                            E.wait("vector", "sp", gelu_done[qq])
                            E.emit("vector", lambda v, p=p: v.tensor_scalar(
                                u2st[:, 1, :], gout_ap(p), bcol(31), None, ALU.mult), inc="vp")
                            E.emit("vector", lambda v: v.tensor_copy(
                                ln1[:, 0, :].bitcast(I32), u2st[:, 1, :]), inc="vp")
                            E.emit("vector", lambda v, rt=rt, qq=qq: v.tensor_scalar(
                                u2tok[:, rt % 2, qq * D:(qq + 1) * D],
                                ln1[:, 0, :].bitcast(I32),
                                -128.0, 127.0, ALU.max, op1=ALU.min), inc="vp")

                        emit_mult(0)
                        emit_mult(1)
                        emit_post(0)
                        emit_mult(2)
                        emit_post(1)
                        emit_mult(3)
                        emit_post(2)
                        emit_post(3)
                        zdslot[zs] = (mult_done[3], 0)
                        chain_done[rt] = E.val("vp")
                        E.wait("tensor", "vp", chain_done[rt])
                        for kh in range(32):
                            pt = ptrs[kh % 4]
                            E.wait("tensor", "sp", ptr_free[kh % 4])
                            E.emit("tensor", lambda t, rt=rt, kh=kh, pt=pt: t.transpose(
                                pt[:].bitcast(BF16)[:, 0:128],
                                u2tok[:, rt % 2, kh * 128:(kh + 1) * 128], identb[:]), inc="tp")
                            E.wait("scalar", "tp")
                            E.emit("scalar", lambda s, kh=kh, pt=pt: s.copy(
                                u2fm[:, kh, :], pt[:].bitcast(BF16)[:, 0:128]), inc="sp")
                            ptr_free[kh % 4] = E.val("sp")
                        trans_done[rt % 2] = E.val("tp")
                        E.wait("tensor", "sp")
                        E.wait("tensor", "vp", pair_free[rt % 2])
                        pms = (pmm0, pmm1) if rt % 2 == 0 else (pmm2, pmm3)
                        for dcol in range(2):
                            pm = pms[dcol]
                            for kh in range(32):
                                E.emit("tensor", lambda t, kh=kh, dcol=dcol, pm=pm: t.matmul(
                                    pm[:], u2fm[:, kh, :], t2[:, kh, dcol * 512:(dcol + 1) * 512],
                                    start=(kh == 0), stop=(kh == 31)),
                                    inc=("tp" if kh == 31 else None))
                        mm2_done[rt] = E.val("tp")
                    if it == 0:
                        continue
                    pr = it - 1
                    n, b = pr // NB, pr % NB
                    pms = (pmm0, pmm1) if pr % 2 == 0 else (pmm2, pmm3)
                    E.wait("vector", "tp", mm2_done[pr])
                    E.emit("vector", lambda v, pms=pms: v.tensor_scalar(
                        o2b[:, 0:512], pms[0][:], bcol(25), None, ALU.mult), inc="vp")
                    E.emit("vector", lambda v, pms=pms: v.tensor_scalar(
                        o2b[:, 512:1024], pms[1][:], bcol(25), None, ALU.mult), inc="vp")
                    pair_free[pr % 2] = E.val("vp")
                    E.fence("vector", "vp")
                    E.emit("vector", lambda v: v.reduce_sum(nstat[:, 0:1], o2b[:], axis=AX), inc="vp")
                    E.emit("vector", lambda v: v.scalar_tensor_tensor(
                        junk[:], o2b[:], 1.0, o2b[:], ALU.mult, ALU.mult,
                        accum_out=nstat[:, 4:5]), inc="vp")
                    E.fence("vector", "vp")
                    E.emit("vector", lambda v: v.tensor_scalar(nstat[:, 8:9], nstat[:, 0:1],
                                                               1.0 / D, None, ALU.mult), inc="vp")
                    E.emit("vector", lambda v: v.tensor_scalar(nstat[:, 9:10], nstat[:, 4:5],
                                                               1.0 / D, None, ALU.mult), inc="vp")
                    E.fence("vector", "vp")
                    E.emit("vector", lambda v: v.tensor_mul(nstat[:, 10:11], nstat[:, 8:9],
                                                            nstat[:, 8:9]), inc="vp")
                    E.fence("vector", "vp")
                    E.emit("vector", lambda v: v.tensor_sub(nstat[:, 11:12], nstat[:, 9:10],
                                                            nstat[:, 10:11]), inc="vp")
                    E.fence("vector", "vp")
                    E.emit("vector", lambda v: v.tensor_scalar(nstat[:, 11:12], nstat[:, 11:12],
                                                               1.0, EPS, ALU.mult,
                                                               op1=ALU.add), inc="vp")
                    rsqrt_chain(11, 12, 13, 14, 1)
                    sl = pr % 2
                    lxs = "lxa" if sl == 0 else "lxb"
                    xp = pr % 2
                    E.wait("vector", "so", oslot[xp])
                    E.fence("vector", "vp")
                    E.emit("vector", lambda v, xp=xp: v.tensor_scalar(
                        x2t[:, xp, :], o2b[:], nstat[:, 8:9], nstat[:, 12:13],
                        ALU.subtract, op1=ALU.mult), inc="vp")
                    E.wait("vector", lxs)
                    E.emit("vector", lambda v, sl=sl, xp=xp: v.tensor_tensor(
                        x2t[:, xp, :], x2t[:, xp, :], xt[:, sl, :], ALU.add), inc="vp")
                    xslot_free[sl] = (E.val("vp"), E.val("sp"))
                    if pr + 2 < NT:
                        nsl = pr % 2
                        nlxs = "lxa" if nsl == 0 else "lxb"
                        vv2, ss2 = xslot_free[nsl]
                        E.wait("sync", "vp", vv2)
                        E.wait("sync", "sp", ss2)
                        E.emit("sync", lambda s, pr=pr, nsl=nsl: s.dma_start(
                            out=xt[:, nsl, :], in_=x2d[pr + 2]), inc=nlxs, n=16)
                    E.wait("sync", "vp")
                    E.emit("sync", lambda s, n=n, b=b, xp=xp: s.dma_start(
                        out=out[n, b * 128:(b + 1) * 128, :], in_=x2t[:, xp, :]), inc="so", n=16)
                    oslot[xp] = E.val("so")
                E.wait("sync", "so")
                E.wait("vector", "so")

            if "D" not in phases:
                barrier()
                for rt in range(NT):
                    n, b = rt // NB, rt % NB
                    E.emit("sync", lambda s, rt=rt: s.dma_start(
                        out=junk[:], in_=x2d[rt]), inc="lxa", n=16)
                    E.wait("sync", "lxa")
                    E.emit("sync", lambda s, n=n, b=b: s.dma_start(
                        out=out[n, b * 128:(b + 1) * 128, :], in_=junk[:]), inc="so", n=16)
                    E.wait("sync", "so")
            E.wait("sync", "so")
            E.wait("vector", "so")

        with nc.Block() as block:
            for eng in ENGINES:
                def make(eng):
                    def f(_q):
                        program(Emitter(nc, eng, handles))
                    return f
                getattr(block, eng)(make(eng))

    return nc


def _prep_inputs(L0_attn_x, L0_attn_Wqkv, L0_attn_bqkv, L0_attn_Wproj, L0_attn_bproj,
                 L0_ffn_ln_g, L0_ffn_ln_b, L0_ffn_w1, L0_ffn_w2):
    x = np.ascontiguousarray(L0_attn_x, dtype=np.float32)
    shared = {
        "wqkvT": np.ascontiguousarray(np.asarray(L0_attn_Wqkv, dtype=np.float32).T),
        "bqkv": np.ascontiguousarray(np.asarray(L0_attn_bqkv, dtype=np.float32).reshape(1, -1)),
        "wprojT": np.ascontiguousarray(np.asarray(L0_attn_Wproj, dtype=np.float32).T),
        "bproj": np.ascontiguousarray(np.asarray(L0_attn_bproj, dtype=np.float32).reshape(1, -1)),
        "ln_g": np.ascontiguousarray(np.asarray(L0_ffn_ln_g, dtype=np.float32).reshape(1, -1)),
        "ln_b": np.ascontiguousarray(np.asarray(L0_ffn_ln_b, dtype=np.float32).reshape(1, -1)),
        "w1T": np.ascontiguousarray(np.asarray(L0_ffn_w1, dtype=np.float32).T),
        "w2T": np.ascontiguousarray(np.asarray(L0_ffn_w2, dtype=np.float32).T),
    }
    in_maps = []
    for c in range(NCORES):
        m = dict(shared)
        m["x"] = np.ascontiguousarray(x[:, c * T:(c + 1) * T, :])
        in_maps.append(m)
    return in_maps


def kernel(**inputs):
    in_maps = _prep_inputs(**inputs)
    nc = build_program()
    res = run_bass_kernel_spmd(nc, in_maps, core_ids=list(range(NCORES)))
    shards = [np.asarray(res.results[c]["out"]) for c in range(NCORES)]
    return np.concatenate(shards, axis=1)


# version 10 (final)
# speedup vs baseline: 1.2748x; 1.0209x over previous
import numpy as np
import concourse.bass as bass
import concourse.mybir as mybir
from concourse.bass_utils import run_bass_kernel_spmd

F32 = mybir.dt.float32
BF16 = mybir.dt.bfloat16
I32 = mybir.dt.int32
ALU = mybir.AluOpType
AF = mybir.ActivationFunctionType
AX = mybir.AxisListType.X

NCORES = 8
NSEQ = 8
T = 512
D = 1024
H = 16
HD = 64
HID = 4096
NB = T // 128
NT = NSEQ * NB
DCH = D // 128
SCALE = float(HD) ** -0.5
EPS = 1e-5
NEG_BIG = -1.0e30

ENGINES = ("sync", "tensor", "vector", "scalar", "gpsimd")


class Emitter:

    def __init__(self, nc, engine, handles):
        self.nc = nc
        self.engine = engine
        self.h = handles
        self.c = {k: 0 for k in handles}

    def eng(self, engine):
        return getattr(self.nc, engine)

    def emit(self, engine, fn, inc=None, n=1):
        if self.engine == engine:
            ins = fn(self.eng(engine))
            if inc is not None:
                ins.then_inc(self.h[inc], n)
        if inc is not None:
            self.c[inc] += n

    def wait(self, engine, sem, val=None):
        if self.engine == engine:
            self.eng(engine).wait_ge(self.h[sem], self.c[sem] if val is None else val)

    def fence(self, engine, sem):
        self.wait(engine, sem)

    def val(self, sem):
        return self.c[sem]


def build_program(debug=(), limit_b=NB, phases=("A", "B", "C", "D")):
    nc = bass.Bass("TRN2", target_bir_lowering=False, detect_race_conditions=False)

    x_in = nc.dram_tensor("x", [NSEQ, T, D], F32, kind="ExternalInput").ap()
    wqkvT = nc.dram_tensor("wqkvT", [D, 3 * D], F32, kind="ExternalInput").ap()
    bqkv = nc.dram_tensor("bqkv", [1, 3 * D], F32, kind="ExternalInput").ap()
    wprojT = nc.dram_tensor("wprojT", [D, D], F32, kind="ExternalInput").ap()
    bproj = nc.dram_tensor("bproj", [1, D], F32, kind="ExternalInput").ap()
    ln_g = nc.dram_tensor("ln_g", [1, D], F32, kind="ExternalInput").ap()
    ln_b = nc.dram_tensor("ln_b", [1, D], F32, kind="ExternalInput").ap()
    w1T = nc.dram_tensor("w1T", [D, HID], F32, kind="ExternalInput").ap()
    w2T = nc.dram_tensor("w2T", [HID, D], F32, kind="ExternalInput").ap()
    out = nc.dram_tensor("out", [NSEQ, T, D], F32, kind="ExternalOutput").ap()

    x2_kind = "ExternalOutput" if "x2" in debug else "Internal"
    x2d = nc.dram_tensor("dbg_x2" if "x2" in debug else "x2d",
                         [NT, 128, D], F32, kind=x2_kind).ap()
    zd = nc.dram_tensor("zd", [NT, 128, HID], F32, kind="Internal").ap()
    cc_in = nc.dram_tensor("cc_in", [1, 1], F32, kind="Internal").ap()
    cc_out = nc.dram_tensor("cc_out", [1, 1], F32, kind="Internal").ap()
    h1_dbg = (nc.dram_tensor("dbg_h1", [NT, 128, D], F32, kind="ExternalOutput").ap()
              if "h1" in debug else None)
    q2_dbg = (nc.dram_tensor("dbg_q2", [6, 128, 32], F32, kind="ExternalOutput").ap()
              if "q2" in debug else None)
    c8_dbg = (nc.dram_tensor("dbg_c8", [2, 128, HID], mybir.dt.int8, kind="ExternalOutput").ap()
              if "q2" in debug else None)
    st_dbg = (nc.dram_tensor("dbg_st", [128, 32], F32, kind="ExternalOutput").ap()
              if "st" in debug else None)

    sem_names = ["lxa", "lxb", "lwa", "lwb", "tp", "vp", "sp",
                 "gpa", "gpb", "gi", "so", "cc", "lb"]

    import contextlib
    with contextlib.ExitStack() as _stk:
        def sb(name, shape, dtype=F32):
            return _stk.enter_context(nc.sbuf_tensor(name, shape, dtype))

        def psb(name, shape, dtype=F32):
            return _stk.enter_context(nc.psum_tensor(name, shape, dtype))

        ident = sb("ident", [128, 128])
        identb = sb("identb", [128, 128], BF16)
        ones1 = sb("ones1", [1, 128])
        bqkv_t = sb("bqkv_t", [1, 3 * D])
        bproj_t = sb("bproj_t", [1, D])
        xt = sb("xt", [128, 2, D])
        ln1 = sb("ln1", [128, 2, D])
        smat = sb("smat", [128, 8, NSEQ, NSEQ])
        emat = sb("emat", [128, 8, NSEQ, NSEQ])
        mx = sb("mx", [128, 64])
        sm = sb("sm", [128, 64])
        prods = sb("prods", [128, 8, HD])
        junk = sb("junk", [128, D])
        junkS = sb("junkS", [128, D])
        stats = sb("stats", [128, 8, 8])
        nstat = sb("nstat", [128, 16])
        x2t = sb("x2t", [128, 2, D])
        wstage = sb("wstage", [128, 2, 512])
        bst = sb("bst", [128, 32])
        cnts = sb("cnts", [128, 64])
        ccrow = sb("ccrow", [1, 8])
        ARENA_F32 = 36864
        arena = sb("arena", [128, ARENA_F32])
        arena_addr = nc.lookup_mloc(arena).addr

        def overlay(name, shape, dtype, byte_off):
            h = nc.virtual_sbuf_tensor(name, shape, dtype)
            nc._tensor(h.name, list(h.shape), h.dtype, type="SB")
            mloc = nc.lookup_mloc(h)
            mloc.allocated = True
            mloc.addr = arena_addr + byte_off
            return h

        xfm = overlay("xfm", [128, NSEQ, DCH, 128], F32, 0)
        wbuf = overlay("wbuf", [128, 2, DCH, 512], F32, 32768)
        ybuf = overlay("ybuf", [128, NSEQ, 3, 512], F32, 65536)
        aout = overlay("aout", [128, NSEQ, D], F32, 114688)
        h1 = overlay("h1", [128, NT, D], F32, 0)
        u1tok = overlay("u1tok", [128, NT, D], BF16, 0)
        t1 = overlay("t1", [128, DCH, HID], BF16, 65536)
        c8 = overlay("c8", [128, NT, HID], mybir.dt.int8, 0)
        u1f = overlay("u1f", [128, DCH, 128], BF16, 131072)
        zstage = overlay("zstage", [128, 2, 512], F32, 133120)
        zrt2 = overlay("zrt2", [128, 2, 1024], F32, 131072)
        t2 = overlay("t2", [128, 32, D], BF16, 0)
        u2fm = overlay("u2fm", [128, 32, 128], BF16, 65536)
        u2tok = overlay("u2tok", [128, 2, HID], BF16, 73728)
        zrt = overlay("zrt", [128, 2, HID], F32, 90112)
        o2b = overlay("o2b", [128, D], F32, 122880)
        u2st = overlay("u2st", [128, 2, D], F32, 126976)
        ptr0 = psb("ptr0", [128, 128])
        ptr1 = psb("ptr1", [128, 128])
        ptr2 = psb("ptr2", [128, 128])
        ptr3 = psb("ptr3", [128, 128])
        ptrs = [ptr0, ptr1, ptr2, ptr3]
        pmm0 = psb("pmm0", [128, 512])
        pmm1 = psb("pmm1", [128, 512])
        pmm2 = psb("pmm2", [128, 512])
        pmm3 = psb("pmm3", [128, 512])

        handles = {nm: _stk.enter_context(nc.semaphore(nm)) for nm in sem_names}

        def program(E: Emitter):
            FS, FS2, FM, FT, FR = 0, 1, 2, 3, 4

            E.emit("gpsimd", lambda g: g.iota(junk[:, 0:128], [[1, 128]], channel_multiplier=0,
                                              allow_small_or_imprecise_dtypes=True), inc="gi")
            E.emit("gpsimd", lambda g: g.iota(junkS[:, 0:128], [[0, 128]], channel_multiplier=1,
                                              allow_small_or_imprecise_dtypes=True), inc="gi")
            E.wait("vector", "gi")
            E.emit("vector", lambda v: v.tensor_tensor(ident[:], junk[:, 0:128], junkS[:, 0:128],
                                                       ALU.is_equal), inc="vp")
            E.emit("vector", lambda v: v.memset(ones1[:], 1.0), inc="vp")
            E.fence("vector", "vp")
            E.emit("vector", lambda v: v.tensor_copy(identb[:], ident[:]), inc="vp")
            E.emit("sync", lambda s: s.dma_start(out=bqkv_t[:], in_=bqkv[:]), inc="lb", n=16)
            E.emit("sync", lambda s: s.dma_start(out=bproj_t[:], in_=bproj[:]), inc="lb", n=16)

            E.wait("tensor", "vp")
            E.wait("tensor", "lb")

            xslot_free = {0: (0, 0), 1: (0, 0)}
            lnslot_free = {0: 0, 1: 0}
            wseq = [0]
            wslot_free = {0: 0, 1: 0}
            pslot = {0: 0, 1: 1}
            pwsem = {0: "lwa", 1: "lwb"}
            x2_gp = []
            pmm_free = {0: 0, 1: 0}
            pmm_free_s = {0: 0, 1: 0, 2: 0, 3: 0}
            att_sc_done = [0]
            att_av_done = [0]
            pmmS_free = {0: 0, 1: 0}
            ptr_free = {0: 0, 1: 0, 2: 0, 3: 0}

            def load_x(n, b):
                sl = n % 2
                lxs = "lxa" if sl == 0 else "lxb"
                vv, ss = xslot_free[sl]
                E.wait("sync", "vp", vv)
                E.wait("sync", "sp", ss)
                E.emit("sync", lambda s, n=n, b=b, sl=sl: s.dma_start(
                    out=xt[:, sl, :], in_=x_in[n, b * 128:(b + 1) * 128, :]),
                    inc=lxs, n=16)
                return sl, lxs

            def rsqrt_chain(tcol, rcol, jcol, thcol, w):
                tA = nstat[:, tcol:tcol + w]
                rA = nstat[:, rcol:rcol + w]
                jA = nstat[:, jcol:jcol + w]
                thA = nstat[:, thcol:thcol + w]
                E.wait("scalar", "vp")
                E.emit("scalar", lambda s: s.activation(jA, tA, AF.Ln), inc="sp")
                E.fence("scalar", "sp")
                E.emit("scalar", lambda s: s.activation(rA, jA, AF.Exp, scale=-0.5), inc="sp")
                E.wait("vector", "sp")
                E.emit("vector", lambda v: v.tensor_scalar(thA, tA, 0.5, None, ALU.mult), inc="vp")
                for _ in range(2):
                    E.fence("vector", "vp")
                    E.emit("vector", lambda v: v.tensor_mul(jA, rA, rA), inc="vp")
                    E.fence("vector", "vp")
                    E.emit("vector", lambda v: v.tensor_mul(jA, jA, thA), inc="vp")
                    E.fence("vector", "vp")
                    E.emit("vector", lambda v: v.tensor_scalar(jA, jA, -1.0, 1.5,
                                                               ALU.mult, op1=ALU.add), inc="vp")
                    E.fence("vector", "vp")
                    E.emit("vector", lambda v: v.tensor_mul(rA, rA, jA), inc="vp")

            allowed = [(n, m) for n in range(NSEQ) for m in range(NSEQ)
                       if not (n < 4 and m >= 4)]

            def stats_chain(fs, fs2, fm, ft, fr):
                E.wait("vector", "sp")
                E.fence("vector", "vp")
                sA = stats[:, :, fs]
                s2A = stats[:, :, fs2]
                mA = stats[:, :, fm]
                tA = stats[:, :, ft]
                rA = stats[:, :, fr]
                E.emit("vector", lambda v: v.tensor_scalar(mA, sA, 1.0 / D, None, ALU.mult), inc="vp")
                E.emit("vector", lambda v: v.tensor_scalar(nstat[:, 0:8], s2A, 1.0 / D,
                                                           None, ALU.mult), inc="vp")
                E.fence("vector", "vp")
                E.emit("vector", lambda v: v.tensor_mul(nstat[:, 8:16], mA, mA), inc="vp")
                E.fence("vector", "vp")
                E.emit("vector", lambda v: v.tensor_sub(tA, nstat[:, 0:8], nstat[:, 8:16]), inc="vp")
                E.fence("vector", "vp")
                E.emit("vector", lambda v: v.tensor_scalar(tA, tA, 1.0, EPS,
                                                           ALU.mult, op1=ALU.add), inc="vp")
                E.wait("scalar", "vp")
                E.emit("scalar", lambda s: s.activation(nstat[:, 0:8], tA, AF.Ln), inc="sp")
                E.fence("scalar", "sp")
                E.emit("scalar", lambda s: s.activation(rA, nstat[:, 0:8], AF.Exp,
                                                        scale=-0.5), inc="sp")
                E.wait("vector", "sp")
                E.emit("vector", lambda v: v.tensor_scalar(nstat[:, 8:16], tA, 0.5,
                                                           None, ALU.mult), inc="vp")
                for _ in range(2):
                    E.fence("vector", "vp")
                    E.emit("vector", lambda v: v.tensor_mul(nstat[:, 0:8], rA, rA), inc="vp")
                    E.fence("vector", "vp")
                    E.emit("vector", lambda v: v.tensor_mul(nstat[:, 0:8], nstat[:, 0:8],
                                                            nstat[:, 8:16]), inc="vp")
                    E.fence("vector", "vp")
                    E.emit("vector", lambda v: v.tensor_scalar(nstat[:, 0:8], nstat[:, 0:8],
                                                               -1.0, 1.5, ALU.mult,
                                                               op1=ALU.add), inc="vp")
                    E.fence("vector", "vp")
                    E.emit("vector", lambda v: v.tensor_mul(rA, rA, nstat[:, 0:8]), inc="vp")
                E.fence("vector", "vp")

            for b in range(limit_b if "A" in phases else 0):
                for n in range(NSEQ):
                    sl, lxs = load_x(n, b)
                    E.wait("vector", lxs)
                    E.wait("scalar", lxs)
                    E.emit("scalar", lambda s, n=n, sl=sl: s.activation(
                        junkS[:], xt[:, sl, :], AF.Copy,
                        accum_out=stats[:, n, FS:FS + 1]), inc="sp")
                    E.emit("vector", lambda v, n=n, sl=sl: v.scalar_tensor_tensor(
                        junk[:], xt[:, sl, :], 1.0, xt[:, sl, :], ALU.mult, ALU.mult,
                        accum_out=stats[:, n, FS2:FS2 + 1]), inc="vp")
                    xslot_free[sl] = (E.val("vp"), E.val("sp"))
                E.wait("vector", "sp")
                E.fence("vector", "vp")
                sA = stats[:, :, FS]
                s2A = stats[:, :, FS2]
                mA = stats[:, :, FM]
                tA = stats[:, :, FT]
                rA = stats[:, :, FR]
                E.emit("vector", lambda v: v.tensor_scalar(mA, sA, 1.0 / D, None, ALU.mult), inc="vp")
                E.emit("vector", lambda v: v.tensor_scalar(nstat[:, 0:8], s2A, 1.0 / D,
                                                           None, ALU.mult), inc="vp")
                E.fence("vector", "vp")
                E.emit("vector", lambda v: v.tensor_mul(nstat[:, 8:16], mA, mA), inc="vp")
                E.fence("vector", "vp")
                E.emit("vector", lambda v: v.tensor_sub(tA, nstat[:, 0:8], nstat[:, 8:16]), inc="vp")
                E.fence("vector", "vp")
                E.emit("vector", lambda v: v.tensor_scalar(tA, tA, 1.0, EPS,
                                                           ALU.mult, op1=ALU.add), inc="vp")
                E.wait("scalar", "vp")
                E.emit("scalar", lambda s: s.activation(nstat[:, 0:8], tA, AF.Ln), inc="sp")
                E.fence("scalar", "sp")
                E.emit("scalar", lambda s: s.activation(rA, nstat[:, 0:8], AF.Exp,
                                                        scale=-0.5), inc="sp")
                E.wait("vector", "sp")
                E.emit("vector", lambda v: v.tensor_scalar(nstat[:, 8:16], tA, 0.5,
                                                           None, ALU.mult), inc="vp")
                for _ in range(2):
                    E.fence("vector", "vp")
                    E.emit("vector", lambda v: v.tensor_mul(nstat[:, 0:8], rA, rA), inc="vp")
                    E.fence("vector", "vp")
                    E.emit("vector", lambda v: v.tensor_mul(nstat[:, 0:8], nstat[:, 0:8],
                                                            nstat[:, 8:16]), inc="vp")
                    E.fence("vector", "vp")
                    E.emit("vector", lambda v: v.tensor_scalar(nstat[:, 0:8], nstat[:, 0:8],
                                                               -1.0, 1.5, ALU.mult,
                                                               op1=ALU.add), inc="vp")
                    E.fence("vector", "vp")
                    E.emit("vector", lambda v: v.tensor_mul(rA, rA, nstat[:, 0:8]), inc="vp")
                E.fence("vector", "vp")
                for n in range(NSEQ):
                    sl, lxs = load_x(n, b)
                    E.wait("vector", lxs)
                    E.wait("vector", "sp", lnslot_free[sl])
                    E.emit("vector", lambda v, n=n, sl=sl: v.tensor_scalar(
                        ln1[:, sl, :], xt[:, sl, :], stats[:, n, FM:FM + 1],
                        stats[:, n, FR:FR + 1], ALU.subtract, op1=ALU.mult), inc="vp")
                    xslot_free[sl] = (E.val("vp"), E.val("sp"))
                    E.wait("tensor", "vp")
                    for k in range(DCH):
                        pt = ptrs[k % 4]
                        E.wait("tensor", "sp", ptr_free[k % 4])
                        E.emit("tensor", lambda t, k=k, pt=pt, sl=sl: t.transpose(
                            pt[:], ln1[:, sl, k * 128:(k + 1) * 128], ident[:]), inc="tp")
                        E.wait("scalar", "tp")
                        E.emit("scalar", lambda s, n=n, k=k, pt=pt: s.copy(
                            xfm[:, n, k, :], pt[:]), inc="sp")
                        ptr_free[k % 4] = E.val("sp")
                    lnslot_free[sl] = E.val("sp")
                for hc in range(2):
                    for cg in range(3):
                        ws = wseq[0] % 2
                        lws = "lwa" if ws == 0 else "lwb"
                        E.wait("sync", "tp", wslot_free[ws])
                        for k in range(DCH):
                            E.emit("sync", lambda s, k=k, cg=cg, hc=hc, ws=ws: s.dma_start(
                                out=wbuf[:, ws, k, :],
                                in_=wqkvT[k * 128:(k + 1) * 128,
                                          cg * D + hc * 512: cg * D + hc * 512 + 512]),
                                inc=lws, n=16)
                        wseq[0] += 1
                        E.wait("tensor", lws)
                        if cg == 0:
                            E.wait("tensor", "sp")
                            E.wait("tensor", "vp", max(pmm_free[0], pmm_free[1]))
                        E.wait("scalar", "vp",
                               att_av_done[0] if cg == 2 else att_sc_done[0])
                        for n in range(NSEQ):
                            pm = (pmm0, pmm1, pmm2, pmm3)[n % 4]
                            E.wait("tensor", "sp", pmm_free_s[n % 4])
                            E.emit("tensor", lambda t, cg=cg, hc=hc, pm=pm: t.matmul(
                                pm[:], ones1[:],
                                bqkv_t[0:1, cg * D + hc * 512: cg * D + hc * 512 + 512],
                                start=True, stop=False))
                            for k in range(DCH):
                                E.emit("tensor", lambda t, n=n, k=k, ws=ws, pm=pm: t.matmul(
                                    pm[:], xfm[:, n, k, :], wbuf[:, ws, k, :],
                                    start=False, stop=(k == DCH - 1)),
                                    inc=("tp" if k == DCH - 1 else None))
                            E.wait("scalar", "tp")
                            E.emit("scalar", lambda s, n=n, cg=cg, pm=pm: s.copy(
                                ybuf[:, n, cg, :], pm[:]), inc="sp")
                            pmm_free_s[n % 4] = E.val("sp")
                        wslot_free[ws] = E.val("tp")
                    E.wait("vector", "tp")
                    E.wait("vector", "sp")
                    E.fence("vector", "vp")
                    E.emit("vector", lambda v: v.memset(smat[:, :, 0:4, 4:8], NEG_BIG), inc="vp")
                    for hl in range(8):
                        for (n, m) in allowed:
                            E.emit("vector", lambda v, hl=hl, n=n, m=m: v.scalar_tensor_tensor(
                                junk[:, 0:HD],
                                ybuf[:, n, 0, hl * HD:(hl + 1) * HD], SCALE,
                                ybuf[:, m, 1, hl * HD:(hl + 1) * HD],
                                ALU.mult, ALU.mult,
                                accum_out=smat[:, hl, n, m:m + 1]), inc="vp")
                    att_sc_done[0] = E.val("vp")
                    E.fence("vector", "vp")
                    E.emit("vector", lambda v: v.reduce_max(
                        mx[:].unsqueeze(2),
                        smat[:].rearrange("p h n m -> p (h n) m"), axis=AX), inc="vp")
                    E.fence("vector", "vp")
                    E.emit("vector", lambda v: v.tensor_tensor(
                        emat[:].rearrange("p h n m -> p (h n) m"),
                        smat[:].rearrange("p h n m -> p (h n) m"),
                        mx[:].unsqueeze(2).to_broadcast([128, 64, 8]), ALU.subtract), inc="vp")
                    E.wait("scalar", "vp")
                    E.emit("scalar", lambda s: s.activation(emat[:], emat[:], AF.Exp), inc="sp")
                    E.wait("vector", "sp")
                    E.emit("vector", lambda v: v.reduce_sum(
                        sm[:].unsqueeze(2),
                        emat[:].rearrange("p h n m -> p (h n) m"), axis=AX), inc="vp")
                    E.fence("vector", "vp")
                    E.emit("vector", lambda v: v.reciprocal(sm[:], sm[:]), inc="vp")
                    E.fence("vector", "vp")
                    E.emit("vector", lambda v: v.tensor_tensor(
                        emat[:].rearrange("p h n m -> p (h n) m"),
                        emat[:].rearrange("p h n m -> p (h n) m"),
                        sm[:].unsqueeze(2).to_broadcast([128, 64, 8]), ALU.mult), inc="vp")
                    E.fence("vector", "vp")
                    for hl in range(8):
                        for n in range(NSEQ):
                            nm = 4 if n < 4 else 8
                            for m in range(nm):
                                E.emit("vector", lambda v, hl=hl, n=n, m=m: v.tensor_scalar(
                                    prods[:, m, :],
                                    ybuf[:, m, 2, hl * HD:(hl + 1) * HD],
                                    emat[:, hl, n, m:m + 1], None, ALU.mult), inc="vp")
                            E.emit("vector", lambda v, hl=hl, n=n, hc=hc, nm=nm: v.reduce_sum(
                                aout[:, n, hc * 512 + hl * HD: hc * 512 + (hl + 1) * HD]
                                .unsqueeze(2),
                                prods[:, 0:nm, :].transpose([0, 2, 1]), axis=AX), inc="vp")
                    att_av_done[0] = E.val("vp")
                for cg in range(2):
                    ws = wseq[0] % 2
                    lws = "lwa" if ws == 0 else "lwb"
                    E.wait("sync", "tp", wslot_free[ws])
                    for k in range(DCH):
                        E.emit("sync", lambda s, k=k, cg=cg, ws=ws: s.dma_start(
                            out=wbuf[:, ws, k, :],
                            in_=wprojT[k * 128:(k + 1) * 128, cg * 512:(cg + 1) * 512]),
                            inc=lws, n=16)
                    pslot[cg] = ws
                    pwsem[cg] = lws
                    wseq[0] += 1
                attn_done_vp = E.val("vp")
                for n in range(NSEQ):
                    E.wait("tensor", "vp", attn_done_vp)
                    for k in range(DCH):
                        pt = ptrs[k % 4]
                        E.wait("tensor", "sp", ptr_free[k % 4])
                        E.emit("tensor", lambda t, n=n, k=k, pt=pt: t.transpose(
                            pt[:], aout[:, n, k * 128:(k + 1) * 128], ident[:]), inc="tp")
                        E.wait("scalar", "tp")
                        E.emit("scalar", lambda s, n=n, k=k, pt=pt: s.copy(
                            xfm[:, n, k, :], pt[:]), inc="sp")
                        ptr_free[k % 4] = E.val("sp")
                    E.wait("tensor", pwsem[0])
                    E.wait("tensor", pwsem[1])
                    E.wait("tensor", "sp")
                    E.wait("tensor", "vp", max(pmm_free[0], pmm_free[1]))
                    for cg in range(2):
                        pm = pmm0 if cg == 0 else pmm1
                        E.emit("tensor", lambda t, cg=cg, pm=pm: t.matmul(
                            pm[:], ones1[:], bproj_t[0:1, cg * 512:(cg + 1) * 512],
                            start=True, stop=False))
                        for k in range(DCH):
                            E.emit("tensor", lambda t, n=n, k=k, cg=cg, pm=pm: t.matmul(
                                pm[:], xfm[:, n, k, :], wbuf[:, pslot[cg], k, :],
                                start=False, stop=(k == DCH - 1)),
                                inc=("tp" if k == DCH - 1 else None))
                    E.wait("vector", "tp")
                    E.emit("vector", lambda v: v.tensor_copy(junk[:, 0:512], pmm0[:]), inc="vp")
                    E.emit("vector", lambda v: v.tensor_copy(junk[:, 512:1024], pmm1[:]), inc="vp")
                    pmm_free[0] = E.val("vp")
                    pmm_free[1] = E.val("vp")
                    E.emit("vector", lambda v: v.reduce_sum(nstat[:, 0:1], junk[:], axis=AX), inc="vp")
                    E.emit("vector", lambda v: v.scalar_tensor_tensor(
                        junkS[:], junk[:], 1.0, junk[:], ALU.mult, ALU.mult,
                        accum_out=nstat[:, 4:5]), inc="vp")
                    E.fence("vector", "vp")
                    E.emit("vector", lambda v: v.tensor_scalar(nstat[:, 8:9], nstat[:, 0:1],
                                                               1.0 / D, None, ALU.mult), inc="vp")
                    E.emit("vector", lambda v: v.tensor_scalar(nstat[:, 9:10], nstat[:, 4:5],
                                                               1.0 / D, None, ALU.mult), inc="vp")
                    E.fence("vector", "vp")
                    E.emit("vector", lambda v: v.tensor_mul(nstat[:, 10:11], nstat[:, 8:9],
                                                            nstat[:, 8:9]), inc="vp")
                    E.fence("vector", "vp")
                    E.emit("vector", lambda v: v.tensor_sub(nstat[:, 11:12], nstat[:, 9:10],
                                                            nstat[:, 10:11]), inc="vp")
                    E.fence("vector", "vp")
                    E.emit("vector", lambda v: v.tensor_scalar(nstat[:, 11:12], nstat[:, 11:12],
                                                               1.0, EPS, ALU.mult,
                                                               op1=ALU.add), inc="vp")
                    rsqrt_chain(11, 12, 13, 14, 1)
                    sl, lxs = load_x(n, b)
                    xp = len(x2_gp) % 2
                    gps = "gpa" if xp == 0 else "gpb"
                    if len(x2_gp) >= 2:
                        E.wait("vector", gps, x2_gp[-2])
                    E.fence("vector", "vp")
                    E.emit("vector", lambda v, xp=xp: v.tensor_scalar(
                        x2t[:, xp, :], junk[:], nstat[:, 8:9], nstat[:, 12:13],
                        ALU.subtract, op1=ALU.mult), inc="vp")
                    E.wait("vector", lxs)
                    E.emit("vector", lambda v, sl=sl, xp=xp: v.tensor_tensor(
                        x2t[:, xp, :], x2t[:, xp, :], xt[:, sl, :], ALU.add), inc="vp")
                    xslot_free[sl] = (E.val("vp"), E.val("sp"))
                    rt = 4 * n + b
                    E.wait("gpsimd", "vp")
                    E.emit("gpsimd", lambda g, rt=rt, xp=xp: g.dma_start(
                        out=x2d[rt], in_=x2t[:, xp, :]), inc=gps, n=16)
                    x2_gp.append(E.val(gps))
                wslot_free[pslot[0]] = E.val("tp")
                wslot_free[pslot[1]] = E.val("tp")

            def barrier():
                for eng in ENGINES:
                    for s in ("tp", "vp", "sp", "lxa", "lxb", "lwa", "lwb",
                              "gpa", "gpb", "so", "cc"):
                        E.wait(eng, s)

            def greduce(op, incol, outcol, rowslot):
                E.wait("tensor", "vp")
                E.emit("tensor", lambda t: t.transpose(ptr0[0:1, :], incol, ident[:]), inc="tp")
                E.wait("vector", "tp")
                E.emit("vector", lambda v: v.tensor_copy(junk[0:1, 0:128], ptr0[0:1, 0:128]), inc="vp")
                E.fence("vector", "vp")
                E.emit("vector", lambda v: v.tensor_reduce(
                    ccrow[0:1, rowslot:rowslot + 1], junk[0:1, 0:128], AX, op), inc="vp")
                E.wait("gpsimd", "vp")
                E.emit("gpsimd", lambda g: g.dma_start(
                    out=cc_in[0:1, 0:1], in_=ccrow[0:1, rowslot:rowslot + 1]), inc="gpa", n=16)
                E.wait("gpsimd", "gpa")
                E.emit("gpsimd", lambda g: g.collective_compute(
                    "AllReduce", op, ins=[cc_in[:]], outs=[cc_out[:]],
                    replica_groups=[[0, 1, 2, 3, 4, 5, 6, 7]]), inc="cc")
                E.wait("gpsimd", "cc")
                E.emit("gpsimd", lambda g: g.dma_start(
                    out=ccrow[0:1, rowslot + 1:rowslot + 2], in_=cc_out[0:1, 0:1]), inc="gpb", n=16)
                E.wait("tensor", "gpb")
                E.wait("tensor", "vp")
                E.emit("tensor", lambda t: t.matmul(
                    ptr1[:, 0:1], ones1[:], ccrow[0:1, rowslot + 1:rowslot + 2],
                    start=True, stop=True), inc="tp")
                E.wait("vector", "tp")
                E.emit("vector", lambda v: v.tensor_copy(outcol, ptr1[:, 0:1]), inc="vp")
                E.fence("vector", "vp")

            def lreduce(op, incol, outcol, rowslot):
                E.wait("tensor", "vp")
                E.emit("tensor", lambda t: t.transpose(ptr0[0:1, :], incol, ident[:]), inc="tp")
                E.wait("vector", "tp")
                E.emit("vector", lambda v: v.tensor_copy(junk[0:1, 0:128], ptr0[0:1, 0:128]), inc="vp")
                E.fence("vector", "vp")
                E.emit("vector", lambda v: v.tensor_reduce(
                    ccrow[0:1, rowslot:rowslot + 1], junk[0:1, 0:128], AX, op), inc="vp")
                E.wait("tensor", "vp")
                E.emit("tensor", lambda t: t.matmul(
                    ptr1[:, 0:1], ones1[:], ccrow[0:1, rowslot:rowslot + 1],
                    start=True, stop=True), inc="tp")
                E.wait("vector", "tp")
                E.emit("vector", lambda v: v.tensor_copy(outcol, ptr1[:, 0:1]), inc="vp")
                E.fence("vector", "vp")

            def bcol(i):
                return bst[:, i:i + 1]

            def bis_select():
                E.fence("vector", "vp")
                E.emit("vector", lambda v: v.tensor_tensor(bcol(5), bcol(4), bcol(6),
                                                           ALU.is_ge), inc="vp")
                E.fence("vector", "vp")
                E.emit("vector", lambda v: v.tensor_sub(bcol(20), bcol(2), bcol(0)), inc="vp")
                E.emit("vector", lambda v: v.tensor_sub(bcol(21), bcol(2), bcol(1)), inc="vp")
                E.fence("vector", "vp")
                E.emit("vector", lambda v: v.tensor_mul(bcol(20), bcol(20), bcol(5)), inc="vp")
                E.fence("vector", "vp")
                E.emit("vector", lambda v: v.tensor_add(bcol(0), bcol(0), bcol(20)), inc="vp")
                E.emit("vector", lambda v: v.tensor_sub(bcol(21), bcol(1), bcol(2)), inc="vp")
                E.fence("vector", "vp")
                E.emit("vector", lambda v: v.tensor_mul(bcol(21), bcol(21), bcol(5)), inc="vp")
                E.fence("vector", "vp")
                E.emit("vector", lambda v: v.tensor_add(bcol(1), bcol(2), bcol(21)), inc="vp")
                E.fence("vector", "vp")

            if "B" in phases:
                barrier()
                E.emit("vector", lambda v: v.memset(bst[:], 0.0), inc="vp")
                E.emit("vector", lambda v: v.memset(cnts[:], 0.0), inc="vp")
                E.emit("vector", lambda v: v.memset(ccrow[:], 0.0), inc="vp")
                E.fence("vector", "vp")
                FS, FS2, FM, FT, FR = 0, 1, 2, 3, 4
                FU, FU2, FM3 = 5, 6, 7
                gbc = overlay("gbc", [128, D], F32, 131072)
                bbc = overlay("bbc", [128, D], F32, 135168)
                jI8 = overlay("jI8", [128, 8192], mybir.dt.int8, 139264)
                for half in range(2):
                    E.emit("sync", lambda s, half=half: s.dma_start(
                        out=ln1[0:1, 0, 0:512],
                        in_=ln_g[0:1, half * 512:(half + 1) * 512]), inc="lb", n=16)
                    E.emit("sync", lambda s, half=half: s.dma_start(
                        out=ln1[0:1, 1, 0:512],
                        in_=ln_b[0:1, half * 512:(half + 1) * 512]), inc="lb", n=16)
                    E.wait("tensor", "lb")
                    E.wait("tensor", "vp")
                    E.emit("tensor", lambda t: t.matmul(
                        pmm0[:], ones1[:], ln1[0:1, 0, 0:512],
                        start=True, stop=True), inc="tp")
                    E.emit("tensor", lambda t: t.matmul(
                        pmm1[:], ones1[:], ln1[0:1, 1, 0:512],
                        start=True, stop=True), inc="tp")
                    E.wait("vector", "tp")
                    E.emit("vector", lambda v, half=half: v.tensor_copy(
                        gbc[:, half * 512:(half + 1) * 512], pmm0[:]), inc="vp")
                    E.emit("vector", lambda v, half=half: v.tensor_copy(
                        bbc[:, half * 512:(half + 1) * 512], pmm1[:]), inc="vp")
                E.fence("vector", "vp")
                NW1CH = (D // 128) * (HID // 512)
                for ch in range(NW1CH):
                    kd, hcol = ch // 8, ch % 8
                    ws = ch % 2
                    lws = "lwa" if ws == 0 else "lwb"
                    E.wait("sync", "sp", wslot_free.get(("g1", ws), 0))
                    E.emit("sync", lambda s, kd=kd, hcol=hcol, ws=ws: s.dma_start(
                        out=wstage[:, ws, :],
                        in_=w1T[kd * 128:(kd + 1) * 128, hcol * 512:(hcol + 1) * 512]),
                        inc=lws, n=16)
                    E.wait("scalar", lws)
                    E.emit("scalar", lambda s, ch=ch, ws=ws: s.activation(
                        junkS[:, 0:512], wstage[:, ws, :], AF.Abs,
                        accum_out=cnts[:, ch % 64:ch % 64 + 1]), inc="sp")
                    wslot_free[("g1", ws)] = E.val("sp")
                E.wait("vector", "sp")
                E.emit("vector", lambda v: v.reduce_sum(bcol(6), cnts[:, 0:64], axis=AX), inc="vp")
                lreduce(ALU.add, bcol(6), bcol(19), 0)
                E.emit("vector", lambda v: v.tensor_scalar(
                    bcol(19), bcol(19), 1.0 / (D * HID), EPS, ALU.mult, op1=ALU.add), inc="vp")
                ln2slot = {0: (0, 0), 1: (0, 0)}
                for g8 in range(NT // 8):
                    for j in range(8):
                        rt = g8 * 8 + j
                        sl = rt % 2
                        lxs = "lxa" if sl == 0 else "lxb"
                        vv, ss = xslot_free[sl]
                        E.wait("sync", "vp", vv)
                        E.wait("sync", "sp", ss)
                        E.emit("sync", lambda s, rt=rt, sl=sl: s.dma_start(
                            out=xt[:, sl, :], in_=x2d[rt]), inc=lxs, n=16)
                        E.wait("vector", lxs)
                        E.wait("scalar", lxs)
                        E.emit("scalar", lambda s, j=j, sl=sl: s.activation(
                            junkS[:], xt[:, sl, :], AF.Copy,
                            accum_out=stats[:, j, FS:FS + 1]), inc="sp")
                        E.emit("vector", lambda v, j=j, sl=sl: v.scalar_tensor_tensor(
                            junk[:], xt[:, sl, :], 1.0, xt[:, sl, :], ALU.mult, ALU.mult,
                            accum_out=stats[:, j, FS2:FS2 + 1]), inc="vp")
                        xslot_free[sl] = (E.val("vp"), E.val("sp"))
                    stats_chain(FS, FS2, FM, FT, FR)
                    for j in range(8):
                        rt = g8 * 8 + j
                        sl = rt % 2
                        lxs = "lxa" if sl == 0 else "lxb"
                        vv, ss = xslot_free[sl]
                        E.wait("sync", "vp", vv)
                        E.wait("sync", "sp", ss)
                        E.emit("sync", lambda s, rt=rt, sl=sl: s.dma_start(
                            out=xt[:, sl, :], in_=x2d[rt]), inc=lxs, n=16)
                        E.wait("vector", lxs)
                        lv, ls = ln2slot[sl]
                        E.wait("vector", "vp", lv)
                        E.wait("vector", "sp", ls)
                        E.emit("vector", lambda v, j=j, sl=sl: v.tensor_scalar(
                            ln1[:, sl, :], xt[:, sl, :], stats[:, j, FM:FM + 1],
                            stats[:, j, FR:FR + 1], ALU.subtract, op1=ALU.mult), inc="vp")
                        E.wait("scalar", "vp")
                        E.emit("scalar", lambda s, j=j, sl=sl: s.activation(
                            junkS[:], ln1[:, sl, :], AF.Copy,
                            accum_out=stats[:, j, FU:FU + 1]), inc="sp")
                        E.emit("vector", lambda v, j=j, sl=sl: v.scalar_tensor_tensor(
                            junk[:], ln1[:, sl, :], 1.0, ln1[:, sl, :], ALU.mult, ALU.mult,
                            accum_out=stats[:, j, FU2:FU2 + 1]), inc="vp")
                        xslot_free[sl] = (E.val("vp"), E.val("sp"))
                        ln2slot[sl] = (E.val("vp"), E.val("sp"))
                    stats_chain(FU, FU2, FM3, FT, FU2)
                    for j in range(8):
                        rt = g8 * 8 + j
                        sl = rt % 2
                        lxs = "lxa" if sl == 0 else "lxb"
                        vv, ss = xslot_free[sl]
                        E.wait("sync", "vp", vv)
                        E.wait("sync", "sp", ss)
                        E.emit("sync", lambda s, rt=rt, sl=sl: s.dma_start(
                            out=xt[:, sl, :], in_=x2d[rt]), inc=lxs, n=16)
                        E.wait("vector", lxs)
                        lv, ls = ln2slot[sl]
                        E.wait("vector", "sp", ls)
                        E.emit("vector", lambda v, j=j, sl=sl: v.tensor_scalar(
                            ln1[:, sl, :], xt[:, sl, :], stats[:, j, FM:FM + 1],
                            stats[:, j, FR:FR + 1], ALU.subtract, op1=ALU.mult), inc="vp")
                        E.emit("vector", lambda v, j=j, sl=sl: v.tensor_scalar(
                            ln1[:, sl, :], ln1[:, sl, :], stats[:, j, FM3:FM3 + 1],
                            stats[:, j, FU2:FU2 + 1], ALU.subtract, op1=ALU.mult), inc="vp")
                        E.emit("vector", lambda v, sl=sl: v.tensor_mul(
                            ln1[:, sl, :], ln1[:, sl, :], gbc[:]), inc="vp")
                        E.emit("vector", lambda v, rt=rt, sl=sl: v.tensor_add(
                            h1[:, rt, :], ln1[:, sl, :], bbc[:]), inc="vp")
                        xslot_free[sl] = (E.val("vp"), E.val("sp"))
                        E.emit("vector", lambda v, rt=rt: v.reduce_max(
                            bcol(6), h1[:, rt, :], axis=AX,
                            apply_absolute_value=True), inc="vp")
                        E.fence("vector", "vp")
                        if rt == 0:
                            E.emit("vector", lambda v: v.tensor_copy(bcol(7), bcol(6)), inc="vp")
                        else:
                            E.emit("vector", lambda v: v.tensor_max(bcol(7), bcol(7), bcol(6)), inc="vp")
                greduce(ALU.max, bcol(7), bcol(17), 0)
                E.emit("vector", lambda v: v.memset(bcol(10), float(NSEQ * 4096 * D)), inc="vp")
                E.fence("vector", "vp")
                E.emit("vector", lambda v: v.tensor_scalar(bcol(10), bcol(10), -1.0,
                                                           None, ALU.add), inc="vp")
                E.fence("vector", "vp")
                E.emit("vector", lambda v: v.tensor_scalar(bcol(10), bcol(10),
                                                           float(np.float32(0.995)),
                                                           None, ALU.mult), inc="vp")
                E.fence("vector", "vp")
                E.emit("vector", lambda v: v.tensor_copy(x2t[0:128, 0, 0:1].bitcast(I32), bcol(10)), inc="vp")
                E.fence("vector", "vp")
                E.emit("vector", lambda v: v.tensor_copy(bcol(6), x2t[0:128, 0, 0:1].bitcast(I32)), inc="vp")
                E.fence("vector", "vp")
                E.emit("vector", lambda v: v.tensor_tensor(bcol(20), bcol(6), bcol(10), ALU.is_gt), inc="vp")
                E.emit("vector", lambda v: v.tensor_tensor(bcol(21), bcol(6), bcol(10), ALU.is_lt), inc="vp")
                E.fence("vector", "vp")
                E.emit("vector", lambda v: v.tensor_sub(bcol(20), bcol(6), bcol(20)), inc="vp")
                E.emit("vector", lambda v: v.tensor_add(bcol(21), bcol(6), bcol(21)), inc="vp")
                E.fence("vector", "vp")
                E.emit("vector", lambda v: v.tensor_sub(bcol(11), bcol(10), bcol(20)), inc="vp")
                E.fence("vector", "vp")
                E.emit("vector", lambda v: v.tensor_scalar(bcol(12), bcol(11), -1.0, 1.0,
                                                           ALU.mult, op1=ALU.add), inc="vp")
                E.emit("vector", lambda v: v.memset(bcol(6), float(NSEQ * 4096 * D)), inc="vp")
                E.fence("vector", "vp")
                E.emit("vector", lambda v: v.tensor_scalar(bcol(6), bcol(6), -1.0,
                                                           None, ALU.add), inc="vp")
                E.fence("vector", "vp")
                E.emit("vector", lambda v: v.tensor_sub(bcol(8), bcol(6), bcol(21)), inc="vp")
                E.emit("vector", lambda v: v.tensor_sub(bcol(9), bcol(6), bcol(20)), inc="vp")
                E.fence("vector", "vp")
                E.emit("vector", lambda v: v.tensor_scalar(bcol(8), bcol(8), 1.0, 1.0,
                                                           ALU.mult, op1=ALU.add), inc="vp")
                E.emit("vector", lambda v: v.tensor_scalar(bcol(9), bcol(9), 1.0, 1.0,
                                                           ALU.mult, op1=ALU.add), inc="vp")
                E.emit("vector", lambda v: v.memset(bcol(0), -1.0), inc="vp")
                E.fence("vector", "vp")
                E.emit("vector", lambda v: v.tensor_copy(bcol(1), bcol(17)), inc="vp")
                h1flat = h1[:].rearrange("p a b -> p (a b)")
                def count_pass_h1(taucol, cmp1, cmp2, dstcol):
                    E.fence("vector", "vp")
                    E.emit("vector", lambda v: v.tensor_scalar(bcol(22), taucol, -1.0,
                                                               None, ALU.mult), inc="vp")
                    E.fence("vector", "vp")
                    for q in range(4):
                        E.emit("vector", lambda v, q=q: v.tensor_scalar(
                            jI8[:], h1flat[:, q * 8192:(q + 1) * 8192], taucol, None,
                            cmp1, op1=ALU.add, accum_out=cnts[:, q:q + 1]), inc="vp")
                        E.emit("vector", lambda v, q=q: v.tensor_scalar(
                            jI8[:], h1flat[:, q * 8192:(q + 1) * 8192], bcol(22), None,
                            cmp2, op1=ALU.add, accum_out=cnts[:, 4 + q:5 + q]), inc="vp")
                    E.fence("vector", "vp")
                    E.emit("vector", lambda v: v.reduce_sum(bcol(25), cnts[:, 0:8], axis=AX), inc="vp")
                    greduce(ALU.add, bcol(25), dstcol, 0)
                R1 = 30
                for rnd in range(R1):
                    E.fence("vector", "vp")
                    E.emit("vector", lambda v: v.tensor_add(bcol(2), bcol(0), bcol(1)), inc="vp")
                    E.fence("vector", "vp")
                    E.emit("vector", lambda v: v.tensor_scalar(bcol(2), bcol(2), 0.5,
                                                               None, ALU.mult), inc="vp")
                    count_pass_h1(bcol(2), ALU.is_gt, ALU.is_lt, bcol(4))
                    E.fence("vector", "vp")
                    E.emit("vector", lambda v: v.tensor_copy(bcol(6), bcol(8)), inc="vp")
                    bis_select()
                E.emit("vector", lambda v: v.tensor_copy(bcol(13), bcol(1)), inc="vp")
                count_pass_h1(bcol(13), ALU.is_ge, ALU.is_le, bcol(4))
                for rt in range(NT):
                    E.wait("scalar", "vp")
                    E.emit("scalar", lambda s, rt=rt: s.activation(
                        junkS[:], h1flat[:, rt * D:(rt + 1) * D], AF.Abs), inc="sp")
                    E.wait("vector", "sp")
                    E.emit("vector", lambda v: v.scalar_tensor_tensor(
                        junk[:], junkS[:], bcol(13), junkS[:], ALU.is_lt, ALU.mult), inc="vp")
                    E.fence("vector", "vp")
                    E.emit("vector", lambda v: v.reduce_max(bcol(6), junk[:, 0:D], axis=AX), inc="vp")
                    E.fence("vector", "vp")
                    if rt == 0:
                        E.emit("vector", lambda v: v.tensor_copy(bcol(7), bcol(6)), inc="vp")
                    else:
                        E.emit("vector", lambda v: v.tensor_max(bcol(7), bcol(7), bcol(6)), inc="vp")
                greduce(ALU.max, bcol(7), bcol(14), 0)
                E.fence("vector", "vp")
                E.emit("vector", lambda v: v.tensor_tensor(bcol(5), bcol(4), bcol(9), ALU.is_ge), inc="vp")
                E.fence("vector", "vp")
                E.emit("vector", lambda v: v.tensor_sub(bcol(20), bcol(13), bcol(14)), inc="vp")
                E.fence("vector", "vp")
                E.emit("vector", lambda v: v.tensor_mul(bcol(20), bcol(20), bcol(5)), inc="vp")
                E.fence("vector", "vp")
                E.emit("vector", lambda v: v.tensor_add(bcol(14), bcol(14), bcol(20)), inc="vp")
                E.fence("vector", "vp")
                E.emit("vector", lambda v: v.tensor_mul(bcol(20), bcol(14), bcol(12)), inc="vp")
                E.emit("vector", lambda v: v.tensor_mul(bcol(21), bcol(13), bcol(11)), inc="vp")
                E.fence("vector", "vp")
                E.emit("vector", lambda v: v.tensor_add(bcol(15), bcol(20), bcol(21)), inc="vp")
                E.fence("vector", "vp")
                E.emit("vector", lambda v: v.tensor_scalar(bcol(15), bcol(15), EPS,
                                                           None, ALU.max), inc="vp")
                E.fence("vector", "vp")
                E.emit("vector", lambda v: v.reciprocal(bcol(16), bcol(15)), inc="vp")
                E.fence("vector", "vp")
                E.emit("vector", lambda v: v.tensor_scalar(bcol(16), bcol(16), 127.0,
                                                           None, ALU.mult), inc="vp")
                E.fence("vector", "vp")
                if st_dbg is not None:
                    E.wait("gpsimd", "vp")
                    E.emit("gpsimd", lambda g: g.dma_start(out=st_dbg, in_=bst[:]), inc="gpa", n=16)
                if h1_dbg is not None:
                    for rt in range(NT):
                        E.wait("gpsimd", "vp")
                        E.emit("gpsimd", lambda g, rt=rt: g.dma_start(
                            out=h1_dbg[rt], in_=h1[:, rt, :]), inc="gpa", n=16)

            def quantile_targets(nval, hwcol, lwcol, t1col, t2col):
                E.emit("vector", lambda v: v.memset(bcol(10), float(nval)), inc="vp")
                E.fence("vector", "vp")
                E.emit("vector", lambda v: v.tensor_scalar(bcol(10), bcol(10), -1.0,
                                                           None, ALU.add), inc="vp")
                E.fence("vector", "vp")
                E.emit("vector", lambda v: v.tensor_scalar(bcol(10), bcol(10),
                                                           float(np.float32(0.995)),
                                                           None, ALU.mult), inc="vp")
                E.fence("vector", "vp")
                E.emit("vector", lambda v: v.tensor_copy(x2t[0:128, 0, 0:1].bitcast(I32), bcol(10)), inc="vp")
                E.fence("vector", "vp")
                E.emit("vector", lambda v: v.tensor_copy(bcol(6), x2t[0:128, 0, 0:1].bitcast(I32)), inc="vp")
                E.fence("vector", "vp")
                E.emit("vector", lambda v: v.tensor_tensor(bcol(20), bcol(6), bcol(10), ALU.is_gt), inc="vp")
                E.emit("vector", lambda v: v.tensor_tensor(bcol(21), bcol(6), bcol(10), ALU.is_lt), inc="vp")
                E.fence("vector", "vp")
                E.emit("vector", lambda v: v.tensor_sub(bcol(20), bcol(6), bcol(20)), inc="vp")
                E.emit("vector", lambda v: v.tensor_add(bcol(21), bcol(6), bcol(21)), inc="vp")
                E.fence("vector", "vp")
                E.emit("vector", lambda v: v.tensor_sub(hwcol, bcol(10), bcol(20)), inc="vp")
                E.fence("vector", "vp")
                E.emit("vector", lambda v: v.tensor_scalar(lwcol, hwcol, -1.0, 1.0,
                                                           ALU.mult, op1=ALU.add), inc="vp")
                E.emit("vector", lambda v: v.memset(bcol(6), float(nval)), inc="vp")
                E.fence("vector", "vp")
                E.emit("vector", lambda v: v.tensor_scalar(bcol(6), bcol(6), -1.0,
                                                           None, ALU.add), inc="vp")
                E.fence("vector", "vp")
                E.emit("vector", lambda v: v.tensor_sub(t1col, bcol(6), bcol(21)), inc="vp")
                E.emit("vector", lambda v: v.tensor_sub(t2col, bcol(6), bcol(20)), inc="vp")
                E.fence("vector", "vp")
                E.emit("vector", lambda v: v.tensor_scalar(t1col, t1col, 1.0, 1.0,
                                                           ALU.mult, op1=ALU.add), inc="vp")
                E.emit("vector", lambda v: v.tensor_scalar(t2col, t2col, 1.0, 1.0,
                                                           ALU.mult, op1=ALU.add), inc="vp")
                E.fence("vector", "vp")

            if "C" in phases:
                barrier()
                E.emit("vector", lambda v: v.reciprocal(bcol(26), bcol(19)), inc="vp")
                E.fence("vector", "vp")
                for rt in range(NT):
                    sl = rt % 2
                    E.emit("vector", lambda v, rt=rt, sl=sl: v.tensor_scalar(
                        ln1[:, sl, :], h1[:, rt, :], bcol(16), None, ALU.mult), inc="vp")
                    E.emit("vector", lambda v, sl=sl: v.tensor_copy(
                        x2t[:, sl, :].bitcast(I32), ln1[:, sl, :]), inc="vp")
                    E.emit("vector", lambda v, rt=rt, sl=sl: v.tensor_scalar(
                        u1tok[:, rt, :], x2t[:, sl, :].bitcast(I32), -128.0, 127.0,
                        ALU.max, op1=ALU.min), inc="vp")
                wslot2 = {0: (0, 0), 1: (0, 0)}
                for ch in range(64):
                    kd, hcol = ch // 8, ch % 8
                    ws = ch % 2
                    lws = "lwa" if ws == 0 else "lwb"
                    vv, _ = wslot2[ws]
                    E.wait("sync", "vp", vv)
                    E.emit("sync", lambda s, kd=kd, hcol=hcol, ws=ws: s.dma_start(
                        out=wstage[:, ws, :],
                        in_=w1T[kd * 128:(kd + 1) * 128, hcol * 512:(hcol + 1) * 512]),
                        inc=lws, n=16)
                    E.wait("vector", lws)
                    E.emit("vector", lambda v, ws=ws: v.tensor_scalar(
                        junk[:, 0:512], wstage[:, ws, :], bcol(26), None, ALU.mult), inc="vp")
                    wslot2[ws] = (E.val("vp"), 0)
                    E.emit("vector", lambda v: v.tensor_copy(
                        x2t[:, 0, 0:512].bitcast(I32), junk[:, 0:512]), inc="vp")
                    E.emit("vector", lambda v, kd=kd, hcol=hcol: v.tensor_scalar(
                        t1[:, kd, hcol * 512:(hcol + 1) * 512],
                        x2t[:, 0, 0:512].bitcast(I32), -1.0, 1.0,
                        ALU.max, op1=ALU.min), inc="vp")
                zslot = {0: (0, 0), 1: (0, 0)}
                zseq = [0]
                E.emit("vector", lambda v: v.memset(bcol(27), -3.0e38), inc="vp")
                E.fence("vector", "vp")
                conv_done_vp = E.val("vp")
                for rt in range(NT):
                    E.wait("tensor", "vp", conv_done_vp)
                    for k in range(DCH):
                        pt = ptrs[k % 4]
                        E.wait("tensor", "sp", ptr_free[k % 4])
                        E.emit("tensor", lambda t, rt=rt, k=k, pt=pt: t.transpose(
                            pt[:].bitcast(BF16)[:, 0:128],
                            u1tok[:, rt, k * 128:(k + 1) * 128], identb[:]), inc="tp")
                        E.wait("scalar", "tp")
                        E.emit("scalar", lambda s, k=k, pt=pt: s.copy(
                            u1f[:, k, :], pt[:].bitcast(BF16)[:, 0:128]), inc="sp")
                        ptr_free[k % 4] = E.val("sp")
                    E.wait("tensor", "sp")
                    for hcol in range(DCH):
                        pm = pmm0 if hcol % 2 == 0 else pmm1
                        E.wait("tensor", "sp", pmmS_free[hcol % 2])
                        for kd in range(DCH):
                            E.emit("tensor", lambda t, kd=kd, hcol=hcol, pm=pm: t.matmul(
                                pm[:], u1f[:, kd, :], t1[:, kd, hcol * 512:(hcol + 1) * 512],
                                start=(kd == 0), stop=(kd == DCH - 1)),
                                inc=("tp" if kd == DCH - 1 else None))
                        zs = zseq[0] % 2
                        gps = "gpa" if zs == 0 else "gpb"
                        gv, vv = zslot[zs]
                        E.wait("scalar", "tp")
                        E.wait("scalar", gps, gv)
                        E.wait("scalar", "vp", vv)
                        E.emit("scalar", lambda s, zs=zs, pm=pm: s.copy(
                            zstage[:, zs, :], pm[:]), inc="sp")
                        pmmS_free[hcol % 2] = E.val("sp")
                        E.wait("vector", "sp")
                        E.emit("vector", lambda v, zs=zs: v.reduce_max(
                            bcol(6), zstage[:, zs, :], axis=AX), inc="vp")
                        E.fence("vector", "vp")
                        E.emit("vector", lambda v: v.tensor_max(bcol(27), bcol(27), bcol(6)), inc="vp")
                        E.wait("gpsimd", "sp")
                        E.emit("gpsimd", lambda g, rt=rt, hcol=hcol, zs=zs: g.dma_start(
                            out=zd[rt, :, hcol * 512:(hcol + 1) * 512],
                            in_=zstage[:, zs, :]), inc=gps, n=16)
                        zslot[zs] = (E.val(gps), E.val("vp"))
                        zseq[0] += 1
                greduce(ALU.max, bcol(27), bcol(28), 0)

            if "C" in phases:
                barrier()
                c8flat = c8[:].rearrange("p a b -> p (a b)")
                E.emit("vector", lambda v: v.reciprocal(bcol(29), bcol(28)), inc="vp")
                E.fence("vector", "vp")
                E.emit("vector", lambda v: v.tensor_scalar(bcol(29), bcol(29), 127.0,
                                                           None, ALU.mult), inc="vp")
                E.fence("vector", "vp")
                zqslot = {0: 0, 1: 0}
                for rt in range(NT):
                    for qq in range(4):
                        zs = (rt * 4 + qq) % 2
                        lws = "lwa" if zs == 0 else "lwb"
                        qeng = "sync" if zs == 0 else "gpsimd"
                        E.wait(qeng, "vp", zqslot[zs])
                        E.emit(qeng, lambda s, rt=rt, qq=qq, zs=zs: s.dma_start(
                            out=zrt2[:, zs, :],
                            in_=zd[rt, :, qq * D:(qq + 1) * D]), inc=lws, n=16)
                        E.wait("vector", lws)
                        E.emit("vector", lambda v, zs=zs: v.tensor_scalar(
                            x2t[:, 0, :], zrt2[:, zs, :],
                            bcol(29), -127.0, ALU.mult, op1=ALU.max), inc="vp")
                        zqslot[zs] = E.val("vp")
                        E.emit("vector", lambda v, rt=rt, qq=qq: v.tensor_scalar(
                            c8flat[:, rt * HID + qq * D: rt * HID + (qq + 1) * D],
                            x2t[:, 0, :], 127.0, None,
                            ALU.min), inc="vp")
                def count_c8(taucol, dstcol, cmp=ALU.is_gt):
                    E.fence("vector", "vp")
                    off = -0.5 if cmp in (ALU.is_gt,) else 0.5
                    E.emit("vector", lambda v: v.tensor_scalar(
                        nstat[:, 13:14], taucol, -1.0, off, ALU.mult, op1=ALU.add), inc="vp")
                    E.fence("vector", "vp")
                    for q in range(8):
                        E.emit("vector", lambda v, q=q: v.tensor_scalar(
                            jI8[:], c8flat[:, q * 8192:(q + 1) * 8192], taucol, None,
                            cmp, op1=ALU.add, accum_out=cnts[:, q:q + 1]), inc="vp")
                    E.wait("scalar", "vp")
                    for q in range(8, 16):
                        E.emit("scalar", lambda s, q=q: s.activation(
                            jI8[:], c8flat[:, q * 8192:(q + 1) * 8192], AF.Sign,
                            bias=nstat[:, 13:14], accum_out=cnts[:, q:q + 1]), inc="sp")
                    E.wait("vector", "sp")
                    E.fence("vector", "vp")
                    E.emit("vector", lambda v: v.reduce_sum(bcol(25), cnts[:, 0:8], axis=AX), inc="vp")
                    E.emit("vector", lambda v: v.reduce_sum(bcol(22), cnts[:, 8:16], axis=AX), inc="vp")
                    E.fence("vector", "vp")
                    E.emit("vector", lambda v: v.tensor_scalar(bcol(22), bcol(22), 0.5,
                                                               32768.0, ALU.mult, op1=ALU.add), inc="vp")
                    E.fence("vector", "vp")
                    E.emit("vector", lambda v: v.tensor_add(bcol(25), bcol(25), bcol(22)), inc="vp")
                    greduce(ALU.add, bcol(25), dstcol, 0)

                def int_bisect(tgtcol, rounds=9):
                    E.emit("vector", lambda v: v.memset(bcol(0), -128.0), inc="vp")
                    E.emit("vector", lambda v: v.memset(bcol(1), 127.0), inc="vp")
                    for rnd in range(rounds):
                        E.fence("vector", "vp")
                        E.emit("vector", lambda v: v.tensor_add(bcol(2), bcol(0), bcol(1)), inc="vp")
                        E.fence("vector", "vp")
                        E.emit("vector", lambda v: v.tensor_scalar(bcol(2), bcol(2), 0.5,
                                                                   None, ALU.mult), inc="vp")
                        E.fence("vector", "vp")
                        E.emit("vector", lambda v: v.tensor_copy(x2t[0:128, 0, 0:1].bitcast(I32), bcol(2)), inc="vp")
                        E.fence("vector", "vp")
                        E.emit("vector", lambda v: v.tensor_copy(bcol(2), x2t[0:128, 0, 0:1].bitcast(I32)), inc="vp")
                        count_c8(bcol(2), bcol(4))
                        E.fence("vector", "vp")
                        E.emit("vector", lambda v: v.tensor_copy(bcol(6), tgtcol), inc="vp")
                        bis_select()

                quantile_targets(NSEQ * 4096 * HID, bcol(11), bcol(12), bcol(8), bcol(9))
                if q2_dbg is not None:
                    E.wait("gpsimd", "vp")
                    E.emit("gpsimd", lambda g: g.dma_start(out=q2_dbg[0], in_=bst[:]), inc="gpa", n=16)
                if q2_dbg is not None:
                    E.wait("gpsimd", "vp")
                    E.emit("gpsimd", lambda g: g.dma_start(out=c8_dbg[0], in_=c8[:, 0, :]), inc="gpa", n=16)
                    E.emit("gpsimd", lambda g: g.dma_start(out=c8_dbg[1], in_=c8[:, 1, :]), inc="gpa", n=16)
                int_bisect(bcol(8), 9)
                if q2_dbg is not None:
                    E.wait("gpsimd", "vp")
                    E.emit("gpsimd", lambda g: g.dma_start(out=q2_dbg[1], in_=bst[:]), inc="gpa", n=16)
                E.fence("vector", "vp")
                E.emit("vector", lambda v: v.tensor_mul(bcol(20), bcol(1), bcol(28)), inc="vp")
                E.fence("vector", "vp")
                E.emit("vector", lambda v: v.tensor_scalar(bcol(20), bcol(20), 1.0 / 127.0,
                                                           None, ALU.mult), inc="vp")
                E.fence("vector", "vp")
                E.emit("vector", lambda v: v.tensor_copy(x2t[0:128, 0, 0:1].bitcast(I32), bcol(20)), inc="vp")
                E.fence("vector", "vp")
                E.emit("vector", lambda v: v.tensor_copy(nstat[:, 15:16], x2t[0:128, 0, 0:1].bitcast(I32)), inc="vp")
                E.fence("vector", "vp")
                if q2_dbg is not None:
                    E.wait("gpsimd", "vp")
                    E.emit("gpsimd", lambda g: g.dma_start(out=q2_dbg[2], in_=bst[:]), inc="gpa", n=16)
                for rt in range(NT):
                    for qq in range(4):
                        zs = (rt * 4 + qq) % 2
                        lws = "lwa" if zs == 0 else "lwb"
                        qeng = "sync" if zs == 0 else "gpsimd"
                        E.wait(qeng, "vp", zqslot[zs])
                        E.emit(qeng, lambda s, rt=rt, qq=qq, zs=zs: s.dma_start(
                            out=zrt2[:, zs, :],
                            in_=zd[rt, :, qq * D:(qq + 1) * D]), inc=lws, n=16)
                        E.wait("vector", lws)
                        E.emit("vector", lambda v, zs=zs: v.tensor_scalar(
                            x2t[:, 0, :], zrt2[:, zs, :],
                            nstat[:, 15:16], -127.0, ALU.subtract, op1=ALU.max), inc="vp")
                        zqslot[zs] = E.val("vp")
                        E.emit("vector", lambda v, rt=rt, qq=qq: v.tensor_scalar(
                            c8flat[:, rt * HID + qq * D: rt * HID + (qq + 1) * D],
                            x2t[:, 0, :], 127.0, None,
                            ALU.min), inc="vp")
                int_bisect(bcol(8), 9)
                if q2_dbg is not None:
                    E.wait("gpsimd", "vp")
                    E.emit("gpsimd", lambda g: g.dma_start(out=q2_dbg[3], in_=bst[:]), inc="gpa", n=16)
                E.fence("vector", "vp")
                E.emit("vector", lambda v: v.tensor_add(bcol(13), nstat[:, 15:16], bcol(1)), inc="vp")
                E.fence("vector", "vp")
                E.emit("vector", lambda v: v.tensor_scalar(bcol(22), bcol(1), -1.0, None,
                                                           ALU.add), inc="vp")
                count_c8(bcol(22), bcol(4))
                E.fence("vector", "vp")
                E.emit("vector", lambda v: v.tensor_scalar(bcol(23), bcol(1), 1.0, 200.0,
                                                           ALU.mult, op1=ALU.add), inc="vp")
                E.fence("vector", "vp")
                def mm_bufs(q):
                    if q % 2 == 0:
                        return junk[:], junkS[:]
                    return x2t[:, 0, :], x2t[:, 1, :]
                for half in range(2):
                    for qi in range(65):
                        q = half * 64 + qi
                        if qi < 64:
                            ja, jb = mm_bufs(q)
                            E.emit("vector", lambda v, q=q, ja=ja: v.tensor_scalar(
                                ja, c8flat[:, q * 1024:(q + 1) * 1024], 200.0, None,
                                ALU.add), inc="vp")
                            E.emit("vector", lambda v, ja=ja, jb=jb: v.scalar_tensor_tensor(
                                jb, ja, bcol(23), ja, ALU.is_lt, ALU.mult), inc="vp")
                        if qi > 0:
                            _, jbp = mm_bufs(q - 1)
                            E.emit("vector", lambda v, q=q, jbp=jbp: v.reduce_max(
                                cnts[:, (q - 1) % 64:(q - 1) % 64 + 1], jbp, axis=AX), inc="vp")
                    E.fence("vector", "vp")
                    if half == 0:
                        E.emit("vector", lambda v: v.reduce_max(bcol(7), cnts[:, 0:64], axis=AX), inc="vp")
                    else:
                        E.emit("vector", lambda v: v.reduce_max(bcol(6), cnts[:, 0:64], axis=AX), inc="vp")
                        E.fence("vector", "vp")
                        E.emit("vector", lambda v: v.tensor_max(bcol(7), bcol(7), bcol(6)), inc="vp")
                greduce(ALU.max, bcol(7), bcol(14), 0)
                E.fence("vector", "vp")
                E.emit("vector", lambda v: v.tensor_scalar(bcol(14), bcol(14), 1.0, -200.0,
                                                           ALU.mult, op1=ALU.add), inc="vp")
                E.fence("vector", "vp")
                E.emit("vector", lambda v: v.tensor_add(bcol(14), bcol(14), nstat[:, 15:16]), inc="vp")
                E.fence("vector", "vp")
                E.emit("vector", lambda v: v.tensor_tensor(bcol(5), bcol(4), bcol(9), ALU.is_ge), inc="vp")
                E.fence("vector", "vp")
                E.emit("vector", lambda v: v.tensor_sub(bcol(21), bcol(13), bcol(14)), inc="vp")
                E.fence("vector", "vp")
                E.emit("vector", lambda v: v.tensor_mul(bcol(21), bcol(21), bcol(5)), inc="vp")
                E.fence("vector", "vp")
                E.emit("vector", lambda v: v.tensor_add(bcol(14), bcol(14), bcol(21)), inc="vp")
                E.fence("vector", "vp")
                E.emit("vector", lambda v: v.tensor_mul(bcol(18), bcol(19), bcol(15)), inc="vp")
                E.fence("vector", "vp")
                E.emit("vector", lambda v: v.tensor_scalar(bcol(18), bcol(18), 1.0 / 127.0,
                                                           None, ALU.mult), inc="vp")
                E.fence("vector", "vp")
                for (zc_col, out_col) in ((13, 13), (14, 14)):
                    E.emit("vector", lambda v, zc_col=zc_col: v.tensor_mul(
                        bcol(22), bcol(zc_col), bcol(18)), inc="vp")
                    E.wait("scalar", "vp")
                    E.emit("scalar", lambda s, out_col=out_col: s.activation(
                        bcol(out_col), bcol(22), AF.Gelu), inc="sp")
                    E.wait("vector", "sp")
                E.emit("vector", lambda v: v.tensor_mul(bcol(20), bcol(14), bcol(12)), inc="vp")
                E.emit("vector", lambda v: v.tensor_mul(bcol(21), bcol(13), bcol(11)), inc="vp")
                E.fence("vector", "vp")
                E.emit("vector", lambda v: v.tensor_add(bcol(30), bcol(20), bcol(21)), inc="vp")
                E.fence("vector", "vp")
                E.emit("vector", lambda v: v.tensor_scalar(bcol(30), bcol(30), EPS,
                                                           None, ALU.max), inc="vp")
                E.fence("vector", "vp")
                E.emit("vector", lambda v: v.reciprocal(bcol(31), bcol(30)), inc="vp")
                E.fence("vector", "vp")
                E.emit("vector", lambda v: v.tensor_scalar(bcol(31), bcol(31), 127.0,
                                                           None, ALU.mult), inc="vp")
                E.fence("vector", "vp")
                if st_dbg is not None:
                    E.wait("gpsimd", "vp")
                    E.emit("gpsimd", lambda g: g.dma_start(out=st_dbg, in_=bst[:]), inc="gpa", n=16)

            if "D" in phases:
                barrier()
                for ch in range(64):
                    kh, dcol = ch // 2, ch % 2
                    ws = ch % 2
                    lws = "lwa" if ws == 0 else "lwb"
                    E.wait("sync", "sp", wslot_free.get(("g2", ws), 0))
                    E.emit("sync", lambda s, kh=kh, dcol=dcol, ws=ws: s.dma_start(
                        out=wstage[:, ws, :],
                        in_=w2T[kh * 128:(kh + 1) * 128, dcol * 512:(dcol + 1) * 512]),
                        inc=lws, n=16)
                    E.wait("scalar", lws)
                    E.emit("scalar", lambda s, ch=ch, ws=ws: s.activation(
                        junkS[:, 0:512], wstage[:, ws, :], AF.Abs,
                        accum_out=cnts[:, ch % 64:ch % 64 + 1]), inc="sp")
                    wslot_free[("g2", ws)] = E.val("sp")
                E.wait("vector", "sp")
                E.emit("vector", lambda v: v.reduce_sum(bcol(6), cnts[:, 0:64], axis=AX), inc="vp")
                lreduce(ALU.add, bcol(6), bcol(19), 0)
                E.emit("vector", lambda v: v.tensor_scalar(
                    bcol(19), bcol(19), 1.0 / (D * HID), EPS, ALU.mult, op1=ALU.add), inc="vp")
                E.fence("vector", "vp")
                E.emit("vector", lambda v: v.reciprocal(bcol(26), bcol(19)), inc="vp")
                E.fence("vector", "vp")
                E.emit("vector", lambda v: v.tensor_mul(bcol(25), bcol(19), bcol(30)), inc="vp")
                E.fence("vector", "vp")
                E.emit("vector", lambda v: v.tensor_scalar(bcol(25), bcol(25), 1.0 / 127.0,
                                                           None, ALU.mult), inc="vp")
                E.emit("vector", lambda v: v.tensor_scalar(bcol(24), bcol(31), 0.5,
                                                           None, ALU.mult), inc="vp")
                E.fence("vector", "vp")
                wslot3 = {0: (0, 0), 1: (0, 0)}
                for ch in range(64):
                    kh, dcol = ch // 2, ch % 2
                    ws = ch % 2
                    lws = "lwa" if ws == 0 else "lwb"
                    vv, _ = wslot3[ws]
                    E.wait("sync", "vp", vv)
                    E.emit("sync", lambda s, kh=kh, dcol=dcol, ws=ws: s.dma_start(
                        out=wstage[:, ws, :],
                        in_=w2T[kh * 128:(kh + 1) * 128, dcol * 512:(dcol + 1) * 512]),
                        inc=lws, n=16)
                    E.wait("vector", lws)
                    E.emit("vector", lambda v, ws=ws: v.tensor_scalar(
                        junk[:, 0:512], wstage[:, ws, :], bcol(26), None, ALU.mult), inc="vp")
                    wslot3[ws] = (E.val("vp"), 0)
                    E.emit("vector", lambda v: v.tensor_copy(
                        x2t[:, 0, 0:512].bitcast(I32), junk[:, 0:512]), inc="vp")
                    E.emit("vector", lambda v, kh=kh, dcol=dcol: v.tensor_scalar(
                        t2[:, kh, dcol * 512:(dcol + 1) * 512],
                        x2t[:, 0, 0:512].bitcast(I32), -1.0, 1.0,
                        ALU.max, op1=ALU.min), inc="vp")
                zdslot = {0: (0, 0), 1: (0, 0)}
                oslot = {0: 0, 1: 0}
                trans_done = {0: 0, 1: 0}
                pair_free = {0: 0, 1: 0}
                chain_done = {}
                mm2_done = {}
                for sl0 in range(2):
                    pv2, ps2 = xslot_free[sl0]
                    E.wait("sync", "vp", pv2)
                    E.wait("sync", "sp", ps2)
                    E.emit("sync", lambda s, sl0=sl0: s.dma_start(
                        out=xt[:, sl0, :], in_=x2d[sl0]),
                        inc=("lxa" if sl0 == 0 else "lxb"), n=16)
                E.emit("sync", lambda s: s.dma_start(
                    out=zrt[:, 0, :], in_=zd[0]), inc="lwa", n=16)
                for it in range(NT + 1):
                    rt = it
                    if rt < NT:
                        zs = rt % 2
                        lws = "lwa" if zs == 0 else "lwb"
                        if rt + 1 < NT:
                            nzs = (rt + 1) % 2
                            nlws = "lwa" if nzs == 0 else "lwb"
                            vv, _ = zdslot[nzs]
                            E.wait("sync", "vp", vv)
                            E.emit("sync", lambda s, rt=rt, nzs=nzs: s.dma_start(
                                out=zrt[:, nzs, :], in_=zd[rt + 1]), inc=nlws, n=16)
                        E.wait("vector", lws)
                        E.wait("vector", "tp", trans_done[rt % 2])
                        mult_done = {}
                        gelu_done = {}

                        def pre_ap(p):
                            return u2st[:, 0, :] if p == 0 else ln1[:, 1, :]

                        def gout_ap(p):
                            return junkS[:] if p == 0 else junk[:]

                        def emit_mult(qq):
                            p = qq % 2
                            if qq >= 2:
                                E.wait("vector", "sp", gelu_done[qq - 2])
                            E.emit("vector", lambda v, zs=zs, qq=qq, p=p: v.tensor_scalar(
                                pre_ap(p), zrt[:, zs, qq * D:(qq + 1) * D],
                                bcol(18), None, ALU.mult), inc="vp")
                            mult_done[qq] = E.val("vp")
                            E.wait("scalar", "vp", mult_done[qq])
                            E.emit("scalar", lambda s, p=p: s.activation(
                                gout_ap(p), pre_ap(p), AF.Gelu), inc="sp")
                            gelu_done[qq] = E.val("sp")

                        def emit_post(qq):
                            p = qq % 2
                            E.wait("vector", "sp", gelu_done[qq])
                            E.emit("vector", lambda v, p=p: v.tensor_scalar(
                                u2st[:, 1, :], gout_ap(p), bcol(31), None, ALU.mult), inc="vp")
                            E.emit("vector", lambda v: v.tensor_copy(
                                ln1[:, 0, :].bitcast(I32), u2st[:, 1, :]), inc="vp")
                            E.emit("vector", lambda v, rt=rt, qq=qq: v.tensor_scalar(
                                u2tok[:, rt % 2, qq * D:(qq + 1) * D],
                                ln1[:, 0, :].bitcast(I32),
                                -128.0, 127.0, ALU.max, op1=ALU.min), inc="vp")

                        emit_mult(0)
                        emit_mult(1)
                        emit_post(0)
                        emit_mult(2)
                        emit_post(1)
                        emit_mult(3)
                        emit_post(2)
                        emit_post(3)
                        zdslot[zs] = (mult_done[3], 0)
                        chain_done[rt] = E.val("vp")
                        E.wait("tensor", "vp", chain_done[rt])
                        for kh in range(32):
                            pt = ptrs[kh % 4]
                            E.wait("tensor", "sp", ptr_free[kh % 4])
                            E.emit("tensor", lambda t, rt=rt, kh=kh, pt=pt: t.transpose(
                                pt[:].bitcast(BF16)[:, 0:128],
                                u2tok[:, rt % 2, kh * 128:(kh + 1) * 128], identb[:]), inc="tp")
                            E.wait("scalar", "tp")
                            E.emit("scalar", lambda s, kh=kh, pt=pt: s.copy(
                                u2fm[:, kh, :], pt[:].bitcast(BF16)[:, 0:128]), inc="sp")
                            ptr_free[kh % 4] = E.val("sp")
                        trans_done[rt % 2] = E.val("tp")
                        E.wait("tensor", "sp")
                        E.wait("tensor", "vp", pair_free[rt % 2])
                        pms = (pmm0, pmm1) if rt % 2 == 0 else (pmm2, pmm3)
                        for dcol in range(2):
                            pm = pms[dcol]
                            for kh in range(32):
                                E.emit("tensor", lambda t, kh=kh, dcol=dcol, pm=pm: t.matmul(
                                    pm[:], u2fm[:, kh, :], t2[:, kh, dcol * 512:(dcol + 1) * 512],
                                    start=(kh == 0), stop=(kh == 31)),
                                    inc=("tp" if kh == 31 else None))
                        mm2_done[rt] = E.val("tp")
                    if it == 0:
                        continue
                    pr = it - 1
                    n, b = pr // NB, pr % NB
                    pms = (pmm0, pmm1) if pr % 2 == 0 else (pmm2, pmm3)
                    E.wait("vector", "tp", mm2_done[pr])
                    E.emit("vector", lambda v, pms=pms: v.tensor_scalar(
                        o2b[:, 0:512], pms[0][:], bcol(25), None, ALU.mult), inc="vp")
                    E.emit("vector", lambda v, pms=pms: v.tensor_scalar(
                        o2b[:, 512:1024], pms[1][:], bcol(25), None, ALU.mult), inc="vp")
                    pair_free[pr % 2] = E.val("vp")
                    E.fence("vector", "vp")
                    E.emit("vector", lambda v: v.reduce_sum(nstat[:, 0:1], o2b[:], axis=AX), inc="vp")
                    E.emit("vector", lambda v: v.scalar_tensor_tensor(
                        junk[:], o2b[:], 1.0, o2b[:], ALU.mult, ALU.mult,
                        accum_out=nstat[:, 4:5]), inc="vp")
                    E.fence("vector", "vp")
                    E.emit("vector", lambda v: v.tensor_scalar(nstat[:, 8:9], nstat[:, 0:1],
                                                               1.0 / D, None, ALU.mult), inc="vp")
                    E.emit("vector", lambda v: v.tensor_scalar(nstat[:, 9:10], nstat[:, 4:5],
                                                               1.0 / D, None, ALU.mult), inc="vp")
                    E.fence("vector", "vp")
                    E.emit("vector", lambda v: v.tensor_mul(nstat[:, 10:11], nstat[:, 8:9],
                                                            nstat[:, 8:9]), inc="vp")
                    E.fence("vector", "vp")
                    E.emit("vector", lambda v: v.tensor_sub(nstat[:, 11:12], nstat[:, 9:10],
                                                            nstat[:, 10:11]), inc="vp")
                    E.fence("vector", "vp")
                    E.emit("vector", lambda v: v.tensor_scalar(nstat[:, 11:12], nstat[:, 11:12],
                                                               1.0, EPS, ALU.mult,
                                                               op1=ALU.add), inc="vp")
                    rsqrt_chain(11, 12, 13, 14, 1)
                    sl = pr % 2
                    lxs = "lxa" if sl == 0 else "lxb"
                    xp = pr % 2
                    E.wait("vector", "so", oslot[xp])
                    E.fence("vector", "vp")
                    E.emit("vector", lambda v, xp=xp: v.tensor_scalar(
                        x2t[:, xp, :], o2b[:], nstat[:, 8:9], nstat[:, 12:13],
                        ALU.subtract, op1=ALU.mult), inc="vp")
                    E.wait("vector", lxs)
                    E.emit("vector", lambda v, sl=sl, xp=xp: v.tensor_tensor(
                        x2t[:, xp, :], x2t[:, xp, :], xt[:, sl, :], ALU.add), inc="vp")
                    xslot_free[sl] = (E.val("vp"), E.val("sp"))
                    if pr + 2 < NT:
                        nsl = pr % 2
                        nlxs = "lxa" if nsl == 0 else "lxb"
                        vv2, ss2 = xslot_free[nsl]
                        E.wait("sync", "vp", vv2)
                        E.wait("sync", "sp", ss2)
                        E.emit("sync", lambda s, pr=pr, nsl=nsl: s.dma_start(
                            out=xt[:, nsl, :], in_=x2d[pr + 2]), inc=nlxs, n=16)
                    E.wait("sync", "vp")
                    E.emit("sync", lambda s, n=n, b=b, xp=xp: s.dma_start(
                        out=out[n, b * 128:(b + 1) * 128, :], in_=x2t[:, xp, :]), inc="so", n=16)
                    oslot[xp] = E.val("so")
                E.wait("sync", "so")
                E.wait("vector", "so")

            if "D" not in phases:
                barrier()
                for rt in range(NT):
                    n, b = rt // NB, rt % NB
                    E.emit("sync", lambda s, rt=rt: s.dma_start(
                        out=junk[:], in_=x2d[rt]), inc="lxa", n=16)
                    E.wait("sync", "lxa")
                    E.emit("sync", lambda s, n=n, b=b: s.dma_start(
                        out=out[n, b * 128:(b + 1) * 128, :], in_=junk[:]), inc="so", n=16)
                    E.wait("sync", "so")
            E.wait("sync", "so")
            E.wait("vector", "so")

        with nc.Block() as block:
            for eng in ENGINES:
                def make(eng):
                    def f(_q):
                        program(Emitter(nc, eng, handles))
                    return f
                getattr(block, eng)(make(eng))

    return nc


def _prep_inputs(L0_attn_x, L0_attn_Wqkv, L0_attn_bqkv, L0_attn_Wproj, L0_attn_bproj,
                 L0_ffn_ln_g, L0_ffn_ln_b, L0_ffn_w1, L0_ffn_w2):
    x = np.ascontiguousarray(L0_attn_x, dtype=np.float32)
    shared = {
        "wqkvT": np.ascontiguousarray(np.asarray(L0_attn_Wqkv, dtype=np.float32).T),
        "bqkv": np.ascontiguousarray(np.asarray(L0_attn_bqkv, dtype=np.float32).reshape(1, -1)),
        "wprojT": np.ascontiguousarray(np.asarray(L0_attn_Wproj, dtype=np.float32).T),
        "bproj": np.ascontiguousarray(np.asarray(L0_attn_bproj, dtype=np.float32).reshape(1, -1)),
        "ln_g": np.ascontiguousarray(np.asarray(L0_ffn_ln_g, dtype=np.float32).reshape(1, -1)),
        "ln_b": np.ascontiguousarray(np.asarray(L0_ffn_ln_b, dtype=np.float32).reshape(1, -1)),
        "w1T": np.ascontiguousarray(np.asarray(L0_ffn_w1, dtype=np.float32).T),
        "w2T": np.ascontiguousarray(np.asarray(L0_ffn_w2, dtype=np.float32).T),
    }
    in_maps = []
    for c in range(NCORES):
        m = dict(shared)
        m["x"] = np.ascontiguousarray(x[:, c * T:(c + 1) * T, :])
        in_maps.append(m)
    return in_maps


def kernel(**inputs):
    in_maps = _prep_inputs(**inputs)
    nc = build_program()
    res = run_bass_kernel_spmd(nc, in_maps, core_ids=list(range(NCORES)))
    shards = [np.asarray(res.results[c]["out"]) for c in range(NCORES)]
    return np.concatenate(shards, axis=1)
